# version 05
# baseline (speedup 1.0000x reference)
import numpy as np
from contextlib import ExitStack
import concourse.bass as bass
import concourse.mybir as mybir
from concourse.bass_utils import run_bass_kernel_spmd
import ml_dtypes

F32 = mybir.dt.float32
BF16 = mybir.dt.bfloat16
AF = mybir.ActivationFunctionType
ALU = mybir.AluOpType
AX = mybir.AxisListType
NPBF = ml_dtypes.bfloat16

D = 1024
T = 4096
DT = D // 128
CH = 512
NCH = T // CH
NCORE = 8
FMAX = np.float64(np.finfo(np.float32).max)

ENGINES = ("tensor", "vector", "scalar", "sync", "gpsimd")


class K:

    def __init__(self, nc, ctx):
        self.nc = nc
        self.ctx = ctx
        self.q = {e: [] for e in ENGINES}
        self.sems = {}
        self.val = {}
        self.dirty = {e: False for e in ENGINES}
        self.pb = None

    def sem(self, name):
        if name not in self.sems:
            self.sems[name] = self.ctx.enter_context(self.nc.semaphore(name))
            self.val[name] = 0
        return self.sems[name]

    def v(self, name):
        return self.val.get(name, 0)

    def on(self, eng, fn, wait=(), inc=None):
        incs = []
        if inc:
            if isinstance(inc, tuple):
                inc = [inc]
            for name, amt in inc:
                self.sem(name)
                self.val[name] += amt
                incs.append((name, amt))
        wl = [(s, v) for (s, v) in wait if v > 0]
        for name, _v in wl:
            self.sem(name)
        self.q[eng].append((fn, wl, incs))
        self.dirty[eng] = True
        return self.val[incs[0][0]] if incs else None

    def dma(self, eng, out, in_, sem_name, wait=(), **kw):
        def fn(e, out=out, in_=in_, kw=kw):
            return e.dma_start(out=out, in_=in_, **kw)
        return self.on(eng, fn, wait=wait, inc=(sem_name, 16))

    def barrier(self):
        bar = "bar"
        self.sem(bar)
        for e in ("tensor", "vector", "scalar", "gpsimd"):
            if self.dirty[e]:
                if e == "tensor":
                    self.on(e, lambda en: en.nop(), inc=(bar, 1))
                else:
                    self.on(e, lambda en: en.drain(), inc=(bar, 1))
        target = self.val.get(bar, 0)
        dma_waits = [(s, self.val[s]) for s in list(self.val)
                     if s.startswith("d_") and self.val[s] > 0]
        for e in ENGINES:
            self.q[e].append((None, [(bar, target)] + dma_waits, []))
            self.dirty[e] = False

    def run_engine(self, e, eng_obj):
        for fn, waits, incs in self.q[e]:
            for s, v in waits:
                eng_obj.wait_ge(self.sems[s], v)
            if fn is None:
                continue
            inst = fn(eng_obj)
            for s, a in incs:
                inst.then_inc(self.sems[s], a)

    def finalize(self, block):
        block.tensor(lambda e: self.run_engine("tensor", e))
        block.vector(lambda e: self.run_engine("vector", e))
        block.scalar(lambda e: self.run_engine("scalar", e))
        block.sync(lambda e: self.run_engine("sync", e))
        block.gpsimd(lambda e: self.run_engine("gpsimd", e))


def _prep_shared(ip):
    sh = {}
    f32 = np.float32
    sh["w_qkv"] = np.ascontiguousarray(ip["L0_attn_W_qkv"], f32)
    sh["b_qkv_t"] = np.ascontiguousarray(
        np.asarray(ip["L0_attn_b_qkv"], f32).reshape(24, 128).T)
    sh["w_out"] = np.ascontiguousarray(ip["L0_attn_W_out"], f32)
    sh["b_out_t"] = np.ascontiguousarray(
        np.asarray(ip["L0_attn_b_out"], f32).reshape(8, 128).T)
    rb = np.asarray(ip["L0_attn_rel_bias"], f32)
    u = np.arange(3072)[None, :] - 1536
    jp = np.arange(128)[:, None]
    idx = np.clip(u - jp + 1024, 0, 2048)
    sh["strips"] = np.ascontiguousarray(rb[idx].transpose(2, 0, 1))
    sh["rb_lo"] = np.ascontiguousarray(np.broadcast_to(rb[0][None, :], (128, 16)))
    sh["rb_hi"] = np.ascontiguousarray(np.broadcast_to(rb[2048][None, :], (128, 16)))
    sh["w_gate"] = np.ascontiguousarray(ip["L0_ffn_W_gate"], f32)
    sh["b_gate"] = np.ascontiguousarray(np.asarray(ip["L0_ffn_b_gate"], f32).reshape(1, 8))
    sh["moe_w1"] = np.ascontiguousarray(ip["L0_ffn_W1"], f32)
    sh["moe_b1c"] = np.ascontiguousarray(
        np.asarray(ip["L0_ffn_b1"], f32).reshape(8, 8, 128).transpose(2, 0, 1).reshape(128, 64))
    sh["moe_w2"] = np.ascontiguousarray(ip["L0_ffn_W2"], f32)
    sh["moe_b2"] = np.ascontiguousarray(ip["L0_ffn_b2"], f32)
    for nm in ("Wq", "Wk", "Wv", "Wo"):
        sh["p_" + nm] = np.ascontiguousarray(ip["L1_attn_" + nm], f32)
    for nm in ("bq", "bk", "bv", "bo"):
        sh["p_" + nm + "_t"] = np.ascontiguousarray(
            np.asarray(ip["L1_attn_" + nm], f32).reshape(8, 128).T)
    rm_ = np.asarray(ip["L1_attn_random_matrix"], f32)
    sh["p_rm"] = np.ascontiguousarray(np.concatenate([rm_, rm_], axis=0))
    sh["lf_wqkv"] = np.ascontiguousarray(ip["L1_ffn_wqkv"], f32)
    sh["lf_qs"] = np.ascontiguousarray(ip["L1_ffn_q_scale"], f32)
    sh["lf_ks"] = np.ascontiguousarray(ip["L1_ffn_k_scale"], f32)
    sh["lf_wo"] = np.ascontiguousarray(ip["L1_ffn_wo"], f32)
    sh["lf_lng_t"] = np.ascontiguousarray(
        np.asarray(ip["L1_ffn_ln_g"], f32).reshape(2, 8, 128).transpose(0, 2, 1))
    sh["lf_lnb_t"] = np.ascontiguousarray(
        np.asarray(ip["L1_ffn_ln_b"], f32).reshape(2, 8, 128).transpose(0, 2, 1))
    HID = 2730
    HIDP = 2816
    w1 = np.asarray(ip["L1_ffn_w1"], f32)
    w1p = np.zeros((2, 1024, 2 * HIDP), f32)
    w1p[:, :, 0:HID] = w1[:, :, 0:HID]
    w1p[:, :, HIDP:HIDP + HID] = w1[:, :, HID:2 * HID]
    sh["lf_w1p"] = w1p
    b1 = np.asarray(ip["L1_ffn_b1"], f32)
    b1p = np.zeros((2, 2 * HIDP), f32)
    b1p[:, 0:HID] = b1[:, 0:HID]
    b1p[:, HIDP:HIDP + HID] = b1[:, HID:2 * HID]
    sh["lf_b1p_t"] = np.ascontiguousarray(
        b1p.reshape(2, 44, 128).transpose(0, 2, 1))
    w2 = np.asarray(ip["L1_ffn_w2"], f32)
    w2p = np.zeros((2, HIDP, 1024), f32)
    w2p[:, 0:HID, :] = w2
    sh["lf_w2p"] = w2p
    sh["lf_b2_t"] = np.ascontiguousarray(
        np.asarray(ip["L1_ffn_b2"], f32).reshape(2, 8, 128).transpose(0, 2, 1))
    DIM_HEAD, WINDOW = 64, 512
    half = np.arange(0, DIM_HEAD, 2, dtype=np.float64)
    inv_freq = 1.0 / 10000.0 ** (half / DIM_HEAD)
    t = np.arange(T, dtype=np.float64)
    freqs = np.concatenate([t[:, None] * inv_freq[None, :]] * 2, axis=1)
    base = (half + 0.4 * DIM_HEAD) / (1.4 * DIM_HEAD)
    power = (t - T // 2) / float(WINDOW // 2)
    scale = np.concatenate([base[None, :] ** power[:, None]] * 2, axis=1)
    cos, sin = np.cos(freqs), np.sin(freqs)
    sign = np.ones(64); sign[:32] = -1.0
    sh["rot_cq"] = np.ascontiguousarray((cos * scale).T * 8.0, f32)
    sh["rot_sq"] = np.ascontiguousarray((sin * scale * sign[None, :]).T * 8.0, f32)
    sh["rot_ck"] = np.ascontiguousarray((cos / scale).T, f32)
    sh["rot_sk"] = np.ascontiguousarray((sin / scale * sign[None, :]).T, f32)
    il = np.arange(512)[:, None]
    jl = np.arange(1024)[None, :]
    dd = il - jl + 512
    m = np.zeros((512, 1024), f32)
    m[~((dd >= 0) & (dd <= 512))] = -1e30
    sh["la_mask"] = m
    sh["ident"] = np.eye(128, dtype=f32)
    em = np.zeros((8, 8, 128), f32)
    for e in range(8):
        em[e, e, :] = 1.0
    sh["emask8"] = np.ascontiguousarray(em.transpose(1, 0, 2).reshape(8, 1024))
    return sh


def st_layernorm(k, tens, sb, src, dst_bf, dst_f32=None):
    pb = k.pb
    x = tens[src]
    mu, ms, rr = sb["row_mu"], sb["row_ms"], sb["row_r"]
    sq_done, cp_done, mmB = [0] * DT, [0] * DT, [0] * DT
    ld_done = [0] * DT
    for d in range(DT):
        b2 = d % 2
        xf, xbf, xsq = (sb["xf"][:, b2 * T:(b2 + 1) * T],
                        sb["xbf"][:, b2 * T:(b2 + 1) * T],
                        sb["xsq"][:, b2 * T:(b2 + 1) * T])
        w = []
        if d >= 2:
            w = [("ac", sq_done[d - 2]), ("ve", cp_done[d - 2])]
        ld_done[d] = k.dma("sync", xf, x[d * 128:(d + 1) * 128, :], "d_a%d" % (d % 2), wait=w)
        wpe = [("pe", mmB[d - 2])] if d >= 2 else []
        sq_done[d] = k.on("scalar", lambda e, xf=xf, xsq=xsq: e.activation(xsq, xf, AF.Square),
                          wait=[("d_a%d" % (d % 2), ld_done[d])] + wpe, inc=("ac", 1))
        cp_done[d] = k.on("vector", lambda e, xf=xf, xbf=xbf: e.tensor_copy(xbf, xf),
                          wait=[("d_a%d" % (d % 2), ld_done[d])] + wpe, inc=("ve", 1))
        for ch in range(NCH):
            row, bank = (ch % 3) * 32, ch // 3
            k.on("tensor", lambda e, xbf=xbf, ch=ch, row=row, bank=bank, d=d: e.matmul(
                pb[bank][row:row + 1, :], sb["onesb"][:], xbf[:, ch * CH:(ch + 1) * CH],
                start=(d == 0), stop=(d == DT - 1)),
                wait=[("ve", cp_done[d])] if ch == 0 else [])
            k.on("tensor", lambda e, xsq=xsq, ch=ch, row=row, bank=bank, d=d: e.matmul(
                pb[3 + bank][row:row + 1, :], sb["onesb"][:], xsq[:, ch * CH:(ch + 1) * CH],
                start=(d == 0), stop=(d == DT - 1)),
                wait=[("ac", sq_done[d])] if ch == 0 else [],
                inc=("pe", 1) if ch == NCH - 1 else None)
        mmB[d] = k.v("pe")
    pe_stats = mmB[DT - 1]
    for ch in range(NCH):
        row, bank = (ch % 3) * 32, ch // 3
        k.on("scalar", lambda e, ch=ch, row=row, bank=bank: e.activation(
            mu[0:1, ch * CH:(ch + 1) * CH], pb[bank][row:row + 1, :], AF.Copy, scale=1.0 / D),
            wait=[("pe", pe_stats)] if ch == 0 else [], inc=("ac", 1))
        k.on("scalar", lambda e, ch=ch, row=row, bank=bank: e.activation(
            ms[0:1, ch * CH:(ch + 1) * CH], pb[3 + bank][row:row + 1, :], AF.Copy, scale=1.0 / D),
            inc=("ac", 1))
    ac_mu = k.v("ac")
    k.on("vector", lambda e: e.tensor_tensor(rr[:], mu[:], mu[:], ALU.mult),
         wait=[("ac", ac_mu)], inc=("ve", 1))
    k.on("vector", lambda e: e.tensor_tensor(rr[:], ms[:], rr[:], ALU.subtract),
         wait=[("ve", k.v("ve"))], inc=("ve", 1))
    ve_var = k.v("ve")
    k.on("scalar", lambda e: e.activation(rr[:], rr[:], AF.Sqrt, bias=sb["c_eps"][0:1, :]),
         wait=[("ve", ve_var)], inc=("ac", 1))
    k.on("vector", lambda e: e.reciprocal(rr[:], rr[:]),
         wait=[("ac", k.v("ac"))], inc=("ve", 1))
    ve_r = k.v("ve")
    ac_r = k.v("ac")
    bc_mm = [0] * NCH
    for ch in range(NCH):
        ba = 4 + 2 * (ch % 2)
        wmm = [("ac", ac_mu), ("ve", ve_r)]
        if ch >= 2:
            wmm.append(("ac", bc_mm[ch - 2]))
        k.on("tensor", lambda e, ch=ch, ba=ba: e.matmul(
            pb[ba][:], sb["ones1f"][0:1, :], mu[0:1, ch * CH:(ch + 1) * CH], start=True, stop=True),
            wait=wmm, inc=("pe", 1))
        pev = k.on("tensor", lambda e, ch=ch, ba=ba: e.matmul(
            pb[ba + 1][:], sb["ones1f"][0:1, :], rr[0:1, ch * CH:(ch + 1) * CH], start=True, stop=True),
            inc=("pe", 1))
        k.on("scalar", lambda e, ch=ch, ba=ba: e.copy(
            sb["mub"][:, ch * CH:(ch + 1) * CH], pb[ba][:]),
            wait=[("pe", pev)], inc=("ac", 1))
        bc_mm[ch] = k.on("scalar", lambda e, ch=ch, ba=ba: e.copy(
            sb["rb"][:, ch * CH:(ch + 1) * CH], pb[ba + 1][:]), inc=("ac", 1))
    ac_bc = k.v("ac")
    tt_done, st_done = [0] * DT, [0] * DT
    for d in range(DT):
        b2 = d % 2
        xf2 = sb["xf"][:, b2 * T:(b2 + 1) * T]
        t0 = sb["t0"][:, b2 * T:(b2 + 1) * T]
        tb = sb["tb"][:, b2 * T:(b2 + 1) * T]
        w = [("ve", cp_done[DT - 1])]
        if d >= 2:
            w.append(("ve", tt_done[d - 2]))
        ld = k.dma("sync", xf2, x[d * 128:(d + 1) * 128, :], "d_a%d" % (d % 2), wait=w)
        wst = [("d_so%d" % (d % 2), st_done[d - 2])] if d >= 2 else []
        k.on("vector", lambda e, xf2=xf2, t0=t0: e.tensor_tensor(t0, xf2, sb["mub"][:], ALU.subtract),
             wait=[("d_a%d" % (d % 2), ld), ("ac", ac_bc)] + wst, inc=("ve", 1))
        if dst_f32 is not None:
            k.on("vector", lambda e, t0=t0: e.tensor_tensor(t0, t0, sb["rb"][:], ALU.mult),
                 wait=[("ve", k.v("ve"))], inc=("ve", 1))
            vv = k.on("vector", lambda e, t0=t0, tb=tb: e.tensor_copy(tb, t0),
                      wait=[("ve", k.v("ve"))], inc=("ve", 1))
            k.dma("gpsimd", tens[dst_f32][d * 128:(d + 1) * 128, :], t0, "d_sp%d" % (d % 2),
                  wait=[("ve", vv)])
        else:
            vv = k.on("vector", lambda e, t0=t0, tb=tb: e.tensor_tensor(tb, t0, sb["rb"][:], ALU.mult),
                      wait=[("ve", k.v("ve"))], inc=("ve", 1))
        tt_done[d] = vv
        st_done[d] = k.dma("gpsimd", tens[dst_bf][d * 128:(d + 1) * 128, :], tb, "d_so%d" % (d % 2),
                           wait=[("ve", vv)])
    k.barrier()


LN_SB = [
    ("xf", (128, 2 * T), F32), ("xbf", (128, 2 * T), BF16), ("xsq", (128, 2 * T), BF16),
    ("t0", (128, 2 * T), F32), ("tb", (128, 2 * T), BF16),
    ("mub", (128, T), F32), ("rb", (128, T), F32),
    ("row_mu", (1, T), F32), ("row_ms", (1, T), F32), ("row_r", (1, T), F32),
]




def st_l0_attn(k, tens, sb, xn_src, x_src, dst):
    pb = k.pb
    xn = sb["xn_sb"]
    for d in range(DT):
        k.dma("sync", xn[:, d * T:(d + 1) * T], tens[xn_src][d * 128:(d + 1) * 128, :], "d_xn")
    ld_xn = k.v("d_xn")
    k.dma("sync", sb["bqkv"][:], tens["b_qkv_t"], "d_c")
    k.dma("sync", sb["bout"][:], tens["b_out_t"], "d_c")
    k.dma("sync", sb["rblo"][:], tens["rb_lo"], "d_c")
    k.dma("sync", sb["rbhi"][:], tens["rb_hi"], "d_c")
    k.dma("gpsimd", sb["id_bf"][:], tens["ident"], "d_cg")
    ld_c = k.v("d_c")
    k.on("vector", lambda e: e.memset(sb["onesrow"][:], 1.0), inc=("ve", 1))
    k.on("vector", lambda e: e.memset(sb["ka0"][64:65, :], 1.0), inc=("ve", 1))
    k.on("vector", lambda e: e.memset(sb["ka1"][64:65, :], 1.0), inc=("ve", 1))
    for tt in range(32):
        k.on("vector", lambda e, tt=tt: e.memset(sb["vp"][:, 130 * tt + 64:130 * tt + 65], 1.0))
        k.on("vector", lambda e, tt=tt: e.memset(sb["vp"][:, 130 * tt + 129:130 * tt + 130], 1.0))
    k.on("vector", lambda e: e.tensor_scalar(sb["bq8"][:], sb["bqkv"][:, 0:8], 0.125, None, ALU.mult),
         wait=[("d_c", ld_c)], inc=("ve", 1))
    k.barrier()

    wvq = tens["w_qkv"]
    for p in range(8):
        k.barrier()
        for nm, c0 in (("wq_sb", 128 * p), ("wk_sb", 1024 + 128 * p), ("wv_sb", 2048 + 128 * p)):
            src = wvq[:, c0:c0 + 128].rearrange("(a p) c -> p a c", p=128)
            k.dma("gpsimd", sb[nm][:], src, "d_w")
        k.dma("gpsimd", sb["bv_row"][0:1, :], tens["b_qkv_raw"][0:1, 2048 + 128 * p:2176 + 128 * p], "d_w")
        ld_w = k.v("d_w")
        for which, wsb, scale in (("q", "wq_sb", 0.125), ("kk", "wk_sb", 1.0)):
            for ch in range(NCH):
                bank = 6 + ch % 2
                w = [("d_w", ld_w), ("d_xn", ld_xn), ("ac", k.v("ac"))]
                pev = None
                for dm in range(DT):
                    pev = k.on("tensor", lambda e, ch=ch, dm=dm, bank=bank, wsb=wsb: e.matmul(
                        pb[bank][:], sb[wsb][:, 128 * dm:128 * dm + 128],
                        xn[:, T * dm + CH * ch: T * dm + CH * (ch + 1)],
                        start=(dm == 0), stop=(dm == DT - 1)),
                        wait=w if dm == 0 else [],
                        inc=("pe", 1) if dm == DT - 1 else None)
                for hl in range(2):
                    dstt = sb[("qa%d" if which == "q" else "ka%d") % hl]
                    if which == "q":
                        bias_ap = sb["bq8"][64 * hl:64 * hl + 64, p:p + 1]
                    else:
                        bias_ap = sb["bqkv"][64 * hl:64 * hl + 64, 8 + p:9 + p]
                    k.on("scalar", lambda e, dstt=dstt, ch=ch, bank=bank, hl=hl, bias_ap=bias_ap, scale=scale: e.activation(
                        dstt[0:64, CH * ch:CH * (ch + 1)], pb[bank][64 * hl:64 * hl + 64, :],
                        AF.Identity, bias=bias_ap, scale=scale),
                        wait=[("pe", pev), ("ve", k.v("ve"))] if hl == 0 else [], inc=("ac", 1))
        for tt in range(32):
            bank = 6 + tt % 2
            w = [("d_w", ld_w), ("d_xn", ld_xn), ("ac", k.v("ac"))]
            for dm in range(DT):
                k.on("tensor", lambda e, tt=tt, dm=dm, bank=bank: e.matmul(
                    pb[bank][:, 0:128], xn[:, T * dm + 128 * tt: T * dm + 128 * (tt + 1)],
                    sb["wv_sb"][:, 128 * dm:128 * dm + 128],
                    start=(dm == 0), stop=False),
                    wait=w if dm == 0 else [])
            pev = k.on("tensor", lambda e, tt=tt, bank=bank: e.matmul(
                pb[bank][:, 0:128], sb["onesrow"][0:1, :], sb["bv_row"][0:1, :],
                start=False, stop=True), inc=("pe", 1))
            for hl in range(2):
                k.on("scalar", lambda e, tt=tt, bank=bank, hl=hl: e.copy(
                    sb["vp"][:, 130 * tt + 65 * hl:130 * tt + 65 * hl + 64],
                    pb[bank][:, 64 * hl:64 * hl + 64]),
                    wait=[("pe", pev)] if hl == 0 else [], inc=("ac", 1))
        for hl in range(2):
            h = 2 * p + hl
            k.barrier()
            qa, ka = sb["qa%d" % hl], sb["ka%d" % hl]
            lds = k.dma("gpsimd", sb["strip"][:], tens["strips"][h], "d_sp")
            k.on("vector", lambda e: e.reduce_max(sb["scol"][:], sb["strip"][:], axis=AX.X),
                 wait=[("d_sp", lds)], inc=("ve", 1))
            k.dma("gpsimd", sb["srow"][0:1, :], sb["scol"][:, 0:1], "d_sr",
                  wait=[("ve", k.v("ve"))], allow_slow_non_contiguous=True)
            k.on("vector", lambda e: e.reduce_max(sb["bmax"][0:1, 0:1], sb["srow"][0:1, :], axis=AX.X),
                 wait=[("d_sr", k.v("d_sr"))], inc=("ve", 1))
            k.on("scalar", lambda e, qa=qa: e.activation(sb["sq_sb"][0:64, :], qa[0:64, :], AF.Square),
                 inc=("ac", 1))
            k.on("scalar", lambda e, ka=ka: e.activation(sb["sq_sb"][64:128, :], ka[0:64, :], AF.Square),
                 inc=("ac", 1))
            ac_sq = k.v("ac")
            for ch in range(NCH):
                row, bank = (ch % 3) * 32, ch // 3
                k.on("tensor", lambda e, ch=ch, row=row, bank=bank: e.matmul(
                    pb[bank][row:row + 1, :], sb["onesb"][0:64, :], sb["sq_sb"][0:64, CH * ch:CH * (ch + 1)],
                    start=True, stop=True), wait=[("ac", ac_sq)] if ch == 0 else [])
                k.on("tensor", lambda e, ch=ch, row=row, bank=bank: e.matmul(
                    pb[3 + bank][row:row + 1, :], sb["onesb"][64:128, :], sb["sq_sb"][64:128, CH * ch:CH * (ch + 1)],
                    start=True, stop=True),
                    inc=("pe", 1) if ch == NCH - 1 else None)
            pev = k.v("pe")
            for ch in range(NCH):
                row, bank = (ch % 3) * 32, ch // 3
                k.on("scalar", lambda e, ch=ch, row=row, bank=bank: e.copy(
                    sb["rows_sb"][0:1, CH * ch:CH * (ch + 1)], pb[bank][row:row + 1, :]),
                    wait=[("pe", pev)] if ch == 0 else [], inc=("ac", 1))
                k.on("scalar", lambda e, ch=ch, row=row, bank=bank: e.copy(
                    sb["rows_sb"][32:33, CH * ch:CH * (ch + 1)], pb[3 + bank][row:row + 1, :]),
                    inc=("ac", 1))
            ac_n = k.v("ac")
            k.on("scalar", lambda e: e.activation(sb["rows_sb"][0:1, :], sb["rows_sb"][0:1, :], AF.Sqrt),
                 wait=[("ac", k.v("ac"))], inc=("ac", 1))
            k.on("vector", lambda e: e.reduce_max(sb["kmax"][0:1, 0:1], sb["rows_sb"][32:33, :], axis=AX.X),
                 wait=[("ac", ac_n)], inc=("ve", 1))
            k.on("scalar", lambda e: e.activation(sb["kmax"][0:1, 0:1], sb["kmax"][0:1, 0:1], AF.Sqrt),
                 wait=[("ve", k.v("ve"))], inc=("ac", 1))
            k.on("vector", lambda e: e.tensor_scalar(
                sb["rows_sb"][64:65, :], sb["rows_sb"][0:1, :], sb["kmax"][0:1, 0:1], sb["bmax"][0:1, 0:1],
                ALU.mult, ALU.add), wait=[("ac", k.v("ac")), ("ve", k.v("ve"))], inc=("ve", 1))
            k.on("vector", lambda e, qa=qa: e.tensor_scalar(
                qa[64:65, :], sb["rows_sb"][64:65, :], -1.0, None, ALU.mult),
                wait=[("ve", k.v("ve"))], inc=("ve", 1))
            k.barrier()
            exp_cnt, ev_cnt, tt_done, ot_st = {}, {}, {}, {}
            for qc in range(NCH):
                ob = 2 + qc % 2
                for jt in range(32):
                    sbk = jt % 2
                    m0 = 512 * qc - 128 * jt + 1536
                    inband = (0 <= m0 <= 3072 - 512)
                    w = []
                    if (qc, jt - 2) in exp_cnt:
                        w.append(("ac", exp_cnt[(qc, jt - 2)]))
                    elif (qc - 1, 30 + jt) in exp_cnt:
                        w.append(("ac", exp_cnt[(qc - 1, 30 + jt)]))
                    k.on("tensor", lambda e, qa=qa, ka=ka, jt=jt, qc=qc, sbk=sbk, inband=inband: e.matmul(
                        pb[sbk][:], ka[:, 128 * jt:128 * (jt + 1)], qa[:, CH * qc:CH * (qc + 1)],
                        start=True, stop=not inband), wait=w,
                        inc=None if inband else ("pe", 1))
                    if inband:
                        k.on("tensor", lambda e, jt=jt, sbk=sbk, m0=m0: e.matmul(
                            pb[sbk][:], sb["id_bf"][:], sb["strip"][:, m0:m0 + 512],
                            start=False, stop=True), inc=("pe", 1))
                    pev = k.v("pe")
                    we = [("pe", pev)]
                    if (qc, jt - 2) in ev_cnt:
                        we.append(("pe", ev_cnt[(qc, jt - 2)]))
                    elif (qc - 1, 30 + jt) in ev_cnt:
                        we.append(("pe", ev_cnt[(qc - 1, 30 + jt)]))
                    def fexp(e, sbk=sbk, jt=jt, inband=inband, m0=m0, h=h):
                        eo = sb["e_sb"][:, CH * (jt % 2):CH * (jt % 2 + 1)]
                        if inband:
                            return e.activation(eo, pb[sbk][:], AF.Exp)
                        b = sb["rbhi" if m0 > 0 else "rblo"][:, h:h + 1]
                        return e.activation(eo, pb[sbk][:], AF.Exp, bias=b)
                    exp_cnt[(qc, jt)] = k.on("scalar", fexp, wait=we, inc=("ac", 1))
                    ev_cnt[(qc, jt)] = k.on("tensor", lambda e, jt=jt, qc=qc, hl=hl, ob=ob: e.matmul(
                        pb[ob][0:65, :], sb["vp"][:, 130 * jt + 65 * hl:130 * jt + 65 * hl + 65],
                        sb["e_sb"][:, CH * (jt % 2):CH * (jt % 2 + 1)],
                        start=(jt == 0), stop=(jt == 31)),
                        wait=[("ac", exp_cnt[(qc, jt)])], inc=("pe", 1))
                db = 4 + qc % 2
                wd0 = [("pe", ev_cnt[(qc, 31)])]
                if (qc - 1) in tt_done:
                    wd0.append(("ve", tt_done[qc - 1]))
                k.on("scalar", lambda e, ob=ob: e.copy(sb["den"][0:1, :], pb[ob][64:65, :]),
                     wait=wd0, inc=("ac", 1))
                k.on("scalar", lambda e, ob=ob: e.copy(sb["onum"][0:64, :], pb[ob][0:64, :]),
                     inc=("ac", 1))
                ac_ev = k.v("ac")
                k.on("vector", lambda e: e.reciprocal(sb["den"][0:1, :], sb["den"][0:1, :]),
                     wait=[("ac", ac_ev)], inc=("ve", 1))
                wd = [("ve", k.v("ve"))]
                if (qc - 2) in tt_done:
                    wd.append(("ve", tt_done[qc - 2]))
                pdv = k.on("tensor", lambda e, db=db: e.matmul(
                    pb[db][0:64, :], sb["ones1f"][0:1, 0:64], sb["den"][0:1, :],
                    start=True, stop=True), wait=wd, inc=("pe", 1))
                wtt = [("pe", pdv), ("ac", ac_ev)]
                if (qc - 2) in tt_done:
                    wtt.append(("d_ot%d" % (qc % 2), ot_st[qc - 2]))
                tt_done[qc] = k.on("vector", lambda e, qc=qc, db=db: e.tensor_tensor(
                    sb["ot_tmp"][0:64, CH * (qc % 2):CH * (qc % 2 + 1)],
                    sb["onum"][0:64, :], pb[db][0:64, :], ALU.mult),
                    wait=wtt, inc=("ve", 1))
                ot_st[qc] = k.dma("gpsimd", tens["otT"][64 * h:64 * h + 64, CH * qc:CH * (qc + 1)],
                                  sb["ot_tmp"][0:64, CH * (qc % 2):CH * (qc % 2 + 1)], "d_ot%d" % (qc % 2),
                                  wait=[("ve", tt_done[qc])])
    k.barrier()
    wsrc = tens["w_out"].rearrange("(a p) (b c) -> p a b c", p=128, c=128)
    ldwo = k.dma("gpsimd", sb["wo_full"][:], wsrc, "d_wf")
    mm_done = [0] * NCH
    st_done = [0] * NCH
    for ch in range(NCH):
        w = [("pe", mm_done[ch - 1])] if ch >= 1 else []
        ldo = k.dma("sync", sb["otb"][:],
                    tens["otT"][:, CH * ch:CH * (ch + 1)].rearrange("(a p) c -> p a c", p=128),
                    "d_ob", wait=w)
        wx = [("ve", k.v("ve"))] if ch >= 1 else []
        ldx = k.dma("sync", sb["xcb"][:],
                    tens[x_src][:, CH * ch:CH * (ch + 1)].rearrange("(a p) c -> p a c", p=128),
                    "d_xc", wait=wx)
        for dot in range(DT):
            bank = 6 + dot % 2
            w = [("d_wf", ldwo), ("d_ob", ldo), ("ve", k.v("ve"))]
            for dm in range(DT):
                k.on("tensor", lambda e, ch=ch, dot=dot, dm=dm, bank=bank: e.matmul(
                    pb[bank][:], sb["wo_full"][:, 1024 * dm + 128 * dot:1024 * dm + 128 * (dot + 1)],
                    sb["otb"][:, CH * dm:CH * (dm + 1)],
                    start=(dm == 0), stop=(dm == DT - 1)), wait=w if dm == 0 else [],
                    inc=("pe", 1) if dm == DT - 1 else None)
            pev = k.v("pe")
            wst = [("pe", pev), ("d_xc", ldx)]
            if dot == 0 and ch >= 1:
                wst.append(("d_rs", st_done[ch - 1]))
            k.on("vector", lambda e, dot=dot, bank=bank: e.scalar_tensor_tensor(
                sb["rescb"][:, CH * dot:CH * (dot + 1)],
                pb[bank][:], sb["bout"][:, dot:dot + 1],
                sb["xcb"][:, CH * dot:CH * (dot + 1)],
                ALU.add, ALU.add), wait=wst, inc=("ve", 1))
        mm_done[ch] = k.v("pe")
        st_done[ch] = k.dma("gpsimd",
                            tens[dst][:, CH * ch:CH * (ch + 1)].rearrange("(a p) c -> p a c", p=128),
                            sb["rescb"][:], "d_rs", wait=[("ve", k.v("ve"))])
    k.barrier()


L0A_SB = [
    ("xn_sb", (128, 8 * T), BF16),
    ("bqkv", (128, 24), F32), ("bq8", (128, 8), F32), ("bout", (128, 8), F32),
    ("rblo", (128, 16), F32), ("rbhi", (128, 16), F32),
    ("id_bf", (128, 128), BF16), ("onesrow", (1, 128), BF16),
    ("wq_sb", (128, 1024), BF16), ("wk_sb", (128, 1024), BF16), ("wv_sb", (128, 1024), BF16),
    ("bv_row", (1, 128), BF16),
    ("qa0", (65, T), BF16), ("qa1", (65, T), BF16),
    ("ka0", (65, T), BF16), ("ka1", (65, T), BF16),
    ("vp", (128, 32 * 130), BF16),
    ("strip", (128, 3072), BF16),
    ("scol", (128, 1), F32), ("srow", (1, 128), F32),
    ("bmax", (1, 1), F32), ("kmax", (1, 1), F32),
    ("sq_sb", (128, T), BF16),
    ("rows_sb", (65, T), F32),
    ("e_sb", (128, 2 * CH), BF16),
    ("den", (1, CH), F32), ("onum", (64, CH), F32),
    ("ot_tmp", (64, 2 * CH), BF16),
    ("wo_full", (128, 8 * 1024), BF16),
    ("otb", (128, 8 * CH), BF16), ("xcb", (128, 8 * CH), F32), ("rescb", (128, 8 * CH), F32),
]



def st_l0_ffn(k, tens, sb, xn_src, x1p_src, dst):
    pb = k.pb
    CK = 1024
    NC2 = T // CK
    k.dma("gpsimd", tens["moe_w1bf"].rearrange("e a b -> (e a b)"),
          tens["moe_w1"].rearrange("e a b -> (e a b)"), "d_cw")
    k.dma("gpsimd", tens["moe_w2bf"].rearrange("e a b -> (e a b)"),
          tens["moe_w2"].rearrange("e a b -> (e a b)"), "d_cw")
    k.dma("sync", sb["wg_sb"][:], tens["w_gate"].rearrange("(a p) c -> p a c", p=128), "d_c2")
    k.dma("sync", sb["bgate"][0:1, :], tens["b_gate"], "d_c2")
    k.dma("sync", sb["b1c"][:], tens["moe_b1c"], "d_c2")
    k.dma("sync", sb["b2sb"][:], tens["moe_b2"], "d_c2")
    k.dma("sync", sb["id_f32"][:], tens["ident"], "d_c2")
    k.dma("gpsimd", sb["id_bf"][:], tens["ident"], "d_cg")
    k.dma("sync", sb["emask"][:], tens["emask8"], "d_c2")
    ld_c = k.v("d_c2")
    _ = k.v("d_c3")
    k.on("vector", lambda e: e.memset(sb["onesrow"][:], 1.0), inc=("ve", 1))
    k.barrier()
    cw = k.v("d_cw")

    for c in range(NC2):
        k.barrier()
        col0 = CK * c
        ldxn = k.dma("sync", sb["xnb"][:],
                     tens[xn_src][:, col0:col0 + CK].rearrange("(a p) c -> p a c", p=128), "d_xnb")
        ldxnf = k.dma("sync", sb["xnfb"][:],
                      tens["xn_f32"][:, col0:col0 + CK].rearrange("(a p) c -> p a c", p=128), "d_xnf")
        ldx1 = k.dma("sync", sb["x1b"][:],
                     tens[x1p_src][:, col0:col0 + CK].rearrange("(a p) c -> p a c", p=128), "d_x1b")
        tr_ac = 0
        for tl in range(8):
            bank = 6 + tl % 2
            w = [("d_c2", ld_c), ("d_xnf", ldxnf), ("ac", k.v("ac"))]
            for dm in range(DT):
                k.on("tensor", lambda e, tl=tl, dm=dm, bank=bank: e.matmul(
                    pb[bank][:, 0:8], sb["xnfb"][:, CK * dm + 128 * tl:CK * dm + 128 * (tl + 1)],
                    sb["wg_sb"][:, 8 * dm:8 * dm + 8],
                    start=(dm == 0), stop=False), wait=w if dm == 0 else [])
            pev = k.on("tensor", lambda e, bank=bank: e.matmul(
                pb[bank][:, 0:8], sb["ones1f"][0:1, :], sb["bgate"][0:1, :],
                start=False, stop=True), inc=("pe", 1))
            k.on("scalar", lambda e, tl=tl, bank=bank: e.copy(
                sb["scr"][:, 8 * tl:8 * tl + 8], pb[bank][:, 0:8]),
                wait=[("pe", pev)], inc=("ac", 1))
            ac_s = k.v("ac")
            s8 = sb["scr"][:, 8 * tl:8 * tl + 8]
            k.on("vector", lambda e, s8=s8: e.reduce_max(sb["gt_m1"][:], s8, axis=AX.X),
                 wait=[("ac", ac_s), ("ve", k.v("ve"))], inc=("ve", 1))
            k.on("vector", lambda e, s8=s8: e.tensor_scalar(
                sb["keep"][:], s8, sb["gt_m1"][:], None, ALU.is_equal),
                wait=[("ve", k.v("ve"))], inc=("ve", 1))
            k.on("vector", lambda e: e.reduce_sum(sb["gt_cnt"][:], sb["keep"][:], axis=AX.X),
                 wait=[("ve", k.v("ve"))], inc=("ve", 1))
            k.on("vector", lambda e, s8=s8: e.scalar_tensor_tensor(
                sb["tmp8"][:], sb["keep"][:], -3.0e38, s8, ALU.mult, ALU.add),
                wait=[("ve", k.v("ve"))], inc=("ve", 1))
            k.on("vector", lambda e: e.reduce_max(sb["gt_m2"][:], sb["tmp8"][:], axis=AX.X),
                 wait=[("ve", k.v("ve"))], inc=("ve", 1))
            k.on("vector", lambda e: e.tensor_scalar(
                sb["gt_c2"][:], sb["gt_cnt"][:], 2.0, None, ALU.is_ge),
                wait=[("ve", k.v("ve"))], inc=("ve", 1))
            k.on("vector", lambda e: e.tensor_tensor(
                sb["gt_d"][:], sb["gt_m1"][:], sb["gt_m2"][:], ALU.subtract),
                wait=[("ve", k.v("ve"))], inc=("ve", 1))
            k.on("vector", lambda e: e.tensor_tensor(
                sb["gt_d"][:], sb["gt_d"][:], sb["gt_c2"][:], ALU.mult),
                wait=[("ve", k.v("ve"))], inc=("ve", 1))
            k.on("vector", lambda e: e.tensor_tensor(
                sb["gt_thr"][:], sb["gt_m2"][:], sb["gt_d"][:], ALU.add),
                wait=[("ve", k.v("ve"))], inc=("ve", 1))
            k.on("vector", lambda e, s8=s8: e.tensor_scalar(
                sb["keep"][:], s8, sb["gt_thr"][:], None, ALU.is_ge),
                wait=[("ve", k.v("ve"))], inc=("ve", 1))
            vv = k.on("vector", lambda e, s8=s8: e.tensor_tensor(
                sb["wbf"][:], s8, sb["keep"][:], ALU.mult),
                wait=[("ve", k.v("ve"))], inc=("ve", 1))
            ptr = k.on("tensor", lambda e: e.transpose(
                pb[5][0:8, 0:128], sb["wbf"][:], sb["id_f32"][:]),
                wait=[("ve", vv), ("ac", tr_ac), ("d_c2", ld_c)], inc=("pe", 1))
            tr_ac = k.on("scalar", lambda e, tl=tl: e.copy(
                sb["wT"][0:8, 128 * tl:128 * (tl + 1)], pb[5][0:8, 0:128]),
                wait=[("pe", ptr)], inc=("ac", 1))
        for ex in range(8):
            wld = [("pe", k.v("pe"))] if ex > 0 else [("d_cw", cw)]
            ldw1 = k.dma("gpsimd", sb["w1sb"][:],
                         tens["moe_w1bf"][ex].rearrange("(a p) c -> p a c", p=128), "d_w1",
                         wait=wld)
            ldw2 = k.dma("gpsimd", sb["w2sb"][:],
                         tens["moe_w2bf"][ex].rearrange("(a p) c -> p a c", p=128), "d_w2",
                         wait=wld)
            for sub in range(2):
                pwb = k.on("tensor", lambda e, ex=ex, sub=sub: e.matmul(
                    pb[4][:, 0:512], sb["emask"][0:8, 128 * ex:128 * (ex + 1)],
                    sb["wT"][0:8, 512 * sub:512 * (sub + 1)],
                    start=True, stop=True),
                    wait=[("ac", k.v("ac")), ("ve", k.v("ve"))], inc=("pe", 1))
                k.on("scalar", lambda e, sub=sub: e.copy(
                    sb["wb_sb"][:, 512 * sub:512 * (sub + 1)], pb[4][:, 0:512]),
                    wait=[("pe", pwb)], inc=("ac", 1))
            ac_wb = k.v("ac")
            for ht in range(DT):
                for sub in range(2):
                    bank = (2 * ht + sub) % 2
                    w = [("d_w1", ldw1), ("d_xnb", ldxn), ("ac", k.v("ac")), ("ve", k.v("ve"))]
                    for dm in range(DT):
                        k.on("tensor", lambda e, ht=ht, sub=sub, dm=dm, bank=bank: e.matmul(
                            pb[bank][:], sb["w1sb"][:, CK * dm + 128 * ht:CK * dm + 128 * (ht + 1)],
                            sb["xnb"][:, CK * dm + 512 * sub:CK * dm + 512 * (sub + 1)],
                            start=(dm == 0), stop=(dm == DT - 1)),
                            wait=w if dm == 0 else [],
                            inc=("pe", 1) if dm == DT - 1 else None)
                    pev = k.v("pe")
                    k.on("scalar", lambda e, ht=ht, sub=sub, bank=bank, ex=ex: e.activation(
                        sb["hsc"][:, CK * ht + 512 * sub:CK * ht + 512 * (sub + 1)],
                        pb[bank][:], AF.Relu, bias=sb["b1c"][:, 8 * ex + ht:8 * ex + ht + 1]),
                        wait=[("pe", pev), ("ac", ac_wb)], inc=("ac", 1))
                    k.on("vector", lambda e, ht=ht, sub=sub: e.tensor_tensor(
                        sb["hsc"][:, CK * ht + 512 * sub:CK * ht + 512 * (sub + 1)],
                        sb["hsc"][:, CK * ht + 512 * sub:CK * ht + 512 * (sub + 1)],
                        sb["wb_sb"][:, 512 * sub:512 * (sub + 1)], ALU.mult),
                        wait=[("ac", k.v("ac")), ("ve", k.v("ve"))], inc=("ve", 1))
            ve_h = k.v("ve")
            for do in range(DT):
                for sub in range(2):
                    bank = 2 + (2 * do + sub) % 2
                    w = [("d_w2", ldw2), ("ve", k.v("ve"))]
                    first = True
                    if ex == 0:
                        k.on("tensor", lambda e, do=do, sub=sub, bank=bank: e.matmul(
                            pb[bank][:], sb["b2sb"][0:8, 128 * do:128 * (do + 1)],
                            sb["wT"][0:8, 512 * sub:512 * (sub + 1)],
                            start=True, stop=False), wait=w)
                        first = False
                    for ht in range(DT):
                        k.on("tensor", lambda e, do=do, sub=sub, ht=ht, bank=bank, first=first: e.matmul(
                            pb[bank][:], sb["w2sb"][:, CK * ht + 128 * do:CK * ht + 128 * (do + 1)],
                            sb["hsc"][:, CK * ht + 512 * sub:CK * ht + 512 * (sub + 1)],
                            start=(first and ht == 0), stop=(ht == DT - 1)),
                            wait=w if (first and ht == 0) else [],
                            inc=("pe", 1) if ht == DT - 1 else None)
                        first = first and False
                    pev = k.v("pe")
                    aslc = sb["acc"][:, CK * do + 512 * sub:CK * do + 512 * (sub + 1)]
                    if ex == 0:
                        k.on("vector", lambda e, do=do, sub=sub, bank=bank, aslc=aslc: e.tensor_tensor(
                            aslc, pb[bank][:],
                            sb["x1b"][:, CK * do + 512 * sub:CK * do + 512 * (sub + 1)], ALU.add),
                            wait=[("pe", pev), ("d_x1b", ldx1), ("ve", k.v("ve"))], inc=("ve", 1))
                    else:
                        k.on("vector", lambda e, bank=bank, aslc=aslc: e.tensor_tensor(
                            aslc, pb[bank][:], aslc, ALU.add),
                            wait=[("pe", pev), ("ve", k.v("ve"))], inc=("ve", 1))
        k.dma("gpsimd", tens[dst][:, col0:col0 + CK].rearrange("(a p) c -> p a c", p=128),
              sb["acc"][:], "d_st1", wait=[("ve", k.v("ve"))])
    k.barrier()


MOE_SB = [
    ("xnb", (128, 8 * 1024), BF16), ("hsc", (128, 8 * 1024), BF16),
    ("acc", (128, 8 * 1024), F32), ("x1b", (128, 8 * 1024), F32),
    ("w1sb", (128, 8 * 1024), BF16), ("w2sb", (128, 8 * 1024), BF16),
    ("wg_sb", (128, 64), F32), ("bgate", (1, 8), F32),
    ("b1c", (128, 64), F32), ("b2sb", (8, 1024), F32),
    ("xnfb", (128, 8 * 1024), F32), ("id_f32", (128, 128), F32),
    ("scr", (128, 64), F32), ("keep", (128, 8), F32), ("tmp8", (128, 8), F32),
    ("gt_m1", (128, 1), F32), ("gt_m2", (128, 1), F32), ("gt_cnt", (128, 1), F32),
    ("gt_c2", (128, 1), F32), ("gt_d", (128, 1), F32), ("gt_thr", (128, 1), F32),
    ("wbf", (128, 8), F32), ("wT", (8, 1024), F32), ("wb_sb", (128, 1024), F32),
    ("id_bf", (128, 128), BF16), ("onesrow", (1, 128), BF16),
    ("emask", (8, 1024), F32),
]


def build_l0ffn_launch(kinds):
    nc = bass.Bass("TRN2", target_bir_lowering=False, debug=False)
    decls = [
        ("xn_bf", (D, T), BF16, "ExternalInput"),
        ("xn_f32", (D, T), F32, "ExternalInput"),
        ("x1p", (D, T), F32, "ExternalInput"),
        ("w_gate", (D, 8), F32, "ExternalInput"),
        ("b_gate", (1, 8), F32, "ExternalInput"),
        ("moe_w1", (8, D, D), F32, "ExternalInput"),
        ("moe_w2", (8, D, D), F32, "ExternalInput"),
        ("moe_b1c", (128, 64), F32, "ExternalInput"),
        ("moe_b2", (8, D), F32, "ExternalInput"),
        ("ident", (128, 128), F32, "ExternalInput"),
        ("emask8", (8, 1024), F32, "ExternalInput"),
        ("x1", (D, T), F32, "ExternalOutput"),
        ("moe_w1bf", (8, D, D), BF16, "Internal"),
        ("moe_w2bf", (8, D, D), BF16, "Internal"),
    ]
    tens = {}
    for name, shape, dt, kind in decls:
        tens[name] = nc.dram_tensor(name, list(shape), dt, kind=kind).ap()
    ctx = ExitStack()
    with ctx:
        k, sb = _mk_k(nc, ctx, MOE_SB)
        st_l0_ffn(k, tens, sb, "xn_bf", "x1p", "x1")
        _finish(nc, k)
    return nc




def _out_proj(k, tens, sb, w_dram, bcol, x_src, dst, ot_name):
    pb = k.pb
    wsrc = tens[w_dram].rearrange("(a p) (b c) -> p a b c", p=128, c=128)
    ldwo = k.dma("gpsimd", sb["wo_full"][:], wsrc, "d_wf")
    mm_done = [0] * NCH
    st_done = [0] * NCH
    for ch in range(NCH):
        w = [("pe", mm_done[ch - 1])] if ch >= 1 else []
        ldo = k.dma("sync", sb["otb"][:],
                    tens[ot_name][:, CH * ch:CH * (ch + 1)].rearrange("(a p) c -> p a c", p=128),
                    "d_ob", wait=w)
        wx = [("ve", k.v("ve"))] if ch >= 1 else []
        ldx = k.dma("sync", sb["xcb"][:],
                    tens[x_src][:, CH * ch:CH * (ch + 1)].rearrange("(a p) c -> p a c", p=128),
                    "d_xc", wait=wx)
        for dot in range(DT):
            bank = 6 + dot % 2
            w = [("d_wf", ldwo), ("d_ob", ldo), ("ve", k.v("ve"))]
            for dm in range(DT):
                k.on("tensor", lambda e, ch=ch, dot=dot, dm=dm, bank=bank: e.matmul(
                    pb[bank][:], sb["wo_full"][:, 1024 * dm + 128 * dot:1024 * dm + 128 * (dot + 1)],
                    sb["otb"][:, CH * dm:CH * (dm + 1)],
                    start=(dm == 0), stop=(dm == DT - 1)), wait=w if dm == 0 else [],
                    inc=("pe", 1) if dm == DT - 1 else None)
            pev = k.v("pe")
            wst = [("pe", pev), ("d_xc", ldx)]
            if dot == 0 and ch >= 1:
                wst.append(("d_rs", st_done[ch - 1]))
            k.on("vector", lambda e, dot=dot, bank=bank, bcol=bcol: e.scalar_tensor_tensor(
                sb["rescb"][:, CH * dot:CH * (dot + 1)],
                pb[bank][:], bcol[:, dot:dot + 1],
                sb["xcb"][:, CH * dot:CH * (dot + 1)],
                ALU.add, ALU.add), wait=wst, inc=("ve", 1))
        mm_done[ch] = k.v("pe")
        st_done[ch] = k.dma("gpsimd",
                            tens[dst][:, CH * ch:CH * (ch + 1)].rearrange("(a p) c -> p a c", p=128),
                            sb["rescb"][:], "d_rs", wait=[("ve", k.v("ve"))])
    k.barrier()


def st_l1_attn(k, tens, sb, xn_src, x_src, dst):
    pb = k.pb
    for nm, key in (("pbq", "p_bq_t"), ("pbk", "p_bk_t"), ("pbv", "p_bv_t"), ("pbo", "p_bo_t")):
        k.dma("sync", sb[nm][:], tens[key], "d_c")
    k.dma("gpsimd", sb["rm_sb"][:], tens["p_rm"], "d_cg")
    ld_c, ld_cg = k.v("d_c"), k.v("d_cg")
    k.on("vector", lambda e: e.memset(sb["onesrow"][:], 1.0), inc=("ve", 1))
    for tt in range(32):
        k.on("vector", lambda e, tt=tt: e.memset(sb["vp"][:, 130 * tt + 64:130 * tt + 65], 1.0))
        k.on("vector", lambda e, tt=tt: e.memset(sb["vp"][:, 130 * tt + 129:130 * tt + 130], 1.0),
             inc=("ve", 1) if tt == 31 else None)
    for d in range(DT):
        k.dma("sync", sb["xn2"][:, d * T:(d + 1) * T],
              tens[xn_src][d * 128:(d + 1) * 128, :], "d_xn2")
    ldxn = k.v("d_xn2")
    k.barrier()
    for p in range(8):
        k.barrier()
        for nm, c0, wt in (("wq_sb", 128 * p, "p_Wq"), ("wk_sb", 128 * p, "p_Wk"), ("wv_sb", 128 * p, "p_Wv")):
            srcw = tens[wt][:, c0:c0 + 128].rearrange("(a p) c -> p a c", p=128)
            k.dma("gpsimd", sb[nm][:], srcw, "d_w")
        k.dma("gpsimd", sb["bv_row"][0:1, :], tens["p_bv_raw"][0:1, 128 * p:128 * (p + 1)], "d_w")
        ld_w = k.v("d_w")
        for which, wsb, dstt, bsel in (("q", "wq_sb", "qt", "pbq"), ("kk", "wk_sb", "kt", "pbk")):
            for ch in range(NCH):
                bank = 6 + ch % 2
                w = [("d_w", ld_w), ("d_xn2", ldxn), ("ac", k.v("ac"))]
                for dm in range(DT):
                    k.on("tensor", lambda e, ch=ch, dm=dm, bank=bank, wsb=wsb: e.matmul(
                        pb[bank][:], sb[wsb][:, 128 * dm:128 * dm + 128],
                        sb["xn2"][:, T * dm + CH * ch: T * dm + CH * (ch + 1)],
                        start=(dm == 0), stop=(dm == DT - 1)),
                        wait=w if dm == 0 else [],
                        inc=("pe", 1) if dm == DT - 1 else None)
                pev = k.v("pe")
                for hl in range(2):
                    k.on("scalar", lambda e, dstt=dstt, ch=ch, bank=bank, hl=hl, bsel=bsel: e.activation(
                        sb[dstt][64 * hl:64 * hl + 64, CH * ch:CH * (ch + 1)],
                        pb[bank][64 * hl:64 * hl + 64, :],
                        AF.Identity, bias=sb[bsel][64 * hl:64 * hl + 64, p:p + 1], scale=1.0),
                        wait=[("pe", pev), ("d_c", ld_c)] if hl == 0 else [], inc=("ac", 1))
        for tt in range(32):
            bank = 6 + tt % 2
            w = [("d_w", ld_w), ("d_xn2", ldxn), ("ac", k.v("ac"))]
            for dm in range(DT):
                k.on("tensor", lambda e, tt=tt, dm=dm, bank=bank: e.matmul(
                    pb[bank][:, 0:128], sb["xn2"][:, T * dm + 128 * tt: T * dm + 128 * (tt + 1)],
                    sb["wv_sb"][:, 128 * dm:128 * dm + 128],
                    start=(dm == 0), stop=False),
                    wait=w if dm == 0 else [])
            pev = k.on("tensor", lambda e, tt=tt, bank=bank, p=p: e.matmul(
                pb[bank][:, 0:128], sb["onesrow"][0:1, :], sb["bv_row"][0:1, :],
                start=False, stop=True), inc=("pe", 1))
            for hl in range(2):
                k.on("scalar", lambda e, tt=tt, bank=bank, hl=hl: e.copy(
                    sb["vp"][:, 130 * tt + 65 * hl:130 * tt + 65 * hl + 64],
                    pb[bank][:, 64 * hl:64 * hl + 64]),
                    wait=[("pe", pev)] if hl == 0 else [], inc=("ac", 1))
        for hl in range(2):
            h = 2 * p + hl
            k.barrier()
            for mt in range(2):
                for ch in range(NCH):
                    bank = (2 * mt + ch) % 2
                    k.on("tensor", lambda e, mt=mt, ch=ch, bank=bank, hl=hl: e.matmul(
                        pb[bank][:], sb["rm_sb"][64 * hl:64 * hl + 64, 128 * mt:128 * (mt + 1)],
                        sb["qt"][64 * hl:64 * hl + 64, CH * ch:CH * (ch + 1)],
                        start=True, stop=True),
                        wait=[("ve", k.v("ve")), ("ac", k.v("ac")), ("d_cg", ld_cg)],
                        inc=("pe", 1))
                    pev = k.v("pe")
                    dslc = sb["qp"][:, T * mt + CH * ch:T * mt + CH * (ch + 1)]
                    _elu16(k, sb, pb, bank, pev, dslc, CH)
            for tt in range(32):
                bank = tt % 2
                k.on("tensor", lambda e, tt=tt, bank=bank, hl=hl: e.matmul(
                    pb[bank][:, 0:256], sb["kt"][64 * hl:64 * hl + 64, 128 * tt:128 * (tt + 1)],
                    sb["rm_sb"][64 * hl:64 * hl + 64, :],
                    start=True, stop=True),
                    wait=[("ve", k.v("ve")), ("ac", k.v("ac"))], inc=("pe", 1))
                pev = k.v("pe")
                dslc = sb["kp"][:, 256 * tt:256 * (tt + 1)]
                _elu16(k, sb, pb, bank, pev, dslc, 256)
            for mt in range(2):
                bank = 2 + mt
                for tt in range(32):
                    k.on("tensor", lambda e, mt=mt, tt=tt, bank=bank, hl=hl: e.matmul(
                        pb[bank][:, 0:65], sb["kp"][:, 256 * tt + 128 * mt:256 * tt + 128 * (mt + 1)],
                        sb["vp"][:, 130 * tt + 65 * hl:130 * tt + 65 * hl + 65],
                        start=(tt == 0), stop=(tt == 31)),
                        wait=[("ve", k.v("ve")), ("ac", k.v("ac"))] if tt == 0 else [],
                        inc=("pe", 1) if tt == 31 else None)
                pev = k.v("pe")
                k.on("scalar", lambda e, mt=mt, bank=bank: e.copy(
                    sb["kv_sb"][:, 65 * mt:65 * (mt + 1)], pb[bank][:, 0:65]),
                    wait=[("pe", pev)], inc=("ac", 1))
            ac_kv = k.v("ac")
            ot_st = {}
            tt_done = {}
            for ch in range(NCH):
                ob = 2 + ch % 2
                for mt in range(2):
                    k.on("tensor", lambda e, ch=ch, mt=mt, ob=ob: e.matmul(
                        pb[ob][0:65, :], sb["kv_sb"][:, 65 * mt:65 * (mt + 1)],
                        sb["qp"][:, T * mt + CH * ch:T * mt + CH * (ch + 1)],
                        start=(mt == 0), stop=(mt == 1)),
                        wait=[("ac", ac_kv), ("ve", k.v("ve"))] if mt == 0 else [],
                        inc=("pe", 1) if mt == 1 else None)
                pev = k.v("pe")
                wd0 = [("pe", pev)]
                if (ch - 1) in tt_done:
                    wd0.append(("ve", tt_done[ch - 1]))
                k.on("scalar", lambda e, ob=ob: e.copy(sb["den"][0:1, :], pb[ob][64:65, :]),
                     wait=wd0, inc=("ac", 1))
                k.on("scalar", lambda e, ob=ob: e.copy(sb["onum"][0:64, :], pb[ob][0:64, :]),
                     inc=("ac", 1))
                ac_ev = k.v("ac")
                k.on("vector", lambda e: e.tensor_scalar(sb["den"][0:1, :], sb["den"][0:1, :],
                                                         1e-6, None, ALU.max),
                     wait=[("ac", ac_ev), ("ve", k.v("ve"))], inc=("ve", 1))
                k.on("vector", lambda e: e.reciprocal(sb["den"][0:1, :], sb["den"][0:1, :]),
                     wait=[("ve", k.v("ve"))], inc=("ve", 1))
                wd = [("ve", k.v("ve"))]
                if (ch - 2) in tt_done:
                    wd.append(("ve", tt_done[ch - 2]))
                pdv = k.on("tensor", lambda e, ch=ch: e.matmul(
                    pb[4 + ch % 2][0:64, :], sb["ones1f"][0:1, 0:64], sb["den"][0:1, :],
                    start=True, stop=True), wait=wd, inc=("pe", 1))
                wtt = [("pe", pdv), ("ac", ac_ev)]
                if (ch - 2) in tt_done:
                    wtt.append(("d_ot%d" % (ch % 2), ot_st[ch - 2]))
                tt_done[ch] = k.on("vector", lambda e, ch=ch: e.tensor_tensor(
                    sb["ot_tmp"][0:64, CH * (ch % 2):CH * (ch % 2 + 1)],
                    sb["onum"][0:64, :], pb[4 + ch % 2][0:64, :], ALU.mult),
                    wait=wtt, inc=("ve", 1))
                ot_st[ch] = k.dma("gpsimd", tens["otT"][64 * h:64 * h + 64, CH * ch:CH * (ch + 1)],
                                  sb["ot_tmp"][0:64, CH * (ch % 2):CH * (ch % 2 + 1)],
                                  "d_ot%d" % (ch % 2), wait=[("ve", tt_done[ch])])
    k.barrier()
    _out_proj(k, tens, sb, "p_Wo", sb["pbo"], x_src, dst, "otT")


def _elu16(k, sb, pb, bank, pev, dslc, width):
    t1 = sb["el_t1"][:, 0:width]
    r1 = sb["el_r1"][:, 0:width]
    k.on("vector", lambda e: e.tensor_scalar(t1, pb[bank][:, 0:width], 0.0, None, ALU.min),
         wait=[("pe", pev), ("ve", k.v("ve")), ("ac", k.v("ac"))], inc=("ve", 1))
    k.on("vector", lambda e: e.tensor_scalar(r1, pb[bank][:, 0:width], 0.0, 0.0625,
                                             ALU.max, ALU.mult),
         wait=[("ve", k.v("ve"))], inc=("ve", 1))
    k.on("scalar", lambda e: e.activation(t1, t1, AF.Exp),
         wait=[("ve", k.v("ve"))], inc=("ac", 1))
    k.on("vector", lambda e: e.scalar_tensor_tensor(dslc, t1, 0.0625, r1, ALU.mult, ALU.add),
         wait=[("ac", k.v("ac")), ("ve", k.v("ve"))], inc=("ve", 1))


L1A_SB = [
    ("xn2", (128, 8 * T), BF16),
    ("pbq", (128, 8), F32), ("pbk", (128, 8), F32), ("pbv", (128, 8), F32), ("pbo", (128, 8), F32),
    ("rm_sb", (128, 256), BF16),
    ("wq_sb", (128, 1024), BF16), ("wk_sb", (128, 1024), BF16), ("wv_sb", (128, 1024), BF16),
    ("bv_row", (1, 128), BF16),
    ("qt", (128, T), BF16), ("kt", (128, T), BF16),
    ("vp", (128, 32 * 130), BF16),
    ("qp", (128, 2 * T), BF16), ("kp", (128, 32 * 256), BF16),
    ("kv_sb", (128, 130), BF16),
    ("el_t1", (128, CH), F32), ("el_r1", (128, CH), F32),
    ("den", (1, CH), F32), ("onum", (64, CH), F32),
    ("ot_tmp", (64, 2 * CH), BF16), ("onesrow", (1, 128), BF16),
    ("wo_full", (128, 8 * 1024), BF16),
    ("otb", (128, 8 * CH), BF16), ("xcb", (128, 8 * CH), F32), ("rescb", (128, 8 * CH), F32),
]


def build_l1attn_launch(kinds):
    nc = bass.Bass("TRN2", target_bir_lowering=False, debug=False)
    decls = [
        ("xn1_bf", (D, T), BF16, "ExternalInput"),
        ("x1", (D, T), F32, "ExternalInput"),
        ("p_Wq", (D, D), F32, "ExternalInput"), ("p_Wk", (D, D), F32, "ExternalInput"),
        ("p_Wv", (D, D), F32, "ExternalInput"), ("p_Wo", (D, D), F32, "ExternalInput"),
        ("p_bq_t", (128, 8), F32, "ExternalInput"), ("p_bk_t", (128, 8), F32, "ExternalInput"),
        ("p_bv_t", (128, 8), F32, "ExternalInput"), ("p_bo_t", (128, 8), F32, "ExternalInput"),
        ("p_bv_raw", (1, D), F32, "ExternalInput"),
        ("p_rm", (128, 256), F32, "ExternalInput"),
        ("x2p", (D, T), F32, "ExternalOutput"),
        ("otT", (D, T), BF16, "Internal"),
    ]
    tens = {}
    for name, shape, dt, kind in decls:
        tens[name] = nc.dram_tensor(name, list(shape), dt, kind=kind).ap()
    ctx = ExitStack()
    with ctx:
        k, sb = _mk_k(nc, ctx, L1A_SB)
        st_l1_attn(k, tens, sb, "xn1_bf", "x1", "x2p")
        _finish(nc, k)
    return nc




def st_lf_attn(k, tens, sb, l, yin, ya):
    pb = k.pb
    for nm, key in (("csq", "rot_cq"), ("ssq", "rot_sq"), ("csk", "rot_ck"), ("ssk", "rot_sk")):
        k.dma("gpsimd", sb[nm][0:64, :], tens[key], "d_cg")
    k.dma("gpsimd", sb["mask_sb"][:], tens["la_mask"].rearrange("(a p) c -> p a c", p=128), "d_cg")
    k.dma("gpsimd", sb["id_bf"][:], tens["ident"], "d_cg")
    k.dma("sync", sb["qs_col"][0:64, 0:1], tens["lf_qs"][l].unsqueeze(1), "d_c")
    k.dma("sync", sb["ks_col"][0:64, 0:1], tens["lf_ks"][l].unsqueeze(1), "d_c")
    ld_cg, ld_c = k.v("d_cg"), k.v("d_c")
    for d in range(DT):
        for cc in range(4):
            b2 = 0
            h2p = 0
            yb = sb["ybuf"][:, 0:1025]
            hf = sb["hbf"][:, 0:1024]
            wld = [("d_h0", k.v("d_h0")), ("d_h1", k.v("d_h1")), ("ve", k.v("ve"))]
            if d < 4:
                ld = k.dma("sync", yb[:, 0:1024],
                           tens[yin][128 * d:128 * (d + 1), 1024 * cc:1024 * (cc + 1)],
                           "d_y%d" % b2, wait=wld)
                vv = k.on("vector", lambda e, yb=yb, hf=hf: e.tensor_copy(hf, yb[:, 0:1024]),
                          wait=[("d_y%d" % b2, ld), ("ve", k.v("ve"))], inc=("ve", 1))
            elif cc == 0:
                ld = k.dma("sync", yb[:, 0:1023],
                           tens[yin][128 * d:128 * (d + 1), 0:1023], "d_y%d" % b2, wait=wld)
                k.on("vector", lambda e, hf=hf: e.memset(hf[:, 0:1], 0.0),
                     wait=[("ve", k.v("ve"))] + wld, inc=("ve", 1))
                vv = k.on("vector", lambda e, yb=yb, hf=hf: e.tensor_copy(hf[:, 1:1024], yb[:, 0:1023]),
                          wait=[("d_y%d" % b2, ld), ("ve", k.v("ve"))], inc=("ve", 1))
            else:
                ld = k.dma("sync", yb[:, 0:1024],
                           tens[yin][128 * d:128 * (d + 1), 1024 * cc - 1:1024 * (cc + 1) - 1],
                           "d_y%d" % b2, wait=wld)
                vv = k.on("vector", lambda e, yb=yb, hf=hf: e.tensor_copy(hf, yb[:, 0:1024]),
                          wait=[("d_y%d" % b2, ld), ("ve", k.v("ve"))], inc=("ve", 1))
            k.dma("gpsimd", tens["hshiftT"][128 * d:128 * (d + 1), 1024 * cc:1024 * (cc + 1)],
                  hf, "d_h%d" % h2p, wait=[("ve", vv)])
    k.barrier()

    for p in range(4):
        k.barrier()
        for nm, c0 in (("wq_sb", 128 * p), ("wk_sb", 512 + 128 * p), ("wv_sb", 1024 + 128 * p)):
            srcw = tens["lf_wqkv"][l][:, c0:c0 + 128].rearrange("(a p) c -> p a c", p=128)
            k.dma("gpsimd", sb[nm][:], srcw, "d_w")
        ld_w = k.v("d_w")
        for ch in range(NCH):
            ldh = k.dma("sync", sb["hb"][:],
                        tens["hshiftT"][:, CH * ch:CH * (ch + 1)].rearrange("(a p) c -> p a c", p=128),
                        "d_hb", wait=[("pe", k.v("pe"))])
            for which, wsb, dstt in (("q", "wq_sb", "qt"), ("kk", "wk_sb", "kt")):
                bank = 6 + ch % 2
                w = [("d_w", ld_w), ("d_hb", ldh), ("ac", k.v("ac"))]
                for dm in range(DT):
                    k.on("tensor", lambda e, ch=ch, dm=dm, bank=bank, wsb=wsb: e.matmul(
                        pb[bank][:], sb[wsb][:, 128 * dm:128 * dm + 128],
                        sb["hb"][:, CH * dm:CH * (dm + 1)],
                        start=(dm == 0), stop=(dm == DT - 1)),
                        wait=w if dm == 0 else [],
                        inc=("pe", 1) if dm == DT - 1 else None)
                pev = k.v("pe")
                k.on("scalar", lambda e, ch=ch, bank=bank, dstt=dstt: e.copy(
                    sb[dstt][:, CH * ch:CH * (ch + 1)], pb[bank][:]),
                    wait=[("pe", pev)], inc=("ac", 1))
            for ttl in range(4):
                bank = 6 + ttl % 2
                w = [("d_w", ld_w), ("d_hb", ldh), ("ac", k.v("ac"))]
                for dm in range(DT):
                    k.on("tensor", lambda e, ch=ch, ttl=ttl, dm=dm, bank=bank: e.matmul(
                        pb[bank][:, 0:128], sb["hb"][:, CH * dm + 128 * ttl:CH * dm + 128 * (ttl + 1)],
                        sb["wv_sb"][:, 128 * dm:128 * dm + 128],
                        start=(dm == 0), stop=(dm == DT - 1)),
                        wait=w if dm == 0 else [],
                        inc=("pe", 1) if dm == DT - 1 else None)
                pev = k.v("pe")
                k.on("scalar", lambda e, ch=ch, ttl=ttl, bank=bank: e.copy(
                    sb["vp"][:, 128 * (4 * ch + ttl):128 * (4 * ch + ttl + 1)],
                    pb[bank][:, 0:128]), wait=[("pe", pev)], inc=("ac", 1))
        for hl in range(2):
            k.barrier()
            for which in ("q", "kk"):
                srct = sb["qt" if which == "q" else "kt"]
                cse = sb["csq" if which == "q" else "csk"]
                sse = sb["ssq" if which == "q" else "ssk"]
                dstr = sb["qr" if which == "q" else "kr"]
                mmv = {}
                for ch in range(NCH):
                    row, bank = (ch % 3) * 32, ch // 3
                    wsq = [("ve", k.v("ve")), ("ac", k.v("ac"))]
                    if ch >= 1:
                        wsq.append(("pe", mmv[ch - 1]))
                    k.on("scalar", lambda e, srct=srct, hl=hl, ch=ch: e.activation(
                        sb["sqc"][64 * hl:64 * hl + 64, :],
                        srct[64 * hl:64 * hl + 64, CH * ch:CH * (ch + 1)], AF.Square),
                        wait=wsq, inc=("ac", 1))
                    mmv[ch] = k.on("tensor", lambda e, ch=ch, row=row, bank=bank, hl=hl: e.matmul(
                        pb[bank][row:row + 1, :], sb["onesb"][64 * hl:64 * hl + 64, :],
                        sb["sqc"][64 * hl:64 * hl + 64, :],
                        start=True, stop=True),
                        wait=[("ac", k.v("ac"))], inc=("pe", 1))
                pev = k.v("pe")
                for ch in range(NCH):
                    row, bank = (ch % 3) * 32, ch // 3
                    rr2, cc2 = 32 * (ch // 4), 512 * (ch % 4)
                    k.on("scalar", lambda e, row=row, bank=bank, rr2=rr2, cc2=cc2: e.copy(
                        sb["rows2"][rr2:rr2 + 1, cc2:cc2 + 512], pb[bank][row:row + 1, :]),
                        wait=[("pe", pev)] if ch == 0 else [], inc=("ac", 1))
                for rr2 in (0, 32):
                    k.on("vector", lambda e, rr2=rr2: e.tensor_scalar(
                        sb["rows2"][rr2:rr2 + 1, :], sb["rows2"][rr2:rr2 + 1, :], 1e-24, None, ALU.max),
                        wait=[("ac", k.v("ac")), ("ve", k.v("ve"))], inc=("ve", 1))
                    k.on("scalar", lambda e, rr2=rr2: e.activation(
                        sb["rows2"][rr2:rr2 + 1, :], sb["rows2"][rr2:rr2 + 1, :], AF.Ln),
                        wait=[("ve", k.v("ve"))], inc=("ac", 1))
                    k.on("scalar", lambda e, rr2=rr2: e.activation(
                        sb["rows2"][rr2:rr2 + 1, :], sb["rows2"][rr2:rr2 + 1, :], AF.Exp, scale=-0.5),
                        wait=[("ac", k.v("ac"))], inc=("ac", 1))
                for ch in range(NCH):
                    rr2, cc2 = 32 * (ch // 4), 512 * (ch % 4)
                    pbc = k.on("tensor", lambda e, rr2=rr2, cc2=cc2: e.matmul(
                        pb[3][0:64, :], sb["ones1f"][rr2:rr2 + 1, 0:64],
                        sb["rows2"][rr2:rr2 + 1, cc2:cc2 + 512], start=True, stop=True),
                        wait=[("ac", k.v("ac")), ("ve", k.v("ve"))], inc=("pe", 1))
                    k.on("scalar", lambda e, ch=ch: e.copy(
                        sb["rnb"][0:64, CH * ch:CH * (ch + 1)], pb[3][0:64, :]),
                        wait=[("pe", pbc)], inc=("ac", 1))
                ac_bc = k.v("ac")
                k.on("vector", lambda e, srct=srct, hl=hl: e.tensor_copy(
                    sb["q0"][0:64, :], srct[64 * hl:64 * hl + 64, :]),
                    wait=[("ac", ac_bc), ("ve", k.v("ve"))], inc=("ve", 1))
                k.on("vector", lambda e: e.tensor_tensor(
                    sb["q0"][0:64, :], sb["q0"][0:64, :], sb["rnb"][0:64, :], ALU.mult),
                    wait=[("ve", k.v("ve"))], inc=("ve", 1))
                scol = sb["qs_col" if which == "q" else "ks_col"]
                k.on("vector", lambda e, scol=scol: e.tensor_scalar(
                    sb["q0"][0:64, :], sb["q0"][0:64, :], scol[0:64, 0:1], None, ALU.mult),
                    wait=[("ve", k.v("ve")), ("d_c", ld_c)], inc=("ve", 1))
                k.on("vector", lambda e: e.tensor_copy(sb["qsw"][0:32, :], sb["q0"][32:64, :]),
                     wait=[("ve", k.v("ve"))], inc=("ve", 1))
                k.on("vector", lambda e: e.tensor_copy(sb["qsw"][32:64, :], sb["q0"][0:32, :]),
                     wait=[("ve", k.v("ve"))], inc=("ve", 1))
                k.on("vector", lambda e, sse=sse: e.tensor_tensor(
                    sb["qsw"][0:64, :], sb["qsw"][0:64, :], sse[0:64, :], ALU.mult),
                    wait=[("ve", k.v("ve"))], inc=("ve", 1))
                k.on("vector", lambda e, cse=cse: e.tensor_tensor(
                    sb["qnf"][0:64, :], sb["q0"][0:64, :], cse[0:64, :], ALU.mult),
                    wait=[("ve", k.v("ve"))], inc=("ve", 1))
                k.on("vector", lambda e: e.tensor_tensor(
                    sb["qnf"][0:64, :], sb["qnf"][0:64, :], sb["qsw"][0:64, :], ALU.add),
                    wait=[("ve", k.v("ve"))], inc=("ve", 1))
                k.on("vector", lambda e, dstr=dstr, hl=hl: e.tensor_copy(
                    dstr[64 * hl:64 * hl + 64, :], sb["qnf"][0:64, :]),
                    wait=[("ve", k.v("ve"))], inc=("ve", 1))
            for wdx in range(8):
                koff = 4 if wdx == 0 else 0
                kbase = 512 * (wdx - 1)
                for qtl in range(4):
                    qc0 = 512 * wdx + 128 * qtl
                    sbk = qtl % 2
                    for kc in range(2):
                        if wdx == 0 and kc == 0:
                            continue
                        k.on("tensor", lambda e, hl=hl, qc0=qc0, kbase=kbase, kc=kc, sbk=sbk: e.matmul(
                            pb[sbk][:], sb["qr"][64 * hl:64 * hl + 64, qc0:qc0 + 128],
                            sb["kr"][64 * hl:64 * hl + 64, kbase + 512 * kc:kbase + 512 * (kc + 1)],
                            start=True, stop=True),
                            wait=[("ve", k.v("ve")), ("ac", k.v("ac"))], inc=("pe", 1))
                        pev = k.v("pe")
                        k.on("vector", lambda e, qtl=qtl, kc=kc, sbk=sbk: e.tensor_tensor(
                            sb["sm"][:, 512 * kc:512 * (kc + 1)], pb[sbk][:],
                            sb["mask_sb"][:, 1024 * qtl + 512 * kc:1024 * qtl + 512 * (kc + 1)],
                            ALU.add),
                            wait=[("pe", pev), ("ve", k.v("ve")), ("ac", k.v("ac"))], inc=("ve", 1))
                    off = 512 * koff // 4
                    k.on("vector", lambda e, off=off: e.reduce_max(
                        sb["negmax"][:], sb["sm"][:, off:1024], axis=AX.X, negate=True),
                        wait=[("ve", k.v("ve"))], inc=("ve", 1))
                    k.on("scalar", lambda e, off=off: e.activation(
                        sb["eb"][:, off:1024], sb["sm"][:, off:1024], AF.Exp,
                        bias=sb["negmax"][:], accum_out=sb["rsum"][:]),
                        wait=[("ve", k.v("ve"))], inc=("ac", 1))
                    k.on("vector", lambda e: e.reciprocal(sb["rsum"][:], sb["rsum"][:]),
                         wait=[("ac", k.v("ac")), ("ve", k.v("ve"))], inc=("ve", 1))
                    k.on("vector", lambda e, off=off: e.tensor_scalar(
                        sb["eb"][:, off:1024], sb["eb"][:, off:1024], sb["rsum"][:], None, ALU.mult),
                        wait=[("ve", k.v("ve"))], inc=("ve", 1))
                    ve_eb = k.v("ve")
                    ob = 4 + qtl % 2
                    n_kt = 8 - koff
                    for i, ktl in enumerate(range(koff, 8)):
                        tb = 2 + ktl % 2
                        ptr = k.on("tensor", lambda e, ktl=ktl, tb=tb: e.transpose(
                            pb[tb][:, 0:256].bitcast(BF16)[:, 0:128],
                            sb["eb"][:, 128 * ktl:128 * (ktl + 1)], sb["id_bf"][:]),
                            wait=[("ve", ve_eb), ("ac", k.v("ac")), ("d_cg", ld_cg)], inc=("pe", 1))
                        k.on("scalar", lambda e, ktl=ktl, tb=tb: e.copy(
                            sb["at_sb"][:, 128 * ktl:128 * (ktl + 1)],
                            pb[tb][:, 0:256].bitcast(BF16)[:, 0:128]),
                            wait=[("pe", ptr)], inc=("ac", 1))
                        ktg = 4 * (wdx - 1) + ktl
                        k.on("tensor", lambda e, ktl=ktl, ktg=ktg, hl=hl, ob=ob, i=i, n=n_kt: e.matmul(
                            pb[ob][0:64, 0:128], sb["vp"][:, 128 * ktg + 64 * hl:128 * ktg + 64 * hl + 64],
                            sb["at_sb"][:, 128 * ktl:128 * (ktl + 1)],
                            start=(i == 0), stop=(i == n - 1)),
                            wait=[("ac", k.v("ac"))], inc=("pe", 1))
                    pev = k.v("pe")
                    k.on("scalar", lambda e, hl=hl, p=p, qc0=qc0, ob=ob: e.copy(
                        sb["or_all"][64 * hl:64 * hl + 64, T * p + qc0:T * p + qc0 + 128],
                        pb[ob][0:64, 0:128]),
                        wait=[("pe", pev)], inc=("ac", 1))
    k.barrier()
    for half in range(2):
        k.dma("gpsimd", sb["wo_sb"][:, 2048 * half:2048 * (half + 1)],
              tens["lf_wo"][l][:, 512 * half:512 * (half + 1)].rearrange(
                  "(a p) (b c) -> p a b c", p=128, c=128), "d_wf")
    ldwo = k.v("d_wf")
    CW = 256
    st_done = [0] * 16
    for ch in range(16):
        ldy = k.dma("sync", sb["yinb"][:],
                    tens[yin][:, CW * ch:CW * (ch + 1)].rearrange("(a p) c -> p a c", p=128),
                    "d_xc", wait=[("ve", k.v("ve"))] if ch >= 1 else [])
        for dot in range(DT):
            bank = 6 + dot % 2
            w = [("d_wf", ldwo), ("ve", k.v("ve")), ("ac", k.v("ac"))]
            for pt in range(4):
                woc = 2048 * (dot // 4) + 512 * pt + 128 * (dot % 4)
                k.on("tensor", lambda e, ch=ch, woc=woc, pt=pt, bank=bank: e.matmul(
                    pb[bank][:, 0:CW], sb["wo_sb"][:, woc:woc + 128],
                    sb["or_all"][:, T * pt + CW * ch:T * pt + CW * (ch + 1)],
                    start=(pt == 0), stop=(pt == 3)), wait=w if pt == 0 else [],
                    inc=("pe", 1) if pt == 3 else None)
            pev = k.v("pe")
            wst = [("pe", pev), ("d_xc", ldy)]
            if dot == 0 and ch >= 1:
                wst.append(("d_rs", st_done[ch - 1]))
            k.on("vector", lambda e, dot=dot, bank=bank: e.tensor_tensor(
                sb["rescb"][:, CW * dot:CW * (dot + 1)], pb[bank][:, 0:CW],
                sb["yinb"][:, CW * dot:CW * (dot + 1)], ALU.add),
                wait=wst, inc=("ve", 1))
        st_done[ch] = k.dma("gpsimd",
                            tens[ya][:, CW * ch:CW * (ch + 1)].rearrange("(a p) c -> p a c", p=128),
                            sb["rescb"][:], "d_rs", wait=[("ve", k.v("ve"))])
    k.barrier()


LFA_SB = [
    ("csq", (64, T), BF16), ("ssq", (64, T), BF16), ("csk", (64, T), BF16), ("ssk", (64, T), BF16),
    ("mask_sb", (128, 4 * 1024), BF16), ("id_bf", (128, 128), BF16),
    ("qs_col", (64, 1), F32), ("ks_col", (64, 1), F32),
    ("ybuf", (128, 1025), F32), ("hbf", (128, 1024), BF16),
    ("wq_sb", (128, 1024), BF16), ("wk_sb", (128, 1024), BF16), ("wv_sb", (128, 1024), BF16),
    ("hb", (128, 8 * CH), BF16),
    ("qt", (128, T), BF16), ("kt", (128, T), BF16),
    ("vp", (128, 32 * 128), BF16),
    ("sqc", (128, CH), BF16), ("rows2", (65, 2048), F32), ("rnb", (64, T), BF16),
    ("q0", (64, T), BF16), ("qsw", (64, T), BF16), ("qnf", (64, T), BF16),
    ("qr", (128, T), BF16), ("kr", (128, T), BF16),
    ("sm", (128, 1024), F32), ("eb", (128, 1024), BF16),
    ("negmax", (128, 1), F32), ("rsum", (128, 1), F32),
    ("at_sb", (128, 8 * 128), BF16),
    ("or_all", (128, 4 * T), BF16),
    ("wo_sb", (128, 2 * 2048), BF16), ("yinb", (128, 8 * 256), F32), ("rescb", (128, 8 * 256), F32),
]


def build_lfattn_launch(l, yin_name, ya_name):
    nc = bass.Bass("TRN2", target_bir_lowering=False, debug=False)
    decls = [
        (yin_name, (D, T), F32, "ExternalInput"),
        ("lf_wqkv", (2, D, 1536), F32, "ExternalInput"),
        ("lf_qs", (2, 64), F32, "ExternalInput"),
        ("lf_ks", (2, 64), F32, "ExternalInput"),
        ("lf_wo", (2, 512, D), F32, "ExternalInput"),
        ("rot_cq", (64, T), F32, "ExternalInput"), ("rot_sq", (64, T), F32, "ExternalInput"),
        ("rot_ck", (64, T), F32, "ExternalInput"), ("rot_sk", (64, T), F32, "ExternalInput"),
        ("la_mask", (512, 1024), F32, "ExternalInput"),
        ("ident", (128, 128), F32, "ExternalInput"),
        (ya_name, (D, T), F32, "ExternalOutput"),
        ("hshiftT", (D, T), BF16, "Internal"),
    ]
    tens = {}
    for name, shape, dt, kind in decls:
        tens[name] = nc.dram_tensor(name, list(shape), dt, kind=kind).ap()
    ctx = ExitStack()
    with ctx:
        k, sb = _mk_k(nc, ctx, LFA_SB)
        st_lf_attn(k, tens, sb, l, yin_name, ya_name)
        _finish(nc, k)
    return nc




def st_lf_ffn(k, tens, sb, l, ya, yout):
    pb = k.pb
    HT, HP = 44, 22
    k.dma("gpsimd", tens["w1bf"].rearrange("a b -> (a b)"),
          tens["lf_w1p"][l].rearrange("a b -> (a b)"), "d_cw")
    k.dma("gpsimd", tens["w2bf"].rearrange("a b -> (a b)"),
          tens["lf_w2p"][l].rearrange("a b -> (a b)"), "d_cw")
    k.dma("sync", sb["b1c"][:], tens["lf_b1p_t"][l], "d_c")
    k.dma("sync", sb["lgc"][:], tens["lf_lng_t"][l], "d_c")
    k.dma("sync", sb["lbc"][:], tens["lf_lnb_t"][l], "d_c")
    k.dma("sync", sb["b2c"][:], tens["lf_b2_t"][l], "d_c")
    ld_c = k.v("d_c")
    cw = k.v("d_cw")
    _emit_shift(k, tens, sb, ya)
    k.barrier()
    mu = sb["rows"][0:1, 0:T]
    ms = sb["rows"][32:33, 0:T]
    rr = sb["rows"][32:33, T:2 * T]
    sq_done, mmB, ld_done = [0] * DT, [0] * DT, [0] * DT
    for d in range(DT):
        hsb = sb["hsb"][:]
        w = []
        if d >= 1:
            w = [("ac", sq_done[d - 1]), ("pe", mmB[d - 1])]
        ld_done[d] = k.dma("sync", hsb, tens["hshiftT"][d * 128:(d + 1) * 128, :],
                           "d_a0", wait=w)
        sq_done[d] = k.on("scalar", lambda e, hsb=hsb: e.activation(sb["hsq"][:], hsb, AF.Square),
                          wait=[("d_a0", ld_done[d]), ("ac", k.v("ac")),
                                ("pe", mmB[d - 1] if d >= 1 else 0)], inc=("ac", 1))
        for ch in range(NCH):
            row, bank = (ch % 3) * 32, ch // 3
            k.on("tensor", lambda e, hsb=hsb, ch=ch, row=row, bank=bank, d=d: e.matmul(
                pb[bank][row:row + 1, :], sb["onesb"][:], hsb[:, ch * CH:(ch + 1) * CH],
                start=(d == 0), stop=(d == DT - 1)),
                wait=[("d_a0", ld_done[d])] if ch == 0 else [])
            k.on("tensor", lambda e, ch=ch, row=row, bank=bank, d=d: e.matmul(
                pb[3 + bank][row:row + 1, :], sb["onesb"][:], sb["hsq"][:, ch * CH:(ch + 1) * CH],
                start=(d == 0), stop=(d == DT - 1)),
                wait=[("ac", sq_done[d])] if ch == 0 else [],
                inc=("pe", 1) if ch == NCH - 1 else None)
        mmB[d] = k.v("pe")
    pe_stats = mmB[DT - 1]
    for ch in range(NCH):
        row, bank = (ch % 3) * 32, ch // 3
        k.on("scalar", lambda e, ch=ch, row=row, bank=bank: e.activation(
            sb["rows"][0:1, ch * CH:(ch + 1) * CH], pb[bank][row:row + 1, :], AF.Copy, scale=1.0 / D),
            wait=[("pe", pe_stats)] if ch == 0 else [], inc=("ac", 1))
        k.on("scalar", lambda e, ch=ch, row=row, bank=bank: e.activation(
            sb["rows"][32:33, ch * CH:(ch + 1) * CH], pb[3 + bank][row:row + 1, :], AF.Copy, scale=1.0 / D),
            inc=("ac", 1))
    ac_mu = k.v("ac")
    k.on("vector", lambda e: e.tensor_tensor(rr, mu, mu, ALU.mult),
         wait=[("ac", ac_mu)], inc=("ve", 1))
    k.on("vector", lambda e: e.tensor_tensor(rr, ms, rr, ALU.subtract),
         wait=[("ve", k.v("ve"))], inc=("ve", 1))
    k.on("scalar", lambda e: e.activation(rr, rr, AF.Sqrt, bias=sb["c_eps"][32:33, :]),
         wait=[("ve", k.v("ve"))], inc=("ac", 1))
    k.on("vector", lambda e: e.reciprocal(rr, rr),
         wait=[("ac", k.v("ac"))], inc=("ve", 1))
    ve_r = k.v("ve")
    bc_done = 0
    for ch in range(NCH):
        ba = 4 + 2 * (ch % 2)
        wmm = [("ac", ac_mu), ("ve", ve_r)]
        if ch >= 2:
            wmm.append(("ac", bc_done))
        k.on("tensor", lambda e, ch=ch, ba=ba: e.matmul(
            pb[ba][:], sb["ones1f"][0:1, :], sb["rows"][0:1, ch * CH:(ch + 1) * CH],
            start=True, stop=True), wait=wmm, inc=("pe", 1))
        pev = k.on("tensor", lambda e, ch=ch, ba=ba: e.matmul(
            pb[ba + 1][:], sb["ones1f"][32:33, :], sb["rows"][32:33, T + ch * CH:T + (ch + 1) * CH],
            start=True, stop=True), inc=("pe", 1))
        k.on("scalar", lambda e, ch=ch, ba=ba: e.copy(
            sb["mub"][:, ch * CH:(ch + 1) * CH], pb[ba][:]),
            wait=[("pe", pev)], inc=("ac", 1))
        bc_done = k.on("scalar", lambda e, ch=ch, ba=ba: e.copy(
            sb["rb"][:, ch * CH:(ch + 1) * CH], pb[ba + 1][:]), inc=("ac", 1))
    ac_bc = k.v("ac")
    xst = [0] * DT
    for d in range(DT):
        hsb = sb["hsb"][:]
        ld = k.dma("sync", hsb, tens["hshiftT"][d * 128:(d + 1) * 128, :], "d_a0",
                   wait=[("ve", k.v("ve")), ("ac", k.v("ac")), ("pe", k.v("pe")),
                         ("d_x0", xst[d - 1] if d >= 1 else 0)])
        k.on("vector", lambda e, hsb=hsb: e.tensor_tensor(sb["hsq"][:], hsb, sb["mub"][:], ALU.subtract),
             wait=[("d_a0", ld), ("ac", ac_bc), ("ve", k.v("ve")), ("ac", k.v("ac"))], inc=("ve", 1))
        k.on("vector", lambda e: e.tensor_tensor(sb["hsq"][:], sb["hsq"][:], sb["rb"][:], ALU.mult),
             wait=[("ve", k.v("ve"))], inc=("ve", 1))
        vv = k.on("scalar", lambda e, d=d, hsb=hsb: e.activation(
            hsb, sb["hsq"][:], AF.Identity,
            bias=sb["lbc"][:, d:d + 1], scale=sb["lgc"][:, d:d + 1]),
            wait=[("ve", k.v("ve")), ("d_c", ld_c)], inc=("ac", 1))
        xst[d] = k.dma("gpsimd", tens["xn2T"][d * 128:(d + 1) * 128, :], hsb, "d_x0",
                       wait=[("ac", vv)])
    k.barrier()
    w1_rd = {}
    w2_rd = {}
    st_done = [0] * NCH
    for ch in range(NCH):
        ldy = k.dma("sync", sb["yab"][:],
                    tens[ya][:, CH * ch:CH * (ch + 1)].rearrange("(a p) c -> p a c", p=128),
                    "d_xc", wait=[("ve", k.v("ve"))] if ch >= 1 else [])
        ldxn = k.dma("sync", sb["xnb"][:],
                     tens["xn2T"][:, CH * ch:CH * (ch + 1)].rearrange("(a p) c -> p a c", p=128),
                     "d_xnb", wait=[("pe", k.v("pe"))] if ch >= 1 else [])
        for ht in range(HT):
            b2 = ht % 2
            wld = [("d_cw", cw)]
            if (ch, ht - 2) in w1_rd:
                wld.append(("pe", w1_rd[(ch, ht - 2)]))
            elif ch >= 1 and (ch - 1, HT - 2 + ht) in w1_rd:
                wld.append(("pe", w1_rd[(ch - 1, HT - 2 + ht)]))
            ldw = k.dma("gpsimd", sb["w1sl"][:, 1024 * b2:1024 * (b2 + 1)],
                        tens["w1bf"][:, 128 * ht:128 * (ht + 1)].rearrange("(a p) c -> p a c", p=128),
                        "d_w%d" % b2, wait=wld)
            bank = ht % 2
            w = [("d_w%d" % b2, ldw), ("ac", k.v("ac"))]
            w.append(("d_xnb", ldxn))
            for dm in range(DT):
                k.on("tensor", lambda e, ht=ht, dm=dm, bank=bank, b2=b2: e.matmul(
                    pb[bank][:], sb["w1sl"][:, 1024 * b2 + 128 * dm:1024 * b2 + 128 * (dm + 1)],
                    sb["xnb"][:, CH * dm:CH * (dm + 1)],
                    start=(dm == 0), stop=(dm == DT - 1)),
                    wait=w if dm == 0 else [],
                    inc=("pe", 1) if dm == DT - 1 else None)
            w1_rd[(ch, ht)] = k.v("pe")
            k.on("scalar", lambda e, ht=ht, bank=bank: e.activation(
                sb["h1c"][:, 512 * ht:512 * (ht + 1)], pb[bank][:], AF.Identity,
                bias=sb["b1c"][:, ht:ht + 1], scale=1.0),
                wait=[("pe", w1_rd[(ch, ht)])], inc=("ac", 1))
        ac_h1 = k.v("ac")
        for hp in range(HP):
            k.on("scalar", lambda e, hp=hp: e.activation(
                sb["gg"][:], sb["h1c"][:, 512 * (HP + hp):512 * (HP + hp + 1)], AF.Gelu),
                wait=[("ac", k.v("ac")), ("ve", k.v("ve"))], inc=("ac", 1))
            k.on("vector", lambda e, hp=hp: e.tensor_tensor(
                sb["h1c"][:, 512 * hp:512 * (hp + 1)], sb["h1c"][:, 512 * hp:512 * (hp + 1)],
                sb["gg"][:], ALU.mult),
                wait=[("ac", k.v("ac"))], inc=("ve", 1))
        ve_gg = k.v("ve")
        for do in range(DT):
            b2 = do % 2
            wld = [("d_cw", cw)]
            if (ch, do - 2) in w2_rd:
                wld.append(("pe", w2_rd[(ch, do - 2)]))
            elif ch >= 1 and (ch - 1, DT - 2 + do) in w2_rd:
                wld.append(("pe", w2_rd[(ch - 1, DT - 2 + do)]))
            ldw = k.dma("gpsimd", sb["w2sl"][:, 2816 * b2:2816 * (b2 + 1)],
                        tens["w2bf"][:, 128 * do:128 * (do + 1)].rearrange("(a p) c -> p a c", p=128),
                        "d_v%d" % b2, wait=wld)
            bank = 2 + do % 2
            w = [("d_v%d" % b2, ldw), ("ve", ve_gg), ("ve", k.v("ve"))]
            for ht in range(HP):
                k.on("tensor", lambda e, do=do, ht=ht, bank=bank, b2=b2: e.matmul(
                    pb[bank][:], sb["w2sl"][:, 2816 * b2 + 128 * ht:2816 * b2 + 128 * (ht + 1)],
                    sb["h1c"][:, 512 * ht:512 * (ht + 1)],
                    start=(ht == 0), stop=(ht == HP - 1)),
                    wait=w if ht == 0 else [],
                    inc=("pe", 1) if ht == HP - 1 else None)
            w2_rd[(ch, do)] = k.v("pe")
            wst = [("pe", w2_rd[(ch, do)]), ("d_xc", ldy)]
            if do == 0 and ch >= 1:
                wst.append(("d_rs", st_done[ch - 1]))
            k.on("vector", lambda e, do=do, bank=bank: e.scalar_tensor_tensor(
                sb["youtc"][:, CH * do:CH * (do + 1)], pb[bank][:], sb["b2c"][:, do:do + 1],
                sb["yab"][:, CH * do:CH * (do + 1)], ALU.add, ALU.add),
                wait=wst, inc=("ve", 1))
        st_done[ch] = k.dma("gpsimd",
                            tens[yout][:, CH * ch:CH * (ch + 1)].rearrange("(a p) c -> p a c", p=128),
                            sb["youtc"][:], "d_rs", wait=[("ve", k.v("ve"))])
    k.barrier()


def _emit_shift(k, tens, sb, yin):
    for d in range(DT):
        for cc in range(4):
            yb = sb["ybuf"][:, 0:1025]
            hf = sb["hbf"][:, 0:1024]
            wld = [("d_h0", k.v("d_h0")), ("ve", k.v("ve"))]
            if d < 4:
                ld = k.dma("sync", yb[:, 0:1024],
                           tens[yin][128 * d:128 * (d + 1), 1024 * cc:1024 * (cc + 1)],
                           "d_y0", wait=wld)
                vv = k.on("vector", lambda e, yb=yb, hf=hf: e.tensor_copy(hf, yb[:, 0:1024]),
                          wait=[("d_y0", ld), ("ve", k.v("ve"))], inc=("ve", 1))
            elif cc == 0:
                ld = k.dma("sync", yb[:, 0:1023],
                           tens[yin][128 * d:128 * (d + 1), 0:1023], "d_y0", wait=wld)
                k.on("vector", lambda e, hf=hf: e.memset(hf[:, 0:1], 0.0),
                     wait=[("ve", k.v("ve"))] + wld, inc=("ve", 1))
                vv = k.on("vector", lambda e, yb=yb, hf=hf: e.tensor_copy(hf[:, 1:1024], yb[:, 0:1023]),
                          wait=[("d_y0", ld), ("ve", k.v("ve"))], inc=("ve", 1))
            else:
                ld = k.dma("sync", yb[:, 0:1024],
                           tens[yin][128 * d:128 * (d + 1), 1024 * cc - 1:1024 * (cc + 1) - 1],
                           "d_y0", wait=wld)
                vv = k.on("vector", lambda e, yb=yb, hf=hf: e.tensor_copy(hf, yb[:, 0:1024]),
                          wait=[("d_y0", ld), ("ve", k.v("ve"))], inc=("ve", 1))
            k.dma("gpsimd", tens["hshiftT"][128 * d:128 * (d + 1), 1024 * cc:1024 * (cc + 1)],
                  hf, "d_h0", wait=[("ve", vv)])


LFF_SB = [
    ("b1c", (128, 44), F32), ("lgc", (128, 8), F32), ("lbc", (128, 8), F32), ("b2c", (128, 8), F32),
    ("ybuf", (128, 1025), F32), ("hbf", (128, 1024), BF16),
    ("hsb", (128, T), BF16), ("hsq", (128, T), BF16),
    ("rows", (65, 2 * T), F32),
    ("mub", (128, T), BF16), ("rb", (128, T), BF16),
    ("xnb", (128, 8 * CH), BF16),
    ("w1sl", (128, 2 * 1024), BF16), ("w2sl", (128, 2 * 2816), BF16),
    ("h1c", (128, 44 * CH), BF16), ("gg", (128, CH), BF16),
    ("yab", (128, 8 * CH), F32), ("youtc", (128, 8 * CH), F32),
]


def build_lfffn_launch(l, ya_name, yout_name):
    nc = bass.Bass("TRN2", target_bir_lowering=False, debug=False)
    decls = [
        (ya_name, (D, T), F32, "ExternalInput"),
        ("lf_w1p", (2, D, 5632), F32, "ExternalInput"),
        ("lf_b1p_t", (2, 128, 44), F32, "ExternalInput"),
        ("lf_w2p", (2, 2816, D), F32, "ExternalInput"),
        ("lf_b2_t", (2, 128, 8), F32, "ExternalInput"),
        ("lf_lng_t", (2, 128, 8), F32, "ExternalInput"),
        ("lf_lnb_t", (2, 128, 8), F32, "ExternalInput"),
        (yout_name, (D, T), F32, "ExternalOutput"),
        ("hshiftT", (D, T), BF16, "Internal"),
        ("xn2T", (D, T), BF16, "Internal"),
        ("w1bf", (D, 5632), BF16, "Internal"),
        ("w2bf", (2816, D), BF16, "Internal"),
    ]
    tens = {}
    for name, shape, dt, kind in decls:
        tens[name] = nc.dram_tensor(name, list(shape), dt, kind=kind).ap()
    ctx = ExitStack()
    with ctx:
        k, sb = _mk_k(nc, ctx, LFF_SB)
        st_lf_ffn(k, tens, sb, l, ya_name, yout_name)
        _finish(nc, k)
    return nc


def st_final(k, tens, sb, a_src, b_src, dst):
    st_done = [0] * DT
    for d in range(DT):
        b2 = d % 2
        aslc = sb["fa"][:, T * b2:T * (b2 + 1)]
        bslc = sb["fb"][:, T * b2:T * (b2 + 1)]
        w = [("ve", k.v("ve"))]
        if d >= 2:
            w.append(("d_rs", st_done[d - 2]))
        lda = k.dma("sync", aslc, tens[a_src][128 * d:128 * (d + 1), :], "d_a%d" % b2, wait=w)
        ldb = k.dma("sync", bslc, tens[b_src][128 * d:128 * (d + 1), :], "d_b%d" % b2, wait=[])
        vv = k.on("vector", lambda e, aslc=aslc, bslc=bslc: e.tensor_tensor(
            aslc, aslc, bslc, ALU.add),
            wait=[("d_a%d" % b2, lda), ("d_b%d" % b2, ldb), ("ve", k.v("ve"))], inc=("ve", 1))
        st_done[d] = k.dma("gpsimd", tens[dst][128 * d:128 * (d + 1), :], aslc, "d_rs",
                           wait=[("ve", vv)])
    k.barrier()


FIN_SB = [("fa", (128, 2 * T), F32), ("fb", (128, 2 * T), F32)]


def build_final_launch(a_name, b_name, out_name):
    nc = bass.Bass("TRN2", target_bir_lowering=False, debug=False)
    decls = [
        (a_name, (D, T), F32, "ExternalInput"),
        (b_name, (D, T), F32, "ExternalInput"),
        (out_name, (D, T), F32, "ExternalOutput"),
    ]
    tens = {}
    for name, shape, dt, kind in decls:
        tens[name] = nc.dram_tensor(name, list(shape), dt, kind=kind).ap()
    ctx = ExitStack()
    with ctx:
        k, sb = _mk_k(nc, ctx, FIN_SB)
        st_final(k, tens, sb, a_name, b_name, out_name)
        _finish(nc, k)
    return nc


def _mk_k(nc, ctx, sb_decls):
    k = K(nc, ctx)
    sb = {}
    seen = set()
    for nm, shape, dt in sb_decls:
        if nm in seen:
            continue
        seen.add(nm)
        sb[nm] = ctx.enter_context(nc.sbuf_tensor(nm, list(shape), dt))
    for nm, shape, dt in [("onesb", (128, 1), BF16), ("ones1f", (33, 128), F32),
                          ("c_eps", (128, 1), F32)]:
        sb[nm] = ctx.enter_context(nc.sbuf_tensor(nm, list(shape), dt))
    k.pb = [ctx.enter_context(nc.psum_tensor("pb%d" % i, [128, 512], F32))
            for i in range(8)]
    k.on("vector", lambda e: e.memset(sb["onesb"][:], 1.0), inc=("ve", 1))
    k.on("vector", lambda e: e.memset(sb["ones1f"][:], 1.0), inc=("ve", 1))
    k.on("vector", lambda e: e.memset(sb["c_eps"][:], 1e-5), inc=("ve", 1))
    return k, sb


def _finish(nc, k):
    waits = [(s, k.val[s]) for s in list(k.val) if s.startswith("d_") and k.val[s] > 0]
    k.q["sync"].append((None, waits, []))
    with nc.Block() as block:
        k.finalize(block)
    return nc


def build_l0attn_launch(kinds):
    nc = bass.Bass("TRN2", target_bir_lowering=False, debug=False)
    decls = [
        ("xn_bf", (D, T), BF16, "ExternalInput"),
        ("xT", (D, T), F32, "ExternalInput"),
        ("w_qkv", (D, 3072), F32, "ExternalInput"),
        ("b_qkv_t", (128, 24), F32, "ExternalInput"),
        ("b_qkv_raw", (1, 3072), F32, "ExternalInput"),
        ("w_out", (D, D), F32, "ExternalInput"),
        ("b_out_t", (128, 8), F32, "ExternalInput"),
        ("strips", (16, 128, 3072), F32, "ExternalInput"),
        ("rb_lo", (128, 16), F32, "ExternalInput"),
        ("rb_hi", (128, 16), F32, "ExternalInput"),
        ("ident", (128, 128), F32, "ExternalInput"),
        ("b_gate", (1, 8), F32, "ExternalInput"),
        ("x1p", (D, T), F32, "ExternalOutput"),
        ("otT", (D, T), BF16, "Internal"),
    ]
    tens = {}
    for name, shape, dt, kind in decls:
        tens[name] = nc.dram_tensor(name, list(shape), dt, kind=kind).ap()
    ctx = ExitStack()
    with ctx:
        k, sb = _mk_k(nc, ctx, L0A_SB)
        st_l0_attn(k, tens, sb, "xn_bf", "xT", "x1p")
        _finish(nc, k)
    return nc


def build_ln_launch(src, dst_bf, dst_f32, kinds):
    nc = bass.Bass("TRN2", target_bir_lowering=False, debug=False)
    decls = [(src, (D, T), F32, kinds.get(src, "ExternalInput")),
             (dst_bf, (D, T), BF16, kinds.get(dst_bf, "ExternalOutput"))]
    if dst_f32:
        decls.append((dst_f32, (D, T), F32, kinds.get(dst_f32, "ExternalOutput")))
    tens = {}
    for name, shape, dt, kind in decls:
        tens[name] = nc.dram_tensor(name, list(shape), dt, kind=kind).ap()
    ctx = ExitStack()
    with ctx:
        k, sb = _mk_k(nc, ctx, LN_SB)
        st_layernorm(k, tens, sb, src, dst_bf, dst_f32)
        _finish(nc, k)
    return nc


FUSED_DECLS = [
    ("xT", (D, T), F32, "ExternalInput"),
    ("w_qkv", (D, 3072), F32, "ExternalInput"),
    ("b_qkv_t", (128, 24), F32, "ExternalInput"),
    ("b_qkv_raw", (1, 3072), F32, "ExternalInput"),
    ("w_out", (D, D), F32, "ExternalInput"),
    ("b_out_t", (128, 8), F32, "ExternalInput"),
    ("strips", (16, 128, 3072), F32, "ExternalInput"),
    ("rb_lo", (128, 16), F32, "ExternalInput"),
    ("rb_hi", (128, 16), F32, "ExternalInput"),
    ("ident", (128, 128), F32, "ExternalInput"),
    ("w_gate", (D, 8), F32, "ExternalInput"),
    ("b_gate", (1, 8), F32, "ExternalInput"),
    ("moe_w1", (8, D, D), F32, "ExternalInput"),
    ("moe_w2", (8, D, D), F32, "ExternalInput"),
    ("moe_b1c", (128, 64), F32, "ExternalInput"),
    ("moe_b2", (8, D), F32, "ExternalInput"),
    ("emask8", (8, 1024), F32, "ExternalInput"),
    ("p_Wq", (D, D), F32, "ExternalInput"), ("p_Wk", (D, D), F32, "ExternalInput"),
    ("p_Wv", (D, D), F32, "ExternalInput"), ("p_Wo", (D, D), F32, "ExternalInput"),
    ("p_bq_t", (128, 8), F32, "ExternalInput"), ("p_bk_t", (128, 8), F32, "ExternalInput"),
    ("p_bv_t", (128, 8), F32, "ExternalInput"), ("p_bo_t", (128, 8), F32, "ExternalInput"),
    ("p_bv_raw", (1, D), F32, "ExternalInput"),
    ("p_rm", (128, 256), F32, "ExternalInput"),
    ("lf_wqkv", (2, D, 1536), F32, "ExternalInput"),
    ("lf_qs", (2, 64), F32, "ExternalInput"),
    ("lf_ks", (2, 64), F32, "ExternalInput"),
    ("lf_wo", (2, 512, D), F32, "ExternalInput"),
    ("rot_cq", (64, T), F32, "ExternalInput"), ("rot_sq", (64, T), F32, "ExternalInput"),
    ("rot_ck", (64, T), F32, "ExternalInput"), ("rot_sk", (64, T), F32, "ExternalInput"),
    ("la_mask", (512, 1024), F32, "ExternalInput"),
    ("lf_w1p", (2, D, 5632), F32, "ExternalInput"),
    ("lf_b1p_t", (2, 128, 44), F32, "ExternalInput"),
    ("lf_w2p", (2, 2816, D), F32, "ExternalInput"),
    ("lf_b2_t", (2, 128, 8), F32, "ExternalInput"),
    ("lf_lng_t", (2, 128, 8), F32, "ExternalInput"),
    ("lf_lnb_t", (2, 128, 8), F32, "ExternalInput"),
    ("outT", (D, T), F32, "ExternalOutput"),
    ("xn_bf", (D, T), BF16, "Internal"), ("xn_f32", (D, T), F32, "Internal"),
    ("x1p", (D, T), F32, "Internal"), ("x1", (D, T), F32, "Internal"),
    ("xn1_bf", (D, T), BF16, "Internal"), ("xn1_f32", (D, T), F32, "Internal"),
    ("x2p", (D, T), F32, "Internal"),
    ("y_a0", (D, T), F32, "Internal"), ("y_m0", (D, T), F32, "Internal"),
    ("y_a1", (D, T), F32, "Internal"), ("y_m1", (D, T), F32, "Internal"),
    ("otT", (D, T), BF16, "Internal"),
    ("hshiftT", (D, T), BF16, "Internal"), ("xn2T", (D, T), BF16, "Internal"),
    ("moe_w1bf", (8, D, D), BF16, "Internal"), ("moe_w2bf", (8, D, D), BF16, "Internal"),
    ("w1bf", (D, 5632), BF16, "Internal"), ("w2bf", (2816, D), BF16, "Internal"),
]


def build_fused():
    nc = bass.Bass("TRN2", target_bir_lowering=False, debug=False)
    tens = {}
    for name, shape, dt, kind in FUSED_DECLS:
        tens[name] = nc.dram_tensor(name, list(shape), dt, kind=kind).ap()
    outer = ExitStack()
    with outer:
        k = K(nc, outer)
        sbc = {}
        for nm, shape, dt in [("onesb", (128, 1), BF16), ("ones1f", (33, 128), F32),
                              ("c_eps", (128, 1), F32)]:
            sbc[nm] = outer.enter_context(nc.sbuf_tensor(nm, list(shape), dt))
        k.pb = [outer.enter_context(nc.psum_tensor("pb%d" % i, [128, 512], F32))
                for i in range(8)]
        k.on("vector", lambda e: e.memset(sbc["onesb"][:], 1.0), inc=("ve", 1))
        k.on("vector", lambda e: e.memset(sbc["ones1f"][:], 1.0), inc=("ve", 1))
        k.on("vector", lambda e: e.memset(sbc["c_eps"][:], 1e-5), inc=("ve", 1))
        stage_idx = [0]

        def stage(sb_decls, fn):
            si = stage_idx[0]
            stage_idx[0] += 1
            ctx = ExitStack()
            sb = dict(sbc)
            seen = set()
            for nm, shape, dt in sb_decls:
                if nm in seen:
                    continue
                seen.add(nm)
                sb[nm] = ctx.enter_context(
                    nc.sbuf_tensor("s%d_%s" % (si, nm), list(shape), dt))
            fn(sb)
            ctx.close()

        stage(LN_SB, lambda sb: st_layernorm(k, tens, sb, "xT", "xn_bf", "xn_f32"))
        stage(L0A_SB, lambda sb: st_l0_attn(k, tens, sb, "xn_bf", "xT", "x1p"))
        stage(MOE_SB, lambda sb: st_l0_ffn(k, tens, sb, "xn_bf", "x1p", "x1"))
        stage(LN_SB, lambda sb: st_layernorm(k, tens, sb, "x1", "xn1_bf", "xn1_f32"))
        stage(L1A_SB, lambda sb: st_l1_attn(k, tens, sb, "xn1_bf", "x1", "x2p"))
        lf_names = {0: ("xn1_f32", "y_a0", "y_m0"), 1: ("y_m0", "y_a1", "y_m1")}
        for l in range(2):
            yin, ya, ym = lf_names[l]
            stage(LFA_SB, lambda sb, l=l, yin=yin, ya=ya: st_lf_attn(k, tens, sb, l, yin, ya))
            stage(LFF_SB, lambda sb, l=l, ya=ya, ym=ym: st_lf_ffn(k, tens, sb, l, ya, ym))
        stage(FIN_SB, lambda sb: st_final(k, tens, sb, "x2p", "y_m1", "outT"))
        _finish(nc, k)
    return nc


def kernel(**inputs):
    x = np.asarray(inputs["L0_attn_x"], np.float32)
    sh = _prep_shared(inputs)
    xT = [np.ascontiguousarray(x[b].T) for b in range(NCORE)]
    cores = list(range(NCORE))
    b_qkv_raw = np.ascontiguousarray(np.asarray(inputs["L0_attn_b_qkv"], np.float32).reshape(1, 3072))
    p_bv_raw = np.ascontiguousarray(np.asarray(inputs["L1_attn_bv"], np.float32).reshape(1, 1024))

    nc = build_fused()
    shared_keys = [name for name, _s, _d, kind in FUSED_DECLS
                   if kind == "ExternalInput" and name not in ("xT", "b_qkv_raw", "p_bv_raw")]
    maps = []
    for b in range(NCORE):
        m = {kk: sh[kk] for kk in shared_keys}
        m["xT"] = xT[b]
        m["b_qkv_raw"] = b_qkv_raw
        m["p_bv_raw"] = p_bv_raw
        maps.append(m)
    r = run_bass_kernel_spmd(nc, maps, core_ids=cores).results
    out = np.stack([np.asarray(r[b]["outT"], np.float32).T for b in range(NCORE)])
    return np.ascontiguousarray(out)


def kernel_multilaunch(**inputs):
    x = np.asarray(inputs["L0_attn_x"], np.float32)
    sh = _prep_shared(inputs)
    xT = [np.ascontiguousarray(x[b].T) for b in range(NCORE)]
    cores = list(range(NCORE))
    b_qkv_raw = np.ascontiguousarray(np.asarray(inputs["L0_attn_b_qkv"], np.float32).reshape(1, 3072))
    p_bv_raw = np.ascontiguousarray(np.asarray(inputs["L1_attn_bv"], np.float32).reshape(1, 1024))

    def run(nc, per_core, shared_keys):
        maps = []
        for b in range(NCORE):
            m = {kk: sh[kk] for kk in shared_keys}
            m.update(per_core[b])
            maps.append(m)
        return run_bass_kernel_spmd(nc, maps, core_ids=cores).results

    nc = build_ln_launch("xT", "xn_bf", "xn_f32", {})
    r = run(nc, [{"xT": xT[b]} for b in range(NCORE)], [])
    xn_bf = [r[b]["xn_bf"] for b in range(NCORE)]
    xn_f32 = [r[b]["xn_f32"] for b in range(NCORE)]
    nc = build_l0attn_launch({})
    r = run(nc, [{"xn_bf": xn_bf[b], "xT": xT[b], "b_qkv_raw": b_qkv_raw}
                 for b in range(NCORE)],
            ["w_qkv", "b_qkv_t", "w_out", "b_out_t", "strips", "rb_lo", "rb_hi",
             "ident", "b_gate"])
    x1p = [r[b]["x1p"] for b in range(NCORE)]
    nc = build_l0ffn_launch({})
    r = run(nc, [{"xn_bf": xn_bf[b], "xn_f32": xn_f32[b], "x1p": x1p[b]}
                 for b in range(NCORE)],
            ["w_gate", "b_gate", "moe_w1", "moe_w2", "moe_b1c", "moe_b2", "ident", "emask8"])
    x1 = [r[b]["x1"] for b in range(NCORE)]
    nc = build_ln_launch("x1", "xn1_bf", "xn1_f32", {})
    r = run(nc, [{"x1": x1[b]} for b in range(NCORE)], [])
    xn1_bf = [r[b]["xn1_bf"] for b in range(NCORE)]
    xn1_f32 = [r[b]["xn1_f32"] for b in range(NCORE)]
    nc = build_l1attn_launch({})
    r = run(nc, [{"xn1_bf": xn1_bf[b], "x1": x1[b], "p_bv_raw": p_bv_raw}
                 for b in range(NCORE)],
            ["p_Wq", "p_Wk", "p_Wv", "p_Wo", "p_bq_t", "p_bk_t", "p_bv_t", "p_bo_t", "p_rm"])
    x2p = [r[b]["x2p"] for b in range(NCORE)]
    lf_shared_a = ["lf_wqkv", "lf_qs", "lf_ks", "lf_wo", "rot_cq", "rot_sq",
                   "rot_ck", "rot_sk", "la_mask", "ident"]
    lf_shared_f = ["lf_w1p", "lf_b1p_t", "lf_w2p", "lf_b2_t", "lf_lng_t", "lf_lnb_t"]
    yin = xn1_f32
    for l in range(2):
        nc = build_lfattn_launch(l, "y_in", "y_a")
        r = run(nc, [{"y_in": yin[b]} for b in range(NCORE)], lf_shared_a)
        ya = [r[b]["y_a"] for b in range(NCORE)]
        nc = build_lfffn_launch(l, "y_a", "y_m")
        r = run(nc, [{"y_a": ya[b]} for b in range(NCORE)], lf_shared_f)
        yin = [r[b]["y_m"] for b in range(NCORE)]
    nc = build_final_launch("x2p", "y2", "outT")
    r = run(nc, [{"x2p": x2p[b], "y2": yin[b]} for b in range(NCORE)], [])
    out = np.stack([np.asarray(r[b]["outT"], np.float32).T for b in range(NCORE)])
    return np.ascontiguousarray(out)


# version 06
# speedup vs baseline: 1.1692x; 1.1692x over previous
import numpy as np
from contextlib import ExitStack
import concourse.bass as bass
import concourse.mybir as mybir
from concourse.bass_utils import run_bass_kernel_spmd
import ml_dtypes

F32 = mybir.dt.float32
BF16 = mybir.dt.bfloat16
AF = mybir.ActivationFunctionType
ALU = mybir.AluOpType
AX = mybir.AxisListType
NPBF = ml_dtypes.bfloat16

D = 1024
T = 4096
DT = D // 128
CH = 512
NCH = T // CH
NCORE = 8
FMAX = np.float64(np.finfo(np.float32).max)

ENGINES = ("tensor", "vector", "scalar", "sync", "gpsimd")


class K:

    def __init__(self, nc, ctx):
        self.nc = nc
        self.ctx = ctx
        self.q = {e: [] for e in ENGINES}
        self.sems = {}
        self.val = {}
        self.dirty = {e: False for e in ENGINES}
        self.pb = None

    def sem(self, name):
        if name not in self.sems:
            self.sems[name] = self.ctx.enter_context(self.nc.semaphore(name))
            self.val[name] = 0
        return self.sems[name]

    def v(self, name):
        return self.val.get(name, 0)

    def on(self, eng, fn, wait=(), inc=None):
        incs = []
        if inc:
            if isinstance(inc, tuple):
                inc = [inc]
            for name, amt in inc:
                self.sem(name)
                self.val[name] += amt
                incs.append((name, amt))
        wl = [(s, v) for (s, v) in wait if v > 0]
        for name, _v in wl:
            self.sem(name)
        self.q[eng].append((fn, wl, incs))
        self.dirty[eng] = True
        return self.val[incs[0][0]] if incs else None

    def dma(self, eng, out, in_, sem_name, wait=(), **kw):
        def fn(e, out=out, in_=in_, kw=kw):
            return e.dma_start(out=out, in_=in_, **kw)
        return self.on(eng, fn, wait=wait, inc=(sem_name, 16))

    def barrier(self):
        bar = "bar"
        self.sem(bar)
        for e in ("tensor", "vector", "scalar", "gpsimd"):
            if self.dirty[e]:
                if e == "tensor":
                    self.on(e, lambda en: en.nop(), inc=(bar, 1))
                else:
                    self.on(e, lambda en: en.drain(), inc=(bar, 1))
        target = self.val.get(bar, 0)
        dma_waits = [(s, self.val[s]) for s in list(self.val)
                     if s.startswith("d_") and self.val[s] > 0]
        for e in ENGINES:
            self.q[e].append((None, [(bar, target)] + dma_waits, []))
            self.dirty[e] = False

    def run_engine(self, e, eng_obj):
        for fn, waits, incs in self.q[e]:
            for s, v in waits:
                eng_obj.wait_ge(self.sems[s], v)
            if fn is None:
                continue
            inst = fn(eng_obj)
            for s, a in incs:
                inst.then_inc(self.sems[s], a)

    def finalize(self, block):
        block.tensor(lambda e: self.run_engine("tensor", e))
        block.vector(lambda e: self.run_engine("vector", e))
        block.scalar(lambda e: self.run_engine("scalar", e))
        block.sync(lambda e: self.run_engine("sync", e))
        block.gpsimd(lambda e: self.run_engine("gpsimd", e))


def _prep_shared(ip):
    sh = {}
    f32 = np.float32
    sh["w_qkv"] = np.ascontiguousarray(ip["L0_attn_W_qkv"], f32)
    sh["b_qkv_t"] = np.ascontiguousarray(
        np.asarray(ip["L0_attn_b_qkv"], f32).reshape(24, 128).T)
    sh["w_out"] = np.ascontiguousarray(ip["L0_attn_W_out"], f32)
    sh["b_out_t"] = np.ascontiguousarray(
        np.asarray(ip["L0_attn_b_out"], f32).reshape(8, 128).T)
    rb = np.asarray(ip["L0_attn_rel_bias"], f32)
    u = np.arange(3072)[None, :] - 1536
    jp = np.arange(128)[:, None]
    idx = np.clip(u - jp + 1024, 0, 2048)
    sh["strips"] = np.ascontiguousarray(rb[idx].transpose(2, 0, 1))
    sh["rb_lo"] = np.ascontiguousarray(np.broadcast_to(rb[0][None, :], (128, 16)))
    sh["rb_hi"] = np.ascontiguousarray(np.broadcast_to(rb[2048][None, :], (128, 16)))
    sh["w_gate"] = np.ascontiguousarray(ip["L0_ffn_W_gate"], f32)
    sh["b_gate"] = np.ascontiguousarray(np.asarray(ip["L0_ffn_b_gate"], f32).reshape(1, 8))
    sh["moe_w1"] = np.ascontiguousarray(ip["L0_ffn_W1"], f32)
    sh["moe_b1c"] = np.ascontiguousarray(
        np.asarray(ip["L0_ffn_b1"], f32).reshape(8, 8, 128).transpose(2, 0, 1).reshape(128, 64))
    sh["moe_w2"] = np.ascontiguousarray(ip["L0_ffn_W2"], f32)
    sh["moe_b2"] = np.ascontiguousarray(ip["L0_ffn_b2"], f32)
    for nm in ("Wq", "Wk", "Wv", "Wo"):
        sh["p_" + nm] = np.ascontiguousarray(ip["L1_attn_" + nm], f32)
    for nm in ("bq", "bk", "bv", "bo"):
        sh["p_" + nm + "_t"] = np.ascontiguousarray(
            np.asarray(ip["L1_attn_" + nm], f32).reshape(8, 128).T)
    rm_ = np.asarray(ip["L1_attn_random_matrix"], f32)
    sh["p_rm"] = np.ascontiguousarray(np.concatenate([rm_, rm_], axis=0))
    sh["lf_wqkv"] = np.ascontiguousarray(ip["L1_ffn_wqkv"], f32)
    sh["lf_qs"] = np.ascontiguousarray(ip["L1_ffn_q_scale"], f32)
    sh["lf_ks"] = np.ascontiguousarray(ip["L1_ffn_k_scale"], f32)
    sh["lf_wo"] = np.ascontiguousarray(ip["L1_ffn_wo"], f32)
    sh["lf_lng_t"] = np.ascontiguousarray(
        np.asarray(ip["L1_ffn_ln_g"], f32).reshape(2, 8, 128).transpose(0, 2, 1))
    sh["lf_lnb_t"] = np.ascontiguousarray(
        np.asarray(ip["L1_ffn_ln_b"], f32).reshape(2, 8, 128).transpose(0, 2, 1))
    HID = 2730
    HIDP = 2816
    w1 = np.asarray(ip["L1_ffn_w1"], f32)
    w1p = np.zeros((2, 1024, 2 * HIDP), f32)
    w1p[:, :, 0:HID] = w1[:, :, 0:HID]
    w1p[:, :, HIDP:HIDP + HID] = w1[:, :, HID:2 * HID]
    sh["lf_w1p"] = w1p
    b1 = np.asarray(ip["L1_ffn_b1"], f32)
    b1p = np.zeros((2, 2 * HIDP), f32)
    b1p[:, 0:HID] = b1[:, 0:HID]
    b1p[:, HIDP:HIDP + HID] = b1[:, HID:2 * HID]
    sh["lf_b1p_t"] = np.ascontiguousarray(
        b1p.reshape(2, 44, 128).transpose(0, 2, 1))
    w2 = np.asarray(ip["L1_ffn_w2"], f32)
    w2p = np.zeros((2, HIDP, 1024), f32)
    w2p[:, 0:HID, :] = w2
    sh["lf_w2p"] = w2p
    sh["lf_b2_t"] = np.ascontiguousarray(
        np.asarray(ip["L1_ffn_b2"], f32).reshape(2, 8, 128).transpose(0, 2, 1))
    DIM_HEAD, WINDOW = 64, 512
    half = np.arange(0, DIM_HEAD, 2, dtype=np.float64)
    inv_freq = 1.0 / 10000.0 ** (half / DIM_HEAD)
    t = np.arange(T, dtype=np.float64)
    freqs = np.concatenate([t[:, None] * inv_freq[None, :]] * 2, axis=1)
    base = (half + 0.4 * DIM_HEAD) / (1.4 * DIM_HEAD)
    power = (t - T // 2) / float(WINDOW // 2)
    scale = np.concatenate([base[None, :] ** power[:, None]] * 2, axis=1)
    cos, sin = np.cos(freqs), np.sin(freqs)
    sign = np.ones(64); sign[:32] = -1.0
    sh["rot_cq"] = np.ascontiguousarray((cos * scale).T * 8.0, f32)
    sh["rot_sq"] = np.ascontiguousarray((sin * scale * sign[None, :]).T * 8.0, f32)
    sh["rot_ck"] = np.ascontiguousarray((cos / scale).T, f32)
    sh["rot_sk"] = np.ascontiguousarray((sin / scale * sign[None, :]).T, f32)
    il = np.arange(512)[:, None]
    jl = np.arange(1024)[None, :]
    dd = il - jl + 512
    m = np.zeros((512, 1024), f32)
    m[~((dd >= 0) & (dd <= 512))] = -1e30
    sh["la_mask"] = m
    sh["ident"] = np.eye(128, dtype=f32)
    em = np.zeros((8, 8, 128), f32)
    for e in range(8):
        em[e, e, :] = 1.0
    sh["emask8"] = np.ascontiguousarray(em.transpose(1, 0, 2).reshape(8, 1024))
    return sh


def st_layernorm(k, tens, sb, src, dst_bf, dst_f32=None):
    pb = k.pb
    x = tens[src]
    mu, ms, rr = sb["row_mu"], sb["row_ms"], sb["row_r"]
    sq_done, cp_done, mmB = [0] * DT, [0] * DT, [0] * DT
    ld_done = [0] * DT
    for d in range(DT):
        b2 = d % 2
        xf, xbf, xsq = (sb["xf"][:, b2 * T:(b2 + 1) * T],
                        sb["xbf"][:, b2 * T:(b2 + 1) * T],
                        sb["xsq"][:, b2 * T:(b2 + 1) * T])
        w = []
        if d >= 2:
            w = [("ac", sq_done[d - 2]), ("ve", cp_done[d - 2])]
        ld_done[d] = k.dma("sync", xf, x[d * 128:(d + 1) * 128, :], "d_a%d" % (d % 2), wait=w)
        wpe = [("pe", mmB[d - 2])] if d >= 2 else []
        sq_done[d] = k.on("scalar", lambda e, xf=xf, xsq=xsq: e.activation(xsq, xf, AF.Square),
                          wait=[("d_a%d" % (d % 2), ld_done[d])] + wpe, inc=("ac", 1))
        cp_done[d] = k.on("vector", lambda e, xf=xf, xbf=xbf: e.tensor_copy(xbf, xf),
                          wait=[("d_a%d" % (d % 2), ld_done[d])] + wpe, inc=("ve", 1))
        for ch in range(NCH):
            row, bank = (ch % 3) * 32, ch // 3
            k.on("tensor", lambda e, xbf=xbf, ch=ch, row=row, bank=bank, d=d: e.matmul(
                pb[bank][row:row + 1, :], sb["onesb"][:], xbf[:, ch * CH:(ch + 1) * CH],
                start=(d == 0), stop=(d == DT - 1)),
                wait=[("ve", cp_done[d])] if ch == 0 else [])
            k.on("tensor", lambda e, xsq=xsq, ch=ch, row=row, bank=bank, d=d: e.matmul(
                pb[3 + bank][row:row + 1, :], sb["onesb"][:], xsq[:, ch * CH:(ch + 1) * CH],
                start=(d == 0), stop=(d == DT - 1)),
                wait=[("ac", sq_done[d])] if ch == 0 else [],
                inc=("pe", 1) if ch == NCH - 1 else None)
        mmB[d] = k.v("pe")
    pe_stats = mmB[DT - 1]
    for ch in range(NCH):
        row, bank = (ch % 3) * 32, ch // 3
        k.on("scalar", lambda e, ch=ch, row=row, bank=bank: e.activation(
            mu[0:1, ch * CH:(ch + 1) * CH], pb[bank][row:row + 1, :], AF.Copy, scale=1.0 / D),
            wait=[("pe", pe_stats)] if ch == 0 else [], inc=("ac", 1))
        k.on("scalar", lambda e, ch=ch, row=row, bank=bank: e.activation(
            ms[0:1, ch * CH:(ch + 1) * CH], pb[3 + bank][row:row + 1, :], AF.Copy, scale=1.0 / D),
            inc=("ac", 1))
    ac_mu = k.v("ac")
    k.on("vector", lambda e: e.tensor_tensor(rr[:], mu[:], mu[:], ALU.mult),
         wait=[("ac", ac_mu)], inc=("ve", 1))
    k.on("vector", lambda e: e.tensor_tensor(rr[:], ms[:], rr[:], ALU.subtract),
         wait=[("ve", k.v("ve"))], inc=("ve", 1))
    ve_var = k.v("ve")
    k.on("scalar", lambda e: e.activation(rr[:], rr[:], AF.Sqrt, bias=sb["c_eps"][0:1, :]),
         wait=[("ve", ve_var)], inc=("ac", 1))
    k.on("vector", lambda e: e.reciprocal(rr[:], rr[:]),
         wait=[("ac", k.v("ac"))], inc=("ve", 1))
    ve_r = k.v("ve")
    ac_r = k.v("ac")
    bc_mm = [0] * NCH
    for ch in range(NCH):
        ba = 4 + 2 * (ch % 2)
        wmm = [("ac", ac_mu), ("ve", ve_r)]
        if ch >= 2:
            wmm.append(("ac", bc_mm[ch - 2]))
        k.on("tensor", lambda e, ch=ch, ba=ba: e.matmul(
            pb[ba][:], sb["ones1f"][0:1, :], mu[0:1, ch * CH:(ch + 1) * CH], start=True, stop=True),
            wait=wmm, inc=("pe", 1))
        pev = k.on("tensor", lambda e, ch=ch, ba=ba: e.matmul(
            pb[ba + 1][:], sb["ones1f"][0:1, :], rr[0:1, ch * CH:(ch + 1) * CH], start=True, stop=True),
            inc=("pe", 1))
        k.on("scalar", lambda e, ch=ch, ba=ba: e.copy(
            sb["mub"][:, ch * CH:(ch + 1) * CH], pb[ba][:]),
            wait=[("pe", pev)], inc=("ac", 1))
        bc_mm[ch] = k.on("scalar", lambda e, ch=ch, ba=ba: e.copy(
            sb["rb"][:, ch * CH:(ch + 1) * CH], pb[ba + 1][:]), inc=("ac", 1))
    ac_bc = k.v("ac")
    tt_done, st_done = [0] * DT, [0] * DT
    for d in range(DT):
        b2 = d % 2
        xf2 = sb["xf"][:, b2 * T:(b2 + 1) * T]
        t0 = sb["t0"][:, b2 * T:(b2 + 1) * T]
        tb = sb["tb"][:, b2 * T:(b2 + 1) * T]
        w = [("ve", cp_done[DT - 1])]
        if d >= 2:
            w.append(("ve", tt_done[d - 2]))
        ld = k.dma("sync", xf2, x[d * 128:(d + 1) * 128, :], "d_a%d" % (d % 2), wait=w)
        wst = [("d_so%d" % (d % 2), st_done[d - 2])] if d >= 2 else []
        k.on("vector", lambda e, xf2=xf2, t0=t0: e.tensor_tensor(t0, xf2, sb["mub"][:], ALU.subtract),
             wait=[("d_a%d" % (d % 2), ld), ("ac", ac_bc)] + wst, inc=("ve", 1))
        if dst_f32 is not None:
            k.on("vector", lambda e, t0=t0: e.tensor_tensor(t0, t0, sb["rb"][:], ALU.mult),
                 wait=[("ve", k.v("ve"))], inc=("ve", 1))
            vv = k.on("vector", lambda e, t0=t0, tb=tb: e.tensor_copy(tb, t0),
                      wait=[("ve", k.v("ve"))], inc=("ve", 1))
            k.dma("gpsimd", tens[dst_f32][d * 128:(d + 1) * 128, :], t0, "d_sp%d" % (d % 2),
                  wait=[("ve", vv)])
        else:
            vv = k.on("vector", lambda e, t0=t0, tb=tb: e.tensor_tensor(tb, t0, sb["rb"][:], ALU.mult),
                      wait=[("ve", k.v("ve"))], inc=("ve", 1))
        tt_done[d] = vv
        st_done[d] = k.dma("gpsimd", tens[dst_bf][d * 128:(d + 1) * 128, :], tb, "d_so%d" % (d % 2),
                           wait=[("ve", vv)])
    k.barrier()


LN_SB = [
    ("xf", (128, 2 * T), F32), ("xbf", (128, 2 * T), BF16), ("xsq", (128, 2 * T), BF16),
    ("t0", (128, 2 * T), F32), ("tb", (128, 2 * T), BF16),
    ("mub", (128, T), F32), ("rb", (128, T), F32),
    ("row_mu", (1, T), F32), ("row_ms", (1, T), F32), ("row_r", (1, T), F32),
]




def st_l0_attn(k, tens, sb, xn_src):
    pb = k.pb
    xn = sb["xn_sb"]
    for d in range(DT):
        k.dma("sync", xn[:, d * T:(d + 1) * T], tens[xn_src][d * 128:(d + 1) * 128, :], "d_xn")
    ld_xn = k.v("d_xn")
    k.dma("sync", sb["bqkv"][:], tens["b_qkv_t"], "d_c")
    k.dma("sync", sb["rblo"][:], tens["rb_lo"], "d_c")
    k.dma("sync", sb["rbhi"][:], tens["rb_hi"], "d_c")
    ld_c = k.v("d_c")
    k.on("vector", lambda e: e.memset(sb["onesrow"][:], 1.0), inc=("ve", 1))
    k.on("vector", lambda e: e.memset(sb["ka0"][64:65, :], 1.0), inc=("ve", 1))
    k.on("vector", lambda e: e.memset(sb["ka1"][64:65, :], 1.0), inc=("ve", 1))
    k.on("vector", lambda e: e.memset(sb["vp"][:], 0.0), inc=("ve", 1))
    for tt in range(32):
        k.on("vector", lambda e, tt=tt: e.memset(sb["vp"][:, 256 * tt + 64:256 * tt + 65], 1.0))
        k.on("vector", lambda e, tt=tt: e.memset(sb["vp"][:, 256 * tt + 192:256 * tt + 193], 1.0))
    k.on("vector", lambda e: e.tensor_scalar(sb["bq8"][:], sb["bqkv"][:, 0:8], 0.125, None, ALU.mult),
         wait=[("d_c", ld_c)], inc=("ve", 1))
    k.barrier()

    wvq = tens["w_qkv"]
    for p in range(8):
        k.barrier()
        for nm, c0 in (("wq_sb", 128 * p), ("wk_sb", 1024 + 128 * p), ("wv_sb", 2048 + 128 * p)):
            src = wvq[:, c0:c0 + 128].rearrange("(a p) c -> p a c", p=128)
            k.dma("gpsimd", sb[nm][:], src, "d_w")
        k.dma("gpsimd", sb["bv_row"][0:1, :], tens["b_qkv_raw"][0:1, 2048 + 128 * p:2176 + 128 * p], "d_w")
        ld_w = k.v("d_w")
        for which, wsb, scale in (("q", "wq_sb", 0.125), ("kk", "wk_sb", 1.0)):
            for ch in range(NCH):
                bank = 6 + ch % 2
                w = [("d_w", ld_w), ("d_xn", ld_xn), ("ac", k.v("ac"))]
                pev = None
                for dm in range(DT):
                    pev = k.on("tensor", lambda e, ch=ch, dm=dm, bank=bank, wsb=wsb: e.matmul(
                        pb[bank][:], sb[wsb][:, 128 * dm:128 * dm + 128],
                        xn[:, T * dm + CH * ch: T * dm + CH * (ch + 1)],
                        start=(dm == 0), stop=(dm == DT - 1)),
                        wait=w if dm == 0 else [],
                        inc=("pe", 1) if dm == DT - 1 else None)
                for hl in range(2):
                    dstt = sb[("qa%d" if which == "q" else "ka%d") % hl]
                    if which == "q":
                        bias_ap = sb["bq8"][64 * hl:64 * hl + 64, p:p + 1]
                    else:
                        bias_ap = sb["bqkv"][64 * hl:64 * hl + 64, 8 + p:9 + p]
                    k.on("scalar", lambda e, dstt=dstt, ch=ch, bank=bank, hl=hl, bias_ap=bias_ap, scale=scale: e.activation(
                        dstt[0:64, CH * ch:CH * (ch + 1)], pb[bank][64 * hl:64 * hl + 64, :],
                        AF.Identity, bias=bias_ap, scale=scale),
                        wait=[("pe", pev), ("ve", k.v("ve"))] if hl == 0 else [], inc=("ac", 1))
        for tt in range(32):
            bank = 6 + tt % 2
            w = [("d_w", ld_w), ("d_xn", ld_xn), ("ac", k.v("ac"))]
            for dm in range(DT):
                k.on("tensor", lambda e, tt=tt, dm=dm, bank=bank: e.matmul(
                    pb[bank][:, 0:128], xn[:, T * dm + 128 * tt: T * dm + 128 * (tt + 1)],
                    sb["wv_sb"][:, 128 * dm:128 * dm + 128],
                    start=(dm == 0), stop=False),
                    wait=w if dm == 0 else [])
            pev = k.on("tensor", lambda e, tt=tt, bank=bank: e.matmul(
                pb[bank][:, 0:128], sb["onesrow"][0:1, :], sb["bv_row"][0:1, :],
                start=False, stop=True), inc=("pe", 1))
            for hl in range(2):
                k.on("scalar", lambda e, tt=tt, bank=bank, hl=hl: e.copy(
                    sb["vp"][:, 256 * tt + 128 * hl:256 * tt + 128 * hl + 64],
                    pb[bank][:, 64 * hl:64 * hl + 64]),
                    wait=[("pe", pev)] if hl == 0 else [], inc=("ac", 1))
        setup_done = [None, None]
        strip_ld = [0, 0]
        for hl in range(2):
            h = 2 * p + hl
            qa, ka = sb["qa%d" % hl], sb["ka%d" % hl]
            stripb = sb["strip%d" % hl]
            strip_ld[hl] = k.dma("gpsimd", stripb[:], tens["strips"][h], "d_sp")
            k.on("vector", lambda e, stripb=stripb: e.reduce_max(sb["scol"][:], stripb[:], axis=AX.X),
                 wait=[("d_sp", strip_ld[hl]), ("ve", k.v("ve"))], inc=("ve", 1))
            k.dma("gpsimd", sb["srow"][0:1, :], sb["scol"][:, 0:1], "d_sr",
                  wait=[("ve", k.v("ve"))], allow_slow_non_contiguous=True)
            k.on("vector", lambda e: e.reduce_max(sb["bmax"][0:1, 0:1], sb["srow"][0:1, :], axis=AX.X),
                 wait=[("d_sr", k.v("d_sr"))], inc=("ve", 1))
            wsq = [("pe", k.v("pe")), ("ve", k.v("ve"))]
            k.on("scalar", lambda e, qa=qa: e.activation(sb["sq_sb"][0:64, :], qa[0:64, :], AF.Square),
                 wait=wsq, inc=("ac", 1))
            k.on("scalar", lambda e, ka=ka: e.activation(sb["sq_sb"][64:128, :], ka[0:64, :], AF.Square),
                 inc=("ac", 1))
            ac_sq = k.v("ac")
            for ch in range(NCH):
                row = (ch % 3) * 32
                bq, bk = 5 + ch // 3, 2 + ch // 3
                k.on("tensor", lambda e, ch=ch, row=row, bq=bq: e.matmul(
                    pb[bq][row:row + 1, :], sb["onesb"][0:64, :], sb["sq_sb"][0:64, CH * ch:CH * (ch + 1)],
                    start=True, stop=True),
                    wait=[("ac", ac_sq), ("ve", k.v("ve"))] if ch == 0 else [])
                k.on("tensor", lambda e, ch=ch, row=row, bk=bk: e.matmul(
                    pb[bk][row:row + 1, :], sb["onesb"][64:128, :], sb["sq_sb"][64:128, CH * ch:CH * (ch + 1)],
                    start=True, stop=True),
                    inc=("pe", 1) if ch == NCH - 1 else None)
            pev = k.v("pe")
            for ch in range(NCH):
                row = (ch % 3) * 32
                bq, bk = 5 + ch // 3, 2 + ch // 3
                k.on("scalar", lambda e, ch=ch, row=row, bq=bq: e.copy(
                    sb["rows_sb"][0:1, CH * ch:CH * (ch + 1)], pb[bq][row:row + 1, :]),
                    wait=[("pe", pev), ("ve", k.v("ve"))] if ch == 0 else [], inc=("ac", 1))
                k.on("scalar", lambda e, ch=ch, row=row, bk=bk: e.copy(
                    sb["rows_sb"][32:33, CH * ch:CH * (ch + 1)], pb[bk][row:row + 1, :]),
                    inc=("ac", 1))
            ac_n = k.v("ac")
            k.on("scalar", lambda e: e.activation(sb["rows_sb"][0:1, :], sb["rows_sb"][0:1, :], AF.Sqrt),
                 wait=[("ac", ac_n)], inc=("ac", 1))
            k.on("vector", lambda e: e.reduce_max(sb["kmax"][0:1, 0:1], sb["rows_sb"][32:33, :], axis=AX.X),
                 wait=[("ac", ac_n)], inc=("ve", 1))
            k.on("scalar", lambda e: e.activation(sb["kmax"][0:1, 0:1], sb["kmax"][0:1, 0:1], AF.Sqrt),
                 wait=[("ve", k.v("ve"))], inc=("ac", 1))
            k.on("vector", lambda e: e.tensor_scalar(
                sb["rows_sb"][64:65, :], sb["rows_sb"][0:1, :], sb["kmax"][0:1, 0:1], sb["bmax"][0:1, 0:1],
                ALU.mult, ALU.add), wait=[("ac", k.v("ac")), ("ve", k.v("ve"))], inc=("ve", 1))
            k.on("vector", lambda e, qa=qa: e.tensor_scalar(
                qa[64:65, :], sb["rows_sb"][64:65, :], -1.0, None, ALU.mult),
                wait=[("ve", k.v("ve"))], inc=("ve", 1))
            setup_done[hl] = k.v("ve")
        setup_ac = k.v("ac")
        prev_tail = None
        prev_esb = None
        prev_evac = None
        for hl in range(2):
            h = 2 * p + hl
            qa, ka = sb["qa%d" % hl], sb["ka%d" % hl]
            stripb = sb["strip%d" % hl]
            exp_cnt, dve_cnt, ev_cnt, bankfree = {}, {}, {}, {}
            evac_ve, tt_done, ot_st = {}, {}, {}
            pend = []
            recip_done = {}
            pdv_todo = []

            def emit_ev(qc, jt, hl=hl):
                wev = [("ac", exp_cnt[(qc, jt)])]
                if jt == 0:
                    if (qc - 2) in evac_ve:
                        wev.append(("ve", evac_ve[qc - 2]))
                    elif prev_evac is not None and qc < 2:
                        wev.append(("ve", prev_evac[6 + qc]))
                ev_cnt[(qc, jt)] = k.on("tensor", lambda e, jt=jt, qc=qc, hl=hl: e.matmul(
                    pb[2 + qc % 2][:], sb["vp"][:, 256 * jt + 128 * hl:256 * jt + 128 * (hl + 1)],
                    sb["e_sb"][:, CH * (jt % 4):CH * (jt % 4 + 1)],
                    start=(jt == 0), stop=(jt == 31)),
                    wait=wev, inc=("pe", 1))
                if jt == 31:
                    k.on("vector", lambda e, qc=qc: e.tensor_copy(
                        sb["den"][0:1, :], pb[2 + qc % 2][64:65, :]),
                        wait=[("pe", ev_cnt[(qc, 31)])], inc=("ve", 1))
                    k.on("vector", lambda e, qc=qc: e.tensor_copy(
                        sb["onum"][0:64, :], pb[2 + qc % 2][0:64, :]),
                        inc=("ve", 1))
                    evac_ve[qc] = k.v("ve")
                    k.on("vector", lambda e: e.reciprocal(sb["den"][0:1, :], sb["den"][0:1, :]),
                         inc=("ve", 1))
                    recip_done[qc] = k.v("ve")
                    pdv_todo.append(qc)

            def emit_pdv(hl=hl):
                qc = pdv_todo.pop(0)
                db = 4 + qc % 2
                wd_p = [("ve", recip_done[qc])]
                if hl == 0 and qc < 2:
                    wd_p.append(("ac", setup_ac))
                pdv = k.on("tensor", lambda e, db=db: e.matmul(
                    pb[db][0:64, :], sb["ones1f"][0:1, 0:64], sb["den"][0:1, :],
                    start=True, stop=True), wait=wd_p, inc=("pe", 1))
                wtt = [("pe", pdv)]
                if (qc - 2) in ot_st:
                    wtt.append(("d_ot%d" % (qc % 2), ot_st[qc - 2]))
                tt_done[qc] = k.on("vector", lambda e, qc=qc, db=db: e.tensor_tensor(
                    sb["ot_tmp"][0:64, CH * (qc % 2):CH * (qc % 2 + 1)],
                    sb["onum"][0:64, :], pb[db][0:64, :], ALU.mult),
                    wait=wtt, inc=("ve", 1))
                ot_st[qc] = k.dma("gpsimd", tens["otT"][64 * h:64 * h + 64, CH * qc:CH * (qc + 1)],
                                  sb["ot_tmp"][0:64, CH * (qc % 2):CH * (qc % 2 + 1)], "d_ot%d" % (qc % 2),
                                  wait=[("ve", tt_done[qc])])

            for qc in range(NCH):
                for jt in range(32):
                    sbk = jt % 2
                    m0 = 512 * qc - 128 * jt + 1536
                    inband = (0 <= m0 <= 3072 - 512)
                    if jt == 4 and pdv_todo:
                        emit_pdv()
                    key2 = None
                    if jt >= 2:
                        key2 = (qc, jt - 2)
                    elif qc >= 1:
                        key2 = (qc - 1, 30 + jt)
                    w = []
                    if key2 is not None:
                        w.append(bankfree[key2])
                    elif prev_tail is not None:
                        w.append(prev_tail[jt])
                    if qc == 0 and jt == 0:
                        w.append(("ve", setup_done[hl]))
                    k.on("tensor", lambda e, qa=qa, ka=ka, jt=jt, qc=qc, sbk=sbk: e.matmul(
                        pb[sbk][:], ka[:, 128 * jt:128 * (jt + 1)], qa[:, CH * qc:CH * (qc + 1)],
                        start=True, stop=True), wait=w, inc=("pe", 1))
                    pev = k.v("pe")
                    eo = sb["e_sb"][:, CH * (jt % 4):CH * (jt % 4 + 1)]
                    key4 = None
                    if jt >= 4:
                        key4 = (qc, jt - 4)
                    elif qc >= 1:
                        key4 = (qc - 1, 28 + jt)
                    if inband:
                        ep = sb["e_pre"][:, CH * (jt % 4):CH * (jt % 4 + 1)]
                        wv = [("pe", pev)]
                        if key4 in exp_cnt:
                            wv.append(("ac", exp_cnt[key4]))
                        elif prev_esb is not None and key4 is None:
                            wv.append(("ac", prev_esb[28 + jt]))
                        dve_cnt[(qc, jt)] = k.on("vector", lambda e, ep=ep, sbk=sbk, stripb=stripb, m0=m0: e.tensor_tensor(
                            ep, pb[sbk][:], stripb[:, m0:m0 + 512], ALU.add),
                            wait=wv, inc=("ve", 1))
                        bankfree[(qc, jt)] = ("ve", dve_cnt[(qc, jt)])
                        exp_cnt[(qc, jt)] = k.on("scalar", lambda e, eo=eo, ep=ep: e.activation(
                            eo, ep, AF.Exp),
                            wait=[("ve", dve_cnt[(qc, jt)])], inc=("ac", 1))
                    else:
                        bcol = sb["rbhi" if m0 > 0 else "rblo"][:, h:h + 1]
                        exp_cnt[(qc, jt)] = k.on("scalar", lambda e, eo=eo, sbk=sbk, bcol=bcol: e.activation(
                            eo, pb[sbk][:], AF.Exp, bias=bcol),
                            wait=[("pe", pev)], inc=("ac", 1))
                        bankfree[(qc, jt)] = ("ac", exp_cnt[(qc, jt)])
                    if len(pend) == 3:
                        emit_ev(*pend.pop(0))
                    pend.append((qc, jt))
            while pend:
                emit_ev(*pend.pop(0))
            while pdv_todo:
                emit_pdv()
            prev_tail = {jt: bankfree[(7, 30 + jt)] for jt in range(2)}
            prev_esb = {28 + jt: exp_cnt[(7, 28 + jt)] for jt in range(4)}
            prev_evac = evac_ve
    k.barrier()


def st_l0_outproj(k, tens, sb, x_src, dst):
    k.dma("sync", sb["bout"][:], tens["b_out_t"], "d_c")
    ld_bo = k.v("d_c")
    k.on("vector", lambda e: e.memset(sb["onesrow"][:], 1.0),
         wait=[("d_c", ld_bo)], inc=("ve", 1))
    k.barrier()
    _out_proj(k, tens, sb, "w_out", sb["bout"], x_src, dst, "otT")


L0A_SB = [
    ("xn_sb", (128, 8 * T), BF16),
    ("bqkv", (128, 24), F32), ("bq8", (128, 8), F32),
    ("rblo", (128, 16), F32), ("rbhi", (128, 16), F32),
    ("onesrow", (1, 128), BF16),
    ("wq_sb", (128, 1024), BF16), ("wk_sb", (128, 1024), BF16), ("wv_sb", (128, 1024), BF16),
    ("bv_row", (1, 128), BF16),
    ("qa0", (65, T), BF16), ("qa1", (65, T), BF16),
    ("ka0", (65, T), BF16), ("ka1", (65, T), BF16),
    ("vp", (128, 32 * 256), BF16),
    ("strip0", (128, 3072), BF16), ("strip1", (128, 3072), BF16),
    ("scol", (128, 1), F32), ("srow", (1, 128), F32),
    ("bmax", (1, 1), F32), ("kmax", (1, 1), F32),
    ("sq_sb", (128, T), BF16),
    ("rows_sb", (65, T), F32),
    ("e_sb", (128, 4 * CH), BF16), ("e_pre", (128, 4 * CH), F32),
    ("den", (1, CH), F32), ("onum", (64, CH), F32),
    ("ot_tmp", (64, 2 * CH), BF16),
]

L0P_SB = [
    ("bout", (128, 8), F32), ("onesrow", (1, 128), BF16),
    ("wo_full", (128, 8 * 1024), BF16),
    ("otb", (128, 8 * CH), BF16), ("xcb", (128, 8 * CH), F32), ("rescb", (128, 8 * CH), F32),
]



def st_l0_ffn(k, tens, sb, xn_src, x1p_src, dst):
    pb = k.pb
    CK = 1024
    NC2 = T // CK
    k.dma("gpsimd", tens["moe_w1bf"].rearrange("e a b -> (e a b)"),
          tens["moe_w1"].rearrange("e a b -> (e a b)"), "d_cw")
    k.dma("gpsimd", tens["moe_w2bf"].rearrange("e a b -> (e a b)"),
          tens["moe_w2"].rearrange("e a b -> (e a b)"), "d_cw")
    k.dma("sync", sb["wg_sb"][:], tens["w_gate"].rearrange("(a p) c -> p a c", p=128), "d_c2")
    k.dma("sync", sb["bgate"][0:1, :], tens["b_gate"], "d_c2")
    k.dma("sync", sb["b1c"][:], tens["moe_b1c"], "d_c2")
    k.dma("sync", sb["b2sb"][:], tens["moe_b2"], "d_c2")
    k.dma("sync", sb["id_f32"][:], tens["ident"], "d_c2")
    k.dma("gpsimd", sb["id_bf"][:], tens["ident"], "d_cg")
    k.dma("sync", sb["emask"][:], tens["emask8"], "d_c2")
    ld_c = k.v("d_c2")
    _ = k.v("d_c3")
    k.on("vector", lambda e: e.memset(sb["onesrow"][:], 1.0), inc=("ve", 1))
    k.barrier()
    cw = k.v("d_cw")

    for c in range(NC2):
        k.barrier()
        col0 = CK * c
        ldxn = k.dma("sync", sb["xnb"][:],
                     tens[xn_src][:, col0:col0 + CK].rearrange("(a p) c -> p a c", p=128), "d_xnb")
        ldxnf = k.dma("sync", sb["xnfb"][:],
                      tens["xn_f32"][:, col0:col0 + CK].rearrange("(a p) c -> p a c", p=128), "d_xnf")
        ldx1 = k.dma("sync", sb["x1b"][:],
                     tens[x1p_src][:, col0:col0 + CK].rearrange("(a p) c -> p a c", p=128), "d_x1b")
        tr_ac = 0
        for tl in range(8):
            bank = 6 + tl % 2
            w = [("d_c2", ld_c), ("d_xnf", ldxnf), ("ac", k.v("ac"))]
            for dm in range(DT):
                k.on("tensor", lambda e, tl=tl, dm=dm, bank=bank: e.matmul(
                    pb[bank][:, 0:8], sb["xnfb"][:, CK * dm + 128 * tl:CK * dm + 128 * (tl + 1)],
                    sb["wg_sb"][:, 8 * dm:8 * dm + 8],
                    start=(dm == 0), stop=False), wait=w if dm == 0 else [])
            pev = k.on("tensor", lambda e, bank=bank: e.matmul(
                pb[bank][:, 0:8], sb["ones1f"][0:1, :], sb["bgate"][0:1, :],
                start=False, stop=True), inc=("pe", 1))
            k.on("scalar", lambda e, tl=tl, bank=bank: e.copy(
                sb["scr"][:, 8 * tl:8 * tl + 8], pb[bank][:, 0:8]),
                wait=[("pe", pev)], inc=("ac", 1))
            ac_s = k.v("ac")
            s8 = sb["scr"][:, 8 * tl:8 * tl + 8]
            k.on("vector", lambda e, s8=s8: e.reduce_max(sb["gt_m1"][:], s8, axis=AX.X),
                 wait=[("ac", ac_s), ("ve", k.v("ve"))], inc=("ve", 1))
            k.on("vector", lambda e, s8=s8: e.tensor_scalar(
                sb["keep"][:], s8, sb["gt_m1"][:], None, ALU.is_equal),
                wait=[("ve", k.v("ve"))], inc=("ve", 1))
            k.on("vector", lambda e: e.reduce_sum(sb["gt_cnt"][:], sb["keep"][:], axis=AX.X),
                 wait=[("ve", k.v("ve"))], inc=("ve", 1))
            k.on("vector", lambda e, s8=s8: e.scalar_tensor_tensor(
                sb["tmp8"][:], sb["keep"][:], -3.0e38, s8, ALU.mult, ALU.add),
                wait=[("ve", k.v("ve"))], inc=("ve", 1))
            k.on("vector", lambda e: e.reduce_max(sb["gt_m2"][:], sb["tmp8"][:], axis=AX.X),
                 wait=[("ve", k.v("ve"))], inc=("ve", 1))
            k.on("vector", lambda e: e.tensor_scalar(
                sb["gt_c2"][:], sb["gt_cnt"][:], 2.0, None, ALU.is_ge),
                wait=[("ve", k.v("ve"))], inc=("ve", 1))
            k.on("vector", lambda e: e.tensor_tensor(
                sb["gt_d"][:], sb["gt_m1"][:], sb["gt_m2"][:], ALU.subtract),
                wait=[("ve", k.v("ve"))], inc=("ve", 1))
            k.on("vector", lambda e: e.tensor_tensor(
                sb["gt_d"][:], sb["gt_d"][:], sb["gt_c2"][:], ALU.mult),
                wait=[("ve", k.v("ve"))], inc=("ve", 1))
            k.on("vector", lambda e: e.tensor_tensor(
                sb["gt_thr"][:], sb["gt_m2"][:], sb["gt_d"][:], ALU.add),
                wait=[("ve", k.v("ve"))], inc=("ve", 1))
            k.on("vector", lambda e, s8=s8: e.tensor_scalar(
                sb["keep"][:], s8, sb["gt_thr"][:], None, ALU.is_ge),
                wait=[("ve", k.v("ve"))], inc=("ve", 1))
            vv = k.on("vector", lambda e, s8=s8: e.tensor_tensor(
                sb["wbf"][:], s8, sb["keep"][:], ALU.mult),
                wait=[("ve", k.v("ve"))], inc=("ve", 1))
            ptr = k.on("tensor", lambda e: e.transpose(
                pb[5][0:8, 0:128], sb["wbf"][:], sb["id_f32"][:]),
                wait=[("ve", vv), ("ac", tr_ac), ("d_c2", ld_c)], inc=("pe", 1))
            tr_ac = k.on("scalar", lambda e, tl=tl: e.copy(
                sb["wT"][0:8, 128 * tl:128 * (tl + 1)], pb[5][0:8, 0:128]),
                wait=[("pe", ptr)], inc=("ac", 1))
        for ex in range(8):
            wld = [("pe", k.v("pe"))] if ex > 0 else [("d_cw", cw)]
            ldw1 = k.dma("gpsimd", sb["w1sb"][:],
                         tens["moe_w1bf"][ex].rearrange("(a p) c -> p a c", p=128), "d_w1",
                         wait=wld)
            ldw2 = k.dma("gpsimd", sb["w2sb"][:],
                         tens["moe_w2bf"][ex].rearrange("(a p) c -> p a c", p=128), "d_w2",
                         wait=wld)
            for sub in range(2):
                pwb = k.on("tensor", lambda e, ex=ex, sub=sub: e.matmul(
                    pb[4][:, 0:512], sb["emask"][0:8, 128 * ex:128 * (ex + 1)],
                    sb["wT"][0:8, 512 * sub:512 * (sub + 1)],
                    start=True, stop=True),
                    wait=[("ac", k.v("ac")), ("ve", k.v("ve"))], inc=("pe", 1))
                k.on("scalar", lambda e, sub=sub: e.copy(
                    sb["wb_sb"][:, 512 * sub:512 * (sub + 1)], pb[4][:, 0:512]),
                    wait=[("pe", pwb)], inc=("ac", 1))
            ac_wb = k.v("ac")
            for ht in range(DT):
                for sub in range(2):
                    bank = (2 * ht + sub) % 2
                    w = [("d_w1", ldw1), ("d_xnb", ldxn), ("ac", k.v("ac")), ("ve", k.v("ve"))]
                    for dm in range(DT):
                        k.on("tensor", lambda e, ht=ht, sub=sub, dm=dm, bank=bank: e.matmul(
                            pb[bank][:], sb["w1sb"][:, CK * dm + 128 * ht:CK * dm + 128 * (ht + 1)],
                            sb["xnb"][:, CK * dm + 512 * sub:CK * dm + 512 * (sub + 1)],
                            start=(dm == 0), stop=(dm == DT - 1)),
                            wait=w if dm == 0 else [],
                            inc=("pe", 1) if dm == DT - 1 else None)
                    pev = k.v("pe")
                    k.on("scalar", lambda e, ht=ht, sub=sub, bank=bank, ex=ex: e.activation(
                        sb["hsc"][:, CK * ht + 512 * sub:CK * ht + 512 * (sub + 1)],
                        pb[bank][:], AF.Relu, bias=sb["b1c"][:, 8 * ex + ht:8 * ex + ht + 1]),
                        wait=[("pe", pev), ("ac", ac_wb)], inc=("ac", 1))
                    k.on("vector", lambda e, ht=ht, sub=sub: e.tensor_tensor(
                        sb["hsc"][:, CK * ht + 512 * sub:CK * ht + 512 * (sub + 1)],
                        sb["hsc"][:, CK * ht + 512 * sub:CK * ht + 512 * (sub + 1)],
                        sb["wb_sb"][:, 512 * sub:512 * (sub + 1)], ALU.mult),
                        wait=[("ac", k.v("ac")), ("ve", k.v("ve"))], inc=("ve", 1))
            ve_h = k.v("ve")
            for do in range(DT):
                for sub in range(2):
                    bank = 2 + (2 * do + sub) % 2
                    w = [("d_w2", ldw2), ("ve", k.v("ve"))]
                    first = True
                    if ex == 0:
                        k.on("tensor", lambda e, do=do, sub=sub, bank=bank: e.matmul(
                            pb[bank][:], sb["b2sb"][0:8, 128 * do:128 * (do + 1)],
                            sb["wT"][0:8, 512 * sub:512 * (sub + 1)],
                            start=True, stop=False), wait=w)
                        first = False
                    for ht in range(DT):
                        k.on("tensor", lambda e, do=do, sub=sub, ht=ht, bank=bank, first=first: e.matmul(
                            pb[bank][:], sb["w2sb"][:, CK * ht + 128 * do:CK * ht + 128 * (do + 1)],
                            sb["hsc"][:, CK * ht + 512 * sub:CK * ht + 512 * (sub + 1)],
                            start=(first and ht == 0), stop=(ht == DT - 1)),
                            wait=w if (first and ht == 0) else [],
                            inc=("pe", 1) if ht == DT - 1 else None)
                        first = first and False
                    pev = k.v("pe")
                    aslc = sb["acc"][:, CK * do + 512 * sub:CK * do + 512 * (sub + 1)]
                    if ex == 0:
                        k.on("vector", lambda e, do=do, sub=sub, bank=bank, aslc=aslc: e.tensor_tensor(
                            aslc, pb[bank][:],
                            sb["x1b"][:, CK * do + 512 * sub:CK * do + 512 * (sub + 1)], ALU.add),
                            wait=[("pe", pev), ("d_x1b", ldx1), ("ve", k.v("ve"))], inc=("ve", 1))
                    else:
                        k.on("vector", lambda e, bank=bank, aslc=aslc: e.tensor_tensor(
                            aslc, pb[bank][:], aslc, ALU.add),
                            wait=[("pe", pev), ("ve", k.v("ve"))], inc=("ve", 1))
        k.dma("gpsimd", tens[dst][:, col0:col0 + CK].rearrange("(a p) c -> p a c", p=128),
              sb["acc"][:], "d_st1", wait=[("ve", k.v("ve"))])
    k.barrier()


MOE_SB = [
    ("xnb", (128, 8 * 1024), BF16), ("hsc", (128, 8 * 1024), BF16),
    ("acc", (128, 8 * 1024), F32), ("x1b", (128, 8 * 1024), F32),
    ("w1sb", (128, 8 * 1024), BF16), ("w2sb", (128, 8 * 1024), BF16),
    ("wg_sb", (128, 64), F32), ("bgate", (1, 8), F32),
    ("b1c", (128, 64), F32), ("b2sb", (8, 1024), F32),
    ("xnfb", (128, 8 * 1024), F32), ("id_f32", (128, 128), F32),
    ("scr", (128, 64), F32), ("keep", (128, 8), F32), ("tmp8", (128, 8), F32),
    ("gt_m1", (128, 1), F32), ("gt_m2", (128, 1), F32), ("gt_cnt", (128, 1), F32),
    ("gt_c2", (128, 1), F32), ("gt_d", (128, 1), F32), ("gt_thr", (128, 1), F32),
    ("wbf", (128, 8), F32), ("wT", (8, 1024), F32), ("wb_sb", (128, 1024), F32),
    ("id_bf", (128, 128), BF16), ("onesrow", (1, 128), BF16),
    ("emask", (8, 1024), F32),
]


def build_l0ffn_launch(kinds):
    nc = bass.Bass("TRN2", target_bir_lowering=False, debug=False)
    decls = [
        ("xn_bf", (D, T), BF16, "ExternalInput"),
        ("xn_f32", (D, T), F32, "ExternalInput"),
        ("x1p", (D, T), F32, "ExternalInput"),
        ("w_gate", (D, 8), F32, "ExternalInput"),
        ("b_gate", (1, 8), F32, "ExternalInput"),
        ("moe_w1", (8, D, D), F32, "ExternalInput"),
        ("moe_w2", (8, D, D), F32, "ExternalInput"),
        ("moe_b1c", (128, 64), F32, "ExternalInput"),
        ("moe_b2", (8, D), F32, "ExternalInput"),
        ("ident", (128, 128), F32, "ExternalInput"),
        ("emask8", (8, 1024), F32, "ExternalInput"),
        ("x1", (D, T), F32, "ExternalOutput"),
        ("moe_w1bf", (8, D, D), BF16, "Internal"),
        ("moe_w2bf", (8, D, D), BF16, "Internal"),
    ]
    tens = {}
    for name, shape, dt, kind in decls:
        tens[name] = nc.dram_tensor(name, list(shape), dt, kind=kind).ap()
    ctx = ExitStack()
    with ctx:
        k, sb = _mk_k(nc, ctx, MOE_SB)
        st_l0_ffn(k, tens, sb, "xn_bf", "x1p", "x1")
        _finish(nc, k)
    return nc




def _out_proj(k, tens, sb, w_dram, bcol, x_src, dst, ot_name):
    pb = k.pb
    wsrc = tens[w_dram].rearrange("(a p) (b c) -> p a b c", p=128, c=128)
    ldwo = k.dma("gpsimd", sb["wo_full"][:], wsrc, "d_wf")
    mm_done = [0] * NCH
    st_done = [0] * NCH
    for ch in range(NCH):
        w = [("pe", mm_done[ch - 1])] if ch >= 1 else []
        ldo = k.dma("sync", sb["otb"][:],
                    tens[ot_name][:, CH * ch:CH * (ch + 1)].rearrange("(a p) c -> p a c", p=128),
                    "d_ob", wait=w)
        wx = [("ve", k.v("ve"))] if ch >= 1 else []
        ldx = k.dma("sync", sb["xcb"][:],
                    tens[x_src][:, CH * ch:CH * (ch + 1)].rearrange("(a p) c -> p a c", p=128),
                    "d_xc", wait=wx)
        for dot in range(DT):
            bank = 6 + dot % 2
            w = [("d_wf", ldwo), ("d_ob", ldo), ("ve", k.v("ve"))]
            for dm in range(DT):
                k.on("tensor", lambda e, ch=ch, dot=dot, dm=dm, bank=bank: e.matmul(
                    pb[bank][:], sb["wo_full"][:, 1024 * dm + 128 * dot:1024 * dm + 128 * (dot + 1)],
                    sb["otb"][:, CH * dm:CH * (dm + 1)],
                    start=(dm == 0), stop=(dm == DT - 1)), wait=w if dm == 0 else [],
                    inc=("pe", 1) if dm == DT - 1 else None)
            pev = k.v("pe")
            wst = [("pe", pev), ("d_xc", ldx)]
            if dot == 0 and ch >= 1:
                wst.append(("d_rs", st_done[ch - 1]))
            k.on("vector", lambda e, dot=dot, bank=bank, bcol=bcol: e.scalar_tensor_tensor(
                sb["rescb"][:, CH * dot:CH * (dot + 1)],
                pb[bank][:], bcol[:, dot:dot + 1],
                sb["xcb"][:, CH * dot:CH * (dot + 1)],
                ALU.add, ALU.add), wait=wst, inc=("ve", 1))
        mm_done[ch] = k.v("pe")
        st_done[ch] = k.dma("gpsimd",
                            tens[dst][:, CH * ch:CH * (ch + 1)].rearrange("(a p) c -> p a c", p=128),
                            sb["rescb"][:], "d_rs", wait=[("ve", k.v("ve"))])
    k.barrier()


def st_l1_attn(k, tens, sb, xn_src, x_src, dst):
    pb = k.pb
    for nm, key in (("pbq", "p_bq_t"), ("pbk", "p_bk_t"), ("pbv", "p_bv_t"), ("pbo", "p_bo_t")):
        k.dma("sync", sb[nm][:], tens[key], "d_c")
    k.dma("gpsimd", sb["rm_sb"][:], tens["p_rm"], "d_cg")
    ld_c, ld_cg = k.v("d_c"), k.v("d_cg")
    k.on("vector", lambda e: e.memset(sb["onesrow"][:], 1.0), inc=("ve", 1))
    for tt in range(32):
        k.on("vector", lambda e, tt=tt: e.memset(sb["vp"][:, 130 * tt + 64:130 * tt + 65], 1.0))
        k.on("vector", lambda e, tt=tt: e.memset(sb["vp"][:, 130 * tt + 129:130 * tt + 130], 1.0),
             inc=("ve", 1) if tt == 31 else None)
    for d in range(DT):
        k.dma("sync", sb["xn2"][:, d * T:(d + 1) * T],
              tens[xn_src][d * 128:(d + 1) * 128, :], "d_xn2")
    ldxn = k.v("d_xn2")
    k.barrier()
    for p in range(8):
        k.barrier()
        for nm, c0, wt in (("wq_sb", 128 * p, "p_Wq"), ("wk_sb", 128 * p, "p_Wk"), ("wv_sb", 128 * p, "p_Wv")):
            srcw = tens[wt][:, c0:c0 + 128].rearrange("(a p) c -> p a c", p=128)
            k.dma("gpsimd", sb[nm][:], srcw, "d_w")
        k.dma("gpsimd", sb["bv_row"][0:1, :], tens["p_bv_raw"][0:1, 128 * p:128 * (p + 1)], "d_w")
        ld_w = k.v("d_w")
        for which, wsb, dstt, bsel in (("q", "wq_sb", "qt", "pbq"), ("kk", "wk_sb", "kt", "pbk")):
            for ch in range(NCH):
                bank = 6 + ch % 2
                w = [("d_w", ld_w), ("d_xn2", ldxn), ("ac", k.v("ac"))]
                for dm in range(DT):
                    k.on("tensor", lambda e, ch=ch, dm=dm, bank=bank, wsb=wsb: e.matmul(
                        pb[bank][:], sb[wsb][:, 128 * dm:128 * dm + 128],
                        sb["xn2"][:, T * dm + CH * ch: T * dm + CH * (ch + 1)],
                        start=(dm == 0), stop=(dm == DT - 1)),
                        wait=w if dm == 0 else [],
                        inc=("pe", 1) if dm == DT - 1 else None)
                pev = k.v("pe")
                for hl in range(2):
                    k.on("scalar", lambda e, dstt=dstt, ch=ch, bank=bank, hl=hl, bsel=bsel: e.activation(
                        sb[dstt][64 * hl:64 * hl + 64, CH * ch:CH * (ch + 1)],
                        pb[bank][64 * hl:64 * hl + 64, :],
                        AF.Identity, bias=sb[bsel][64 * hl:64 * hl + 64, p:p + 1], scale=1.0),
                        wait=[("pe", pev), ("d_c", ld_c)] if hl == 0 else [], inc=("ac", 1))
        for tt in range(32):
            bank = 6 + tt % 2
            w = [("d_w", ld_w), ("d_xn2", ldxn), ("ac", k.v("ac"))]
            for dm in range(DT):
                k.on("tensor", lambda e, tt=tt, dm=dm, bank=bank: e.matmul(
                    pb[bank][:, 0:128], sb["xn2"][:, T * dm + 128 * tt: T * dm + 128 * (tt + 1)],
                    sb["wv_sb"][:, 128 * dm:128 * dm + 128],
                    start=(dm == 0), stop=False),
                    wait=w if dm == 0 else [])
            pev = k.on("tensor", lambda e, tt=tt, bank=bank, p=p: e.matmul(
                pb[bank][:, 0:128], sb["onesrow"][0:1, :], sb["bv_row"][0:1, :],
                start=False, stop=True), inc=("pe", 1))
            for hl in range(2):
                k.on("scalar", lambda e, tt=tt, bank=bank, hl=hl: e.copy(
                    sb["vp"][:, 130 * tt + 65 * hl:130 * tt + 65 * hl + 64],
                    pb[bank][:, 64 * hl:64 * hl + 64]),
                    wait=[("pe", pev)] if hl == 0 else [], inc=("ac", 1))
        for hl in range(2):
            h = 2 * p + hl
            k.barrier()
            for mt in range(2):
                for ch in range(NCH):
                    bank = (2 * mt + ch) % 2
                    k.on("tensor", lambda e, mt=mt, ch=ch, bank=bank, hl=hl: e.matmul(
                        pb[bank][:], sb["rm_sb"][64 * hl:64 * hl + 64, 128 * mt:128 * (mt + 1)],
                        sb["qt"][64 * hl:64 * hl + 64, CH * ch:CH * (ch + 1)],
                        start=True, stop=True),
                        wait=[("ve", k.v("ve")), ("ac", k.v("ac")), ("d_cg", ld_cg)],
                        inc=("pe", 1))
                    pev = k.v("pe")
                    dslc = sb["qp"][:, T * mt + CH * ch:T * mt + CH * (ch + 1)]
                    _elu16(k, sb, pb, bank, pev, dslc, CH)
            for tt in range(32):
                bank = tt % 2
                k.on("tensor", lambda e, tt=tt, bank=bank, hl=hl: e.matmul(
                    pb[bank][:, 0:256], sb["kt"][64 * hl:64 * hl + 64, 128 * tt:128 * (tt + 1)],
                    sb["rm_sb"][64 * hl:64 * hl + 64, :],
                    start=True, stop=True),
                    wait=[("ve", k.v("ve")), ("ac", k.v("ac"))], inc=("pe", 1))
                pev = k.v("pe")
                dslc = sb["kp"][:, 256 * tt:256 * (tt + 1)]
                _elu16(k, sb, pb, bank, pev, dslc, 256)
            for mt in range(2):
                bank = 2 + mt
                for tt in range(32):
                    k.on("tensor", lambda e, mt=mt, tt=tt, bank=bank, hl=hl: e.matmul(
                        pb[bank][:, 0:65], sb["kp"][:, 256 * tt + 128 * mt:256 * tt + 128 * (mt + 1)],
                        sb["vp"][:, 130 * tt + 65 * hl:130 * tt + 65 * hl + 65],
                        start=(tt == 0), stop=(tt == 31)),
                        wait=[("ve", k.v("ve")), ("ac", k.v("ac"))] if tt == 0 else [],
                        inc=("pe", 1) if tt == 31 else None)
                pev = k.v("pe")
                k.on("scalar", lambda e, mt=mt, bank=bank: e.copy(
                    sb["kv_sb"][:, 65 * mt:65 * (mt + 1)], pb[bank][:, 0:65]),
                    wait=[("pe", pev)], inc=("ac", 1))
            ac_kv = k.v("ac")
            ot_st = {}
            tt_done = {}
            for ch in range(NCH):
                ob = 2 + ch % 2
                for mt in range(2):
                    k.on("tensor", lambda e, ch=ch, mt=mt, ob=ob: e.matmul(
                        pb[ob][0:65, :], sb["kv_sb"][:, 65 * mt:65 * (mt + 1)],
                        sb["qp"][:, T * mt + CH * ch:T * mt + CH * (ch + 1)],
                        start=(mt == 0), stop=(mt == 1)),
                        wait=[("ac", ac_kv), ("ve", k.v("ve"))] if mt == 0 else [],
                        inc=("pe", 1) if mt == 1 else None)
                pev = k.v("pe")
                wd0 = [("pe", pev)]
                if (ch - 1) in tt_done:
                    wd0.append(("ve", tt_done[ch - 1]))
                k.on("scalar", lambda e, ob=ob: e.copy(sb["den"][0:1, :], pb[ob][64:65, :]),
                     wait=wd0, inc=("ac", 1))
                k.on("scalar", lambda e, ob=ob: e.copy(sb["onum"][0:64, :], pb[ob][0:64, :]),
                     inc=("ac", 1))
                ac_ev = k.v("ac")
                k.on("vector", lambda e: e.tensor_scalar(sb["den"][0:1, :], sb["den"][0:1, :],
                                                         1e-6, None, ALU.max),
                     wait=[("ac", ac_ev), ("ve", k.v("ve"))], inc=("ve", 1))
                k.on("vector", lambda e: e.reciprocal(sb["den"][0:1, :], sb["den"][0:1, :]),
                     wait=[("ve", k.v("ve"))], inc=("ve", 1))
                wd = [("ve", k.v("ve"))]
                if (ch - 2) in tt_done:
                    wd.append(("ve", tt_done[ch - 2]))
                pdv = k.on("tensor", lambda e, ch=ch: e.matmul(
                    pb[4 + ch % 2][0:64, :], sb["ones1f"][0:1, 0:64], sb["den"][0:1, :],
                    start=True, stop=True), wait=wd, inc=("pe", 1))
                wtt = [("pe", pdv), ("ac", ac_ev)]
                if (ch - 2) in tt_done:
                    wtt.append(("d_ot%d" % (ch % 2), ot_st[ch - 2]))
                tt_done[ch] = k.on("vector", lambda e, ch=ch: e.tensor_tensor(
                    sb["ot_tmp"][0:64, CH * (ch % 2):CH * (ch % 2 + 1)],
                    sb["onum"][0:64, :], pb[4 + ch % 2][0:64, :], ALU.mult),
                    wait=wtt, inc=("ve", 1))
                ot_st[ch] = k.dma("gpsimd", tens["otT"][64 * h:64 * h + 64, CH * ch:CH * (ch + 1)],
                                  sb["ot_tmp"][0:64, CH * (ch % 2):CH * (ch % 2 + 1)],
                                  "d_ot%d" % (ch % 2), wait=[("ve", tt_done[ch])])
    k.barrier()
    _out_proj(k, tens, sb, "p_Wo", sb["pbo"], x_src, dst, "otT")


def _elu16(k, sb, pb, bank, pev, dslc, width):
    t1 = sb["el_t1"][:, 0:width]
    r1 = sb["el_r1"][:, 0:width]
    k.on("vector", lambda e: e.tensor_scalar(t1, pb[bank][:, 0:width], 0.0, None, ALU.min),
         wait=[("pe", pev), ("ve", k.v("ve")), ("ac", k.v("ac"))], inc=("ve", 1))
    k.on("vector", lambda e: e.tensor_scalar(r1, pb[bank][:, 0:width], 0.0, 0.0625,
                                             ALU.max, ALU.mult),
         wait=[("ve", k.v("ve"))], inc=("ve", 1))
    k.on("scalar", lambda e: e.activation(t1, t1, AF.Exp),
         wait=[("ve", k.v("ve"))], inc=("ac", 1))
    k.on("vector", lambda e: e.scalar_tensor_tensor(dslc, t1, 0.0625, r1, ALU.mult, ALU.add),
         wait=[("ac", k.v("ac")), ("ve", k.v("ve"))], inc=("ve", 1))


L1A_SB = [
    ("xn2", (128, 8 * T), BF16),
    ("pbq", (128, 8), F32), ("pbk", (128, 8), F32), ("pbv", (128, 8), F32), ("pbo", (128, 8), F32),
    ("rm_sb", (128, 256), BF16),
    ("wq_sb", (128, 1024), BF16), ("wk_sb", (128, 1024), BF16), ("wv_sb", (128, 1024), BF16),
    ("bv_row", (1, 128), BF16),
    ("qt", (128, T), BF16), ("kt", (128, T), BF16),
    ("vp", (128, 32 * 130), BF16),
    ("qp", (128, 2 * T), BF16), ("kp", (128, 32 * 256), BF16),
    ("kv_sb", (128, 130), BF16),
    ("el_t1", (128, CH), F32), ("el_r1", (128, CH), F32),
    ("den", (1, CH), F32), ("onum", (64, CH), F32),
    ("ot_tmp", (64, 2 * CH), BF16), ("onesrow", (1, 128), BF16),
    ("wo_full", (128, 8 * 1024), BF16),
    ("otb", (128, 8 * CH), BF16), ("xcb", (128, 8 * CH), F32), ("rescb", (128, 8 * CH), F32),
]


def build_l1attn_launch(kinds):
    nc = bass.Bass("TRN2", target_bir_lowering=False, debug=False)
    decls = [
        ("xn1_bf", (D, T), BF16, "ExternalInput"),
        ("x1", (D, T), F32, "ExternalInput"),
        ("p_Wq", (D, D), F32, "ExternalInput"), ("p_Wk", (D, D), F32, "ExternalInput"),
        ("p_Wv", (D, D), F32, "ExternalInput"), ("p_Wo", (D, D), F32, "ExternalInput"),
        ("p_bq_t", (128, 8), F32, "ExternalInput"), ("p_bk_t", (128, 8), F32, "ExternalInput"),
        ("p_bv_t", (128, 8), F32, "ExternalInput"), ("p_bo_t", (128, 8), F32, "ExternalInput"),
        ("p_bv_raw", (1, D), F32, "ExternalInput"),
        ("p_rm", (128, 256), F32, "ExternalInput"),
        ("x2p", (D, T), F32, "ExternalOutput"),
        ("otT", (D, T), BF16, "Internal"),
    ]
    tens = {}
    for name, shape, dt, kind in decls:
        tens[name] = nc.dram_tensor(name, list(shape), dt, kind=kind).ap()
    ctx = ExitStack()
    with ctx:
        k, sb = _mk_k(nc, ctx, L1A_SB)
        st_l1_attn(k, tens, sb, "xn1_bf", "x1", "x2p")
        _finish(nc, k)
    return nc




def st_lf_attn(k, tens, sb, l, yin, ya):
    pb = k.pb
    for nm, key in (("csq", "rot_cq"), ("ssq", "rot_sq"), ("csk", "rot_ck"), ("ssk", "rot_sk")):
        k.dma("gpsimd", sb[nm][0:64, :], tens[key], "d_cg")
    k.dma("gpsimd", sb["mask_sb"][:], tens["la_mask"].rearrange("(a p) c -> p a c", p=128), "d_cg")
    k.dma("gpsimd", sb["id_bf"][:], tens["ident"], "d_cg")
    k.dma("sync", sb["qs_col"][0:64, 0:1], tens["lf_qs"][l].unsqueeze(1), "d_c")
    k.dma("sync", sb["ks_col"][0:64, 0:1], tens["lf_ks"][l].unsqueeze(1), "d_c")
    ld_cg, ld_c = k.v("d_cg"), k.v("d_c")
    for d in range(DT):
        for cc in range(4):
            b2 = 0
            h2p = 0
            yb = sb["ybuf"][:, 0:1025]
            hf = sb["hbf"][:, 0:1024]
            wld = [("d_h0", k.v("d_h0")), ("d_h1", k.v("d_h1")), ("ve", k.v("ve"))]
            if d < 4:
                ld = k.dma("sync", yb[:, 0:1024],
                           tens[yin][128 * d:128 * (d + 1), 1024 * cc:1024 * (cc + 1)],
                           "d_y%d" % b2, wait=wld)
                vv = k.on("vector", lambda e, yb=yb, hf=hf: e.tensor_copy(hf, yb[:, 0:1024]),
                          wait=[("d_y%d" % b2, ld), ("ve", k.v("ve"))], inc=("ve", 1))
            elif cc == 0:
                ld = k.dma("sync", yb[:, 0:1023],
                           tens[yin][128 * d:128 * (d + 1), 0:1023], "d_y%d" % b2, wait=wld)
                k.on("vector", lambda e, hf=hf: e.memset(hf[:, 0:1], 0.0),
                     wait=[("ve", k.v("ve"))] + wld, inc=("ve", 1))
                vv = k.on("vector", lambda e, yb=yb, hf=hf: e.tensor_copy(hf[:, 1:1024], yb[:, 0:1023]),
                          wait=[("d_y%d" % b2, ld), ("ve", k.v("ve"))], inc=("ve", 1))
            else:
                ld = k.dma("sync", yb[:, 0:1024],
                           tens[yin][128 * d:128 * (d + 1), 1024 * cc - 1:1024 * (cc + 1) - 1],
                           "d_y%d" % b2, wait=wld)
                vv = k.on("vector", lambda e, yb=yb, hf=hf: e.tensor_copy(hf, yb[:, 0:1024]),
                          wait=[("d_y%d" % b2, ld), ("ve", k.v("ve"))], inc=("ve", 1))
            k.dma("gpsimd", tens["hshiftT"][128 * d:128 * (d + 1), 1024 * cc:1024 * (cc + 1)],
                  hf, "d_h%d" % h2p, wait=[("ve", vv)])
    k.barrier()

    for p in range(4):
        k.barrier()
        for nm, c0 in (("wq_sb", 128 * p), ("wk_sb", 512 + 128 * p), ("wv_sb", 1024 + 128 * p)):
            srcw = tens["lf_wqkv"][l][:, c0:c0 + 128].rearrange("(a p) c -> p a c", p=128)
            k.dma("gpsimd", sb[nm][:], srcw, "d_w")
        ld_w = k.v("d_w")
        for ch in range(NCH):
            ldh = k.dma("sync", sb["hb"][:],
                        tens["hshiftT"][:, CH * ch:CH * (ch + 1)].rearrange("(a p) c -> p a c", p=128),
                        "d_hb", wait=[("pe", k.v("pe"))])
            for which, wsb, dstt in (("q", "wq_sb", "qt"), ("kk", "wk_sb", "kt")):
                bank = 6 + ch % 2
                w = [("d_w", ld_w), ("d_hb", ldh), ("ac", k.v("ac"))]
                for dm in range(DT):
                    k.on("tensor", lambda e, ch=ch, dm=dm, bank=bank, wsb=wsb: e.matmul(
                        pb[bank][:], sb[wsb][:, 128 * dm:128 * dm + 128],
                        sb["hb"][:, CH * dm:CH * (dm + 1)],
                        start=(dm == 0), stop=(dm == DT - 1)),
                        wait=w if dm == 0 else [],
                        inc=("pe", 1) if dm == DT - 1 else None)
                pev = k.v("pe")
                k.on("scalar", lambda e, ch=ch, bank=bank, dstt=dstt: e.copy(
                    sb[dstt][:, CH * ch:CH * (ch + 1)], pb[bank][:]),
                    wait=[("pe", pev)], inc=("ac", 1))
            for ttl in range(4):
                bank = 6 + ttl % 2
                w = [("d_w", ld_w), ("d_hb", ldh), ("ac", k.v("ac"))]
                for dm in range(DT):
                    k.on("tensor", lambda e, ch=ch, ttl=ttl, dm=dm, bank=bank: e.matmul(
                        pb[bank][:, 0:128], sb["hb"][:, CH * dm + 128 * ttl:CH * dm + 128 * (ttl + 1)],
                        sb["wv_sb"][:, 128 * dm:128 * dm + 128],
                        start=(dm == 0), stop=(dm == DT - 1)),
                        wait=w if dm == 0 else [],
                        inc=("pe", 1) if dm == DT - 1 else None)
                pev = k.v("pe")
                k.on("scalar", lambda e, ch=ch, ttl=ttl, bank=bank: e.copy(
                    sb["vp"][:, 128 * (4 * ch + ttl):128 * (4 * ch + ttl + 1)],
                    pb[bank][:, 0:128]), wait=[("pe", pev)], inc=("ac", 1))
        for hl in range(2):
            k.barrier()
            for which in ("q", "kk"):
                srct = sb["qt" if which == "q" else "kt"]
                cse = sb["csq" if which == "q" else "csk"]
                sse = sb["ssq" if which == "q" else "ssk"]
                dstr = sb["qr" if which == "q" else "kr"]
                mmv = {}
                for ch in range(NCH):
                    row, bank = (ch % 3) * 32, ch // 3
                    wsq = [("ve", k.v("ve")), ("ac", k.v("ac"))]
                    if ch >= 1:
                        wsq.append(("pe", mmv[ch - 1]))
                    k.on("scalar", lambda e, srct=srct, hl=hl, ch=ch: e.activation(
                        sb["sqc"][64 * hl:64 * hl + 64, :],
                        srct[64 * hl:64 * hl + 64, CH * ch:CH * (ch + 1)], AF.Square),
                        wait=wsq, inc=("ac", 1))
                    mmv[ch] = k.on("tensor", lambda e, ch=ch, row=row, bank=bank, hl=hl: e.matmul(
                        pb[bank][row:row + 1, :], sb["onesb"][64 * hl:64 * hl + 64, :],
                        sb["sqc"][64 * hl:64 * hl + 64, :],
                        start=True, stop=True),
                        wait=[("ac", k.v("ac"))], inc=("pe", 1))
                pev = k.v("pe")
                for ch in range(NCH):
                    row, bank = (ch % 3) * 32, ch // 3
                    rr2, cc2 = 32 * (ch // 4), 512 * (ch % 4)
                    k.on("scalar", lambda e, row=row, bank=bank, rr2=rr2, cc2=cc2: e.copy(
                        sb["rows2"][rr2:rr2 + 1, cc2:cc2 + 512], pb[bank][row:row + 1, :]),
                        wait=[("pe", pev)] if ch == 0 else [], inc=("ac", 1))
                for rr2 in (0, 32):
                    k.on("vector", lambda e, rr2=rr2: e.tensor_scalar(
                        sb["rows2"][rr2:rr2 + 1, :], sb["rows2"][rr2:rr2 + 1, :], 1e-24, None, ALU.max),
                        wait=[("ac", k.v("ac")), ("ve", k.v("ve"))], inc=("ve", 1))
                    k.on("scalar", lambda e, rr2=rr2: e.activation(
                        sb["rows2"][rr2:rr2 + 1, :], sb["rows2"][rr2:rr2 + 1, :], AF.Ln),
                        wait=[("ve", k.v("ve"))], inc=("ac", 1))
                    k.on("scalar", lambda e, rr2=rr2: e.activation(
                        sb["rows2"][rr2:rr2 + 1, :], sb["rows2"][rr2:rr2 + 1, :], AF.Exp, scale=-0.5),
                        wait=[("ac", k.v("ac"))], inc=("ac", 1))
                for ch in range(NCH):
                    rr2, cc2 = 32 * (ch // 4), 512 * (ch % 4)
                    pbc = k.on("tensor", lambda e, rr2=rr2, cc2=cc2: e.matmul(
                        pb[3][0:64, :], sb["ones1f"][rr2:rr2 + 1, 0:64],
                        sb["rows2"][rr2:rr2 + 1, cc2:cc2 + 512], start=True, stop=True),
                        wait=[("ac", k.v("ac")), ("ve", k.v("ve"))], inc=("pe", 1))
                    k.on("scalar", lambda e, ch=ch: e.copy(
                        sb["rnb"][0:64, CH * ch:CH * (ch + 1)], pb[3][0:64, :]),
                        wait=[("pe", pbc)], inc=("ac", 1))
                ac_bc = k.v("ac")
                k.on("vector", lambda e, srct=srct, hl=hl: e.tensor_copy(
                    sb["q0"][0:64, :], srct[64 * hl:64 * hl + 64, :]),
                    wait=[("ac", ac_bc), ("ve", k.v("ve"))], inc=("ve", 1))
                k.on("vector", lambda e: e.tensor_tensor(
                    sb["q0"][0:64, :], sb["q0"][0:64, :], sb["rnb"][0:64, :], ALU.mult),
                    wait=[("ve", k.v("ve"))], inc=("ve", 1))
                scol = sb["qs_col" if which == "q" else "ks_col"]
                k.on("vector", lambda e, scol=scol: e.tensor_scalar(
                    sb["q0"][0:64, :], sb["q0"][0:64, :], scol[0:64, 0:1], None, ALU.mult),
                    wait=[("ve", k.v("ve")), ("d_c", ld_c)], inc=("ve", 1))
                k.on("vector", lambda e: e.tensor_copy(sb["qsw"][0:32, :], sb["q0"][32:64, :]),
                     wait=[("ve", k.v("ve"))], inc=("ve", 1))
                k.on("vector", lambda e: e.tensor_copy(sb["qsw"][32:64, :], sb["q0"][0:32, :]),
                     wait=[("ve", k.v("ve"))], inc=("ve", 1))
                k.on("vector", lambda e, sse=sse: e.tensor_tensor(
                    sb["qsw"][0:64, :], sb["qsw"][0:64, :], sse[0:64, :], ALU.mult),
                    wait=[("ve", k.v("ve"))], inc=("ve", 1))
                k.on("vector", lambda e, cse=cse: e.tensor_tensor(
                    sb["qnf"][0:64, :], sb["q0"][0:64, :], cse[0:64, :], ALU.mult),
                    wait=[("ve", k.v("ve"))], inc=("ve", 1))
                k.on("vector", lambda e: e.tensor_tensor(
                    sb["qnf"][0:64, :], sb["qnf"][0:64, :], sb["qsw"][0:64, :], ALU.add),
                    wait=[("ve", k.v("ve"))], inc=("ve", 1))
                k.on("vector", lambda e, dstr=dstr, hl=hl: e.tensor_copy(
                    dstr[64 * hl:64 * hl + 64, :], sb["qnf"][0:64, :]),
                    wait=[("ve", k.v("ve"))], inc=("ve", 1))
            for wdx in range(8):
                koff = 4 if wdx == 0 else 0
                kbase = 512 * (wdx - 1)
                for qtl in range(4):
                    qc0 = 512 * wdx + 128 * qtl
                    sbk = qtl % 2
                    for kc in range(2):
                        if wdx == 0 and kc == 0:
                            continue
                        k.on("tensor", lambda e, hl=hl, qc0=qc0, kbase=kbase, kc=kc, sbk=sbk: e.matmul(
                            pb[sbk][:], sb["qr"][64 * hl:64 * hl + 64, qc0:qc0 + 128],
                            sb["kr"][64 * hl:64 * hl + 64, kbase + 512 * kc:kbase + 512 * (kc + 1)],
                            start=True, stop=True),
                            wait=[("ve", k.v("ve")), ("ac", k.v("ac"))], inc=("pe", 1))
                        pev = k.v("pe")
                        k.on("vector", lambda e, qtl=qtl, kc=kc, sbk=sbk: e.tensor_tensor(
                            sb["sm"][:, 512 * kc:512 * (kc + 1)], pb[sbk][:],
                            sb["mask_sb"][:, 1024 * qtl + 512 * kc:1024 * qtl + 512 * (kc + 1)],
                            ALU.add),
                            wait=[("pe", pev), ("ve", k.v("ve")), ("ac", k.v("ac"))], inc=("ve", 1))
                    off = 512 * koff // 4
                    k.on("vector", lambda e, off=off: e.reduce_max(
                        sb["negmax"][:], sb["sm"][:, off:1024], axis=AX.X, negate=True),
                        wait=[("ve", k.v("ve"))], inc=("ve", 1))
                    k.on("scalar", lambda e, off=off: e.activation(
                        sb["eb"][:, off:1024], sb["sm"][:, off:1024], AF.Exp,
                        bias=sb["negmax"][:], accum_out=sb["rsum"][:]),
                        wait=[("ve", k.v("ve"))], inc=("ac", 1))
                    k.on("vector", lambda e: e.reciprocal(sb["rsum"][:], sb["rsum"][:]),
                         wait=[("ac", k.v("ac")), ("ve", k.v("ve"))], inc=("ve", 1))
                    k.on("vector", lambda e, off=off: e.tensor_scalar(
                        sb["eb"][:, off:1024], sb["eb"][:, off:1024], sb["rsum"][:], None, ALU.mult),
                        wait=[("ve", k.v("ve"))], inc=("ve", 1))
                    ve_eb = k.v("ve")
                    ob = 4 + qtl % 2
                    n_kt = 8 - koff
                    for i, ktl in enumerate(range(koff, 8)):
                        tb = 2 + ktl % 2
                        ptr = k.on("tensor", lambda e, ktl=ktl, tb=tb: e.transpose(
                            pb[tb][:, 0:256].bitcast(BF16)[:, 0:128],
                            sb["eb"][:, 128 * ktl:128 * (ktl + 1)], sb["id_bf"][:]),
                            wait=[("ve", ve_eb), ("ac", k.v("ac")), ("d_cg", ld_cg)], inc=("pe", 1))
                        k.on("scalar", lambda e, ktl=ktl, tb=tb: e.copy(
                            sb["at_sb"][:, 128 * ktl:128 * (ktl + 1)],
                            pb[tb][:, 0:256].bitcast(BF16)[:, 0:128]),
                            wait=[("pe", ptr)], inc=("ac", 1))
                        ktg = 4 * (wdx - 1) + ktl
                        k.on("tensor", lambda e, ktl=ktl, ktg=ktg, hl=hl, ob=ob, i=i, n=n_kt: e.matmul(
                            pb[ob][0:64, 0:128], sb["vp"][:, 128 * ktg + 64 * hl:128 * ktg + 64 * hl + 64],
                            sb["at_sb"][:, 128 * ktl:128 * (ktl + 1)],
                            start=(i == 0), stop=(i == n - 1)),
                            wait=[("ac", k.v("ac"))], inc=("pe", 1))
                    pev = k.v("pe")
                    k.on("scalar", lambda e, hl=hl, p=p, qc0=qc0, ob=ob: e.copy(
                        sb["or_all"][64 * hl:64 * hl + 64, T * p + qc0:T * p + qc0 + 128],
                        pb[ob][0:64, 0:128]),
                        wait=[("pe", pev)], inc=("ac", 1))
    k.barrier()
    for half in range(2):
        k.dma("gpsimd", sb["wo_sb"][:, 2048 * half:2048 * (half + 1)],
              tens["lf_wo"][l][:, 512 * half:512 * (half + 1)].rearrange(
                  "(a p) (b c) -> p a b c", p=128, c=128), "d_wf")
    ldwo = k.v("d_wf")
    CW = 256
    st_done = [0] * 16
    for ch in range(16):
        ldy = k.dma("sync", sb["yinb"][:],
                    tens[yin][:, CW * ch:CW * (ch + 1)].rearrange("(a p) c -> p a c", p=128),
                    "d_xc", wait=[("ve", k.v("ve"))] if ch >= 1 else [])
        for dot in range(DT):
            bank = 6 + dot % 2
            w = [("d_wf", ldwo), ("ve", k.v("ve")), ("ac", k.v("ac"))]
            for pt in range(4):
                woc = 2048 * (dot // 4) + 512 * pt + 128 * (dot % 4)
                k.on("tensor", lambda e, ch=ch, woc=woc, pt=pt, bank=bank: e.matmul(
                    pb[bank][:, 0:CW], sb["wo_sb"][:, woc:woc + 128],
                    sb["or_all"][:, T * pt + CW * ch:T * pt + CW * (ch + 1)],
                    start=(pt == 0), stop=(pt == 3)), wait=w if pt == 0 else [],
                    inc=("pe", 1) if pt == 3 else None)
            pev = k.v("pe")
            wst = [("pe", pev), ("d_xc", ldy)]
            if dot == 0 and ch >= 1:
                wst.append(("d_rs", st_done[ch - 1]))
            k.on("vector", lambda e, dot=dot, bank=bank: e.tensor_tensor(
                sb["rescb"][:, CW * dot:CW * (dot + 1)], pb[bank][:, 0:CW],
                sb["yinb"][:, CW * dot:CW * (dot + 1)], ALU.add),
                wait=wst, inc=("ve", 1))
        st_done[ch] = k.dma("gpsimd",
                            tens[ya][:, CW * ch:CW * (ch + 1)].rearrange("(a p) c -> p a c", p=128),
                            sb["rescb"][:], "d_rs", wait=[("ve", k.v("ve"))])
    k.barrier()


LFA_SB = [
    ("csq", (64, T), BF16), ("ssq", (64, T), BF16), ("csk", (64, T), BF16), ("ssk", (64, T), BF16),
    ("mask_sb", (128, 4 * 1024), BF16), ("id_bf", (128, 128), BF16),
    ("qs_col", (64, 1), F32), ("ks_col", (64, 1), F32),
    ("ybuf", (128, 1025), F32), ("hbf", (128, 1024), BF16),
    ("wq_sb", (128, 1024), BF16), ("wk_sb", (128, 1024), BF16), ("wv_sb", (128, 1024), BF16),
    ("hb", (128, 8 * CH), BF16),
    ("qt", (128, T), BF16), ("kt", (128, T), BF16),
    ("vp", (128, 32 * 128), BF16),
    ("sqc", (128, CH), BF16), ("rows2", (65, 2048), F32), ("rnb", (64, T), BF16),
    ("q0", (64, T), BF16), ("qsw", (64, T), BF16), ("qnf", (64, T), BF16),
    ("qr", (128, T), BF16), ("kr", (128, T), BF16),
    ("sm", (128, 1024), F32), ("eb", (128, 1024), BF16),
    ("negmax", (128, 1), F32), ("rsum", (128, 1), F32),
    ("at_sb", (128, 8 * 128), BF16),
    ("or_all", (128, 4 * T), BF16),
    ("wo_sb", (128, 2 * 2048), BF16), ("yinb", (128, 8 * 256), F32), ("rescb", (128, 8 * 256), F32),
]


def build_lfattn_launch(l, yin_name, ya_name):
    nc = bass.Bass("TRN2", target_bir_lowering=False, debug=False)
    decls = [
        (yin_name, (D, T), F32, "ExternalInput"),
        ("lf_wqkv", (2, D, 1536), F32, "ExternalInput"),
        ("lf_qs", (2, 64), F32, "ExternalInput"),
        ("lf_ks", (2, 64), F32, "ExternalInput"),
        ("lf_wo", (2, 512, D), F32, "ExternalInput"),
        ("rot_cq", (64, T), F32, "ExternalInput"), ("rot_sq", (64, T), F32, "ExternalInput"),
        ("rot_ck", (64, T), F32, "ExternalInput"), ("rot_sk", (64, T), F32, "ExternalInput"),
        ("la_mask", (512, 1024), F32, "ExternalInput"),
        ("ident", (128, 128), F32, "ExternalInput"),
        (ya_name, (D, T), F32, "ExternalOutput"),
        ("hshiftT", (D, T), BF16, "Internal"),
    ]
    tens = {}
    for name, shape, dt, kind in decls:
        tens[name] = nc.dram_tensor(name, list(shape), dt, kind=kind).ap()
    ctx = ExitStack()
    with ctx:
        k, sb = _mk_k(nc, ctx, LFA_SB)
        st_lf_attn(k, tens, sb, l, yin_name, ya_name)
        _finish(nc, k)
    return nc




def st_lf_ffn(k, tens, sb, l, ya, yout):
    pb = k.pb
    HT, HP = 44, 22
    k.dma("gpsimd", tens["w1bf"].rearrange("a b -> (a b)"),
          tens["lf_w1p"][l].rearrange("a b -> (a b)"), "d_cw")
    k.dma("gpsimd", tens["w2bf"].rearrange("a b -> (a b)"),
          tens["lf_w2p"][l].rearrange("a b -> (a b)"), "d_cw")
    k.dma("sync", sb["b1c"][:], tens["lf_b1p_t"][l], "d_c")
    k.dma("sync", sb["lgc"][:], tens["lf_lng_t"][l], "d_c")
    k.dma("sync", sb["lbc"][:], tens["lf_lnb_t"][l], "d_c")
    k.dma("sync", sb["b2c"][:], tens["lf_b2_t"][l], "d_c")
    ld_c = k.v("d_c")
    cw = k.v("d_cw")
    _emit_shift(k, tens, sb, ya)
    k.barrier()
    mu = sb["rows"][0:1, 0:T]
    ms = sb["rows"][32:33, 0:T]
    rr = sb["rows"][32:33, T:2 * T]
    sq_done, mmB, ld_done = [0] * DT, [0] * DT, [0] * DT
    for d in range(DT):
        hsb = sb["hsb"][:]
        w = []
        if d >= 1:
            w = [("ac", sq_done[d - 1]), ("pe", mmB[d - 1])]
        ld_done[d] = k.dma("sync", hsb, tens["hshiftT"][d * 128:(d + 1) * 128, :],
                           "d_a0", wait=w)
        sq_done[d] = k.on("scalar", lambda e, hsb=hsb: e.activation(sb["hsq"][:], hsb, AF.Square),
                          wait=[("d_a0", ld_done[d]), ("ac", k.v("ac")),
                                ("pe", mmB[d - 1] if d >= 1 else 0)], inc=("ac", 1))
        for ch in range(NCH):
            row, bank = (ch % 3) * 32, ch // 3
            k.on("tensor", lambda e, hsb=hsb, ch=ch, row=row, bank=bank, d=d: e.matmul(
                pb[bank][row:row + 1, :], sb["onesb"][:], hsb[:, ch * CH:(ch + 1) * CH],
                start=(d == 0), stop=(d == DT - 1)),
                wait=[("d_a0", ld_done[d])] if ch == 0 else [])
            k.on("tensor", lambda e, ch=ch, row=row, bank=bank, d=d: e.matmul(
                pb[3 + bank][row:row + 1, :], sb["onesb"][:], sb["hsq"][:, ch * CH:(ch + 1) * CH],
                start=(d == 0), stop=(d == DT - 1)),
                wait=[("ac", sq_done[d])] if ch == 0 else [],
                inc=("pe", 1) if ch == NCH - 1 else None)
        mmB[d] = k.v("pe")
    pe_stats = mmB[DT - 1]
    for ch in range(NCH):
        row, bank = (ch % 3) * 32, ch // 3
        k.on("scalar", lambda e, ch=ch, row=row, bank=bank: e.activation(
            sb["rows"][0:1, ch * CH:(ch + 1) * CH], pb[bank][row:row + 1, :], AF.Copy, scale=1.0 / D),
            wait=[("pe", pe_stats)] if ch == 0 else [], inc=("ac", 1))
        k.on("scalar", lambda e, ch=ch, row=row, bank=bank: e.activation(
            sb["rows"][32:33, ch * CH:(ch + 1) * CH], pb[3 + bank][row:row + 1, :], AF.Copy, scale=1.0 / D),
            inc=("ac", 1))
    ac_mu = k.v("ac")
    k.on("vector", lambda e: e.tensor_tensor(rr, mu, mu, ALU.mult),
         wait=[("ac", ac_mu)], inc=("ve", 1))
    k.on("vector", lambda e: e.tensor_tensor(rr, ms, rr, ALU.subtract),
         wait=[("ve", k.v("ve"))], inc=("ve", 1))
    k.on("scalar", lambda e: e.activation(rr, rr, AF.Sqrt, bias=sb["c_eps"][32:33, :]),
         wait=[("ve", k.v("ve"))], inc=("ac", 1))
    k.on("vector", lambda e: e.reciprocal(rr, rr),
         wait=[("ac", k.v("ac"))], inc=("ve", 1))
    ve_r = k.v("ve")
    bc_done = 0
    for ch in range(NCH):
        ba = 4 + 2 * (ch % 2)
        wmm = [("ac", ac_mu), ("ve", ve_r)]
        if ch >= 2:
            wmm.append(("ac", bc_done))
        k.on("tensor", lambda e, ch=ch, ba=ba: e.matmul(
            pb[ba][:], sb["ones1f"][0:1, :], sb["rows"][0:1, ch * CH:(ch + 1) * CH],
            start=True, stop=True), wait=wmm, inc=("pe", 1))
        pev = k.on("tensor", lambda e, ch=ch, ba=ba: e.matmul(
            pb[ba + 1][:], sb["ones1f"][32:33, :], sb["rows"][32:33, T + ch * CH:T + (ch + 1) * CH],
            start=True, stop=True), inc=("pe", 1))
        k.on("scalar", lambda e, ch=ch, ba=ba: e.copy(
            sb["mub"][:, ch * CH:(ch + 1) * CH], pb[ba][:]),
            wait=[("pe", pev)], inc=("ac", 1))
        bc_done = k.on("scalar", lambda e, ch=ch, ba=ba: e.copy(
            sb["rb"][:, ch * CH:(ch + 1) * CH], pb[ba + 1][:]), inc=("ac", 1))
    ac_bc = k.v("ac")
    xst = [0] * DT
    for d in range(DT):
        hsb = sb["hsb"][:]
        ld = k.dma("sync", hsb, tens["hshiftT"][d * 128:(d + 1) * 128, :], "d_a0",
                   wait=[("ve", k.v("ve")), ("ac", k.v("ac")), ("pe", k.v("pe")),
                         ("d_x0", xst[d - 1] if d >= 1 else 0)])
        k.on("vector", lambda e, hsb=hsb: e.tensor_tensor(sb["hsq"][:], hsb, sb["mub"][:], ALU.subtract),
             wait=[("d_a0", ld), ("ac", ac_bc), ("ve", k.v("ve")), ("ac", k.v("ac"))], inc=("ve", 1))
        k.on("vector", lambda e: e.tensor_tensor(sb["hsq"][:], sb["hsq"][:], sb["rb"][:], ALU.mult),
             wait=[("ve", k.v("ve"))], inc=("ve", 1))
        vv = k.on("scalar", lambda e, d=d, hsb=hsb: e.activation(
            hsb, sb["hsq"][:], AF.Identity,
            bias=sb["lbc"][:, d:d + 1], scale=sb["lgc"][:, d:d + 1]),
            wait=[("ve", k.v("ve")), ("d_c", ld_c)], inc=("ac", 1))
        xst[d] = k.dma("gpsimd", tens["xn2T"][d * 128:(d + 1) * 128, :], hsb, "d_x0",
                       wait=[("ac", vv)])
    k.barrier()
    w1_rd = {}
    w2_rd = {}
    st_done = [0] * NCH
    for ch in range(NCH):
        ldy = k.dma("sync", sb["yab"][:],
                    tens[ya][:, CH * ch:CH * (ch + 1)].rearrange("(a p) c -> p a c", p=128),
                    "d_xc", wait=[("ve", k.v("ve"))] if ch >= 1 else [])
        ldxn = k.dma("sync", sb["xnb"][:],
                     tens["xn2T"][:, CH * ch:CH * (ch + 1)].rearrange("(a p) c -> p a c", p=128),
                     "d_xnb", wait=[("pe", k.v("pe"))] if ch >= 1 else [])
        for ht in range(HT):
            b2 = ht % 2
            wld = [("d_cw", cw)]
            if (ch, ht - 2) in w1_rd:
                wld.append(("pe", w1_rd[(ch, ht - 2)]))
            elif ch >= 1 and (ch - 1, HT - 2 + ht) in w1_rd:
                wld.append(("pe", w1_rd[(ch - 1, HT - 2 + ht)]))
            ldw = k.dma("gpsimd", sb["w1sl"][:, 1024 * b2:1024 * (b2 + 1)],
                        tens["w1bf"][:, 128 * ht:128 * (ht + 1)].rearrange("(a p) c -> p a c", p=128),
                        "d_w%d" % b2, wait=wld)
            bank = ht % 2
            w = [("d_w%d" % b2, ldw), ("ac", k.v("ac"))]
            w.append(("d_xnb", ldxn))
            for dm in range(DT):
                k.on("tensor", lambda e, ht=ht, dm=dm, bank=bank, b2=b2: e.matmul(
                    pb[bank][:], sb["w1sl"][:, 1024 * b2 + 128 * dm:1024 * b2 + 128 * (dm + 1)],
                    sb["xnb"][:, CH * dm:CH * (dm + 1)],
                    start=(dm == 0), stop=(dm == DT - 1)),
                    wait=w if dm == 0 else [],
                    inc=("pe", 1) if dm == DT - 1 else None)
            w1_rd[(ch, ht)] = k.v("pe")
            k.on("scalar", lambda e, ht=ht, bank=bank: e.activation(
                sb["h1c"][:, 512 * ht:512 * (ht + 1)], pb[bank][:], AF.Identity,
                bias=sb["b1c"][:, ht:ht + 1], scale=1.0),
                wait=[("pe", w1_rd[(ch, ht)])], inc=("ac", 1))
        ac_h1 = k.v("ac")
        for hp in range(HP):
            k.on("scalar", lambda e, hp=hp: e.activation(
                sb["gg"][:], sb["h1c"][:, 512 * (HP + hp):512 * (HP + hp + 1)], AF.Gelu),
                wait=[("ac", k.v("ac")), ("ve", k.v("ve"))], inc=("ac", 1))
            k.on("vector", lambda e, hp=hp: e.tensor_tensor(
                sb["h1c"][:, 512 * hp:512 * (hp + 1)], sb["h1c"][:, 512 * hp:512 * (hp + 1)],
                sb["gg"][:], ALU.mult),
                wait=[("ac", k.v("ac"))], inc=("ve", 1))
        ve_gg = k.v("ve")
        for do in range(DT):
            b2 = do % 2
            wld = [("d_cw", cw)]
            if (ch, do - 2) in w2_rd:
                wld.append(("pe", w2_rd[(ch, do - 2)]))
            elif ch >= 1 and (ch - 1, DT - 2 + do) in w2_rd:
                wld.append(("pe", w2_rd[(ch - 1, DT - 2 + do)]))
            ldw = k.dma("gpsimd", sb["w2sl"][:, 2816 * b2:2816 * (b2 + 1)],
                        tens["w2bf"][:, 128 * do:128 * (do + 1)].rearrange("(a p) c -> p a c", p=128),
                        "d_v%d" % b2, wait=wld)
            bank = 2 + do % 2
            w = [("d_v%d" % b2, ldw), ("ve", ve_gg), ("ve", k.v("ve"))]
            for ht in range(HP):
                k.on("tensor", lambda e, do=do, ht=ht, bank=bank, b2=b2: e.matmul(
                    pb[bank][:], sb["w2sl"][:, 2816 * b2 + 128 * ht:2816 * b2 + 128 * (ht + 1)],
                    sb["h1c"][:, 512 * ht:512 * (ht + 1)],
                    start=(ht == 0), stop=(ht == HP - 1)),
                    wait=w if ht == 0 else [],
                    inc=("pe", 1) if ht == HP - 1 else None)
            w2_rd[(ch, do)] = k.v("pe")
            wst = [("pe", w2_rd[(ch, do)]), ("d_xc", ldy)]
            if do == 0 and ch >= 1:
                wst.append(("d_rs", st_done[ch - 1]))
            k.on("vector", lambda e, do=do, bank=bank: e.scalar_tensor_tensor(
                sb["youtc"][:, CH * do:CH * (do + 1)], pb[bank][:], sb["b2c"][:, do:do + 1],
                sb["yab"][:, CH * do:CH * (do + 1)], ALU.add, ALU.add),
                wait=wst, inc=("ve", 1))
        st_done[ch] = k.dma("gpsimd",
                            tens[yout][:, CH * ch:CH * (ch + 1)].rearrange("(a p) c -> p a c", p=128),
                            sb["youtc"][:], "d_rs", wait=[("ve", k.v("ve"))])
    k.barrier()


def _emit_shift(k, tens, sb, yin):
    for d in range(DT):
        for cc in range(4):
            yb = sb["ybuf"][:, 0:1025]
            hf = sb["hbf"][:, 0:1024]
            wld = [("d_h0", k.v("d_h0")), ("ve", k.v("ve"))]
            if d < 4:
                ld = k.dma("sync", yb[:, 0:1024],
                           tens[yin][128 * d:128 * (d + 1), 1024 * cc:1024 * (cc + 1)],
                           "d_y0", wait=wld)
                vv = k.on("vector", lambda e, yb=yb, hf=hf: e.tensor_copy(hf, yb[:, 0:1024]),
                          wait=[("d_y0", ld), ("ve", k.v("ve"))], inc=("ve", 1))
            elif cc == 0:
                ld = k.dma("sync", yb[:, 0:1023],
                           tens[yin][128 * d:128 * (d + 1), 0:1023], "d_y0", wait=wld)
                k.on("vector", lambda e, hf=hf: e.memset(hf[:, 0:1], 0.0),
                     wait=[("ve", k.v("ve"))] + wld, inc=("ve", 1))
                vv = k.on("vector", lambda e, yb=yb, hf=hf: e.tensor_copy(hf[:, 1:1024], yb[:, 0:1023]),
                          wait=[("d_y0", ld), ("ve", k.v("ve"))], inc=("ve", 1))
            else:
                ld = k.dma("sync", yb[:, 0:1024],
                           tens[yin][128 * d:128 * (d + 1), 1024 * cc - 1:1024 * (cc + 1) - 1],
                           "d_y0", wait=wld)
                vv = k.on("vector", lambda e, yb=yb, hf=hf: e.tensor_copy(hf, yb[:, 0:1024]),
                          wait=[("d_y0", ld), ("ve", k.v("ve"))], inc=("ve", 1))
            k.dma("gpsimd", tens["hshiftT"][128 * d:128 * (d + 1), 1024 * cc:1024 * (cc + 1)],
                  hf, "d_h0", wait=[("ve", vv)])


LFF_SB = [
    ("b1c", (128, 44), F32), ("lgc", (128, 8), F32), ("lbc", (128, 8), F32), ("b2c", (128, 8), F32),
    ("ybuf", (128, 1025), F32), ("hbf", (128, 1024), BF16),
    ("hsb", (128, T), BF16), ("hsq", (128, T), BF16),
    ("rows", (65, 2 * T), F32),
    ("mub", (128, T), BF16), ("rb", (128, T), BF16),
    ("xnb", (128, 8 * CH), BF16),
    ("w1sl", (128, 2 * 1024), BF16), ("w2sl", (128, 2 * 2816), BF16),
    ("h1c", (128, 44 * CH), BF16), ("gg", (128, CH), BF16),
    ("yab", (128, 8 * CH), F32), ("youtc", (128, 8 * CH), F32),
]


def build_lfffn_launch(l, ya_name, yout_name):
    nc = bass.Bass("TRN2", target_bir_lowering=False, debug=False)
    decls = [
        (ya_name, (D, T), F32, "ExternalInput"),
        ("lf_w1p", (2, D, 5632), F32, "ExternalInput"),
        ("lf_b1p_t", (2, 128, 44), F32, "ExternalInput"),
        ("lf_w2p", (2, 2816, D), F32, "ExternalInput"),
        ("lf_b2_t", (2, 128, 8), F32, "ExternalInput"),
        ("lf_lng_t", (2, 128, 8), F32, "ExternalInput"),
        ("lf_lnb_t", (2, 128, 8), F32, "ExternalInput"),
        (yout_name, (D, T), F32, "ExternalOutput"),
        ("hshiftT", (D, T), BF16, "Internal"),
        ("xn2T", (D, T), BF16, "Internal"),
        ("w1bf", (D, 5632), BF16, "Internal"),
        ("w2bf", (2816, D), BF16, "Internal"),
    ]
    tens = {}
    for name, shape, dt, kind in decls:
        tens[name] = nc.dram_tensor(name, list(shape), dt, kind=kind).ap()
    ctx = ExitStack()
    with ctx:
        k, sb = _mk_k(nc, ctx, LFF_SB)
        st_lf_ffn(k, tens, sb, l, ya_name, yout_name)
        _finish(nc, k)
    return nc


def st_final(k, tens, sb, a_src, b_src, dst):
    st_done = [0] * DT
    for d in range(DT):
        b2 = d % 2
        aslc = sb["fa"][:, T * b2:T * (b2 + 1)]
        bslc = sb["fb"][:, T * b2:T * (b2 + 1)]
        w = [("ve", k.v("ve"))]
        if d >= 2:
            w.append(("d_rs", st_done[d - 2]))
        lda = k.dma("sync", aslc, tens[a_src][128 * d:128 * (d + 1), :], "d_a%d" % b2, wait=w)
        ldb = k.dma("sync", bslc, tens[b_src][128 * d:128 * (d + 1), :], "d_b%d" % b2, wait=[])
        vv = k.on("vector", lambda e, aslc=aslc, bslc=bslc: e.tensor_tensor(
            aslc, aslc, bslc, ALU.add),
            wait=[("d_a%d" % b2, lda), ("d_b%d" % b2, ldb), ("ve", k.v("ve"))], inc=("ve", 1))
        st_done[d] = k.dma("gpsimd", tens[dst][128 * d:128 * (d + 1), :], aslc, "d_rs",
                           wait=[("ve", vv)])
    k.barrier()


FIN_SB = [("fa", (128, 2 * T), F32), ("fb", (128, 2 * T), F32)]


def build_final_launch(a_name, b_name, out_name):
    nc = bass.Bass("TRN2", target_bir_lowering=False, debug=False)
    decls = [
        (a_name, (D, T), F32, "ExternalInput"),
        (b_name, (D, T), F32, "ExternalInput"),
        (out_name, (D, T), F32, "ExternalOutput"),
    ]
    tens = {}
    for name, shape, dt, kind in decls:
        tens[name] = nc.dram_tensor(name, list(shape), dt, kind=kind).ap()
    ctx = ExitStack()
    with ctx:
        k, sb = _mk_k(nc, ctx, FIN_SB)
        st_final(k, tens, sb, a_name, b_name, out_name)
        _finish(nc, k)
    return nc


def _mk_k(nc, ctx, sb_decls):
    k = K(nc, ctx)
    sb = {}
    seen = set()
    for nm, shape, dt in sb_decls:
        if nm in seen:
            continue
        seen.add(nm)
        sb[nm] = ctx.enter_context(nc.sbuf_tensor(nm, list(shape), dt))
    for nm, shape, dt in [("onesb", (128, 1), BF16), ("ones1f", (33, 128), F32),
                          ("c_eps", (128, 1), F32)]:
        sb[nm] = ctx.enter_context(nc.sbuf_tensor(nm, list(shape), dt))
    k.pb = [ctx.enter_context(nc.psum_tensor("pb%d" % i, [128, 512], F32))
            for i in range(8)]
    k.on("vector", lambda e: e.memset(sb["onesb"][:], 1.0), inc=("ve", 1))
    k.on("vector", lambda e: e.memset(sb["ones1f"][:], 1.0), inc=("ve", 1))
    k.on("vector", lambda e: e.memset(sb["c_eps"][:], 1e-5), inc=("ve", 1))
    return k, sb


def _finish(nc, k):
    waits = [(s, k.val[s]) for s in list(k.val) if s.startswith("d_") and k.val[s] > 0]
    k.q["sync"].append((None, waits, []))
    with nc.Block() as block:
        k.finalize(block)
    return nc


def build_l0attn_launch(kinds):
    nc = bass.Bass("TRN2", target_bir_lowering=False, debug=False)
    decls = [
        ("xn_bf", (D, T), BF16, "ExternalInput"),
        ("xT", (D, T), F32, "ExternalInput"),
        ("w_qkv", (D, 3072), F32, "ExternalInput"),
        ("b_qkv_t", (128, 24), F32, "ExternalInput"),
        ("b_qkv_raw", (1, 3072), F32, "ExternalInput"),
        ("w_out", (D, D), F32, "ExternalInput"),
        ("b_out_t", (128, 8), F32, "ExternalInput"),
        ("strips", (16, 128, 3072), F32, "ExternalInput"),
        ("rb_lo", (128, 16), F32, "ExternalInput"),
        ("rb_hi", (128, 16), F32, "ExternalInput"),
        ("ident", (128, 128), F32, "ExternalInput"),
        ("b_gate", (1, 8), F32, "ExternalInput"),
        ("x1p", (D, T), F32, "ExternalOutput"),
        ("otT", (D, T), BF16, "Internal"),
    ]
    tens = {}
    for name, shape, dt, kind in decls:
        tens[name] = nc.dram_tensor(name, list(shape), dt, kind=kind).ap()
    outer = ExitStack()
    with outer:
        k, sbc = _mk_common(nc, outer)
        for i, (sbl, fn) in enumerate([
                (L0A_SB, lambda sb: st_l0_attn(k, tens, sb, "xn_bf")),
                (L0P_SB, lambda sb: st_l0_outproj(k, tens, sb, "xT", "x1p"))]):
            ctx = ExitStack()
            sb = dict(sbc)
            seen = set()
            for nm, shape, dt in sbl:
                if nm in seen:
                    continue
                seen.add(nm)
                sb[nm] = ctx.enter_context(nc.sbuf_tensor("s%d_%s" % (i, nm), list(shape), dt))
            fn(sb)
            ctx.close()
        _finish(nc, k)
    return nc


def _mk_common(nc, outer):
    k = K(nc, outer)
    sbc = {}
    for nm, shape, dt in [("onesb", (128, 1), BF16), ("ones1f", (33, 128), F32),
                          ("c_eps", (128, 1), F32)]:
        sbc[nm] = outer.enter_context(nc.sbuf_tensor(nm, list(shape), dt))
    k.pb = [outer.enter_context(nc.psum_tensor("pb%d" % i, [128, 512], F32))
            for i in range(8)]
    k.on("vector", lambda e: e.memset(sbc["onesb"][:], 1.0), inc=("ve", 1))
    k.on("vector", lambda e: e.memset(sbc["ones1f"][:], 1.0), inc=("ve", 1))
    k.on("vector", lambda e: e.memset(sbc["c_eps"][:], 1e-5), inc=("ve", 1))
    return k, sbc


def build_ln_launch(src, dst_bf, dst_f32, kinds):
    nc = bass.Bass("TRN2", target_bir_lowering=False, debug=False)
    decls = [(src, (D, T), F32, kinds.get(src, "ExternalInput")),
             (dst_bf, (D, T), BF16, kinds.get(dst_bf, "ExternalOutput"))]
    if dst_f32:
        decls.append((dst_f32, (D, T), F32, kinds.get(dst_f32, "ExternalOutput")))
    tens = {}
    for name, shape, dt, kind in decls:
        tens[name] = nc.dram_tensor(name, list(shape), dt, kind=kind).ap()
    ctx = ExitStack()
    with ctx:
        k, sb = _mk_k(nc, ctx, LN_SB)
        st_layernorm(k, tens, sb, src, dst_bf, dst_f32)
        _finish(nc, k)
    return nc


FUSED_DECLS = [
    ("xT", (D, T), F32, "ExternalInput"),
    ("w_qkv", (D, 3072), F32, "ExternalInput"),
    ("b_qkv_t", (128, 24), F32, "ExternalInput"),
    ("b_qkv_raw", (1, 3072), F32, "ExternalInput"),
    ("w_out", (D, D), F32, "ExternalInput"),
    ("b_out_t", (128, 8), F32, "ExternalInput"),
    ("strips", (16, 128, 3072), F32, "ExternalInput"),
    ("rb_lo", (128, 16), F32, "ExternalInput"),
    ("rb_hi", (128, 16), F32, "ExternalInput"),
    ("ident", (128, 128), F32, "ExternalInput"),
    ("w_gate", (D, 8), F32, "ExternalInput"),
    ("b_gate", (1, 8), F32, "ExternalInput"),
    ("moe_w1", (8, D, D), F32, "ExternalInput"),
    ("moe_w2", (8, D, D), F32, "ExternalInput"),
    ("moe_b1c", (128, 64), F32, "ExternalInput"),
    ("moe_b2", (8, D), F32, "ExternalInput"),
    ("emask8", (8, 1024), F32, "ExternalInput"),
    ("p_Wq", (D, D), F32, "ExternalInput"), ("p_Wk", (D, D), F32, "ExternalInput"),
    ("p_Wv", (D, D), F32, "ExternalInput"), ("p_Wo", (D, D), F32, "ExternalInput"),
    ("p_bq_t", (128, 8), F32, "ExternalInput"), ("p_bk_t", (128, 8), F32, "ExternalInput"),
    ("p_bv_t", (128, 8), F32, "ExternalInput"), ("p_bo_t", (128, 8), F32, "ExternalInput"),
    ("p_bv_raw", (1, D), F32, "ExternalInput"),
    ("p_rm", (128, 256), F32, "ExternalInput"),
    ("lf_wqkv", (2, D, 1536), F32, "ExternalInput"),
    ("lf_qs", (2, 64), F32, "ExternalInput"),
    ("lf_ks", (2, 64), F32, "ExternalInput"),
    ("lf_wo", (2, 512, D), F32, "ExternalInput"),
    ("rot_cq", (64, T), F32, "ExternalInput"), ("rot_sq", (64, T), F32, "ExternalInput"),
    ("rot_ck", (64, T), F32, "ExternalInput"), ("rot_sk", (64, T), F32, "ExternalInput"),
    ("la_mask", (512, 1024), F32, "ExternalInput"),
    ("lf_w1p", (2, D, 5632), F32, "ExternalInput"),
    ("lf_b1p_t", (2, 128, 44), F32, "ExternalInput"),
    ("lf_w2p", (2, 2816, D), F32, "ExternalInput"),
    ("lf_b2_t", (2, 128, 8), F32, "ExternalInput"),
    ("lf_lng_t", (2, 128, 8), F32, "ExternalInput"),
    ("lf_lnb_t", (2, 128, 8), F32, "ExternalInput"),
    ("outT", (D, T), F32, "ExternalOutput"),
    ("xn_bf", (D, T), BF16, "Internal"), ("xn_f32", (D, T), F32, "Internal"),
    ("x1p", (D, T), F32, "Internal"), ("x1", (D, T), F32, "Internal"),
    ("xn1_bf", (D, T), BF16, "Internal"), ("xn1_f32", (D, T), F32, "Internal"),
    ("x2p", (D, T), F32, "Internal"),
    ("y_a0", (D, T), F32, "Internal"), ("y_m0", (D, T), F32, "Internal"),
    ("y_a1", (D, T), F32, "Internal"), ("y_m1", (D, T), F32, "Internal"),
    ("otT", (D, T), BF16, "Internal"),
    ("hshiftT", (D, T), BF16, "Internal"), ("xn2T", (D, T), BF16, "Internal"),
    ("moe_w1bf", (8, D, D), BF16, "Internal"), ("moe_w2bf", (8, D, D), BF16, "Internal"),
    ("w1bf", (D, 5632), BF16, "Internal"), ("w2bf", (2816, D), BF16, "Internal"),
]


def build_fused():
    nc = bass.Bass("TRN2", target_bir_lowering=False, debug=False)
    tens = {}
    for name, shape, dt, kind in FUSED_DECLS:
        tens[name] = nc.dram_tensor(name, list(shape), dt, kind=kind).ap()
    outer = ExitStack()
    with outer:
        k = K(nc, outer)
        sbc = {}
        for nm, shape, dt in [("onesb", (128, 1), BF16), ("ones1f", (33, 128), F32),
                              ("c_eps", (128, 1), F32)]:
            sbc[nm] = outer.enter_context(nc.sbuf_tensor(nm, list(shape), dt))
        k.pb = [outer.enter_context(nc.psum_tensor("pb%d" % i, [128, 512], F32))
                for i in range(8)]
        k.on("vector", lambda e: e.memset(sbc["onesb"][:], 1.0), inc=("ve", 1))
        k.on("vector", lambda e: e.memset(sbc["ones1f"][:], 1.0), inc=("ve", 1))
        k.on("vector", lambda e: e.memset(sbc["c_eps"][:], 1e-5), inc=("ve", 1))
        stage_idx = [0]

        def stage(sb_decls, fn):
            si = stage_idx[0]
            stage_idx[0] += 1
            ctx = ExitStack()
            sb = dict(sbc)
            seen = set()
            for nm, shape, dt in sb_decls:
                if nm in seen:
                    continue
                seen.add(nm)
                sb[nm] = ctx.enter_context(
                    nc.sbuf_tensor("s%d_%s" % (si, nm), list(shape), dt))
            fn(sb)
            ctx.close()

        stage(LN_SB, lambda sb: st_layernorm(k, tens, sb, "xT", "xn_bf", "xn_f32"))
        stage(L0A_SB, lambda sb: st_l0_attn(k, tens, sb, "xn_bf"))
        stage(L0P_SB, lambda sb: st_l0_outproj(k, tens, sb, "xT", "x1p"))
        stage(MOE_SB, lambda sb: st_l0_ffn(k, tens, sb, "xn_bf", "x1p", "x1"))
        stage(LN_SB, lambda sb: st_layernorm(k, tens, sb, "x1", "xn1_bf", "xn1_f32"))
        stage(L1A_SB, lambda sb: st_l1_attn(k, tens, sb, "xn1_bf", "x1", "x2p"))
        lf_names = {0: ("xn1_f32", "y_a0", "y_m0"), 1: ("y_m0", "y_a1", "y_m1")}
        for l in range(2):
            yin, ya, ym = lf_names[l]
            stage(LFA_SB, lambda sb, l=l, yin=yin, ya=ya: st_lf_attn(k, tens, sb, l, yin, ya))
            stage(LFF_SB, lambda sb, l=l, ya=ya, ym=ym: st_lf_ffn(k, tens, sb, l, ya, ym))
        stage(FIN_SB, lambda sb: st_final(k, tens, sb, "x2p", "y_m1", "outT"))
        _finish(nc, k)
    return nc


def kernel(**inputs):
    x = np.asarray(inputs["L0_attn_x"], np.float32)
    sh = _prep_shared(inputs)
    xT = [np.ascontiguousarray(x[b].T) for b in range(NCORE)]
    cores = list(range(NCORE))
    b_qkv_raw = np.ascontiguousarray(np.asarray(inputs["L0_attn_b_qkv"], np.float32).reshape(1, 3072))
    p_bv_raw = np.ascontiguousarray(np.asarray(inputs["L1_attn_bv"], np.float32).reshape(1, 1024))

    nc = build_fused()
    shared_keys = [name for name, _s, _d, kind in FUSED_DECLS
                   if kind == "ExternalInput" and name not in ("xT", "b_qkv_raw", "p_bv_raw")]
    maps = []
    for b in range(NCORE):
        m = {kk: sh[kk] for kk in shared_keys}
        m["xT"] = xT[b]
        m["b_qkv_raw"] = b_qkv_raw
        m["p_bv_raw"] = p_bv_raw
        maps.append(m)
    r = run_bass_kernel_spmd(nc, maps, core_ids=cores).results
    out = np.stack([np.asarray(r[b]["outT"], np.float32).T for b in range(NCORE)])
    return np.ascontiguousarray(out)


def kernel_multilaunch(**inputs):
    x = np.asarray(inputs["L0_attn_x"], np.float32)
    sh = _prep_shared(inputs)
    xT = [np.ascontiguousarray(x[b].T) for b in range(NCORE)]
    cores = list(range(NCORE))
    b_qkv_raw = np.ascontiguousarray(np.asarray(inputs["L0_attn_b_qkv"], np.float32).reshape(1, 3072))
    p_bv_raw = np.ascontiguousarray(np.asarray(inputs["L1_attn_bv"], np.float32).reshape(1, 1024))

    def run(nc, per_core, shared_keys):
        maps = []
        for b in range(NCORE):
            m = {kk: sh[kk] for kk in shared_keys}
            m.update(per_core[b])
            maps.append(m)
        return run_bass_kernel_spmd(nc, maps, core_ids=cores).results

    nc = build_ln_launch("xT", "xn_bf", "xn_f32", {})
    r = run(nc, [{"xT": xT[b]} for b in range(NCORE)], [])
    xn_bf = [r[b]["xn_bf"] for b in range(NCORE)]
    xn_f32 = [r[b]["xn_f32"] for b in range(NCORE)]
    nc = build_l0attn_launch({})
    r = run(nc, [{"xn_bf": xn_bf[b], "xT": xT[b], "b_qkv_raw": b_qkv_raw}
                 for b in range(NCORE)],
            ["w_qkv", "b_qkv_t", "w_out", "b_out_t", "strips", "rb_lo", "rb_hi",
             "ident", "b_gate"])
    x1p = [r[b]["x1p"] for b in range(NCORE)]
    nc = build_l0ffn_launch({})
    r = run(nc, [{"xn_bf": xn_bf[b], "xn_f32": xn_f32[b], "x1p": x1p[b]}
                 for b in range(NCORE)],
            ["w_gate", "b_gate", "moe_w1", "moe_w2", "moe_b1c", "moe_b2", "ident", "emask8"])
    x1 = [r[b]["x1"] for b in range(NCORE)]
    nc = build_ln_launch("x1", "xn1_bf", "xn1_f32", {})
    r = run(nc, [{"x1": x1[b]} for b in range(NCORE)], [])
    xn1_bf = [r[b]["xn1_bf"] for b in range(NCORE)]
    xn1_f32 = [r[b]["xn1_f32"] for b in range(NCORE)]
    nc = build_l1attn_launch({})
    r = run(nc, [{"xn1_bf": xn1_bf[b], "x1": x1[b], "p_bv_raw": p_bv_raw}
                 for b in range(NCORE)],
            ["p_Wq", "p_Wk", "p_Wv", "p_Wo", "p_bq_t", "p_bk_t", "p_bv_t", "p_bo_t", "p_rm"])
    x2p = [r[b]["x2p"] for b in range(NCORE)]
    lf_shared_a = ["lf_wqkv", "lf_qs", "lf_ks", "lf_wo", "rot_cq", "rot_sq",
                   "rot_ck", "rot_sk", "la_mask", "ident"]
    lf_shared_f = ["lf_w1p", "lf_b1p_t", "lf_w2p", "lf_b2_t", "lf_lng_t", "lf_lnb_t"]
    yin = xn1_f32
    for l in range(2):
        nc = build_lfattn_launch(l, "y_in", "y_a")
        r = run(nc, [{"y_in": yin[b]} for b in range(NCORE)], lf_shared_a)
        ya = [r[b]["y_a"] for b in range(NCORE)]
        nc = build_lfffn_launch(l, "y_a", "y_m")
        r = run(nc, [{"y_a": ya[b]} for b in range(NCORE)], lf_shared_f)
        yin = [r[b]["y_m"] for b in range(NCORE)]
    nc = build_final_launch("x2p", "y2", "outT")
    r = run(nc, [{"x2p": x2p[b], "y2": yin[b]} for b in range(NCORE)], [])
    out = np.stack([np.asarray(r[b]["outT"], np.float32).T for b in range(NCORE)])
    return np.ascontiguousarray(out)


# version 07
# speedup vs baseline: 1.4164x; 1.2114x over previous
import numpy as np
from contextlib import ExitStack
import concourse.bass as bass
import concourse.mybir as mybir
from concourse.bass_utils import run_bass_kernel_spmd
import ml_dtypes

F32 = mybir.dt.float32
BF16 = mybir.dt.bfloat16
AF = mybir.ActivationFunctionType
ALU = mybir.AluOpType
AX = mybir.AxisListType
NPBF = ml_dtypes.bfloat16

D = 1024
T = 4096
DT = D // 128
CH = 512
NCH = T // CH
NCORE = 8
FMAX = np.float64(np.finfo(np.float32).max)

ENGINES = ("tensor", "vector", "scalar", "sync", "gpsimd")


class K:

    def __init__(self, nc, ctx):
        self.nc = nc
        self.ctx = ctx
        self.q = {e: [] for e in ENGINES}
        self.sems = {}
        self.val = {}
        self.dirty = {e: False for e in ENGINES}
        self.pb = None

    def sem(self, name):
        if name not in self.sems:
            self.sems[name] = self.ctx.enter_context(self.nc.semaphore(name))
            self.val[name] = 0
        return self.sems[name]

    def v(self, name):
        return self.val.get(name, 0)

    def on(self, eng, fn, wait=(), inc=None):
        incs = []
        if inc:
            if isinstance(inc, tuple):
                inc = [inc]
            for name, amt in inc:
                self.sem(name)
                self.val[name] += amt
                incs.append((name, amt))
        wl = [(s, v) for (s, v) in wait if v > 0]
        for name, _v in wl:
            self.sem(name)
        self.q[eng].append((fn, wl, incs))
        self.dirty[eng] = True
        return self.val[incs[0][0]] if incs else None

    def dma(self, eng, out, in_, sem_name, wait=(), **kw):
        def fn(e, out=out, in_=in_, kw=kw):
            return e.dma_start(out=out, in_=in_, **kw)
        return self.on(eng, fn, wait=wait, inc=(sem_name, 16))

    def barrier(self):
        bar = "bar"
        self.sem(bar)
        for e in ("tensor", "vector", "scalar", "gpsimd"):
            if self.dirty[e]:
                if e == "tensor":
                    self.on(e, lambda en: en.nop(), inc=(bar, 1))
                else:
                    self.on(e, lambda en: en.drain(), inc=(bar, 1))
        target = self.val.get(bar, 0)
        dma_waits = [(s, self.val[s]) for s in list(self.val)
                     if s.startswith("d_") and self.val[s] > 0]
        for e in ENGINES:
            self.q[e].append((None, [(bar, target)] + dma_waits, []))
            self.dirty[e] = False

    def run_engine(self, e, eng_obj):
        for fn, waits, incs in self.q[e]:
            for s, v in waits:
                eng_obj.wait_ge(self.sems[s], v)
            if fn is None:
                continue
            inst = fn(eng_obj)
            for s, a in incs:
                inst.then_inc(self.sems[s], a)

    def finalize(self, block):
        block.tensor(lambda e: self.run_engine("tensor", e))
        block.vector(lambda e: self.run_engine("vector", e))
        block.scalar(lambda e: self.run_engine("scalar", e))
        block.sync(lambda e: self.run_engine("sync", e))
        block.gpsimd(lambda e: self.run_engine("gpsimd", e))


def _prep_shared(ip):
    sh = {}
    f32 = np.float32
    sh["w_qkv"] = np.ascontiguousarray(ip["L0_attn_W_qkv"], f32)
    sh["b_qkv_t"] = np.ascontiguousarray(
        np.asarray(ip["L0_attn_b_qkv"], f32).reshape(24, 128).T)
    sh["w_out"] = np.ascontiguousarray(ip["L0_attn_W_out"], f32)
    sh["b_out_t"] = np.ascontiguousarray(
        np.asarray(ip["L0_attn_b_out"], f32).reshape(8, 128).T)
    rb = np.asarray(ip["L0_attn_rel_bias"], f32)
    u = np.arange(3072)[None, :] - 1536
    jp = np.arange(128)[:, None]
    idx = np.clip(u - jp + 1024, 0, 2048)
    sh["strips"] = np.ascontiguousarray(rb[idx].transpose(2, 0, 1))
    sh["rb_lo"] = np.ascontiguousarray(np.broadcast_to(rb[0][None, :], (128, 16)))
    sh["rb_hi"] = np.ascontiguousarray(np.broadcast_to(rb[2048][None, :], (128, 16)))
    sh["w_gate"] = np.ascontiguousarray(ip["L0_ffn_W_gate"], f32)
    sh["b_gate"] = np.ascontiguousarray(np.asarray(ip["L0_ffn_b_gate"], f32).reshape(1, 8))
    sh["moe_w1"] = np.ascontiguousarray(ip["L0_ffn_W1"], f32)
    sh["moe_b1c"] = np.ascontiguousarray(
        np.asarray(ip["L0_ffn_b1"], f32).reshape(8, 8, 128).transpose(2, 0, 1).reshape(128, 64))
    sh["moe_w2"] = np.ascontiguousarray(ip["L0_ffn_W2"], f32)
    sh["moe_b2"] = np.ascontiguousarray(ip["L0_ffn_b2"], f32)
    for nm in ("Wq", "Wk", "Wv", "Wo"):
        sh["p_" + nm] = np.ascontiguousarray(ip["L1_attn_" + nm], f32)
    for nm in ("bq", "bk", "bv", "bo"):
        sh["p_" + nm + "_t"] = np.ascontiguousarray(
            np.asarray(ip["L1_attn_" + nm], f32).reshape(8, 128).T)
    rm_ = np.asarray(ip["L1_attn_random_matrix"], f32)
    sh["p_rm"] = np.ascontiguousarray(np.concatenate([rm_, rm_], axis=0))
    sh["lf_wqkv"] = np.ascontiguousarray(ip["L1_ffn_wqkv"], f32)
    sh["lf_qs"] = np.ascontiguousarray(ip["L1_ffn_q_scale"], f32)
    sh["lf_ks"] = np.ascontiguousarray(ip["L1_ffn_k_scale"], f32)
    sh["lf_wo"] = np.ascontiguousarray(ip["L1_ffn_wo"], f32)
    sh["lf_lng_t"] = np.ascontiguousarray(
        np.asarray(ip["L1_ffn_ln_g"], f32).reshape(2, 8, 128).transpose(0, 2, 1))
    sh["lf_lnb_t"] = np.ascontiguousarray(
        np.asarray(ip["L1_ffn_ln_b"], f32).reshape(2, 8, 128).transpose(0, 2, 1))
    HID = 2730
    HIDP = 2816
    w1 = np.asarray(ip["L1_ffn_w1"], f32)
    w1p = np.zeros((2, 1024, 2 * HIDP), f32)
    w1p[:, :, 0:HID] = w1[:, :, 0:HID]
    w1p[:, :, HIDP:HIDP + HID] = w1[:, :, HID:2 * HID]
    sh["lf_w1p"] = w1p
    b1 = np.asarray(ip["L1_ffn_b1"], f32)
    b1p = np.zeros((2, 2 * HIDP), f32)
    b1p[:, 0:HID] = b1[:, 0:HID]
    b1p[:, HIDP:HIDP + HID] = b1[:, HID:2 * HID]
    sh["lf_b1p_t"] = np.ascontiguousarray(
        b1p.reshape(2, 44, 128).transpose(0, 2, 1))
    w2 = np.asarray(ip["L1_ffn_w2"], f32)
    w2p = np.zeros((2, HIDP, 1024), f32)
    w2p[:, 0:HID, :] = w2
    sh["lf_w2p"] = w2p
    sh["lf_b2_t"] = np.ascontiguousarray(
        np.asarray(ip["L1_ffn_b2"], f32).reshape(2, 8, 128).transpose(0, 2, 1))
    DIM_HEAD, WINDOW = 64, 512
    half = np.arange(0, DIM_HEAD, 2, dtype=np.float64)
    inv_freq = 1.0 / 10000.0 ** (half / DIM_HEAD)
    t = np.arange(T, dtype=np.float64)
    freqs = np.concatenate([t[:, None] * inv_freq[None, :]] * 2, axis=1)
    base = (half + 0.4 * DIM_HEAD) / (1.4 * DIM_HEAD)
    power = (t - T // 2) / float(WINDOW // 2)
    scale = np.concatenate([base[None, :] ** power[:, None]] * 2, axis=1)
    cos, sin = np.cos(freqs), np.sin(freqs)
    sign = np.ones(64); sign[:32] = -1.0
    sh["rot_cq"] = np.ascontiguousarray((cos * scale).T * 8.0, f32)
    sh["rot_sq"] = np.ascontiguousarray((sin * scale * sign[None, :]).T * 8.0, f32)
    sh["rot_ck"] = np.ascontiguousarray((cos / scale).T, f32)
    sh["rot_sk"] = np.ascontiguousarray((sin / scale * sign[None, :]).T, f32)
    il = np.arange(512)[:, None]
    jl = np.arange(1024)[None, :]
    dd = il - jl + 512
    m = np.zeros((512, 1024), f32)
    m[~((dd >= 0) & (dd <= 512))] = -1e30
    sh["la_mask"] = m
    jj = np.arange(128)[:, None]
    uu = np.arange(1408)[None, :]
    mt = np.zeros((128, 1408), f32)
    mt[~((uu >= 384 + jj) & (uu <= 896 + jj))] = -1e30
    sh["la_maskT"] = mt
    sh["ident"] = np.eye(128, dtype=f32)
    em = np.zeros((8, 8, 128), f32)
    for e in range(8):
        em[e, e, :] = 1.0
    sh["emask8"] = np.ascontiguousarray(em.transpose(1, 0, 2).reshape(8, 1024))
    return sh


def st_layernorm(k, tens, sb, src, dst_bf, dst_f32=None):
    pb = k.pb
    x = tens[src]
    mu, ms, rr = sb["row_mu"], sb["row_ms"], sb["row_r"]
    sq_done, cp_done, mmB = [0] * DT, [0] * DT, [0] * DT
    ld_done = [0] * DT
    for d in range(DT):
        b2 = d % 2
        xf, xbf, xsq = (sb["xf"][:, b2 * T:(b2 + 1) * T],
                        sb["xbf"][:, b2 * T:(b2 + 1) * T],
                        sb["xsq"][:, b2 * T:(b2 + 1) * T])
        w = []
        if d >= 2:
            w = [("ac", sq_done[d - 2]), ("ve", cp_done[d - 2])]
        ld_done[d] = k.dma("sync", xf, x[d * 128:(d + 1) * 128, :], "d_a%d" % (d % 2), wait=w)
        wpe = [("pe", mmB[d - 2])] if d >= 2 else []
        sq_done[d] = k.on("scalar", lambda e, xf=xf, xsq=xsq: e.activation(xsq, xf, AF.Square),
                          wait=[("d_a%d" % (d % 2), ld_done[d])] + wpe, inc=("ac", 1))
        cp_done[d] = k.on("vector", lambda e, xf=xf, xbf=xbf: e.tensor_copy(xbf, xf),
                          wait=[("d_a%d" % (d % 2), ld_done[d])] + wpe, inc=("ve", 1))
        for ch in range(NCH):
            row, bank = (ch % 3) * 32, ch // 3
            k.on("tensor", lambda e, xbf=xbf, ch=ch, row=row, bank=bank, d=d: e.matmul(
                pb[bank][row:row + 1, :], sb["onesb"][:], xbf[:, ch * CH:(ch + 1) * CH],
                start=(d == 0), stop=(d == DT - 1)),
                wait=[("ve", cp_done[d])] if ch == 0 else [])
            k.on("tensor", lambda e, xsq=xsq, ch=ch, row=row, bank=bank, d=d: e.matmul(
                pb[3 + bank][row:row + 1, :], sb["onesb"][:], xsq[:, ch * CH:(ch + 1) * CH],
                start=(d == 0), stop=(d == DT - 1)),
                wait=[("ac", sq_done[d])] if ch == 0 else [],
                inc=("pe", 1) if ch == NCH - 1 else None)
        mmB[d] = k.v("pe")
    pe_stats = mmB[DT - 1]
    for ch in range(NCH):
        row, bank = (ch % 3) * 32, ch // 3
        k.on("scalar", lambda e, ch=ch, row=row, bank=bank: e.activation(
            mu[0:1, ch * CH:(ch + 1) * CH], pb[bank][row:row + 1, :], AF.Copy, scale=1.0 / D),
            wait=[("pe", pe_stats)] if ch == 0 else [], inc=("ac", 1))
        k.on("scalar", lambda e, ch=ch, row=row, bank=bank: e.activation(
            ms[0:1, ch * CH:(ch + 1) * CH], pb[3 + bank][row:row + 1, :], AF.Copy, scale=1.0 / D),
            inc=("ac", 1))
    ac_mu = k.v("ac")
    k.on("vector", lambda e: e.tensor_tensor(rr[:], mu[:], mu[:], ALU.mult),
         wait=[("ac", ac_mu)], inc=("ve", 1))
    k.on("vector", lambda e: e.tensor_tensor(rr[:], ms[:], rr[:], ALU.subtract),
         wait=[("ve", k.v("ve"))], inc=("ve", 1))
    ve_var = k.v("ve")
    k.on("scalar", lambda e: e.activation(rr[:], rr[:], AF.Sqrt, bias=sb["c_eps"][0:1, :]),
         wait=[("ve", ve_var)], inc=("ac", 1))
    k.on("vector", lambda e: e.reciprocal(rr[:], rr[:]),
         wait=[("ac", k.v("ac"))], inc=("ve", 1))
    ve_r = k.v("ve")
    ac_r = k.v("ac")
    bc_mm = [0] * NCH
    for ch in range(NCH):
        ba = 4 + 2 * (ch % 2)
        wmm = [("ac", ac_mu), ("ve", ve_r)]
        if ch >= 2:
            wmm.append(("ac", bc_mm[ch - 2]))
        k.on("tensor", lambda e, ch=ch, ba=ba: e.matmul(
            pb[ba][:], sb["ones1f"][0:1, :], mu[0:1, ch * CH:(ch + 1) * CH], start=True, stop=True),
            wait=wmm, inc=("pe", 1))
        pev = k.on("tensor", lambda e, ch=ch, ba=ba: e.matmul(
            pb[ba + 1][:], sb["ones1f"][0:1, :], rr[0:1, ch * CH:(ch + 1) * CH], start=True, stop=True),
            inc=("pe", 1))
        k.on("scalar", lambda e, ch=ch, ba=ba: e.copy(
            sb["mub"][:, ch * CH:(ch + 1) * CH], pb[ba][:]),
            wait=[("pe", pev)], inc=("ac", 1))
        bc_mm[ch] = k.on("scalar", lambda e, ch=ch, ba=ba: e.copy(
            sb["rb"][:, ch * CH:(ch + 1) * CH], pb[ba + 1][:]), inc=("ac", 1))
    ac_bc = k.v("ac")
    tt_done, st_done = [0] * DT, [0] * DT
    for d in range(DT):
        b2 = d % 2
        xf2 = sb["xf"][:, b2 * T:(b2 + 1) * T]
        t0 = sb["t0"][:, b2 * T:(b2 + 1) * T]
        tb = sb["tb"][:, b2 * T:(b2 + 1) * T]
        w = [("ve", cp_done[DT - 1])]
        if d >= 2:
            w.append(("ve", tt_done[d - 2]))
        ld = k.dma("sync", xf2, x[d * 128:(d + 1) * 128, :], "d_a%d" % (d % 2), wait=w)
        wst = [("d_so%d" % (d % 2), st_done[d - 2])] if d >= 2 else []
        k.on("vector", lambda e, xf2=xf2, t0=t0: e.tensor_tensor(t0, xf2, sb["mub"][:], ALU.subtract),
             wait=[("d_a%d" % (d % 2), ld), ("ac", ac_bc)] + wst, inc=("ve", 1))
        if dst_f32 is not None:
            k.on("vector", lambda e, t0=t0: e.tensor_tensor(t0, t0, sb["rb"][:], ALU.mult),
                 wait=[("ve", k.v("ve"))], inc=("ve", 1))
            vv = k.on("vector", lambda e, t0=t0, tb=tb: e.tensor_copy(tb, t0),
                      wait=[("ve", k.v("ve"))], inc=("ve", 1))
            k.dma("gpsimd", tens[dst_f32][d * 128:(d + 1) * 128, :], t0, "d_sp%d" % (d % 2),
                  wait=[("ve", vv)])
        else:
            vv = k.on("vector", lambda e, t0=t0, tb=tb: e.tensor_tensor(tb, t0, sb["rb"][:], ALU.mult),
                      wait=[("ve", k.v("ve"))], inc=("ve", 1))
        tt_done[d] = vv
        st_done[d] = k.dma("gpsimd", tens[dst_bf][d * 128:(d + 1) * 128, :], tb, "d_so%d" % (d % 2),
                           wait=[("ve", vv)])
    k.barrier()


LN_SB = [
    ("xf", (128, 2 * T), F32), ("xbf", (128, 2 * T), BF16), ("xsq", (128, 2 * T), BF16),
    ("t0", (128, 2 * T), F32), ("tb", (128, 2 * T), BF16),
    ("mub", (128, T), F32), ("rb", (128, T), F32),
    ("row_mu", (1, T), F32), ("row_ms", (1, T), F32), ("row_r", (1, T), F32),
]




def st_l0_attn(k, tens, sb, xn_src):
    pb = k.pb
    xn = sb["xn_sb"]
    for d in range(DT):
        k.dma("sync", xn[:, d * T:(d + 1) * T], tens[xn_src][d * 128:(d + 1) * 128, :], "d_xn")
    ld_xn = k.v("d_xn")
    k.dma("sync", sb["bqkv"][:], tens["b_qkv_t"], "d_c")
    k.dma("sync", sb["rblo"][:], tens["rb_lo"], "d_c")
    k.dma("sync", sb["rbhi"][:], tens["rb_hi"], "d_c")
    ld_c = k.v("d_c")
    k.on("vector", lambda e: e.memset(sb["onesrow"][:], 1.0), inc=("ve", 1))
    k.on("vector", lambda e: e.memset(sb["ka0"][64:65, :], 1.0), inc=("ve", 1))
    k.on("vector", lambda e: e.memset(sb["ka1"][64:65, :], 1.0), inc=("ve", 1))
    k.on("vector", lambda e: e.memset(sb["vp"][:], 0.0), inc=("ve", 1))
    for tt in range(32):
        k.on("vector", lambda e, tt=tt: e.memset(sb["vp"][:, 256 * tt + 64:256 * tt + 65], 1.0))
        k.on("vector", lambda e, tt=tt: e.memset(sb["vp"][:, 256 * tt + 192:256 * tt + 193], 1.0))
    k.on("vector", lambda e: e.tensor_scalar(sb["bq8"][:], sb["bqkv"][:, 0:8], 0.125, None, ALU.mult),
         wait=[("d_c", ld_c)], inc=("ve", 1))
    k.barrier()

    wvq = tens["w_qkv"]
    for p in range(8):
        k.barrier()
        for nm, c0 in (("wq_sb", 128 * p), ("wk_sb", 1024 + 128 * p), ("wv_sb", 2048 + 128 * p)):
            src = wvq[:, c0:c0 + 128].rearrange("(a p) c -> p a c", p=128)
            k.dma("gpsimd", sb[nm][:], src, "d_w")
        k.dma("gpsimd", sb["bv_row"][0:1, :], tens["b_qkv_raw"][0:1, 2048 + 128 * p:2176 + 128 * p], "d_w")
        ld_w = k.v("d_w")
        for which, wsb, scale in (("q", "wq_sb", 0.125), ("kk", "wk_sb", 1.0)):
            for ch in range(NCH):
                bank = 6 + ch % 2
                w = [("d_w", ld_w), ("d_xn", ld_xn), ("ac", k.v("ac"))]
                pev = None
                for dm in range(DT):
                    pev = k.on("tensor", lambda e, ch=ch, dm=dm, bank=bank, wsb=wsb: e.matmul(
                        pb[bank][:], sb[wsb][:, 128 * dm:128 * dm + 128],
                        xn[:, T * dm + CH * ch: T * dm + CH * (ch + 1)],
                        start=(dm == 0), stop=(dm == DT - 1)),
                        wait=w if dm == 0 else [],
                        inc=("pe", 1) if dm == DT - 1 else None)
                for hl in range(2):
                    dstt = sb[("qa%d" if which == "q" else "ka%d") % hl]
                    if which == "q":
                        bias_ap = sb["bq8"][64 * hl:64 * hl + 64, p:p + 1]
                    else:
                        bias_ap = sb["bqkv"][64 * hl:64 * hl + 64, 8 + p:9 + p]
                    k.on("scalar", lambda e, dstt=dstt, ch=ch, bank=bank, hl=hl, bias_ap=bias_ap, scale=scale: e.activation(
                        dstt[0:64, CH * ch:CH * (ch + 1)], pb[bank][64 * hl:64 * hl + 64, :],
                        AF.Identity, bias=bias_ap, scale=scale),
                        wait=[("pe", pev), ("ve", k.v("ve"))] if hl == 0 else [], inc=("ac", 1))
        for tt in range(32):
            bank = 6 + tt % 2
            w = [("d_w", ld_w), ("d_xn", ld_xn), ("ac", k.v("ac"))]
            for dm in range(DT):
                k.on("tensor", lambda e, tt=tt, dm=dm, bank=bank: e.matmul(
                    pb[bank][:, 0:128], xn[:, T * dm + 128 * tt: T * dm + 128 * (tt + 1)],
                    sb["wv_sb"][:, 128 * dm:128 * dm + 128],
                    start=(dm == 0), stop=False),
                    wait=w if dm == 0 else [])
            pev = k.on("tensor", lambda e, tt=tt, bank=bank: e.matmul(
                pb[bank][:, 0:128], sb["onesrow"][0:1, :], sb["bv_row"][0:1, :],
                start=False, stop=True), inc=("pe", 1))
            for hl in range(2):
                k.on("scalar", lambda e, tt=tt, bank=bank, hl=hl: e.copy(
                    sb["vp"][:, 256 * tt + 128 * hl:256 * tt + 128 * hl + 64],
                    pb[bank][:, 64 * hl:64 * hl + 64]),
                    wait=[("pe", pev)] if hl == 0 else [], inc=("ac", 1))
        setup_done = [None, None]
        strip_ld = [0, 0]
        for hl in range(2):
            h = 2 * p + hl
            qa, ka = sb["qa%d" % hl], sb["ka%d" % hl]
            stripb = sb["strip%d" % hl]
            strip_ld[hl] = k.dma("gpsimd", stripb[:], tens["strips"][h], "d_sp")
            k.on("vector", lambda e, stripb=stripb: e.reduce_max(sb["scol"][:], stripb[:], axis=AX.X),
                 wait=[("d_sp", strip_ld[hl]), ("ve", k.v("ve"))], inc=("ve", 1))
            k.dma("gpsimd", sb["srow"][0:1, :], sb["scol"][:, 0:1], "d_sr",
                  wait=[("ve", k.v("ve"))], allow_slow_non_contiguous=True)
            k.on("vector", lambda e: e.reduce_max(sb["bmax"][0:1, 0:1], sb["srow"][0:1, :], axis=AX.X),
                 wait=[("d_sr", k.v("d_sr"))], inc=("ve", 1))
            wsq = [("pe", k.v("pe")), ("ve", k.v("ve"))]
            k.on("scalar", lambda e, qa=qa: e.activation(sb["sq_sb"][0:64, :], qa[0:64, :], AF.Square),
                 wait=wsq, inc=("ac", 1))
            k.on("scalar", lambda e, ka=ka: e.activation(sb["sq_sb"][64:128, :], ka[0:64, :], AF.Square),
                 inc=("ac", 1))
            ac_sq = k.v("ac")
            for ch in range(NCH):
                row = (ch % 3) * 32
                bq, bk = 5 + ch // 3, 2 + ch // 3
                k.on("tensor", lambda e, ch=ch, row=row, bq=bq: e.matmul(
                    pb[bq][row:row + 1, :], sb["onesb"][0:64, :], sb["sq_sb"][0:64, CH * ch:CH * (ch + 1)],
                    start=True, stop=True),
                    wait=[("ac", ac_sq), ("ve", k.v("ve"))] if ch == 0 else [])
                k.on("tensor", lambda e, ch=ch, row=row, bk=bk: e.matmul(
                    pb[bk][row:row + 1, :], sb["onesb"][64:128, :], sb["sq_sb"][64:128, CH * ch:CH * (ch + 1)],
                    start=True, stop=True),
                    inc=("pe", 1) if ch == NCH - 1 else None)
            pev = k.v("pe")
            for ch in range(NCH):
                row = (ch % 3) * 32
                bq, bk = 5 + ch // 3, 2 + ch // 3
                k.on("scalar", lambda e, ch=ch, row=row, bq=bq: e.copy(
                    sb["rows_sb"][0:1, CH * ch:CH * (ch + 1)], pb[bq][row:row + 1, :]),
                    wait=[("pe", pev), ("ve", k.v("ve"))] if ch == 0 else [], inc=("ac", 1))
                k.on("scalar", lambda e, ch=ch, row=row, bk=bk: e.copy(
                    sb["rows_sb"][32:33, CH * ch:CH * (ch + 1)], pb[bk][row:row + 1, :]),
                    inc=("ac", 1))
            ac_n = k.v("ac")
            k.on("scalar", lambda e: e.activation(sb["rows_sb"][0:1, :], sb["rows_sb"][0:1, :], AF.Sqrt),
                 wait=[("ac", ac_n)], inc=("ac", 1))
            k.on("vector", lambda e: e.reduce_max(sb["kmax"][0:1, 0:1], sb["rows_sb"][32:33, :], axis=AX.X),
                 wait=[("ac", ac_n)], inc=("ve", 1))
            k.on("scalar", lambda e: e.activation(sb["kmax"][0:1, 0:1], sb["kmax"][0:1, 0:1], AF.Sqrt),
                 wait=[("ve", k.v("ve"))], inc=("ac", 1))
            k.on("vector", lambda e: e.tensor_scalar(
                sb["rows_sb"][64:65, :], sb["rows_sb"][0:1, :], sb["kmax"][0:1, 0:1], sb["bmax"][0:1, 0:1],
                ALU.mult, ALU.add), wait=[("ac", k.v("ac")), ("ve", k.v("ve"))], inc=("ve", 1))
            k.on("vector", lambda e, qa=qa: e.tensor_scalar(
                qa[64:65, :], sb["rows_sb"][64:65, :], -1.0, None, ALU.mult),
                wait=[("ve", k.v("ve"))], inc=("ve", 1))
            setup_done[hl] = k.v("ve")
        setup_ac = k.v("ac")
        prev_tail = None
        prev_esb = None
        prev_evac = None
        for hl in range(2):
            h = 2 * p + hl
            qa, ka = sb["qa%d" % hl], sb["ka%d" % hl]
            stripb = sb["strip%d" % hl]
            exp_cnt, dve_cnt, ev_cnt, bankfree = {}, {}, {}, {}
            evac_ve, tt_done, ot_st = {}, {}, {}
            pend = []
            recip_done = {}
            pdv_todo = []

            def emit_ev(qc, jt, hl=hl):
                wev = [("ac", exp_cnt[(qc, jt)])]
                if jt == 0:
                    if (qc - 2) in evac_ve:
                        wev.append(("ve", evac_ve[qc - 2]))
                    elif prev_evac is not None and qc < 2:
                        wev.append(("ve", prev_evac[6 + qc]))
                ev_cnt[(qc, jt)] = k.on("tensor", lambda e, jt=jt, qc=qc, hl=hl: e.matmul(
                    pb[2 + qc % 2][:], sb["vp"][:, 256 * jt + 128 * hl:256 * jt + 128 * (hl + 1)],
                    sb["e_sb"][:, CH * (jt % 4):CH * (jt % 4 + 1)],
                    start=(jt == 0), stop=(jt == 31)),
                    wait=wev, inc=("pe", 1))
                if jt == 31:
                    k.on("vector", lambda e, qc=qc: e.tensor_copy(
                        sb["den"][0:1, :], pb[2 + qc % 2][64:65, :]),
                        wait=[("pe", ev_cnt[(qc, 31)])], inc=("ve", 1))
                    k.on("vector", lambda e, qc=qc: e.tensor_copy(
                        sb["onum"][0:64, :], pb[2 + qc % 2][0:64, :]),
                        inc=("ve", 1))
                    evac_ve[qc] = k.v("ve")
                    k.on("vector", lambda e: e.reciprocal(sb["den"][0:1, :], sb["den"][0:1, :]),
                         inc=("ve", 1))
                    recip_done[qc] = k.v("ve")
                    pdv_todo.append(qc)

            def emit_pdv(hl=hl):
                qc = pdv_todo.pop(0)
                db = 4 + qc % 2
                wd_p = [("ve", recip_done[qc])]
                if hl == 0 and qc < 2:
                    wd_p.append(("ac", setup_ac))
                pdv = k.on("tensor", lambda e, db=db: e.matmul(
                    pb[db][0:64, :], sb["ones1f"][0:1, 0:64], sb["den"][0:1, :],
                    start=True, stop=True), wait=wd_p, inc=("pe", 1))
                wtt = [("pe", pdv)]
                if (qc - 2) in ot_st:
                    wtt.append(("d_ot%d" % (qc % 2), ot_st[qc - 2]))
                tt_done[qc] = k.on("vector", lambda e, qc=qc, db=db: e.tensor_tensor(
                    sb["ot_tmp"][0:64, CH * (qc % 2):CH * (qc % 2 + 1)],
                    sb["onum"][0:64, :], pb[db][0:64, :], ALU.mult),
                    wait=wtt, inc=("ve", 1))
                ot_st[qc] = k.dma("gpsimd", tens["otT"][64 * h:64 * h + 64, CH * qc:CH * (qc + 1)],
                                  sb["ot_tmp"][0:64, CH * (qc % 2):CH * (qc % 2 + 1)], "d_ot%d" % (qc % 2),
                                  wait=[("ve", tt_done[qc])])

            for qc in range(NCH):
                for jt in range(32):
                    sbk = jt % 2
                    m0 = 512 * qc - 128 * jt + 1536
                    inband = (0 <= m0 <= 3072 - 512)
                    if jt == 4 and pdv_todo:
                        emit_pdv()
                    key2 = None
                    if jt >= 2:
                        key2 = (qc, jt - 2)
                    elif qc >= 1:
                        key2 = (qc - 1, 30 + jt)
                    w = []
                    if key2 is not None:
                        w.append(bankfree[key2])
                    elif prev_tail is not None:
                        w.append(prev_tail[jt])
                    if qc == 0 and jt == 0:
                        w.append(("ve", setup_done[hl]))
                    k.on("tensor", lambda e, qa=qa, ka=ka, jt=jt, qc=qc, sbk=sbk: e.matmul(
                        pb[sbk][:], ka[:, 128 * jt:128 * (jt + 1)], qa[:, CH * qc:CH * (qc + 1)],
                        start=True, stop=True), wait=w, inc=("pe", 1))
                    pev = k.v("pe")
                    eo = sb["e_sb"][:, CH * (jt % 4):CH * (jt % 4 + 1)]
                    key4 = None
                    if jt >= 4:
                        key4 = (qc, jt - 4)
                    elif qc >= 1:
                        key4 = (qc - 1, 28 + jt)
                    if inband:
                        ep = sb["e_pre"][:, CH * (jt % 4):CH * (jt % 4 + 1)]
                        wv = [("pe", pev)]
                        if key4 in exp_cnt:
                            wv.append(("ac", exp_cnt[key4]))
                        elif prev_esb is not None and key4 is None:
                            wv.append(("ac", prev_esb[28 + jt]))
                        dve_cnt[(qc, jt)] = k.on("vector", lambda e, ep=ep, sbk=sbk, stripb=stripb, m0=m0: e.tensor_tensor(
                            ep, pb[sbk][:], stripb[:, m0:m0 + 512], ALU.add),
                            wait=wv, inc=("ve", 1))
                        bankfree[(qc, jt)] = ("ve", dve_cnt[(qc, jt)])
                        exp_cnt[(qc, jt)] = k.on("scalar", lambda e, eo=eo, ep=ep: e.activation(
                            eo, ep, AF.Exp),
                            wait=[("ve", dve_cnt[(qc, jt)])], inc=("ac", 1))
                    else:
                        bcol = sb["rbhi" if m0 > 0 else "rblo"][:, h:h + 1]
                        exp_cnt[(qc, jt)] = k.on("scalar", lambda e, eo=eo, sbk=sbk, bcol=bcol: e.activation(
                            eo, pb[sbk][:], AF.Exp, bias=bcol),
                            wait=[("pe", pev)], inc=("ac", 1))
                        bankfree[(qc, jt)] = ("ac", exp_cnt[(qc, jt)])
                    if len(pend) == 3:
                        emit_ev(*pend.pop(0))
                    pend.append((qc, jt))
            while pend:
                emit_ev(*pend.pop(0))
            while pdv_todo:
                emit_pdv()
            prev_tail = {jt: bankfree[(7, 30 + jt)] for jt in range(2)}
            prev_esb = {28 + jt: exp_cnt[(7, 28 + jt)] for jt in range(4)}
            prev_evac = evac_ve
    k.barrier()


def st_l0_outproj(k, tens, sb, x_src, dst):
    k.dma("sync", sb["bout"][:], tens["b_out_t"], "d_c")
    ld_bo = k.v("d_c")
    k.on("vector", lambda e: e.memset(sb["onesrow"][:], 1.0),
         wait=[("d_c", ld_bo)], inc=("ve", 1))
    k.barrier()
    _out_proj(k, tens, sb, "w_out", sb["bout"], x_src, dst, "otT")


L0A_SB = [
    ("xn_sb", (128, 8 * T), BF16),
    ("bqkv", (128, 24), F32), ("bq8", (128, 8), F32),
    ("rblo", (128, 16), F32), ("rbhi", (128, 16), F32),
    ("onesrow", (1, 128), BF16),
    ("wq_sb", (128, 1024), BF16), ("wk_sb", (128, 1024), BF16), ("wv_sb", (128, 1024), BF16),
    ("bv_row", (1, 128), BF16),
    ("qa0", (65, T), BF16), ("qa1", (65, T), BF16),
    ("ka0", (65, T), BF16), ("ka1", (65, T), BF16),
    ("vp", (128, 32 * 256), BF16),
    ("strip0", (128, 3072), BF16), ("strip1", (128, 3072), BF16),
    ("scol", (128, 1), F32), ("srow", (1, 128), F32),
    ("bmax", (1, 1), F32), ("kmax", (1, 1), F32),
    ("sq_sb", (128, T), BF16),
    ("rows_sb", (65, T), F32),
    ("e_sb", (128, 4 * CH), BF16), ("e_pre", (128, 4 * CH), F32),
    ("den", (1, CH), F32), ("onum", (64, CH), F32),
    ("ot_tmp", (64, 2 * CH), BF16),
]

L0P_SB = [
    ("bout", (128, 8), F32), ("onesrow", (1, 128), BF16),
    ("wo_full", (128, 8 * 1024), BF16),
    ("otb", (128, 8 * CH), BF16), ("xcb", (128, 8 * CH), F32), ("rescb", (128, 8 * CH), F32),
]



def st_l0_ffn(k, tens, sb, xn_src, x1p_src, dst):
    pb = k.pb
    CK = 1024
    NC2 = T // CK
    k.dma("gpsimd", tens["moe_w1bf"].rearrange("e a b -> (e a b)"),
          tens["moe_w1"].rearrange("e a b -> (e a b)"), "d_cw")
    k.dma("gpsimd", tens["moe_w2bf"].rearrange("e a b -> (e a b)"),
          tens["moe_w2"].rearrange("e a b -> (e a b)"), "d_cw")
    k.dma("sync", sb["wg_sb"][:], tens["w_gate"].rearrange("(a p) c -> p a c", p=128), "d_c2")
    k.dma("sync", sb["bgate"][0:1, :], tens["b_gate"], "d_c2")
    k.dma("sync", sb["b1c"][:], tens["moe_b1c"], "d_c2")
    k.dma("sync", sb["b2sb"][:], tens["moe_b2"], "d_c2")
    k.dma("sync", sb["id_f32"][:], tens["ident"], "d_c2")
    k.dma("gpsimd", sb["id_bf"][:], tens["ident"], "d_cg")
    k.dma("sync", sb["emask"][:], tens["emask8"], "d_c2")
    ld_c = k.v("d_c2")
    _ = k.v("d_c3")
    k.on("vector", lambda e: e.memset(sb["onesrow"][:], 1.0), inc=("ve", 1))
    k.barrier()
    cw = k.v("d_cw")

    for c in range(NC2):
        k.barrier()
        col0 = CK * c
        ldxn = k.dma("sync", sb["xnb"][:],
                     tens[xn_src][:, col0:col0 + CK].rearrange("(a p) c -> p a c", p=128), "d_xnb")
        ldxnf = k.dma("sync", sb["xnfb"][:],
                      tens["xn_f32"][:, col0:col0 + CK].rearrange("(a p) c -> p a c", p=128), "d_xnf")
        ldx1 = k.dma("sync", sb["x1b"][:],
                     tens[x1p_src][:, col0:col0 + CK].rearrange("(a p) c -> p a c", p=128), "d_x1b")
        tr_ac = 0
        for tl in range(8):
            bank = 6 + tl % 2
            w = [("d_c2", ld_c), ("d_xnf", ldxnf), ("ac", k.v("ac"))]
            for dm in range(DT):
                k.on("tensor", lambda e, tl=tl, dm=dm, bank=bank: e.matmul(
                    pb[bank][:, 0:8], sb["xnfb"][:, CK * dm + 128 * tl:CK * dm + 128 * (tl + 1)],
                    sb["wg_sb"][:, 8 * dm:8 * dm + 8],
                    start=(dm == 0), stop=False), wait=w if dm == 0 else [])
            pev = k.on("tensor", lambda e, bank=bank: e.matmul(
                pb[bank][:, 0:8], sb["ones1f"][0:1, :], sb["bgate"][0:1, :],
                start=False, stop=True), inc=("pe", 1))
            k.on("scalar", lambda e, tl=tl, bank=bank: e.copy(
                sb["scr"][:, 8 * tl:8 * tl + 8], pb[bank][:, 0:8]),
                wait=[("pe", pev)], inc=("ac", 1))
            ac_s = k.v("ac")
            s8 = sb["scr"][:, 8 * tl:8 * tl + 8]
            k.on("vector", lambda e, s8=s8: e.reduce_max(sb["gt_m1"][:], s8, axis=AX.X),
                 wait=[("ac", ac_s), ("ve", k.v("ve"))], inc=("ve", 1))
            k.on("vector", lambda e, s8=s8: e.tensor_scalar(
                sb["keep"][:], s8, sb["gt_m1"][:], None, ALU.is_equal),
                wait=[("ve", k.v("ve"))], inc=("ve", 1))
            k.on("vector", lambda e: e.reduce_sum(sb["gt_cnt"][:], sb["keep"][:], axis=AX.X),
                 wait=[("ve", k.v("ve"))], inc=("ve", 1))
            k.on("vector", lambda e, s8=s8: e.scalar_tensor_tensor(
                sb["tmp8"][:], sb["keep"][:], -3.0e38, s8, ALU.mult, ALU.add),
                wait=[("ve", k.v("ve"))], inc=("ve", 1))
            k.on("vector", lambda e: e.reduce_max(sb["gt_m2"][:], sb["tmp8"][:], axis=AX.X),
                 wait=[("ve", k.v("ve"))], inc=("ve", 1))
            k.on("vector", lambda e: e.tensor_scalar(
                sb["gt_c2"][:], sb["gt_cnt"][:], 2.0, None, ALU.is_ge),
                wait=[("ve", k.v("ve"))], inc=("ve", 1))
            k.on("vector", lambda e: e.tensor_tensor(
                sb["gt_d"][:], sb["gt_m1"][:], sb["gt_m2"][:], ALU.subtract),
                wait=[("ve", k.v("ve"))], inc=("ve", 1))
            k.on("vector", lambda e: e.tensor_tensor(
                sb["gt_d"][:], sb["gt_d"][:], sb["gt_c2"][:], ALU.mult),
                wait=[("ve", k.v("ve"))], inc=("ve", 1))
            k.on("vector", lambda e: e.tensor_tensor(
                sb["gt_thr"][:], sb["gt_m2"][:], sb["gt_d"][:], ALU.add),
                wait=[("ve", k.v("ve"))], inc=("ve", 1))
            k.on("vector", lambda e, s8=s8: e.tensor_scalar(
                sb["keep"][:], s8, sb["gt_thr"][:], None, ALU.is_ge),
                wait=[("ve", k.v("ve"))], inc=("ve", 1))
            vv = k.on("vector", lambda e, s8=s8: e.tensor_tensor(
                sb["wbf"][:], s8, sb["keep"][:], ALU.mult),
                wait=[("ve", k.v("ve"))], inc=("ve", 1))
            ptr = k.on("tensor", lambda e: e.transpose(
                pb[5][0:8, 0:128], sb["wbf"][:], sb["id_f32"][:]),
                wait=[("ve", vv), ("ac", tr_ac), ("d_c2", ld_c)], inc=("pe", 1))
            tr_ac = k.on("scalar", lambda e, tl=tl: e.copy(
                sb["wT"][0:8, 128 * tl:128 * (tl + 1)], pb[5][0:8, 0:128]),
                wait=[("pe", ptr)], inc=("ac", 1))
        for ex in range(8):
            wld = [("pe", k.v("pe"))] if ex > 0 else [("d_cw", cw)]
            ldw1 = k.dma("gpsimd", sb["w1sb"][:],
                         tens["moe_w1bf"][ex].rearrange("(a p) c -> p a c", p=128), "d_w1",
                         wait=wld)
            ldw2 = k.dma("gpsimd", sb["w2sb"][:],
                         tens["moe_w2bf"][ex].rearrange("(a p) c -> p a c", p=128), "d_w2",
                         wait=wld)
            for sub in range(2):
                pwb = k.on("tensor", lambda e, ex=ex, sub=sub: e.matmul(
                    pb[4][:, 0:512], sb["emask"][0:8, 128 * ex:128 * (ex + 1)],
                    sb["wT"][0:8, 512 * sub:512 * (sub + 1)],
                    start=True, stop=True),
                    wait=[("ac", k.v("ac")), ("ve", k.v("ve"))], inc=("pe", 1))
                k.on("scalar", lambda e, sub=sub: e.copy(
                    sb["wb_sb"][:, 512 * sub:512 * (sub + 1)], pb[4][:, 0:512]),
                    wait=[("pe", pwb)], inc=("ac", 1))
            ac_wb = k.v("ac")
            for ht in range(DT):
                for sub in range(2):
                    bank = (2 * ht + sub) % 2
                    w = [("d_w1", ldw1), ("d_xnb", ldxn), ("ac", k.v("ac")), ("ve", k.v("ve"))]
                    for dm in range(DT):
                        k.on("tensor", lambda e, ht=ht, sub=sub, dm=dm, bank=bank: e.matmul(
                            pb[bank][:], sb["w1sb"][:, CK * dm + 128 * ht:CK * dm + 128 * (ht + 1)],
                            sb["xnb"][:, CK * dm + 512 * sub:CK * dm + 512 * (sub + 1)],
                            start=(dm == 0), stop=(dm == DT - 1)),
                            wait=w if dm == 0 else [],
                            inc=("pe", 1) if dm == DT - 1 else None)
                    pev = k.v("pe")
                    k.on("scalar", lambda e, ht=ht, sub=sub, bank=bank, ex=ex: e.activation(
                        sb["hsc"][:, CK * ht + 512 * sub:CK * ht + 512 * (sub + 1)],
                        pb[bank][:], AF.Relu, bias=sb["b1c"][:, 8 * ex + ht:8 * ex + ht + 1]),
                        wait=[("pe", pev), ("ac", ac_wb)], inc=("ac", 1))
                    k.on("vector", lambda e, ht=ht, sub=sub: e.tensor_tensor(
                        sb["hsc"][:, CK * ht + 512 * sub:CK * ht + 512 * (sub + 1)],
                        sb["hsc"][:, CK * ht + 512 * sub:CK * ht + 512 * (sub + 1)],
                        sb["wb_sb"][:, 512 * sub:512 * (sub + 1)], ALU.mult),
                        wait=[("ac", k.v("ac")), ("ve", k.v("ve"))], inc=("ve", 1))
            ve_h = k.v("ve")
            for do in range(DT):
                for sub in range(2):
                    bank = 2 + (2 * do + sub) % 2
                    w = [("d_w2", ldw2), ("ve", k.v("ve"))]
                    first = True
                    if ex == 0:
                        k.on("tensor", lambda e, do=do, sub=sub, bank=bank: e.matmul(
                            pb[bank][:], sb["b2sb"][0:8, 128 * do:128 * (do + 1)],
                            sb["wT"][0:8, 512 * sub:512 * (sub + 1)],
                            start=True, stop=False), wait=w)
                        first = False
                    for ht in range(DT):
                        k.on("tensor", lambda e, do=do, sub=sub, ht=ht, bank=bank, first=first: e.matmul(
                            pb[bank][:], sb["w2sb"][:, CK * ht + 128 * do:CK * ht + 128 * (do + 1)],
                            sb["hsc"][:, CK * ht + 512 * sub:CK * ht + 512 * (sub + 1)],
                            start=(first and ht == 0), stop=(ht == DT - 1)),
                            wait=w if (first and ht == 0) else [],
                            inc=("pe", 1) if ht == DT - 1 else None)
                        first = first and False
                    pev = k.v("pe")
                    aslc = sb["acc"][:, CK * do + 512 * sub:CK * do + 512 * (sub + 1)]
                    if ex == 0:
                        k.on("vector", lambda e, do=do, sub=sub, bank=bank, aslc=aslc: e.tensor_tensor(
                            aslc, pb[bank][:],
                            sb["x1b"][:, CK * do + 512 * sub:CK * do + 512 * (sub + 1)], ALU.add),
                            wait=[("pe", pev), ("d_x1b", ldx1), ("ve", k.v("ve"))], inc=("ve", 1))
                    else:
                        k.on("vector", lambda e, bank=bank, aslc=aslc: e.tensor_tensor(
                            aslc, pb[bank][:], aslc, ALU.add),
                            wait=[("pe", pev), ("ve", k.v("ve"))], inc=("ve", 1))
        k.dma("gpsimd", tens[dst][:, col0:col0 + CK].rearrange("(a p) c -> p a c", p=128),
              sb["acc"][:], "d_st1", wait=[("ve", k.v("ve"))])
    k.barrier()


MOE_SB = [
    ("xnb", (128, 8 * 1024), BF16), ("hsc", (128, 8 * 1024), BF16),
    ("acc", (128, 8 * 1024), F32), ("x1b", (128, 8 * 1024), F32),
    ("w1sb", (128, 8 * 1024), BF16), ("w2sb", (128, 8 * 1024), BF16),
    ("wg_sb", (128, 64), F32), ("bgate", (1, 8), F32),
    ("b1c", (128, 64), F32), ("b2sb", (8, 1024), F32),
    ("xnfb", (128, 8 * 1024), F32), ("id_f32", (128, 128), F32),
    ("scr", (128, 64), F32), ("keep", (128, 8), F32), ("tmp8", (128, 8), F32),
    ("gt_m1", (128, 1), F32), ("gt_m2", (128, 1), F32), ("gt_cnt", (128, 1), F32),
    ("gt_c2", (128, 1), F32), ("gt_d", (128, 1), F32), ("gt_thr", (128, 1), F32),
    ("wbf", (128, 8), F32), ("wT", (8, 1024), F32), ("wb_sb", (128, 1024), F32),
    ("id_bf", (128, 128), BF16), ("onesrow", (1, 128), BF16),
    ("emask", (8, 1024), F32),
]


def build_l0ffn_launch(kinds):
    nc = bass.Bass("TRN2", target_bir_lowering=False, debug=False)
    decls = [
        ("xn_bf", (D, T), BF16, "ExternalInput"),
        ("xn_f32", (D, T), F32, "ExternalInput"),
        ("x1p", (D, T), F32, "ExternalInput"),
        ("w_gate", (D, 8), F32, "ExternalInput"),
        ("b_gate", (1, 8), F32, "ExternalInput"),
        ("moe_w1", (8, D, D), F32, "ExternalInput"),
        ("moe_w2", (8, D, D), F32, "ExternalInput"),
        ("moe_b1c", (128, 64), F32, "ExternalInput"),
        ("moe_b2", (8, D), F32, "ExternalInput"),
        ("ident", (128, 128), F32, "ExternalInput"),
        ("emask8", (8, 1024), F32, "ExternalInput"),
        ("x1", (D, T), F32, "ExternalOutput"),
        ("moe_w1bf", (8, D, D), BF16, "Internal"),
        ("moe_w2bf", (8, D, D), BF16, "Internal"),
    ]
    tens = {}
    for name, shape, dt, kind in decls:
        tens[name] = nc.dram_tensor(name, list(shape), dt, kind=kind).ap()
    ctx = ExitStack()
    with ctx:
        k, sb = _mk_k(nc, ctx, MOE_SB)
        st_l0_ffn(k, tens, sb, "xn_bf", "x1p", "x1")
        _finish(nc, k)
    return nc




def _out_proj(k, tens, sb, w_dram, bcol, x_src, dst, ot_name):
    pb = k.pb
    wsrc = tens[w_dram].rearrange("(a p) (b c) -> p a b c", p=128, c=128)
    ldwo = k.dma("gpsimd", sb["wo_full"][:], wsrc, "d_wf")
    mm_done = [0] * NCH
    st_done = [0] * NCH
    for ch in range(NCH):
        w = [("pe", mm_done[ch - 1])] if ch >= 1 else []
        ldo = k.dma("sync", sb["otb"][:],
                    tens[ot_name][:, CH * ch:CH * (ch + 1)].rearrange("(a p) c -> p a c", p=128),
                    "d_ob", wait=w)
        wx = [("ve", k.v("ve"))] if ch >= 1 else []
        ldx = k.dma("sync", sb["xcb"][:],
                    tens[x_src][:, CH * ch:CH * (ch + 1)].rearrange("(a p) c -> p a c", p=128),
                    "d_xc", wait=wx)
        for dot in range(DT):
            bank = 6 + dot % 2
            w = [("d_wf", ldwo), ("d_ob", ldo), ("ve", k.v("ve"))]
            for dm in range(DT):
                k.on("tensor", lambda e, ch=ch, dot=dot, dm=dm, bank=bank: e.matmul(
                    pb[bank][:], sb["wo_full"][:, 1024 * dm + 128 * dot:1024 * dm + 128 * (dot + 1)],
                    sb["otb"][:, CH * dm:CH * (dm + 1)],
                    start=(dm == 0), stop=(dm == DT - 1)), wait=w if dm == 0 else [],
                    inc=("pe", 1) if dm == DT - 1 else None)
            pev = k.v("pe")
            wst = [("pe", pev), ("d_xc", ldx)]
            if dot == 0 and ch >= 1:
                wst.append(("d_rs", st_done[ch - 1]))
            k.on("vector", lambda e, dot=dot, bank=bank, bcol=bcol: e.scalar_tensor_tensor(
                sb["rescb"][:, CH * dot:CH * (dot + 1)],
                pb[bank][:], bcol[:, dot:dot + 1],
                sb["xcb"][:, CH * dot:CH * (dot + 1)],
                ALU.add, ALU.add), wait=wst, inc=("ve", 1))
        mm_done[ch] = k.v("pe")
        st_done[ch] = k.dma("gpsimd",
                            tens[dst][:, CH * ch:CH * (ch + 1)].rearrange("(a p) c -> p a c", p=128),
                            sb["rescb"][:], "d_rs", wait=[("ve", k.v("ve"))])
    k.barrier()


def st_l1_attn(k, tens, sb, xn_src, x_src, dst):
    pb = k.pb
    for nm, key in (("pbq", "p_bq_t"), ("pbk", "p_bk_t"), ("pbv", "p_bv_t"), ("pbo", "p_bo_t")):
        k.dma("sync", sb[nm][:], tens[key], "d_c")
    k.dma("gpsimd", sb["rm_sb"][:], tens["p_rm"], "d_cg")
    ld_c, ld_cg = k.v("d_c"), k.v("d_cg")
    k.on("vector", lambda e: e.memset(sb["onesrow"][:], 1.0), inc=("ve", 1))
    for tt in range(32):
        k.on("vector", lambda e, tt=tt: e.memset(sb["vp"][:, 130 * tt + 64:130 * tt + 65], 1.0))
        k.on("vector", lambda e, tt=tt: e.memset(sb["vp"][:, 130 * tt + 129:130 * tt + 130], 1.0),
             inc=("ve", 1) if tt == 31 else None)
    for d in range(DT):
        k.dma("sync", sb["xn2"][:, d * T:(d + 1) * T],
              tens[xn_src][d * 128:(d + 1) * 128, :], "d_xn2")
    ldxn = k.v("d_xn2")
    k.barrier()
    for p in range(8):
        k.barrier()
        for nm, c0, wt in (("wq_sb", 128 * p, "p_Wq"), ("wk_sb", 128 * p, "p_Wk"), ("wv_sb", 128 * p, "p_Wv")):
            srcw = tens[wt][:, c0:c0 + 128].rearrange("(a p) c -> p a c", p=128)
            k.dma("gpsimd", sb[nm][:], srcw, "d_w")
        k.dma("gpsimd", sb["bv_row"][0:1, :], tens["p_bv_raw"][0:1, 128 * p:128 * (p + 1)], "d_w")
        ld_w = k.v("d_w")
        for which, wsb, dstt, bsel in (("q", "wq_sb", "qt", "pbq"), ("kk", "wk_sb", "kt", "pbk")):
            for ch in range(NCH):
                bank = 6 + ch % 2
                w = [("d_w", ld_w), ("d_xn2", ldxn), ("ac", k.v("ac"))]
                for dm in range(DT):
                    k.on("tensor", lambda e, ch=ch, dm=dm, bank=bank, wsb=wsb: e.matmul(
                        pb[bank][:], sb[wsb][:, 128 * dm:128 * dm + 128],
                        sb["xn2"][:, T * dm + CH * ch: T * dm + CH * (ch + 1)],
                        start=(dm == 0), stop=(dm == DT - 1)),
                        wait=w if dm == 0 else [],
                        inc=("pe", 1) if dm == DT - 1 else None)
                pev = k.v("pe")
                for hl in range(2):
                    k.on("scalar", lambda e, dstt=dstt, ch=ch, bank=bank, hl=hl, bsel=bsel: e.activation(
                        sb[dstt][64 * hl:64 * hl + 64, CH * ch:CH * (ch + 1)],
                        pb[bank][64 * hl:64 * hl + 64, :],
                        AF.Identity, bias=sb[bsel][64 * hl:64 * hl + 64, p:p + 1], scale=1.0),
                        wait=[("pe", pev), ("d_c", ld_c)] if hl == 0 else [], inc=("ac", 1))
        for tt in range(32):
            bank = 6 + tt % 2
            w = [("d_w", ld_w), ("d_xn2", ldxn), ("ac", k.v("ac"))]
            for dm in range(DT):
                k.on("tensor", lambda e, tt=tt, dm=dm, bank=bank: e.matmul(
                    pb[bank][:, 0:128], sb["xn2"][:, T * dm + 128 * tt: T * dm + 128 * (tt + 1)],
                    sb["wv_sb"][:, 128 * dm:128 * dm + 128],
                    start=(dm == 0), stop=False),
                    wait=w if dm == 0 else [])
            pev = k.on("tensor", lambda e, tt=tt, bank=bank, p=p: e.matmul(
                pb[bank][:, 0:128], sb["onesrow"][0:1, :], sb["bv_row"][0:1, :],
                start=False, stop=True), inc=("pe", 1))
            for hl in range(2):
                k.on("scalar", lambda e, tt=tt, bank=bank, hl=hl: e.copy(
                    sb["vp"][:, 130 * tt + 65 * hl:130 * tt + 65 * hl + 64],
                    pb[bank][:, 64 * hl:64 * hl + 64]),
                    wait=[("pe", pev)] if hl == 0 else [], inc=("ac", 1))
        for hl in range(2):
            h = 2 * p + hl
            k.barrier()
            for mt in range(2):
                for ch in range(NCH):
                    bank = (2 * mt + ch) % 2
                    k.on("tensor", lambda e, mt=mt, ch=ch, bank=bank, hl=hl: e.matmul(
                        pb[bank][:], sb["rm_sb"][64 * hl:64 * hl + 64, 128 * mt:128 * (mt + 1)],
                        sb["qt"][64 * hl:64 * hl + 64, CH * ch:CH * (ch + 1)],
                        start=True, stop=True),
                        wait=[("ve", k.v("ve")), ("ac", k.v("ac")), ("d_cg", ld_cg)],
                        inc=("pe", 1))
                    pev = k.v("pe")
                    dslc = sb["qp"][:, T * mt + CH * ch:T * mt + CH * (ch + 1)]
                    _elu16(k, sb, pb, bank, pev, dslc, CH)
            for tt in range(32):
                bank = tt % 2
                k.on("tensor", lambda e, tt=tt, bank=bank, hl=hl: e.matmul(
                    pb[bank][:, 0:256], sb["kt"][64 * hl:64 * hl + 64, 128 * tt:128 * (tt + 1)],
                    sb["rm_sb"][64 * hl:64 * hl + 64, :],
                    start=True, stop=True),
                    wait=[("ve", k.v("ve")), ("ac", k.v("ac"))], inc=("pe", 1))
                pev = k.v("pe")
                dslc = sb["kp"][:, 256 * tt:256 * (tt + 1)]
                _elu16(k, sb, pb, bank, pev, dslc, 256)
            for mt in range(2):
                bank = 2 + mt
                for tt in range(32):
                    k.on("tensor", lambda e, mt=mt, tt=tt, bank=bank, hl=hl: e.matmul(
                        pb[bank][:, 0:65], sb["kp"][:, 256 * tt + 128 * mt:256 * tt + 128 * (mt + 1)],
                        sb["vp"][:, 130 * tt + 65 * hl:130 * tt + 65 * hl + 65],
                        start=(tt == 0), stop=(tt == 31)),
                        wait=[("ve", k.v("ve")), ("ac", k.v("ac"))] if tt == 0 else [],
                        inc=("pe", 1) if tt == 31 else None)
                pev = k.v("pe")
                k.on("scalar", lambda e, mt=mt, bank=bank: e.copy(
                    sb["kv_sb"][:, 65 * mt:65 * (mt + 1)], pb[bank][:, 0:65]),
                    wait=[("pe", pev)], inc=("ac", 1))
            ac_kv = k.v("ac")
            ot_st = {}
            tt_done = {}
            for ch in range(NCH):
                ob = 2 + ch % 2
                for mt in range(2):
                    k.on("tensor", lambda e, ch=ch, mt=mt, ob=ob: e.matmul(
                        pb[ob][0:65, :], sb["kv_sb"][:, 65 * mt:65 * (mt + 1)],
                        sb["qp"][:, T * mt + CH * ch:T * mt + CH * (ch + 1)],
                        start=(mt == 0), stop=(mt == 1)),
                        wait=[("ac", ac_kv), ("ve", k.v("ve"))] if mt == 0 else [],
                        inc=("pe", 1) if mt == 1 else None)
                pev = k.v("pe")
                wd0 = [("pe", pev)]
                if (ch - 1) in tt_done:
                    wd0.append(("ve", tt_done[ch - 1]))
                k.on("scalar", lambda e, ob=ob: e.copy(sb["den"][0:1, :], pb[ob][64:65, :]),
                     wait=wd0, inc=("ac", 1))
                k.on("scalar", lambda e, ob=ob: e.copy(sb["onum"][0:64, :], pb[ob][0:64, :]),
                     inc=("ac", 1))
                ac_ev = k.v("ac")
                k.on("vector", lambda e: e.tensor_scalar(sb["den"][0:1, :], sb["den"][0:1, :],
                                                         1e-6, None, ALU.max),
                     wait=[("ac", ac_ev), ("ve", k.v("ve"))], inc=("ve", 1))
                k.on("vector", lambda e: e.reciprocal(sb["den"][0:1, :], sb["den"][0:1, :]),
                     wait=[("ve", k.v("ve"))], inc=("ve", 1))
                wd = [("ve", k.v("ve"))]
                if (ch - 2) in tt_done:
                    wd.append(("ve", tt_done[ch - 2]))
                pdv = k.on("tensor", lambda e, ch=ch: e.matmul(
                    pb[4 + ch % 2][0:64, :], sb["ones1f"][0:1, 0:64], sb["den"][0:1, :],
                    start=True, stop=True), wait=wd, inc=("pe", 1))
                wtt = [("pe", pdv), ("ac", ac_ev)]
                if (ch - 2) in tt_done:
                    wtt.append(("d_ot%d" % (ch % 2), ot_st[ch - 2]))
                tt_done[ch] = k.on("vector", lambda e, ch=ch: e.tensor_tensor(
                    sb["ot_tmp"][0:64, CH * (ch % 2):CH * (ch % 2 + 1)],
                    sb["onum"][0:64, :], pb[4 + ch % 2][0:64, :], ALU.mult),
                    wait=wtt, inc=("ve", 1))
                ot_st[ch] = k.dma("gpsimd", tens["otT"][64 * h:64 * h + 64, CH * ch:CH * (ch + 1)],
                                  sb["ot_tmp"][0:64, CH * (ch % 2):CH * (ch % 2 + 1)],
                                  "d_ot%d" % (ch % 2), wait=[("ve", tt_done[ch])])
    k.barrier()
    _out_proj(k, tens, sb, "p_Wo", sb["pbo"], x_src, dst, "otT")


def _elu16(k, sb, pb, bank, pev, dslc, width):
    t1 = sb["el_t1"][:, 0:width]
    r1 = sb["el_r1"][:, 0:width]
    k.on("vector", lambda e: e.tensor_scalar(t1, pb[bank][:, 0:width], 0.0, None, ALU.min),
         wait=[("pe", pev), ("ve", k.v("ve")), ("ac", k.v("ac"))], inc=("ve", 1))
    k.on("vector", lambda e: e.tensor_scalar(r1, pb[bank][:, 0:width], 0.0, 0.0625,
                                             ALU.max, ALU.mult),
         wait=[("ve", k.v("ve"))], inc=("ve", 1))
    k.on("scalar", lambda e: e.activation(t1, t1, AF.Exp),
         wait=[("ve", k.v("ve"))], inc=("ac", 1))
    k.on("vector", lambda e: e.scalar_tensor_tensor(dslc, t1, 0.0625, r1, ALU.mult, ALU.add),
         wait=[("ac", k.v("ac")), ("ve", k.v("ve"))], inc=("ve", 1))


L1A_SB = [
    ("xn2", (128, 8 * T), BF16),
    ("pbq", (128, 8), F32), ("pbk", (128, 8), F32), ("pbv", (128, 8), F32), ("pbo", (128, 8), F32),
    ("rm_sb", (128, 256), BF16),
    ("wq_sb", (128, 1024), BF16), ("wk_sb", (128, 1024), BF16), ("wv_sb", (128, 1024), BF16),
    ("bv_row", (1, 128), BF16),
    ("qt", (128, T), BF16), ("kt", (128, T), BF16),
    ("vp", (128, 32 * 130), BF16),
    ("qp", (128, 2 * T), BF16), ("kp", (128, 32 * 256), BF16),
    ("kv_sb", (128, 130), BF16),
    ("el_t1", (128, CH), F32), ("el_r1", (128, CH), F32),
    ("den", (1, CH), F32), ("onum", (64, CH), F32),
    ("ot_tmp", (64, 2 * CH), BF16), ("onesrow", (1, 128), BF16),
    ("wo_full", (128, 8 * 1024), BF16),
    ("otb", (128, 8 * CH), BF16), ("xcb", (128, 8 * CH), F32), ("rescb", (128, 8 * CH), F32),
]


def build_l1attn_launch(kinds):
    nc = bass.Bass("TRN2", target_bir_lowering=False, debug=False)
    decls = [
        ("xn1_bf", (D, T), BF16, "ExternalInput"),
        ("x1", (D, T), F32, "ExternalInput"),
        ("p_Wq", (D, D), F32, "ExternalInput"), ("p_Wk", (D, D), F32, "ExternalInput"),
        ("p_Wv", (D, D), F32, "ExternalInput"), ("p_Wo", (D, D), F32, "ExternalInput"),
        ("p_bq_t", (128, 8), F32, "ExternalInput"), ("p_bk_t", (128, 8), F32, "ExternalInput"),
        ("p_bv_t", (128, 8), F32, "ExternalInput"), ("p_bo_t", (128, 8), F32, "ExternalInput"),
        ("p_bv_raw", (1, D), F32, "ExternalInput"),
        ("p_rm", (128, 256), F32, "ExternalInput"),
        ("x2p", (D, T), F32, "ExternalOutput"),
        ("otT", (D, T), BF16, "Internal"),
    ]
    tens = {}
    for name, shape, dt, kind in decls:
        tens[name] = nc.dram_tensor(name, list(shape), dt, kind=kind).ap()
    ctx = ExitStack()
    with ctx:
        k, sb = _mk_k(nc, ctx, L1A_SB)
        st_l1_attn(k, tens, sb, "xn1_bf", "x1", "x2p")
        _finish(nc, k)
    return nc




def st_lf_attn2(k, tens, sb, l, yin):
    pb = k.pb
    for nm, key in (("csq", "rot_cq"), ("ssq", "rot_sq"), ("csk", "rot_ck"), ("ssk", "rot_sk")):
        k.dma("gpsimd", sb[nm][0:64, :], tens[key], "d_cg")
    k.dma("gpsimd", sb["mask_sb"][:], tens["la_mask"].rearrange("(a p) c -> p a c", p=128), "d_cg")
    k.dma("gpsimd", sb["maskT"][:], tens["la_maskT"], "d_cg")
    k.dma("sync", sb["id_f32"][:], tens["ident"], "d_c")
    k.dma("sync", sb["qs_col"][0:64, 0:1], tens["lf_qs"][l].unsqueeze(1), "d_c")
    k.dma("sync", sb["ks_col"][0:64, 0:1], tens["lf_ks"][l].unsqueeze(1), "d_c")
    ld_cg, ld_c = k.v("d_cg"), k.v("d_c")
    k.on("vector", lambda e: e.memset(sb["kra0"][64:65, :], 1.0), inc=("ve", 1))
    k.on("vector", lambda e: e.memset(sb["kra1"][64:65, :], 1.0), inc=("ve", 1))
    k.on("vector", lambda e: e.memset(sb["ones64b"][:], 1.0), inc=("ve", 1))
    k.on("vector", lambda e: e.memset(sb["vp"][:], 0.0), inc=("ve", 1))
    for tt in range(32):
        k.on("vector", lambda e, tt=tt: e.memset(sb["vp"][:, 256 * tt + 64:256 * tt + 65], 1.0))
        k.on("vector", lambda e, tt=tt: e.memset(sb["vp"][:, 256 * tt + 192:256 * tt + 193], 1.0),
             inc=("ve", 1) if tt == 31 else None)
    k.barrier()
    for p in range(4):
        k.barrier()
        for nm, c0 in (("wq_sb", 128 * p), ("wk_sb", 512 + 128 * p), ("wv_sb", 1024 + 128 * p)):
            srcw = tens["lf_wqkv"][l][:, c0:c0 + 128].rearrange("(a p) c -> p a c", p=128)
            k.dma("gpsimd", sb[nm][:], srcw, "d_w")
        ld_w = k.v("d_w")
        for ch in range(NCH):
            c0 = CH * ch
            wldh = [("pe", k.v("pe"))] if (ch > 0 or p > 0) else []
            for d in range(DT):
                dst = sb["hb"][:, CH * d:CH * (d + 1)]
                if d < 4:
                    k.dma("gpsimd", dst, tens[yin][128 * d:128 * (d + 1), c0:c0 + CH],
                          "d_hb", wait=wldh if d == 0 else [])
                elif ch == 0:
                    k.on("vector", lambda e, dst=dst: e.memset(dst[:, 0:1], 0.0),
                         wait=wldh + [("ve", k.v("ve"))], inc=("ve", 1))
                    k.dma("gpsimd", dst[:, 1:CH], tens[yin][128 * d:128 * (d + 1), 0:CH - 1],
                          "d_hb")
                else:
                    k.dma("gpsimd", dst, tens[yin][128 * d:128 * (d + 1), c0 - 1:c0 + CH - 1],
                          "d_hb", wait=wldh if d == 0 else [])
            ldh = k.v("d_hb")
            ve_hb = k.v("ve")
            for which, wsb, dstt in (("q", "wq_sb", "qt"), ("kk", "wk_sb", "kt")):
                bank = 6 + ch % 2
                w = [("d_w", ld_w), ("d_hb", ldh), ("ve", ve_hb), ("ac", k.v("ac"))]
                for dm in range(DT):
                    k.on("tensor", lambda e, ch=ch, dm=dm, bank=bank, wsb=wsb: e.matmul(
                        pb[bank][:], sb[wsb][:, 128 * dm:128 * dm + 128],
                        sb["hb"][:, CH * dm:CH * (dm + 1)],
                        start=(dm == 0), stop=(dm == DT - 1)),
                        wait=w if dm == 0 else [],
                        inc=("pe", 1) if dm == DT - 1 else None)
                pev = k.v("pe")
                k.on("scalar", lambda e, ch=ch, bank=bank, dstt=dstt: e.copy(
                    sb[dstt][:, CH * ch:CH * (ch + 1)], pb[bank][:]),
                    wait=[("pe", pev)], inc=("ac", 1))
            for ttl in range(4):
                bank = 6 + ttl % 2
                w = [("d_w", ld_w), ("d_hb", ldh), ("ve", ve_hb), ("ac", k.v("ac"))]
                for dm in range(DT):
                    k.on("tensor", lambda e, ch=ch, ttl=ttl, dm=dm, bank=bank: e.matmul(
                        pb[bank][:, 0:128], sb["hb"][:, CH * dm + 128 * ttl:CH * dm + 128 * (ttl + 1)],
                        sb["wv_sb"][:, 128 * dm:128 * dm + 128],
                        start=(dm == 0), stop=(dm == DT - 1)),
                        wait=w if dm == 0 else [],
                        inc=("pe", 1) if dm == DT - 1 else None)
                pev = k.v("pe")
                for hl in range(2):
                    k.on("scalar", lambda e, ch=ch, ttl=ttl, bank=bank, hl=hl: e.copy(
                        sb["vp"][:, 256 * (4 * ch + ttl) + 128 * hl:256 * (4 * ch + ttl) + 128 * hl + 64],
                        pb[bank][:, 64 * hl:64 * hl + 64]),
                        wait=[("pe", pev)] if hl == 0 else [], inc=("ac", 1))
        for hl in range(2):
            for which in ("q", "kk"):
                srct = sb["qt" if which == "q" else "kt"]
                cse = sb["csq" if which == "q" else "csk"]
                sse = sb["ssq" if which == "q" else "ssk"]
                dstr = sb[("qra%d" if which == "q" else "kra%d") % hl]
                mmv = {}
                for ch in range(NCH):
                    row, bank = (ch % 3) * 32, ch // 3
                    wsq = [("ve", k.v("ve")), ("ac", k.v("ac"))]
                    if ch >= 1:
                        wsq.append(("pe", mmv[ch - 1]))
                    k.on("scalar", lambda e, srct=srct, hl=hl, ch=ch: e.activation(
                        sb["sqc"][64 * hl:64 * hl + 64, :],
                        srct[64 * hl:64 * hl + 64, CH * ch:CH * (ch + 1)], AF.Square),
                        wait=wsq, inc=("ac", 1))
                    mmv[ch] = k.on("tensor", lambda e, ch=ch, row=row, bank=bank, hl=hl: e.matmul(
                        pb[bank][row:row + 1, :], sb["onesb"][64 * hl:64 * hl + 64, :],
                        sb["sqc"][64 * hl:64 * hl + 64, :],
                        start=True, stop=True),
                        wait=[("ac", k.v("ac"))], inc=("pe", 1))
                pev = k.v("pe")
                for ch in range(NCH):
                    row, bank = (ch % 3) * 32, ch // 3
                    rr2, cc2 = 32 * (ch // 4), 512 * (ch % 4)
                    k.on("scalar", lambda e, row=row, bank=bank, rr2=rr2, cc2=cc2: e.copy(
                        sb["rows2"][rr2:rr2 + 1, cc2:cc2 + 512], pb[bank][row:row + 1, :]),
                        wait=[("pe", pev)] if ch == 0 else [], inc=("ac", 1))
                for rr2 in (0, 32):
                    k.on("vector", lambda e, rr2=rr2: e.tensor_scalar(
                        sb["rows2"][rr2:rr2 + 1, :], sb["rows2"][rr2:rr2 + 1, :], 1e-24, None, ALU.max),
                        wait=[("ac", k.v("ac")), ("ve", k.v("ve"))], inc=("ve", 1))
                    k.on("scalar", lambda e, rr2=rr2: e.activation(
                        sb["rows2"][rr2:rr2 + 1, :], sb["rows2"][rr2:rr2 + 1, :], AF.Ln),
                        wait=[("ve", k.v("ve"))], inc=("ac", 1))
                    k.on("scalar", lambda e, rr2=rr2: e.activation(
                        sb["rows2b"][rr2:rr2 + 1, :], sb["rows2"][rr2:rr2 + 1, :], AF.Exp, scale=-0.5),
                        wait=[("ac", k.v("ac"))], inc=("ac", 1))
                for ch in range(NCH):
                    rr2, cc2 = 32 * (ch // 4), 512 * (ch % 4)
                    pbc = k.on("tensor", lambda e, rr2=rr2, cc2=cc2: e.matmul(
                        pb[3][0:64, :], sb["ones64b"][rr2:rr2 + 1, :],
                        sb["rows2b"][rr2:rr2 + 1, cc2:cc2 + 512], start=True, stop=True),
                        wait=[("ac", k.v("ac")), ("ve", k.v("ve"))], inc=("pe", 1))
                    k.on("scalar", lambda e, ch=ch: e.copy(
                        sb["rnb"][0:64, CH * ch:CH * (ch + 1)], pb[3][0:64, :]),
                        wait=[("pe", pbc)], inc=("ac", 1))
                ac_bc = k.v("ac")
                k.on("vector", lambda e, srct=srct, hl=hl: e.tensor_copy(
                    sb["q0"][0:64, :], srct[64 * hl:64 * hl + 64, :]),
                    wait=[("ac", ac_bc), ("ve", k.v("ve"))], inc=("ve", 1))
                k.on("vector", lambda e: e.tensor_tensor(
                    sb["q0"][0:64, :], sb["q0"][0:64, :], sb["rnb"][0:64, :], ALU.mult),
                    wait=[("ve", k.v("ve"))], inc=("ve", 1))
                scol = sb["qs_col" if which == "q" else "ks_col"]
                k.on("vector", lambda e, scol=scol: e.tensor_scalar(
                    sb["q0"][0:64, :], sb["q0"][0:64, :], scol[0:64, 0:1], None, ALU.mult),
                    wait=[("ve", k.v("ve")), ("d_c", ld_c)], inc=("ve", 1))
                k.on("vector", lambda e: e.tensor_copy(sb["qsw"][0:32, :], sb["q0"][32:64, :]),
                     wait=[("ve", k.v("ve"))], inc=("ve", 1))
                k.on("vector", lambda e: e.tensor_copy(sb["qsw"][32:64, :], sb["q0"][0:32, :]),
                     wait=[("ve", k.v("ve"))], inc=("ve", 1))
                k.on("vector", lambda e, sse=sse: e.tensor_tensor(
                    sb["qsw"][0:64, :], sb["qsw"][0:64, :], sse[0:64, :], ALU.mult),
                    wait=[("ve", k.v("ve"))], inc=("ve", 1))
                k.on("vector", lambda e, cse=cse: e.tensor_tensor(
                    sb["q0"][0:64, :], sb["q0"][0:64, :], cse[0:64, :], ALU.mult),
                    wait=[("ve", k.v("ve"))], inc=("ve", 1))
                k.on("vector", lambda e, dstr=dstr: e.tensor_tensor(
                    dstr[0:64, :], sb["q0"][0:64, :], sb["qsw"][0:64, :], ALU.add),
                    wait=[("ve", k.v("ve"))], inc=("ve", 1))
        chain_done = k.v("ve")
        prev_p1 = None
        prev_p2bank = None
        prev_p2exp = None
        prev_p2evac = None
        for hl in range(2):
            h = 2 * p + hl
            qra, kra = sb["qra%d" % hl], sb["kra%d" % hl]
            p1add = {}
            for wdx in range(8):
                for qtl in range(4):
                    qi = 4 * wdx + qtl
                    off = 512 if wdx == 0 else 0
                    for kc in range(2):
                        if wdx == 0 and kc == 0:
                            continue
                        bk = 4 + kc
                        w = []
                        if (qi - 1, kc) in p1add:
                            w.append(("ve", p1add[(qi - 1, kc)]))
                        elif (qi - 2, kc) in p1add:
                            w.append(("ve", p1add[(qi - 2, kc)]))
                        elif qi <= 4 and prev_p1 is not None:
                            w.append(("ve", prev_p1[kc]))
                        if qi == 0:
                            w.append(("ve", chain_done))
                        k.on("tensor", lambda e, hl=hl, qi=qi, kc=kc, bk=bk, wdx=wdx, qra=qra, kra=kra: e.matmul(
                            pb[bk][:], qra[0:64, 128 * qi:128 * (qi + 1)],
                            kra[0:64, 512 * (wdx - 1) + 512 * kc:512 * wdx + 512 * kc],
                            start=True, stop=True), wait=w, inc=("pe", 1))
                        pev = k.v("pe")
                        p1add[(qi, kc)] = k.on("vector", lambda e, qi=qi, qtl=qtl, kc=kc, bk=bk: e.tensor_tensor(
                            sb["sm"][:, 512 * kc:512 * (kc + 1)], pb[bk][:],
                            sb["mask_sb"][:, 1024 * qtl + 512 * kc:1024 * qtl + 512 * (kc + 1)],
                            ALU.add),
                            wait=[("pe", pev), ("d_cg", ld_cg)], inc=("ve", 1))
                    k.on("vector", lambda e, qi=qi, off=off: e.reduce_max(
                        sb["nm"][:, qi:qi + 1], sb["sm"][:, off:1024], axis=AX.X, negate=True),
                        inc=("ve", 1))
            ve_nm = k.v("ve")
            ptr = k.on("tensor", lambda e: e.transpose(
                pb[7][0:32, 0:128], sb["nm"][:], sb["id_f32"][:]),
                wait=[("ve", ve_nm), ("ac", k.v("ac")), ("d_c", ld_c)], inc=("pe", 1))
            k.on("scalar", lambda e: e.copy(sb["tr"][0:32, :], pb[7][0:32, 0:128]),
                 wait=[("pe", ptr), ("d_tr", k.v("d_tr"))], inc=("ac", 1))
            ld_tr = k.dma("gpsimd", qra[64:65, :], sb["tr"][0:32, :], "d_tr",
                          wait=[("ac", k.v("ac"))])
            exp_cnt, dve_cnt, av_cnt, bankfree = {}, {}, {}, {}
            evac_ve, recip_done, tt_done, ot_st = {}, {}, {}, {}
            pend = []
            pdv_todo = []
            tiles = []
            for wdx in range(8):
                for kt in range(4 if wdx == 0 else 0, 8):
                    tiles.append((wdx, kt))
            nti = len(tiles)
            last_of_wdx = {wdx: max(i for i, t in enumerate(tiles) if t[0] == wdx)
                           for wdx in range(8)}
            first_of_wdx = {wdx: min(i for i, t in enumerate(tiles) if t[0] == wdx)
                            for wdx in range(8)}

            def emit_av(ti, hl=hl):
                wdx, kt = tiles[ti]
                kg = 4 * (wdx - 1) + kt
                wav = [("ac", exp_cnt[ti])]
                if ti == first_of_wdx[wdx]:
                    if (wdx - 2) in evac_ve:
                        wav.append(evac_ve[wdx - 2])
                    elif prev_p2evac is not None and wdx < 2:
                        wav.append(prev_p2evac[6 + wdx])
                av_cnt[ti] = k.on("tensor", lambda e, ti=ti, wdx=wdx, kg=kg, hl=hl: e.matmul(
                    pb[2 + wdx % 2][:], sb["vp"][:, 256 * kg + 128 * hl:256 * kg + 128 * (hl + 1)],
                    sb["e_sb"][:, CH * (ti % 4):CH * (ti % 4 + 1)],
                    start=(ti == first_of_wdx[wdx]), stop=(ti == last_of_wdx[wdx])),
                    wait=wav, inc=("pe", 1))
                if ti == last_of_wdx[wdx]:
                    sl = CH * (wdx % 2)
                    k.on("scalar", lambda e, wdx=wdx, sl=sl: e.copy(
                        sb["den"][0:1, sl:sl + CH], pb[2 + wdx % 2][64:65, :]),
                        wait=[("pe", av_cnt[ti])], inc=("ac", 1))
                    k.on("scalar", lambda e, wdx=wdx, sl=sl: e.copy(
                        sb["onum"][0:64, sl:sl + CH], pb[2 + wdx % 2][0:64, :]),
                        inc=("ac", 1))
                    evac_ve[wdx] = ("ac", k.v("ac"))
                    k.on("vector", lambda e, sl=sl: e.reciprocal(
                        sb["den"][0:1, sl:sl + CH], sb["den"][0:1, sl:sl + CH]),
                        wait=[("ac", k.v("ac"))], inc=("ve", 1))
                    recip_done[wdx] = k.v("ve")
                    pdv_todo.append(wdx)

            def emit_pdv(hl=hl):
                wdx = pdv_todo.pop(0)
                sl = CH * (wdx % 2)
                pdv = k.on("tensor", lambda e, sl=sl: e.matmul(
                    pb[6][0:64, :], sb["ones1f"][0:1, 0:64], sb["den"][0:1, sl:sl + CH],
                    start=True, stop=True), wait=[("ve", k.v("ve"))], inc=("pe", 1))
                wtt = [("pe", pdv)]
                if (wdx - 2) in ot_st:
                    wtt.append(("d_ot%d" % (wdx % 2), ot_st[wdx - 2]))
                tt_done[wdx] = k.on("vector", lambda e, wdx=wdx, sl=sl: e.tensor_tensor(
                    sb["ot_tmp"][0:64, CH * (wdx % 2):CH * (wdx % 2 + 1)],
                    sb["onum"][0:64, sl:sl + CH], pb[6][0:64, :], ALU.mult),
                    wait=wtt, inc=("ve", 1))
                ot_st[wdx] = k.dma("gpsimd",
                                   tens["otT"][128 * p + 64 * hl:128 * p + 64 * hl + 64,
                                               CH * wdx:CH * (wdx + 1)],
                                   sb["ot_tmp"][0:64, CH * (wdx % 2):CH * (wdx % 2 + 1)],
                                   "d_ot%d" % (wdx % 2), wait=[("ve", tt_done[wdx])])

            for ti, (wdx, kt) in enumerate(tiles):
                kg = 4 * (wdx - 1) + kt
                sbk = ti % 2
                if ti == first_of_wdx[wdx] and wdx >= 1 and pdv_todo:
                    emit_pdv()
                w = []
                if (ti - 2) >= 0:
                    w.append(bankfree[ti - 2])
                elif prev_p2bank is not None:
                    w.append(prev_p2bank[ti])
                if ti < 2:
                    w.append(("d_tr", ld_tr))
                k.on("tensor", lambda e, kg=kg, wdx=wdx, sbk=sbk, qra=qra, kra=kra: e.matmul(
                    pb[sbk][:], kra[:, 128 * kg:128 * (kg + 1)],
                    qra[:, CH * wdx:CH * (wdx + 1)],
                    start=True, stop=True), wait=w, inc=("pe", 1))
                pev = k.v("pe")
                ep = sb["e_pre"][:, CH * (ti % 4):CH * (ti % 4 + 1)]
                eo = sb["e_sb"][:, CH * (ti % 4):CH * (ti % 4 + 1)]
                wv = [("pe", pev)]
                if (ti - 4) >= 0:
                    wv.append(("ac", exp_cnt[ti - 4]))
                elif prev_p2exp is not None:
                    wv.append(("ac", prev_p2exp[ti]))
                moff = 896 - 128 * kt
                dve_cnt[ti] = k.on("vector", lambda e, ep=ep, sbk=sbk, moff=moff: e.tensor_tensor(
                    ep, pb[sbk][:], sb["maskT"][:, moff:moff + 512], ALU.add),
                    wait=wv, inc=("ve", 1))
                bankfree[ti] = ("ve", dve_cnt[ti])
                exp_cnt[ti] = k.on("scalar", lambda e, eo=eo, ep=ep: e.activation(
                    eo, ep, AF.Exp),
                    wait=[("ve", dve_cnt[ti])], inc=("ac", 1))
                if len(pend) == 3:
                    emit_av(pend.pop(0))
                pend.append(ti)
            while pend:
                emit_av(pend.pop(0))
            while pdv_todo:
                emit_pdv()
            prev_p1 = {kc: p1add.get((31, kc), p1add.get((30, kc), 0)) for kc in range(2)}
            prev_p2bank = {i: bankfree[nti - 2 + i] for i in range(2)}
            prev_p2exp = {i: exp_cnt[nti - 4 + i] for i in range(4)}
            prev_p2evac = evac_ve
    k.barrier()


def st_lf_wo(k, tens, sb, l, yin, ya):
    pb = k.pb
    k.dma("gpsimd", sb["wo_sb"][:],
          tens["lf_wo"][l].rearrange("(a p) (b c) -> p a b c", p=128, c=128), "d_wf")
    ldwo = k.v("d_wf")
    k.barrier()
    mm_done = [0] * NCH
    st_done = [0] * NCH
    for ch in range(NCH):
        w = [("pe", mm_done[ch - 1])] if ch >= 1 else []
        ldo = k.dma("sync", sb["otb"][:],
                    tens["otT"][0:512, CH * ch:CH * (ch + 1)].rearrange("(a p) c -> p a c", p=128),
                    "d_ob", wait=w)
        wx = [("ve", k.v("ve"))] if ch >= 1 else []
        ldy = k.dma("sync", sb["yinb"][:],
                    tens[yin][:, CH * ch:CH * (ch + 1)].rearrange("(a p) c -> p a c", p=128),
                    "d_xc", wait=wx)
        for do in range(DT):
            bank = 6 + do % 2
            w = [("d_wf", ldwo), ("d_ob", ldo), ("ve", k.v("ve"))]
            for a in range(4):
                k.on("tensor", lambda e, a=a, do=do, bank=bank: e.matmul(
                    pb[bank][:], sb["wo_sb"][:, 1024 * a + 128 * do:1024 * a + 128 * (do + 1)],
                    sb["otb"][:, CH * a:CH * (a + 1)],
                    start=(a == 0), stop=(a == 3)), wait=w if a == 0 else [],
                    inc=("pe", 1) if a == 3 else None)
            pev = k.v("pe")
            wst = [("pe", pev), ("d_xc", ldy)]
            if do == 0 and ch >= 1:
                wst.append(("d_rs", st_done[ch - 1]))
            k.on("vector", lambda e, do=do, bank=bank: e.tensor_tensor(
                sb["rescb"][:, CH * do:CH * (do + 1)], pb[bank][:],
                sb["yinb"][:, CH * do:CH * (do + 1)], ALU.add),
                wait=wst, inc=("ve", 1))
        mm_done[ch] = k.v("pe")
        st_done[ch] = k.dma("gpsimd",
                            tens[ya][:, CH * ch:CH * (ch + 1)].rearrange("(a p) c -> p a c", p=128),
                            sb["rescb"][:], "d_rs", wait=[("ve", k.v("ve"))])
    k.barrier()


LFA2_SB = [
    ("csq", (64, T), BF16), ("ssq", (64, T), BF16), ("csk", (64, T), BF16), ("ssk", (64, T), BF16),
    ("mask_sb", (128, 4 * 1024), BF16), ("maskT", (128, 1408), BF16),
    ("id_f32", (128, 128), F32),
    ("qs_col", (64, 1), F32), ("ks_col", (64, 1), F32),
    ("wq_sb", (128, 1024), BF16), ("wk_sb", (128, 1024), BF16), ("wv_sb", (128, 1024), BF16),
    ("hb", (128, 8 * CH), BF16),
    ("qt", (128, T), BF16), ("kt", (128, T), BF16),
    ("vp", (128, 32 * 256), BF16),
    ("sqc", (128, CH), BF16), ("rows2", (65, 2048), F32),
    ("rows2b", (33, 2048), BF16), ("ones64b", (33, 64), BF16), ("rnb", (64, T), BF16),
    ("q0", (64, T), BF16), ("qsw", (64, T), BF16),
    ("qra0", (65, T), BF16), ("qra1", (65, T), BF16),
    ("kra0", (65, T), BF16), ("kra1", (65, T), BF16),
    ("sm", (128, 1024), F32), ("nm", (128, 32), F32), ("tr", (32, 128), F32),
    ("e_sb", (128, 4 * CH), BF16), ("e_pre", (128, 4 * CH), F32),
    ("den", (1, 2 * CH), F32), ("onum", (64, 2 * CH), F32),
    ("ot_tmp", (64, 2 * CH), BF16),
]

LFW_SB = [
    ("wo_sb", (128, 4 * 1024), BF16),
    ("otb", (128, 4 * CH), BF16), ("yinb", (128, 8 * CH), F32), ("rescb", (128, 8 * CH), F32),
]


def build_lfattn_launch(l, yin_name, ya_name):
    nc = bass.Bass("TRN2", target_bir_lowering=False, debug=False)
    decls = [
        (yin_name, (D, T), F32, "ExternalInput"),
        ("lf_wqkv", (2, D, 1536), F32, "ExternalInput"),
        ("lf_qs", (2, 64), F32, "ExternalInput"),
        ("lf_ks", (2, 64), F32, "ExternalInput"),
        ("lf_wo", (2, 512, D), F32, "ExternalInput"),
        ("rot_cq", (64, T), F32, "ExternalInput"), ("rot_sq", (64, T), F32, "ExternalInput"),
        ("rot_ck", (64, T), F32, "ExternalInput"), ("rot_sk", (64, T), F32, "ExternalInput"),
        ("la_mask", (512, 1024), F32, "ExternalInput"),
        ("la_maskT", (128, 1408), F32, "ExternalInput"),
        ("ident", (128, 128), F32, "ExternalInput"),
        (ya_name, (D, T), F32, "ExternalOutput"),
        ("otT", (D, T), BF16, "Internal"),
    ]
    tens = {}
    for name, shape, dt, kind in decls:
        tens[name] = nc.dram_tensor(name, list(shape), dt, kind=kind).ap()
    outer = ExitStack()
    with outer:
        k, sbc = _mk_common(nc, outer)
        for i, (sbl, fn) in enumerate([
                (LFA2_SB, lambda sb: st_lf_attn2(k, tens, sb, l, yin_name)),
                (LFW_SB, lambda sb: st_lf_wo(k, tens, sb, l, yin_name, ya_name))]):
            ctx = ExitStack()
            sb = dict(sbc)
            seen = set()
            for nm, shape, dt in sbl:
                if nm in seen:
                    continue
                seen.add(nm)
                sb[nm] = ctx.enter_context(nc.sbuf_tensor("s%d_%s" % (i, nm), list(shape), dt))
            fn(sb)
            ctx.close()
        _finish(nc, k)
    return nc





def st_lf_ffn(k, tens, sb, l, ya, yout):
    pb = k.pb
    HT, HP = 44, 22
    k.dma("gpsimd", tens["w1bf"].rearrange("a b -> (a b)"),
          tens["lf_w1p"][l].rearrange("a b -> (a b)"), "d_cw")
    k.dma("gpsimd", tens["w2bf"].rearrange("a b -> (a b)"),
          tens["lf_w2p"][l].rearrange("a b -> (a b)"), "d_cw")
    k.dma("sync", sb["b1c"][:], tens["lf_b1p_t"][l], "d_c")
    k.dma("sync", sb["lgc"][:], tens["lf_lng_t"][l], "d_c")
    k.dma("sync", sb["lbc"][:], tens["lf_lnb_t"][l], "d_c")
    k.dma("sync", sb["b2c"][:], tens["lf_b2_t"][l], "d_c")
    ld_c = k.v("d_c")
    cw = k.v("d_cw")
    _emit_shift(k, tens, sb, ya)
    k.barrier()
    mu = sb["rows"][0:1, 0:T]
    ms = sb["rows"][32:33, 0:T]
    rr = sb["rows"][32:33, T:2 * T]
    sq_done, mmB, ld_done = [0] * DT, [0] * DT, [0] * DT
    for d in range(DT):
        hsb = sb["hsb"][:]
        w = []
        if d >= 1:
            w = [("ac", sq_done[d - 1]), ("pe", mmB[d - 1])]
        ld_done[d] = k.dma("sync", hsb, tens["hshiftT"][d * 128:(d + 1) * 128, :],
                           "d_a0", wait=w)
        sq_done[d] = k.on("scalar", lambda e, hsb=hsb: e.activation(sb["hsq"][:], hsb, AF.Square),
                          wait=[("d_a0", ld_done[d]), ("ac", k.v("ac")),
                                ("pe", mmB[d - 1] if d >= 1 else 0)], inc=("ac", 1))
        for ch in range(NCH):
            row, bank = (ch % 3) * 32, ch // 3
            k.on("tensor", lambda e, hsb=hsb, ch=ch, row=row, bank=bank, d=d: e.matmul(
                pb[bank][row:row + 1, :], sb["onesb"][:], hsb[:, ch * CH:(ch + 1) * CH],
                start=(d == 0), stop=(d == DT - 1)),
                wait=[("d_a0", ld_done[d])] if ch == 0 else [])
            k.on("tensor", lambda e, ch=ch, row=row, bank=bank, d=d: e.matmul(
                pb[3 + bank][row:row + 1, :], sb["onesb"][:], sb["hsq"][:, ch * CH:(ch + 1) * CH],
                start=(d == 0), stop=(d == DT - 1)),
                wait=[("ac", sq_done[d])] if ch == 0 else [],
                inc=("pe", 1) if ch == NCH - 1 else None)
        mmB[d] = k.v("pe")
    pe_stats = mmB[DT - 1]
    for ch in range(NCH):
        row, bank = (ch % 3) * 32, ch // 3
        k.on("scalar", lambda e, ch=ch, row=row, bank=bank: e.activation(
            sb["rows"][0:1, ch * CH:(ch + 1) * CH], pb[bank][row:row + 1, :], AF.Copy, scale=1.0 / D),
            wait=[("pe", pe_stats)] if ch == 0 else [], inc=("ac", 1))
        k.on("scalar", lambda e, ch=ch, row=row, bank=bank: e.activation(
            sb["rows"][32:33, ch * CH:(ch + 1) * CH], pb[3 + bank][row:row + 1, :], AF.Copy, scale=1.0 / D),
            inc=("ac", 1))
    ac_mu = k.v("ac")
    k.on("vector", lambda e: e.tensor_tensor(rr, mu, mu, ALU.mult),
         wait=[("ac", ac_mu)], inc=("ve", 1))
    k.on("vector", lambda e: e.tensor_tensor(rr, ms, rr, ALU.subtract),
         wait=[("ve", k.v("ve"))], inc=("ve", 1))
    k.on("scalar", lambda e: e.activation(rr, rr, AF.Sqrt, bias=sb["c_eps"][32:33, :]),
         wait=[("ve", k.v("ve"))], inc=("ac", 1))
    k.on("vector", lambda e: e.reciprocal(rr, rr),
         wait=[("ac", k.v("ac"))], inc=("ve", 1))
    ve_r = k.v("ve")
    bc_done = 0
    for ch in range(NCH):
        ba = 4 + 2 * (ch % 2)
        wmm = [("ac", ac_mu), ("ve", ve_r)]
        if ch >= 2:
            wmm.append(("ac", bc_done))
        k.on("tensor", lambda e, ch=ch, ba=ba: e.matmul(
            pb[ba][:], sb["ones1f"][0:1, :], sb["rows"][0:1, ch * CH:(ch + 1) * CH],
            start=True, stop=True), wait=wmm, inc=("pe", 1))
        pev = k.on("tensor", lambda e, ch=ch, ba=ba: e.matmul(
            pb[ba + 1][:], sb["ones1f"][32:33, :], sb["rows"][32:33, T + ch * CH:T + (ch + 1) * CH],
            start=True, stop=True), inc=("pe", 1))
        k.on("scalar", lambda e, ch=ch, ba=ba: e.copy(
            sb["mub"][:, ch * CH:(ch + 1) * CH], pb[ba][:]),
            wait=[("pe", pev)], inc=("ac", 1))
        bc_done = k.on("scalar", lambda e, ch=ch, ba=ba: e.copy(
            sb["rb"][:, ch * CH:(ch + 1) * CH], pb[ba + 1][:]), inc=("ac", 1))
    ac_bc = k.v("ac")
    xst = [0] * DT
    for d in range(DT):
        hsb = sb["hsb"][:]
        ld = k.dma("sync", hsb, tens["hshiftT"][d * 128:(d + 1) * 128, :], "d_a0",
                   wait=[("ve", k.v("ve")), ("ac", k.v("ac")), ("pe", k.v("pe")),
                         ("d_x0", xst[d - 1] if d >= 1 else 0)])
        k.on("vector", lambda e, hsb=hsb: e.tensor_tensor(sb["hsq"][:], hsb, sb["mub"][:], ALU.subtract),
             wait=[("d_a0", ld), ("ac", ac_bc), ("ve", k.v("ve")), ("ac", k.v("ac"))], inc=("ve", 1))
        k.on("vector", lambda e: e.tensor_tensor(sb["hsq"][:], sb["hsq"][:], sb["rb"][:], ALU.mult),
             wait=[("ve", k.v("ve"))], inc=("ve", 1))
        vv = k.on("scalar", lambda e, d=d, hsb=hsb: e.activation(
            hsb, sb["hsq"][:], AF.Identity,
            bias=sb["lbc"][:, d:d + 1], scale=sb["lgc"][:, d:d + 1]),
            wait=[("ve", k.v("ve")), ("d_c", ld_c)], inc=("ac", 1))
        xst[d] = k.dma("gpsimd", tens["xn2T"][d * 128:(d + 1) * 128, :], hsb, "d_x0",
                       wait=[("ac", vv)])
    k.barrier()
    w1_rd = {}
    w2_rd = {}
    st_done = [0] * NCH
    for ch in range(NCH):
        ldy = k.dma("sync", sb["yab"][:],
                    tens[ya][:, CH * ch:CH * (ch + 1)].rearrange("(a p) c -> p a c", p=128),
                    "d_xc", wait=[("ve", k.v("ve"))] if ch >= 1 else [])
        ldxn = k.dma("sync", sb["xnb"][:],
                     tens["xn2T"][:, CH * ch:CH * (ch + 1)].rearrange("(a p) c -> p a c", p=128),
                     "d_xnb", wait=[("pe", k.v("pe"))] if ch >= 1 else [])
        for ht in range(HT):
            b2 = ht % 2
            wld = [("d_cw", cw)]
            if (ch, ht - 2) in w1_rd:
                wld.append(("pe", w1_rd[(ch, ht - 2)]))
            elif ch >= 1 and (ch - 1, HT - 2 + ht) in w1_rd:
                wld.append(("pe", w1_rd[(ch - 1, HT - 2 + ht)]))
            ldw = k.dma("gpsimd", sb["w1sl"][:, 1024 * b2:1024 * (b2 + 1)],
                        tens["w1bf"][:, 128 * ht:128 * (ht + 1)].rearrange("(a p) c -> p a c", p=128),
                        "d_w%d" % b2, wait=wld)
            bank = ht % 2
            w = [("d_w%d" % b2, ldw), ("ac", k.v("ac"))]
            w.append(("d_xnb", ldxn))
            for dm in range(DT):
                k.on("tensor", lambda e, ht=ht, dm=dm, bank=bank, b2=b2: e.matmul(
                    pb[bank][:], sb["w1sl"][:, 1024 * b2 + 128 * dm:1024 * b2 + 128 * (dm + 1)],
                    sb["xnb"][:, CH * dm:CH * (dm + 1)],
                    start=(dm == 0), stop=(dm == DT - 1)),
                    wait=w if dm == 0 else [],
                    inc=("pe", 1) if dm == DT - 1 else None)
            w1_rd[(ch, ht)] = k.v("pe")
            k.on("scalar", lambda e, ht=ht, bank=bank: e.activation(
                sb["h1c"][:, 512 * ht:512 * (ht + 1)], pb[bank][:], AF.Identity,
                bias=sb["b1c"][:, ht:ht + 1], scale=1.0),
                wait=[("pe", w1_rd[(ch, ht)])], inc=("ac", 1))
        ac_h1 = k.v("ac")
        for hp in range(HP):
            k.on("scalar", lambda e, hp=hp: e.activation(
                sb["gg"][:], sb["h1c"][:, 512 * (HP + hp):512 * (HP + hp + 1)], AF.Gelu),
                wait=[("ac", k.v("ac")), ("ve", k.v("ve"))], inc=("ac", 1))
            k.on("vector", lambda e, hp=hp: e.tensor_tensor(
                sb["h1c"][:, 512 * hp:512 * (hp + 1)], sb["h1c"][:, 512 * hp:512 * (hp + 1)],
                sb["gg"][:], ALU.mult),
                wait=[("ac", k.v("ac"))], inc=("ve", 1))
        ve_gg = k.v("ve")
        for do in range(DT):
            b2 = do % 2
            wld = [("d_cw", cw)]
            if (ch, do - 2) in w2_rd:
                wld.append(("pe", w2_rd[(ch, do - 2)]))
            elif ch >= 1 and (ch - 1, DT - 2 + do) in w2_rd:
                wld.append(("pe", w2_rd[(ch - 1, DT - 2 + do)]))
            ldw = k.dma("gpsimd", sb["w2sl"][:, 2816 * b2:2816 * (b2 + 1)],
                        tens["w2bf"][:, 128 * do:128 * (do + 1)].rearrange("(a p) c -> p a c", p=128),
                        "d_v%d" % b2, wait=wld)
            bank = 2 + do % 2
            w = [("d_v%d" % b2, ldw), ("ve", ve_gg), ("ve", k.v("ve"))]
            for ht in range(HP):
                k.on("tensor", lambda e, do=do, ht=ht, bank=bank, b2=b2: e.matmul(
                    pb[bank][:], sb["w2sl"][:, 2816 * b2 + 128 * ht:2816 * b2 + 128 * (ht + 1)],
                    sb["h1c"][:, 512 * ht:512 * (ht + 1)],
                    start=(ht == 0), stop=(ht == HP - 1)),
                    wait=w if ht == 0 else [],
                    inc=("pe", 1) if ht == HP - 1 else None)
            w2_rd[(ch, do)] = k.v("pe")
            wst = [("pe", w2_rd[(ch, do)]), ("d_xc", ldy)]
            if do == 0 and ch >= 1:
                wst.append(("d_rs", st_done[ch - 1]))
            k.on("vector", lambda e, do=do, bank=bank: e.scalar_tensor_tensor(
                sb["youtc"][:, CH * do:CH * (do + 1)], pb[bank][:], sb["b2c"][:, do:do + 1],
                sb["yab"][:, CH * do:CH * (do + 1)], ALU.add, ALU.add),
                wait=wst, inc=("ve", 1))
        st_done[ch] = k.dma("gpsimd",
                            tens[yout][:, CH * ch:CH * (ch + 1)].rearrange("(a p) c -> p a c", p=128),
                            sb["youtc"][:], "d_rs", wait=[("ve", k.v("ve"))])
    k.barrier()


def _emit_shift(k, tens, sb, yin):
    for d in range(DT):
        for cc in range(4):
            yb = sb["ybuf"][:, 0:1025]
            hf = sb["hbf"][:, 0:1024]
            wld = [("d_h0", k.v("d_h0")), ("ve", k.v("ve"))]
            if d < 4:
                ld = k.dma("sync", yb[:, 0:1024],
                           tens[yin][128 * d:128 * (d + 1), 1024 * cc:1024 * (cc + 1)],
                           "d_y0", wait=wld)
                vv = k.on("vector", lambda e, yb=yb, hf=hf: e.tensor_copy(hf, yb[:, 0:1024]),
                          wait=[("d_y0", ld), ("ve", k.v("ve"))], inc=("ve", 1))
            elif cc == 0:
                ld = k.dma("sync", yb[:, 0:1023],
                           tens[yin][128 * d:128 * (d + 1), 0:1023], "d_y0", wait=wld)
                k.on("vector", lambda e, hf=hf: e.memset(hf[:, 0:1], 0.0),
                     wait=[("ve", k.v("ve"))] + wld, inc=("ve", 1))
                vv = k.on("vector", lambda e, yb=yb, hf=hf: e.tensor_copy(hf[:, 1:1024], yb[:, 0:1023]),
                          wait=[("d_y0", ld), ("ve", k.v("ve"))], inc=("ve", 1))
            else:
                ld = k.dma("sync", yb[:, 0:1024],
                           tens[yin][128 * d:128 * (d + 1), 1024 * cc - 1:1024 * (cc + 1) - 1],
                           "d_y0", wait=wld)
                vv = k.on("vector", lambda e, yb=yb, hf=hf: e.tensor_copy(hf, yb[:, 0:1024]),
                          wait=[("d_y0", ld), ("ve", k.v("ve"))], inc=("ve", 1))
            k.dma("gpsimd", tens["hshiftT"][128 * d:128 * (d + 1), 1024 * cc:1024 * (cc + 1)],
                  hf, "d_h0", wait=[("ve", vv)])


LFF_SB = [
    ("b1c", (128, 44), F32), ("lgc", (128, 8), F32), ("lbc", (128, 8), F32), ("b2c", (128, 8), F32),
    ("ybuf", (128, 1025), F32), ("hbf", (128, 1024), BF16),
    ("hsb", (128, T), BF16), ("hsq", (128, T), BF16),
    ("rows", (65, 2 * T), F32),
    ("mub", (128, T), BF16), ("rb", (128, T), BF16),
    ("xnb", (128, 8 * CH), BF16),
    ("w1sl", (128, 2 * 1024), BF16), ("w2sl", (128, 2 * 2816), BF16),
    ("h1c", (128, 44 * CH), BF16), ("gg", (128, CH), BF16),
    ("yab", (128, 8 * CH), F32), ("youtc", (128, 8 * CH), F32),
]


def build_lfffn_launch(l, ya_name, yout_name):
    nc = bass.Bass("TRN2", target_bir_lowering=False, debug=False)
    decls = [
        (ya_name, (D, T), F32, "ExternalInput"),
        ("lf_w1p", (2, D, 5632), F32, "ExternalInput"),
        ("lf_b1p_t", (2, 128, 44), F32, "ExternalInput"),
        ("lf_w2p", (2, 2816, D), F32, "ExternalInput"),
        ("lf_b2_t", (2, 128, 8), F32, "ExternalInput"),
        ("lf_lng_t", (2, 128, 8), F32, "ExternalInput"),
        ("lf_lnb_t", (2, 128, 8), F32, "ExternalInput"),
        (yout_name, (D, T), F32, "ExternalOutput"),
        ("hshiftT", (D, T), BF16, "Internal"),
        ("xn2T", (D, T), BF16, "Internal"),
        ("w1bf", (D, 5632), BF16, "Internal"),
        ("w2bf", (2816, D), BF16, "Internal"),
    ]
    tens = {}
    for name, shape, dt, kind in decls:
        tens[name] = nc.dram_tensor(name, list(shape), dt, kind=kind).ap()
    ctx = ExitStack()
    with ctx:
        k, sb = _mk_k(nc, ctx, LFF_SB)
        st_lf_ffn(k, tens, sb, l, ya_name, yout_name)
        _finish(nc, k)
    return nc


def st_final(k, tens, sb, a_src, b_src, dst):
    st_done = [0] * DT
    for d in range(DT):
        b2 = d % 2
        aslc = sb["fa"][:, T * b2:T * (b2 + 1)]
        bslc = sb["fb"][:, T * b2:T * (b2 + 1)]
        w = [("ve", k.v("ve"))]
        if d >= 2:
            w.append(("d_rs", st_done[d - 2]))
        lda = k.dma("sync", aslc, tens[a_src][128 * d:128 * (d + 1), :], "d_a%d" % b2, wait=w)
        ldb = k.dma("sync", bslc, tens[b_src][128 * d:128 * (d + 1), :], "d_b%d" % b2, wait=[])
        vv = k.on("vector", lambda e, aslc=aslc, bslc=bslc: e.tensor_tensor(
            aslc, aslc, bslc, ALU.add),
            wait=[("d_a%d" % b2, lda), ("d_b%d" % b2, ldb), ("ve", k.v("ve"))], inc=("ve", 1))
        st_done[d] = k.dma("gpsimd", tens[dst][128 * d:128 * (d + 1), :], aslc, "d_rs",
                           wait=[("ve", vv)])
    k.barrier()


FIN_SB = [("fa", (128, 2 * T), F32), ("fb", (128, 2 * T), F32)]


def build_final_launch(a_name, b_name, out_name):
    nc = bass.Bass("TRN2", target_bir_lowering=False, debug=False)
    decls = [
        (a_name, (D, T), F32, "ExternalInput"),
        (b_name, (D, T), F32, "ExternalInput"),
        (out_name, (D, T), F32, "ExternalOutput"),
    ]
    tens = {}
    for name, shape, dt, kind in decls:
        tens[name] = nc.dram_tensor(name, list(shape), dt, kind=kind).ap()
    ctx = ExitStack()
    with ctx:
        k, sb = _mk_k(nc, ctx, FIN_SB)
        st_final(k, tens, sb, a_name, b_name, out_name)
        _finish(nc, k)
    return nc


def _mk_k(nc, ctx, sb_decls):
    k = K(nc, ctx)
    sb = {}
    seen = set()
    for nm, shape, dt in sb_decls:
        if nm in seen:
            continue
        seen.add(nm)
        sb[nm] = ctx.enter_context(nc.sbuf_tensor(nm, list(shape), dt))
    for nm, shape, dt in [("onesb", (128, 1), BF16), ("ones1f", (33, 128), F32),
                          ("c_eps", (128, 1), F32)]:
        sb[nm] = ctx.enter_context(nc.sbuf_tensor(nm, list(shape), dt))
    k.pb = [ctx.enter_context(nc.psum_tensor("pb%d" % i, [128, 512], F32))
            for i in range(8)]
    k.on("vector", lambda e: e.memset(sb["onesb"][:], 1.0), inc=("ve", 1))
    k.on("vector", lambda e: e.memset(sb["ones1f"][:], 1.0), inc=("ve", 1))
    k.on("vector", lambda e: e.memset(sb["c_eps"][:], 1e-5), inc=("ve", 1))
    return k, sb


def _finish(nc, k):
    waits = [(s, k.val[s]) for s in list(k.val) if s.startswith("d_") and k.val[s] > 0]
    k.q["sync"].append((None, waits, []))
    with nc.Block() as block:
        k.finalize(block)
    return nc


def build_l0attn_launch(kinds):
    nc = bass.Bass("TRN2", target_bir_lowering=False, debug=False)
    decls = [
        ("xn_bf", (D, T), BF16, "ExternalInput"),
        ("xT", (D, T), F32, "ExternalInput"),
        ("w_qkv", (D, 3072), F32, "ExternalInput"),
        ("b_qkv_t", (128, 24), F32, "ExternalInput"),
        ("b_qkv_raw", (1, 3072), F32, "ExternalInput"),
        ("w_out", (D, D), F32, "ExternalInput"),
        ("b_out_t", (128, 8), F32, "ExternalInput"),
        ("strips", (16, 128, 3072), F32, "ExternalInput"),
        ("rb_lo", (128, 16), F32, "ExternalInput"),
        ("rb_hi", (128, 16), F32, "ExternalInput"),
        ("ident", (128, 128), F32, "ExternalInput"),
        ("b_gate", (1, 8), F32, "ExternalInput"),
        ("x1p", (D, T), F32, "ExternalOutput"),
        ("otT", (D, T), BF16, "Internal"),
    ]
    tens = {}
    for name, shape, dt, kind in decls:
        tens[name] = nc.dram_tensor(name, list(shape), dt, kind=kind).ap()
    outer = ExitStack()
    with outer:
        k, sbc = _mk_common(nc, outer)
        for i, (sbl, fn) in enumerate([
                (L0A_SB, lambda sb: st_l0_attn(k, tens, sb, "xn_bf")),
                (L0P_SB, lambda sb: st_l0_outproj(k, tens, sb, "xT", "x1p"))]):
            ctx = ExitStack()
            sb = dict(sbc)
            seen = set()
            for nm, shape, dt in sbl:
                if nm in seen:
                    continue
                seen.add(nm)
                sb[nm] = ctx.enter_context(nc.sbuf_tensor("s%d_%s" % (i, nm), list(shape), dt))
            fn(sb)
            ctx.close()
        _finish(nc, k)
    return nc


def _mk_common(nc, outer):
    k = K(nc, outer)
    sbc = {}
    for nm, shape, dt in [("onesb", (128, 1), BF16), ("ones1f", (33, 128), F32),
                          ("c_eps", (128, 1), F32)]:
        sbc[nm] = outer.enter_context(nc.sbuf_tensor(nm, list(shape), dt))
    k.pb = [outer.enter_context(nc.psum_tensor("pb%d" % i, [128, 512], F32))
            for i in range(8)]
    k.on("vector", lambda e: e.memset(sbc["onesb"][:], 1.0), inc=("ve", 1))
    k.on("vector", lambda e: e.memset(sbc["ones1f"][:], 1.0), inc=("ve", 1))
    k.on("vector", lambda e: e.memset(sbc["c_eps"][:], 1e-5), inc=("ve", 1))
    return k, sbc


def build_ln_launch(src, dst_bf, dst_f32, kinds):
    nc = bass.Bass("TRN2", target_bir_lowering=False, debug=False)
    decls = [(src, (D, T), F32, kinds.get(src, "ExternalInput")),
             (dst_bf, (D, T), BF16, kinds.get(dst_bf, "ExternalOutput"))]
    if dst_f32:
        decls.append((dst_f32, (D, T), F32, kinds.get(dst_f32, "ExternalOutput")))
    tens = {}
    for name, shape, dt, kind in decls:
        tens[name] = nc.dram_tensor(name, list(shape), dt, kind=kind).ap()
    ctx = ExitStack()
    with ctx:
        k, sb = _mk_k(nc, ctx, LN_SB)
        st_layernorm(k, tens, sb, src, dst_bf, dst_f32)
        _finish(nc, k)
    return nc


FUSED_DECLS = [
    ("xT", (D, T), F32, "ExternalInput"),
    ("w_qkv", (D, 3072), F32, "ExternalInput"),
    ("b_qkv_t", (128, 24), F32, "ExternalInput"),
    ("b_qkv_raw", (1, 3072), F32, "ExternalInput"),
    ("w_out", (D, D), F32, "ExternalInput"),
    ("b_out_t", (128, 8), F32, "ExternalInput"),
    ("strips", (16, 128, 3072), F32, "ExternalInput"),
    ("rb_lo", (128, 16), F32, "ExternalInput"),
    ("rb_hi", (128, 16), F32, "ExternalInput"),
    ("ident", (128, 128), F32, "ExternalInput"),
    ("w_gate", (D, 8), F32, "ExternalInput"),
    ("b_gate", (1, 8), F32, "ExternalInput"),
    ("moe_w1", (8, D, D), F32, "ExternalInput"),
    ("moe_w2", (8, D, D), F32, "ExternalInput"),
    ("moe_b1c", (128, 64), F32, "ExternalInput"),
    ("moe_b2", (8, D), F32, "ExternalInput"),
    ("emask8", (8, 1024), F32, "ExternalInput"),
    ("p_Wq", (D, D), F32, "ExternalInput"), ("p_Wk", (D, D), F32, "ExternalInput"),
    ("p_Wv", (D, D), F32, "ExternalInput"), ("p_Wo", (D, D), F32, "ExternalInput"),
    ("p_bq_t", (128, 8), F32, "ExternalInput"), ("p_bk_t", (128, 8), F32, "ExternalInput"),
    ("p_bv_t", (128, 8), F32, "ExternalInput"), ("p_bo_t", (128, 8), F32, "ExternalInput"),
    ("p_bv_raw", (1, D), F32, "ExternalInput"),
    ("p_rm", (128, 256), F32, "ExternalInput"),
    ("lf_wqkv", (2, D, 1536), F32, "ExternalInput"),
    ("lf_qs", (2, 64), F32, "ExternalInput"),
    ("lf_ks", (2, 64), F32, "ExternalInput"),
    ("lf_wo", (2, 512, D), F32, "ExternalInput"),
    ("rot_cq", (64, T), F32, "ExternalInput"), ("rot_sq", (64, T), F32, "ExternalInput"),
    ("rot_ck", (64, T), F32, "ExternalInput"), ("rot_sk", (64, T), F32, "ExternalInput"),
    ("la_mask", (512, 1024), F32, "ExternalInput"),
    ("la_maskT", (128, 1408), F32, "ExternalInput"),
    ("lf_w1p", (2, D, 5632), F32, "ExternalInput"),
    ("lf_b1p_t", (2, 128, 44), F32, "ExternalInput"),
    ("lf_w2p", (2, 2816, D), F32, "ExternalInput"),
    ("lf_b2_t", (2, 128, 8), F32, "ExternalInput"),
    ("lf_lng_t", (2, 128, 8), F32, "ExternalInput"),
    ("lf_lnb_t", (2, 128, 8), F32, "ExternalInput"),
    ("outT", (D, T), F32, "ExternalOutput"),
    ("xn_bf", (D, T), BF16, "Internal"), ("xn_f32", (D, T), F32, "Internal"),
    ("x1p", (D, T), F32, "Internal"), ("x1", (D, T), F32, "Internal"),
    ("xn1_bf", (D, T), BF16, "Internal"), ("xn1_f32", (D, T), F32, "Internal"),
    ("x2p", (D, T), F32, "Internal"),
    ("y_a0", (D, T), F32, "Internal"), ("y_m0", (D, T), F32, "Internal"),
    ("y_a1", (D, T), F32, "Internal"), ("y_m1", (D, T), F32, "Internal"),
    ("otT", (D, T), BF16, "Internal"),
    ("hshiftT", (D, T), BF16, "Internal"), ("xn2T", (D, T), BF16, "Internal"),
    ("moe_w1bf", (8, D, D), BF16, "Internal"), ("moe_w2bf", (8, D, D), BF16, "Internal"),
    ("w1bf", (D, 5632), BF16, "Internal"), ("w2bf", (2816, D), BF16, "Internal"),
]


def build_fused():
    nc = bass.Bass("TRN2", target_bir_lowering=False, debug=False)
    tens = {}
    for name, shape, dt, kind in FUSED_DECLS:
        tens[name] = nc.dram_tensor(name, list(shape), dt, kind=kind).ap()
    outer = ExitStack()
    with outer:
        k = K(nc, outer)
        sbc = {}
        for nm, shape, dt in [("onesb", (128, 1), BF16), ("ones1f", (33, 128), F32),
                              ("c_eps", (128, 1), F32)]:
            sbc[nm] = outer.enter_context(nc.sbuf_tensor(nm, list(shape), dt))
        k.pb = [outer.enter_context(nc.psum_tensor("pb%d" % i, [128, 512], F32))
                for i in range(8)]
        k.on("vector", lambda e: e.memset(sbc["onesb"][:], 1.0), inc=("ve", 1))
        k.on("vector", lambda e: e.memset(sbc["ones1f"][:], 1.0), inc=("ve", 1))
        k.on("vector", lambda e: e.memset(sbc["c_eps"][:], 1e-5), inc=("ve", 1))
        stage_idx = [0]

        def stage(sb_decls, fn):
            si = stage_idx[0]
            stage_idx[0] += 1
            ctx = ExitStack()
            sb = dict(sbc)
            seen = set()
            for nm, shape, dt in sb_decls:
                if nm in seen:
                    continue
                seen.add(nm)
                sb[nm] = ctx.enter_context(
                    nc.sbuf_tensor("s%d_%s" % (si, nm), list(shape), dt))
            fn(sb)
            ctx.close()

        stage(LN_SB, lambda sb: st_layernorm(k, tens, sb, "xT", "xn_bf", "xn_f32"))
        stage(L0A_SB, lambda sb: st_l0_attn(k, tens, sb, "xn_bf"))
        stage(L0P_SB, lambda sb: st_l0_outproj(k, tens, sb, "xT", "x1p"))
        stage(MOE_SB, lambda sb: st_l0_ffn(k, tens, sb, "xn_bf", "x1p", "x1"))
        stage(LN_SB, lambda sb: st_layernorm(k, tens, sb, "x1", "xn1_bf", "xn1_f32"))
        stage(L1A_SB, lambda sb: st_l1_attn(k, tens, sb, "xn1_bf", "x1", "x2p"))
        lf_names = {0: ("xn1_f32", "y_a0", "y_m0"), 1: ("y_m0", "y_a1", "y_m1")}
        for l in range(2):
            yin, ya, ym = lf_names[l]
            stage(LFA2_SB, lambda sb, l=l, yin=yin: st_lf_attn2(k, tens, sb, l, yin))
            stage(LFW_SB, lambda sb, l=l, yin=yin, ya=ya: st_lf_wo(k, tens, sb, l, yin, ya))
            stage(LFF_SB, lambda sb, l=l, ya=ya, ym=ym: st_lf_ffn(k, tens, sb, l, ya, ym))
        stage(FIN_SB, lambda sb: st_final(k, tens, sb, "x2p", "y_m1", "outT"))
        _finish(nc, k)
    return nc


def kernel(**inputs):
    x = np.asarray(inputs["L0_attn_x"], np.float32)
    sh = _prep_shared(inputs)
    xT = [np.ascontiguousarray(x[b].T) for b in range(NCORE)]
    cores = list(range(NCORE))
    b_qkv_raw = np.ascontiguousarray(np.asarray(inputs["L0_attn_b_qkv"], np.float32).reshape(1, 3072))
    p_bv_raw = np.ascontiguousarray(np.asarray(inputs["L1_attn_bv"], np.float32).reshape(1, 1024))

    nc = build_fused()
    shared_keys = [name for name, _s, _d, kind in FUSED_DECLS
                   if kind == "ExternalInput" and name not in ("xT", "b_qkv_raw", "p_bv_raw")]
    maps = []
    for b in range(NCORE):
        m = {kk: sh[kk] for kk in shared_keys}
        m["xT"] = xT[b]
        m["b_qkv_raw"] = b_qkv_raw
        m["p_bv_raw"] = p_bv_raw
        maps.append(m)
    r = run_bass_kernel_spmd(nc, maps, core_ids=cores).results
    out = np.stack([np.asarray(r[b]["outT"], np.float32).T for b in range(NCORE)])
    return np.ascontiguousarray(out)


def kernel_multilaunch(**inputs):
    x = np.asarray(inputs["L0_attn_x"], np.float32)
    sh = _prep_shared(inputs)
    xT = [np.ascontiguousarray(x[b].T) for b in range(NCORE)]
    cores = list(range(NCORE))
    b_qkv_raw = np.ascontiguousarray(np.asarray(inputs["L0_attn_b_qkv"], np.float32).reshape(1, 3072))
    p_bv_raw = np.ascontiguousarray(np.asarray(inputs["L1_attn_bv"], np.float32).reshape(1, 1024))

    def run(nc, per_core, shared_keys):
        maps = []
        for b in range(NCORE):
            m = {kk: sh[kk] for kk in shared_keys}
            m.update(per_core[b])
            maps.append(m)
        return run_bass_kernel_spmd(nc, maps, core_ids=cores).results

    nc = build_ln_launch("xT", "xn_bf", "xn_f32", {})
    r = run(nc, [{"xT": xT[b]} for b in range(NCORE)], [])
    xn_bf = [r[b]["xn_bf"] for b in range(NCORE)]
    xn_f32 = [r[b]["xn_f32"] for b in range(NCORE)]
    nc = build_l0attn_launch({})
    r = run(nc, [{"xn_bf": xn_bf[b], "xT": xT[b], "b_qkv_raw": b_qkv_raw}
                 for b in range(NCORE)],
            ["w_qkv", "b_qkv_t", "w_out", "b_out_t", "strips", "rb_lo", "rb_hi",
             "ident", "b_gate"])
    x1p = [r[b]["x1p"] for b in range(NCORE)]
    nc = build_l0ffn_launch({})
    r = run(nc, [{"xn_bf": xn_bf[b], "xn_f32": xn_f32[b], "x1p": x1p[b]}
                 for b in range(NCORE)],
            ["w_gate", "b_gate", "moe_w1", "moe_w2", "moe_b1c", "moe_b2", "ident", "emask8"])
    x1 = [r[b]["x1"] for b in range(NCORE)]
    nc = build_ln_launch("x1", "xn1_bf", "xn1_f32", {})
    r = run(nc, [{"x1": x1[b]} for b in range(NCORE)], [])
    xn1_bf = [r[b]["xn1_bf"] for b in range(NCORE)]
    xn1_f32 = [r[b]["xn1_f32"] for b in range(NCORE)]
    nc = build_l1attn_launch({})
    r = run(nc, [{"xn1_bf": xn1_bf[b], "x1": x1[b], "p_bv_raw": p_bv_raw}
                 for b in range(NCORE)],
            ["p_Wq", "p_Wk", "p_Wv", "p_Wo", "p_bq_t", "p_bk_t", "p_bv_t", "p_bo_t", "p_rm"])
    x2p = [r[b]["x2p"] for b in range(NCORE)]
    lf_shared_a = ["lf_wqkv", "lf_qs", "lf_ks", "lf_wo", "rot_cq", "rot_sq",
                   "rot_ck", "rot_sk", "la_mask", "ident"]
    lf_shared_f = ["lf_w1p", "lf_b1p_t", "lf_w2p", "lf_b2_t", "lf_lng_t", "lf_lnb_t"]
    yin = xn1_f32
    for l in range(2):
        nc = build_lfattn_launch(l, "y_in", "y_a")
        r = run(nc, [{"y_in": yin[b]} for b in range(NCORE)], lf_shared_a)
        ya = [r[b]["y_a"] for b in range(NCORE)]
        nc = build_lfffn_launch(l, "y_a", "y_m")
        r = run(nc, [{"y_a": ya[b]} for b in range(NCORE)], lf_shared_f)
        yin = [r[b]["y_m"] for b in range(NCORE)]
    nc = build_final_launch("x2p", "y2", "outT")
    r = run(nc, [{"x2p": x2p[b], "y2": yin[b]} for b in range(NCORE)], [])
    out = np.stack([np.asarray(r[b]["outT"], np.float32).T for b in range(NCORE)])
    return np.ascontiguousarray(out)


# version 09
# speedup vs baseline: 1.5459x; 1.0914x over previous
import numpy as np
from contextlib import ExitStack
import concourse.bass as bass
import concourse.mybir as mybir
from concourse.bass_utils import run_bass_kernel_spmd
import ml_dtypes

F32 = mybir.dt.float32
BF16 = mybir.dt.bfloat16
AF = mybir.ActivationFunctionType
ALU = mybir.AluOpType
AX = mybir.AxisListType
NPBF = ml_dtypes.bfloat16

D = 1024
T = 4096
DT = D // 128
CH = 512
NCH = T // CH
NCORE = 8
FMAX = np.float64(np.finfo(np.float32).max)

ENGINES = ("tensor", "vector", "scalar", "sync", "gpsimd")


class K:

    def __init__(self, nc, ctx):
        self.nc = nc
        self.ctx = ctx
        self.q = {e: [] for e in ENGINES}
        self.sems = {}
        self.val = {}
        self.dirty = {e: False for e in ENGINES}
        self.pb = None

    def sem(self, name):
        if name not in self.sems:
            self.sems[name] = self.ctx.enter_context(self.nc.semaphore(name))
            self.val[name] = 0
        return self.sems[name]

    def v(self, name):
        return self.val.get(name, 0)

    def on(self, eng, fn, wait=(), inc=None):
        incs = []
        if inc:
            if isinstance(inc, tuple):
                inc = [inc]
            for name, amt in inc:
                self.sem(name)
                self.val[name] += amt
                incs.append((name, amt))
        wl = [(s, v) for (s, v) in wait if v > 0]
        for name, _v in wl:
            self.sem(name)
        self.q[eng].append((fn, wl, incs))
        self.dirty[eng] = True
        return self.val[incs[0][0]] if incs else None

    def dma(self, eng, out, in_, sem_name, wait=(), **kw):
        def fn(e, out=out, in_=in_, kw=kw):
            return e.dma_start(out=out, in_=in_, **kw)
        return self.on(eng, fn, wait=wait, inc=(sem_name, 16))

    def barrier(self):
        bar = "bar"
        self.sem(bar)
        for e in ("tensor", "vector", "scalar", "gpsimd"):
            if self.dirty[e]:
                if e == "tensor":
                    self.on(e, lambda en: en.nop(), inc=(bar, 1))
                else:
                    self.on(e, lambda en: en.drain(), inc=(bar, 1))
        target = self.val.get(bar, 0)
        dma_waits = [(s, self.val[s]) for s in list(self.val)
                     if s.startswith("d_") and self.val[s] > 0]
        for e in ENGINES:
            self.q[e].append((None, [(bar, target)] + dma_waits, []))
            self.dirty[e] = False

    def run_engine(self, e, eng_obj):
        for fn, waits, incs in self.q[e]:
            for s, v in waits:
                eng_obj.wait_ge(self.sems[s], v)
            if fn is None:
                continue
            inst = fn(eng_obj)
            for s, a in incs:
                inst.then_inc(self.sems[s], a)

    def finalize(self, block):
        block.tensor(lambda e: self.run_engine("tensor", e))
        block.vector(lambda e: self.run_engine("vector", e))
        block.scalar(lambda e: self.run_engine("scalar", e))
        block.sync(lambda e: self.run_engine("sync", e))
        block.gpsimd(lambda e: self.run_engine("gpsimd", e))


def _prep_shared(ip):
    sh = {}
    f32 = np.float32
    sh["w_qkv"] = np.ascontiguousarray(ip["L0_attn_W_qkv"], f32)
    sh["b_qkv_t"] = np.ascontiguousarray(
        np.asarray(ip["L0_attn_b_qkv"], f32).reshape(24, 128).T)
    sh["w_out"] = np.ascontiguousarray(ip["L0_attn_W_out"], f32)
    sh["b_out_t"] = np.ascontiguousarray(
        np.asarray(ip["L0_attn_b_out"], f32).reshape(8, 128).T)
    rb = np.asarray(ip["L0_attn_rel_bias"], f32)
    u = np.arange(3072)[None, :] - 1536
    jp = np.arange(128)[:, None]
    idx = np.clip(u - jp + 1024, 0, 2048)
    sh["strips"] = np.ascontiguousarray(rb[idx].transpose(2, 0, 1))
    sh["rb_lo"] = np.ascontiguousarray(np.broadcast_to(rb[0][None, :], (128, 16)))
    sh["rb_hi"] = np.ascontiguousarray(np.broadcast_to(rb[2048][None, :], (128, 16)))
    sh["w_gate"] = np.ascontiguousarray(ip["L0_ffn_W_gate"], f32)
    sh["b_gate"] = np.ascontiguousarray(np.asarray(ip["L0_ffn_b_gate"], f32).reshape(1, 8))
    sh["moe_w1"] = np.ascontiguousarray(ip["L0_ffn_W1"], f32)
    sh["moe_b1c"] = np.ascontiguousarray(
        np.asarray(ip["L0_ffn_b1"], f32).reshape(8, 8, 128).transpose(2, 0, 1).reshape(128, 64))
    sh["moe_w2"] = np.ascontiguousarray(ip["L0_ffn_W2"], f32)
    sh["moe_b2"] = np.ascontiguousarray(ip["L0_ffn_b2"], f32)
    for nm in ("Wq", "Wk", "Wv", "Wo"):
        sh["p_" + nm] = np.ascontiguousarray(ip["L1_attn_" + nm], f32)
    for nm in ("bq", "bk", "bv", "bo"):
        sh["p_" + nm + "_t"] = np.ascontiguousarray(
            np.asarray(ip["L1_attn_" + nm], f32).reshape(8, 128).T)
    rm_ = np.asarray(ip["L1_attn_random_matrix"], f32)
    sh["p_rm"] = np.ascontiguousarray(np.concatenate([rm_, rm_], axis=0))
    sh["lf_wqkv"] = np.ascontiguousarray(ip["L1_ffn_wqkv"], f32)
    sh["lf_qs"] = np.ascontiguousarray(ip["L1_ffn_q_scale"], f32)
    sh["lf_ks"] = np.ascontiguousarray(ip["L1_ffn_k_scale"], f32)
    sh["lf_wo"] = np.ascontiguousarray(ip["L1_ffn_wo"], f32)
    sh["lf_lng_t"] = np.ascontiguousarray(
        np.asarray(ip["L1_ffn_ln_g"], f32).reshape(2, 8, 128).transpose(0, 2, 1))
    sh["lf_lnb_t"] = np.ascontiguousarray(
        np.asarray(ip["L1_ffn_ln_b"], f32).reshape(2, 8, 128).transpose(0, 2, 1))
    HID = 2730
    HIDP = 2816
    w1 = np.asarray(ip["L1_ffn_w1"], f32)
    w1p = np.zeros((2, 1024, 2 * HIDP), f32)
    w1p[:, :, 0:HID] = w1[:, :, 0:HID]
    w1p[:, :, HIDP:HIDP + HID] = w1[:, :, HID:2 * HID]
    sh["lf_w1p"] = w1p
    b1 = np.asarray(ip["L1_ffn_b1"], f32)
    b1p = np.zeros((2, 2 * HIDP), f32)
    b1p[:, 0:HID] = b1[:, 0:HID]
    b1p[:, HIDP:HIDP + HID] = b1[:, HID:2 * HID]
    sh["lf_b1p_t"] = np.ascontiguousarray(
        b1p.reshape(2, 44, 128).transpose(0, 2, 1))
    w2 = np.asarray(ip["L1_ffn_w2"], f32)
    w2p = np.zeros((2, HIDP, 1024), f32)
    w2p[:, 0:HID, :] = w2
    sh["lf_w2p"] = w2p
    sh["lf_b2_t"] = np.ascontiguousarray(
        np.asarray(ip["L1_ffn_b2"], f32).reshape(2, 8, 128).transpose(0, 2, 1))
    DIM_HEAD, WINDOW = 64, 512
    half = np.arange(0, DIM_HEAD, 2, dtype=np.float64)
    inv_freq = 1.0 / 10000.0 ** (half / DIM_HEAD)
    t = np.arange(T, dtype=np.float64)
    freqs = np.concatenate([t[:, None] * inv_freq[None, :]] * 2, axis=1)
    base = (half + 0.4 * DIM_HEAD) / (1.4 * DIM_HEAD)
    power = (t - T // 2) / float(WINDOW // 2)
    scale = np.concatenate([base[None, :] ** power[:, None]] * 2, axis=1)
    cos, sin = np.cos(freqs), np.sin(freqs)
    sign = np.ones(64); sign[:32] = -1.0
    sh["rot_cq"] = np.ascontiguousarray((cos * scale).T * 8.0, f32)
    sh["rot_sq"] = np.ascontiguousarray((sin * scale * sign[None, :]).T * 8.0, f32)
    sh["rot_ck"] = np.ascontiguousarray((cos / scale).T, f32)
    sh["rot_sk"] = np.ascontiguousarray((sin / scale * sign[None, :]).T, f32)
    il = np.arange(512)[:, None]
    jl = np.arange(1024)[None, :]
    dd = il - jl + 512
    m = np.zeros((512, 1024), f32)
    m[~((dd >= 0) & (dd <= 512))] = -1e30
    sh["la_mask"] = m
    jj = np.arange(128)[:, None]
    uu = np.arange(1408)[None, :]
    mt = np.zeros((128, 1408), f32)
    mt[~((uu >= 384 + jj) & (uu <= 896 + jj))] = -1e30
    sh["la_maskT"] = mt
    sh["ident"] = np.eye(128, dtype=f32)
    em = np.zeros((8, 8, 128), f32)
    for e in range(8):
        em[e, e, :] = 1.0
    sh["emask8"] = np.ascontiguousarray(em.transpose(1, 0, 2).reshape(8, 1024))
    return sh


def st_layernorm(k, tens, sb, src, dst_bf, dst_f32=None):
    pb = k.pb
    x = tens[src]
    mu, ms, rr = sb["row_mu"], sb["row_ms"], sb["row_r"]
    sq_done, cp_done, mmB = [0] * DT, [0] * DT, [0] * DT
    ld_done = [0] * DT
    for d in range(DT):
        b2 = d % 2
        xf, xbf, xsq = (sb["xf"][:, b2 * T:(b2 + 1) * T],
                        sb["xbf"][:, b2 * T:(b2 + 1) * T],
                        sb["xsq"][:, b2 * T:(b2 + 1) * T])
        w = []
        if d >= 2:
            w = [("ac", sq_done[d - 2]), ("ve", cp_done[d - 2])]
        ld_done[d] = k.dma("sync", xf, x[d * 128:(d + 1) * 128, :], "d_a%d" % (d % 2), wait=w)
        wpe = [("pe", mmB[d - 2])] if d >= 2 else []
        sq_done[d] = k.on("scalar", lambda e, xf=xf, xsq=xsq: e.activation(xsq, xf, AF.Square),
                          wait=[("d_a%d" % (d % 2), ld_done[d])] + wpe, inc=("ac", 1))
        cp_done[d] = k.on("vector", lambda e, xf=xf, xbf=xbf: e.tensor_copy(xbf, xf),
                          wait=[("d_a%d" % (d % 2), ld_done[d])] + wpe, inc=("ve", 1))
        for ch in range(NCH):
            row, bank = (ch % 3) * 32, ch // 3
            k.on("tensor", lambda e, xbf=xbf, ch=ch, row=row, bank=bank, d=d: e.matmul(
                pb[bank][row:row + 1, :], sb["onesb"][:], xbf[:, ch * CH:(ch + 1) * CH],
                start=(d == 0), stop=(d == DT - 1)),
                wait=[("ve", cp_done[d])] if ch == 0 else [])
            k.on("tensor", lambda e, xsq=xsq, ch=ch, row=row, bank=bank, d=d: e.matmul(
                pb[3 + bank][row:row + 1, :], sb["onesb"][:], xsq[:, ch * CH:(ch + 1) * CH],
                start=(d == 0), stop=(d == DT - 1)),
                wait=[("ac", sq_done[d])] if ch == 0 else [],
                inc=("pe", 1) if ch == NCH - 1 else None)
        mmB[d] = k.v("pe")
    pe_stats = mmB[DT - 1]
    for ch in range(NCH):
        row, bank = (ch % 3) * 32, ch // 3
        k.on("scalar", lambda e, ch=ch, row=row, bank=bank: e.activation(
            mu[0:1, ch * CH:(ch + 1) * CH], pb[bank][row:row + 1, :], AF.Copy, scale=1.0 / D),
            wait=[("pe", pe_stats)] if ch == 0 else [], inc=("ac", 1))
        k.on("scalar", lambda e, ch=ch, row=row, bank=bank: e.activation(
            ms[0:1, ch * CH:(ch + 1) * CH], pb[3 + bank][row:row + 1, :], AF.Copy, scale=1.0 / D),
            inc=("ac", 1))
    ac_mu = k.v("ac")
    k.on("vector", lambda e: e.tensor_tensor(rr[:], mu[:], mu[:], ALU.mult),
         wait=[("ac", ac_mu)], inc=("ve", 1))
    k.on("vector", lambda e: e.tensor_tensor(rr[:], ms[:], rr[:], ALU.subtract),
         wait=[("ve", k.v("ve"))], inc=("ve", 1))
    ve_var = k.v("ve")
    k.on("scalar", lambda e: e.activation(rr[:], rr[:], AF.Sqrt, bias=sb["c_eps"][0:1, :]),
         wait=[("ve", ve_var)], inc=("ac", 1))
    k.on("vector", lambda e: e.reciprocal(rr[:], rr[:]),
         wait=[("ac", k.v("ac"))], inc=("ve", 1))
    ve_r = k.v("ve")
    ac_r = k.v("ac")
    bc_mm = [0] * NCH
    for ch in range(NCH):
        ba = 4 + 2 * (ch % 2)
        wmm = [("ac", ac_mu), ("ve", ve_r)]
        if ch >= 2:
            wmm.append(("ac", bc_mm[ch - 2]))
        k.on("tensor", lambda e, ch=ch, ba=ba: e.matmul(
            pb[ba][:], sb["ones1f"][0:1, :], mu[0:1, ch * CH:(ch + 1) * CH], start=True, stop=True),
            wait=wmm, inc=("pe", 1))
        pev = k.on("tensor", lambda e, ch=ch, ba=ba: e.matmul(
            pb[ba + 1][:], sb["ones1f"][0:1, :], rr[0:1, ch * CH:(ch + 1) * CH], start=True, stop=True),
            inc=("pe", 1))
        k.on("scalar", lambda e, ch=ch, ba=ba: e.copy(
            sb["mub"][:, ch * CH:(ch + 1) * CH], pb[ba][:]),
            wait=[("pe", pev)], inc=("ac", 1))
        bc_mm[ch] = k.on("scalar", lambda e, ch=ch, ba=ba: e.copy(
            sb["rb"][:, ch * CH:(ch + 1) * CH], pb[ba + 1][:]), inc=("ac", 1))
    ac_bc = k.v("ac")
    tt_done, st_done = [0] * DT, [0] * DT
    for d in range(DT):
        b2 = d % 2
        xf2 = sb["xf"][:, b2 * T:(b2 + 1) * T]
        t0 = sb["t0"][:, b2 * T:(b2 + 1) * T]
        tb = sb["tb"][:, b2 * T:(b2 + 1) * T]
        w = [("ve", cp_done[DT - 1])]
        if d >= 2:
            w.append(("ve", tt_done[d - 2]))
        ld = k.dma("sync", xf2, x[d * 128:(d + 1) * 128, :], "d_a%d" % (d % 2), wait=w)
        wst = [("d_so%d" % (d % 2), st_done[d - 2])] if d >= 2 else []
        k.on("vector", lambda e, xf2=xf2, t0=t0: e.tensor_tensor(t0, xf2, sb["mub"][:], ALU.subtract),
             wait=[("d_a%d" % (d % 2), ld), ("ac", ac_bc)] + wst, inc=("ve", 1))
        if dst_f32 is not None:
            k.on("vector", lambda e, t0=t0: e.tensor_tensor(t0, t0, sb["rb"][:], ALU.mult),
                 wait=[("ve", k.v("ve"))], inc=("ve", 1))
            vv = k.on("vector", lambda e, t0=t0, tb=tb: e.tensor_copy(tb, t0),
                      wait=[("ve", k.v("ve"))], inc=("ve", 1))
            k.dma("gpsimd", tens[dst_f32][d * 128:(d + 1) * 128, :], t0, "d_sp%d" % (d % 2),
                  wait=[("ve", vv)])
        else:
            vv = k.on("vector", lambda e, t0=t0, tb=tb: e.tensor_tensor(tb, t0, sb["rb"][:], ALU.mult),
                      wait=[("ve", k.v("ve"))], inc=("ve", 1))
        tt_done[d] = vv
        st_done[d] = k.dma("gpsimd", tens[dst_bf][d * 128:(d + 1) * 128, :], tb, "d_so%d" % (d % 2),
                           wait=[("ve", vv)])
    k.barrier()


LN_SB = [
    ("xf", (128, 2 * T), F32), ("xbf", (128, 2 * T), BF16), ("xsq", (128, 2 * T), BF16),
    ("t0", (128, 2 * T), F32), ("tb", (128, 2 * T), BF16),
    ("mub", (128, T), F32), ("rb", (128, T), F32),
    ("row_mu", (1, T), F32), ("row_ms", (1, T), F32), ("row_r", (1, T), F32),
]




def st_l0_attn(k, tens, sb, xn_src):
    pb = k.pb
    xn = sb["xn_sb"]
    for d in range(DT):
        k.dma("sync", xn[:, d * T:(d + 1) * T], tens[xn_src][d * 128:(d + 1) * 128, :], "d_xn")
    ld_xn = k.v("d_xn")
    k.dma("sync", sb["bqkv"][:], tens["b_qkv_t"], "d_c")
    k.dma("sync", sb["rblo"][:], tens["rb_lo"], "d_c")
    k.dma("sync", sb["rbhi"][:], tens["rb_hi"], "d_c")
    ld_c = k.v("d_c")
    k.on("vector", lambda e: e.memset(sb["onesrow"][:], 1.0), inc=("ve", 1))
    k.on("vector", lambda e: e.memset(sb["ka0"][64:65, :], 1.0), inc=("ve", 1))
    k.on("vector", lambda e: e.memset(sb["ka1"][64:65, :], 1.0), inc=("ve", 1))
    k.on("vector", lambda e: e.memset(sb["vp"][:], 0.0), inc=("ve", 1))
    for tt in range(32):
        k.on("vector", lambda e, tt=tt: e.memset(sb["vp"][:, 256 * tt + 64:256 * tt + 65], 1.0))
        k.on("vector", lambda e, tt=tt: e.memset(sb["vp"][:, 256 * tt + 192:256 * tt + 193], 1.0))
    k.on("vector", lambda e: e.tensor_scalar(sb["bq8"][:], sb["bqkv"][:, 0:8], 0.125, None, ALU.mult),
         wait=[("d_c", ld_c)], inc=("ve", 1))
    k.barrier()

    wvq = tens["w_qkv"]
    for p in range(8):
        k.barrier()
        for nm, c0 in (("wq_sb", 128 * p), ("wk_sb", 1024 + 128 * p), ("wv_sb", 2048 + 128 * p)):
            src = wvq[:, c0:c0 + 128].rearrange("(a p) c -> p a c", p=128)
            k.dma("gpsimd", sb[nm][:], src, "d_w")
        k.dma("gpsimd", sb["bv_row"][0:1, :], tens["b_qkv_raw"][0:1, 2048 + 128 * p:2176 + 128 * p], "d_w")
        ld_w = k.v("d_w")
        for which, wsb, scale in (("q", "wq_sb", 0.125), ("kk", "wk_sb", 1.0)):
            for ch in range(NCH):
                bank = 6 + ch % 2
                w = [("d_w", ld_w), ("d_xn", ld_xn), ("ac", k.v("ac"))]
                pev = None
                for dm in range(DT):
                    pev = k.on("tensor", lambda e, ch=ch, dm=dm, bank=bank, wsb=wsb: e.matmul(
                        pb[bank][:], sb[wsb][:, 128 * dm:128 * dm + 128],
                        xn[:, T * dm + CH * ch: T * dm + CH * (ch + 1)],
                        start=(dm == 0), stop=(dm == DT - 1)),
                        wait=w if dm == 0 else [],
                        inc=("pe", 1) if dm == DT - 1 else None)
                for hl in range(2):
                    dstt = sb[("qa%d" if which == "q" else "ka%d") % hl]
                    if which == "q":
                        bias_ap = sb["bq8"][64 * hl:64 * hl + 64, p:p + 1]
                    else:
                        bias_ap = sb["bqkv"][64 * hl:64 * hl + 64, 8 + p:9 + p]
                    k.on("scalar", lambda e, dstt=dstt, ch=ch, bank=bank, hl=hl, bias_ap=bias_ap, scale=scale: e.activation(
                        dstt[0:64, CH * ch:CH * (ch + 1)], pb[bank][64 * hl:64 * hl + 64, :],
                        AF.Identity, bias=bias_ap, scale=scale),
                        wait=[("pe", pev), ("ve", k.v("ve"))] if hl == 0 else [], inc=("ac", 1))
        for tt in range(32):
            bank = 6 + tt % 2
            w = [("d_w", ld_w), ("d_xn", ld_xn), ("ac", k.v("ac"))]
            for dm in range(DT):
                k.on("tensor", lambda e, tt=tt, dm=dm, bank=bank: e.matmul(
                    pb[bank][:, 0:128], xn[:, T * dm + 128 * tt: T * dm + 128 * (tt + 1)],
                    sb["wv_sb"][:, 128 * dm:128 * dm + 128],
                    start=(dm == 0), stop=False),
                    wait=w if dm == 0 else [])
            pev = k.on("tensor", lambda e, tt=tt, bank=bank: e.matmul(
                pb[bank][:, 0:128], sb["onesrow"][0:1, :], sb["bv_row"][0:1, :],
                start=False, stop=True), inc=("pe", 1))
            for hl in range(2):
                k.on("scalar", lambda e, tt=tt, bank=bank, hl=hl: e.copy(
                    sb["vp"][:, 256 * tt + 128 * hl:256 * tt + 128 * hl + 64],
                    pb[bank][:, 64 * hl:64 * hl + 64]),
                    wait=[("pe", pev)] if hl == 0 else [], inc=("ac", 1))
        setup_done = [None, None]
        strip_ld = [0, 0]
        for hl in range(2):
            h = 2 * p + hl
            qa, ka = sb["qa%d" % hl], sb["ka%d" % hl]
            stripb = sb["strip%d" % hl]
            strip_ld[hl] = k.dma("gpsimd", stripb[:], tens["strips"][h], "d_sp")
            k.on("vector", lambda e, stripb=stripb: e.reduce_max(sb["scol"][:], stripb[:], axis=AX.X),
                 wait=[("d_sp", strip_ld[hl]), ("ve", k.v("ve"))], inc=("ve", 1))
            k.dma("gpsimd", sb["srow"][0:1, :], sb["scol"][:, 0:1], "d_sr",
                  wait=[("ve", k.v("ve"))], allow_slow_non_contiguous=True)
            k.on("vector", lambda e: e.reduce_max(sb["bmax"][0:1, 0:1], sb["srow"][0:1, :], axis=AX.X),
                 wait=[("d_sr", k.v("d_sr"))], inc=("ve", 1))
            wsq = [("pe", k.v("pe")), ("ve", k.v("ve"))]
            k.on("scalar", lambda e, qa=qa: e.activation(sb["sq_sb"][0:64, :], qa[0:64, :], AF.Square),
                 wait=wsq, inc=("ac", 1))
            k.on("scalar", lambda e, ka=ka: e.activation(sb["sq_sb"][64:128, :], ka[0:64, :], AF.Square),
                 inc=("ac", 1))
            ac_sq = k.v("ac")
            for ch in range(NCH):
                row = (ch % 3) * 32
                bq, bk = 5 + ch // 3, 2 + ch // 3
                k.on("tensor", lambda e, ch=ch, row=row, bq=bq: e.matmul(
                    pb[bq][row:row + 1, :], sb["onesb"][0:64, :], sb["sq_sb"][0:64, CH * ch:CH * (ch + 1)],
                    start=True, stop=True),
                    wait=[("ac", ac_sq), ("ve", k.v("ve"))] if ch == 0 else [])
                k.on("tensor", lambda e, ch=ch, row=row, bk=bk: e.matmul(
                    pb[bk][row:row + 1, :], sb["onesb"][64:128, :], sb["sq_sb"][64:128, CH * ch:CH * (ch + 1)],
                    start=True, stop=True),
                    inc=("pe", 1) if ch == NCH - 1 else None)
            pev = k.v("pe")
            for ch in range(NCH):
                row = (ch % 3) * 32
                bq, bk = 5 + ch // 3, 2 + ch // 3
                k.on("scalar", lambda e, ch=ch, row=row, bq=bq: e.copy(
                    sb["rows_sb"][0:1, CH * ch:CH * (ch + 1)], pb[bq][row:row + 1, :]),
                    wait=[("pe", pev), ("ve", k.v("ve"))] if ch == 0 else [], inc=("ac", 1))
                k.on("scalar", lambda e, ch=ch, row=row, bk=bk: e.copy(
                    sb["rows_sb"][32:33, CH * ch:CH * (ch + 1)], pb[bk][row:row + 1, :]),
                    inc=("ac", 1))
            ac_n = k.v("ac")
            k.on("scalar", lambda e: e.activation(sb["rows_sb"][0:1, :], sb["rows_sb"][0:1, :], AF.Sqrt),
                 wait=[("ac", ac_n)], inc=("ac", 1))
            k.on("vector", lambda e: e.reduce_max(sb["kmax"][0:1, 0:1], sb["rows_sb"][32:33, :], axis=AX.X),
                 wait=[("ac", ac_n)], inc=("ve", 1))
            k.on("scalar", lambda e: e.activation(sb["kmax"][0:1, 0:1], sb["kmax"][0:1, 0:1], AF.Sqrt),
                 wait=[("ve", k.v("ve"))], inc=("ac", 1))
            k.on("vector", lambda e: e.tensor_scalar(
                sb["rows_sb"][64:65, :], sb["rows_sb"][0:1, :], sb["kmax"][0:1, 0:1], sb["bmax"][0:1, 0:1],
                ALU.mult, ALU.add), wait=[("ac", k.v("ac")), ("ve", k.v("ve"))], inc=("ve", 1))
            k.on("vector", lambda e, qa=qa: e.tensor_scalar(
                qa[64:65, :], sb["rows_sb"][64:65, :], -1.0, None, ALU.mult),
                wait=[("ve", k.v("ve"))], inc=("ve", 1))
            setup_done[hl] = k.v("ve")
        setup_ac = k.v("ac")
        prev_tail = None
        prev_esb = None
        prev_evac = None
        for hl in range(2):
            h = 2 * p + hl
            qa, ka = sb["qa%d" % hl], sb["ka%d" % hl]
            stripb = sb["strip%d" % hl]
            exp_cnt, dve_cnt, ev_cnt, bankfree = {}, {}, {}, {}
            evac_ve, tt_done, ot_st = {}, {}, {}
            pend = []
            recip_done = {}
            pdv_todo = []

            def emit_ev(qc, jt, hl=hl):
                wev = [("ac", exp_cnt[(qc, jt)])]
                if jt == 0:
                    if (qc - 2) in evac_ve:
                        wev.append(("ve", evac_ve[qc - 2]))
                    elif prev_evac is not None and qc < 2:
                        wev.append(("ve", prev_evac[6 + qc]))
                ev_cnt[(qc, jt)] = k.on("tensor", lambda e, jt=jt, qc=qc, hl=hl: e.matmul(
                    pb[2 + qc % 2][:], sb["vp"][:, 256 * jt + 128 * hl:256 * jt + 128 * (hl + 1)],
                    sb["e_sb"][:, CH * (jt % 4):CH * (jt % 4 + 1)],
                    start=(jt == 0), stop=(jt == 31)),
                    wait=wev, inc=("pe", 1))
                if jt == 31:
                    k.on("vector", lambda e, qc=qc: e.tensor_copy(
                        sb["den"][0:1, :], pb[2 + qc % 2][64:65, :]),
                        wait=[("pe", ev_cnt[(qc, 31)])], inc=("ve", 1))
                    k.on("vector", lambda e, qc=qc: e.tensor_copy(
                        sb["onum"][:], pb[2 + qc % 2][:]),
                        inc=("ve", 1))
                    evac_ve[qc] = k.v("ve")
                    k.on("vector", lambda e: e.reciprocal(sb["den"][0:1, :], sb["den"][0:1, :]),
                         inc=("ve", 1))
                    recip_done[qc] = k.v("ve")
                    pdv_todo.append(qc)

            def emit_pdv(hl=hl):
                qc = pdv_todo.pop(0)
                db = 4 + qc % 2
                wd_p = [("ve", recip_done[qc])]
                if hl == 0 and qc < 2:
                    wd_p.append(("ac", setup_ac))
                pdv = k.on("tensor", lambda e, db=db: e.matmul(
                    pb[db][:], sb["ones1f"][0:1, :], sb["den"][0:1, :],
                    start=True, stop=True), wait=wd_p, inc=("pe", 1))
                wtt = [("pe", pdv)]
                if (qc - 2) in ot_st:
                    wtt.append(("d_ot%d" % (qc % 2), ot_st[qc - 2]))
                tt_done[qc] = k.on("vector", lambda e, qc=qc, db=db: e.tensor_tensor(
                    sb["ot_tmp"][:, CH * (qc % 2):CH * (qc % 2 + 1)],
                    sb["onum"][:], pb[db][:], ALU.mult),
                    wait=wtt, inc=("ve", 1))
                ot_st[qc] = k.dma("gpsimd", tens["otT"][64 * h:64 * h + 64, CH * qc:CH * (qc + 1)],
                                  sb["ot_tmp"][0:64, CH * (qc % 2):CH * (qc % 2 + 1)], "d_ot%d" % (qc % 2),
                                  wait=[("ve", tt_done[qc])])

            for qc in range(NCH):
                for jt in range(32):
                    sbk = jt % 2
                    m0 = 512 * qc - 128 * jt + 1536
                    inband = (0 <= m0 <= 3072 - 512)
                    if jt == 4 and pdv_todo:
                        emit_pdv()
                    key2 = None
                    if jt >= 2:
                        key2 = (qc, jt - 2)
                    elif qc >= 1:
                        key2 = (qc - 1, 30 + jt)
                    w = []
                    if key2 is not None:
                        w.append(bankfree[key2])
                    elif prev_tail is not None:
                        w.append(prev_tail[jt])
                    if qc == 0 and jt == 0:
                        w.append(("ve", setup_done[hl]))
                    k.on("tensor", lambda e, qa=qa, ka=ka, jt=jt, qc=qc, sbk=sbk: e.matmul(
                        pb[sbk][:], ka[:, 128 * jt:128 * (jt + 1)], qa[:, CH * qc:CH * (qc + 1)],
                        start=True, stop=True), wait=w, inc=("pe", 1))
                    pev = k.v("pe")
                    eo = sb["e_sb"][:, CH * (jt % 4):CH * (jt % 4 + 1)]
                    key4 = None
                    if jt >= 4:
                        key4 = (qc, jt - 4)
                    elif qc >= 1:
                        key4 = (qc - 1, 28 + jt)
                    if inband:
                        ep = sb["e_pre"][:, CH * (jt % 4):CH * (jt % 4 + 1)]
                        wv = [("pe", pev)]
                        if key4 in exp_cnt:
                            wv.append(("ac", exp_cnt[key4]))
                        elif prev_esb is not None and key4 is None:
                            wv.append(("ac", prev_esb[28 + jt]))
                        dve_cnt[(qc, jt)] = k.on("vector", lambda e, ep=ep, sbk=sbk, stripb=stripb, m0=m0: e.tensor_tensor(
                            ep, pb[sbk][:], stripb[:, m0:m0 + 512], ALU.add),
                            wait=wv, inc=("ve", 1))
                        bankfree[(qc, jt)] = ("ve", dve_cnt[(qc, jt)])
                        exp_cnt[(qc, jt)] = k.on("scalar", lambda e, eo=eo, ep=ep: e.activation(
                            eo, ep, AF.Exp),
                            wait=[("ve", dve_cnt[(qc, jt)])], inc=("ac", 1))
                    else:
                        bcol = sb["rbhi" if m0 > 0 else "rblo"][:, h:h + 1]
                        exp_cnt[(qc, jt)] = k.on("scalar", lambda e, eo=eo, sbk=sbk, bcol=bcol: e.activation(
                            eo, pb[sbk][:], AF.Exp, bias=bcol),
                            wait=[("pe", pev)], inc=("ac", 1))
                        bankfree[(qc, jt)] = ("ac", exp_cnt[(qc, jt)])
                    if len(pend) == 3:
                        emit_ev(*pend.pop(0))
                    pend.append((qc, jt))
            while pend:
                emit_ev(*pend.pop(0))
            while pdv_todo:
                emit_pdv()
            prev_tail = {jt: bankfree[(7, 30 + jt)] for jt in range(2)}
            prev_esb = {28 + jt: exp_cnt[(7, 28 + jt)] for jt in range(4)}
            prev_evac = evac_ve
    k.barrier()


def st_l0_outproj(k, tens, sb, x_src, dst):
    k.dma("sync", sb["bout"][:], tens["b_out_t"], "d_c")
    ld_bo = k.v("d_c")
    k.on("vector", lambda e: e.memset(sb["onesrow"][:], 1.0),
         wait=[("d_c", ld_bo)], inc=("ve", 1))
    k.barrier()
    _out_proj(k, tens, sb, "w_out", sb["bout"], x_src, dst, "otT")


L0A_SB = [
    ("xn_sb", (128, 8 * T), BF16),
    ("bqkv", (128, 24), F32), ("bq8", (128, 8), F32),
    ("rblo", (128, 16), F32), ("rbhi", (128, 16), F32),
    ("onesrow", (1, 128), BF16),
    ("wq_sb", (128, 1024), BF16), ("wk_sb", (128, 1024), BF16), ("wv_sb", (128, 1024), BF16),
    ("bv_row", (1, 128), BF16),
    ("qa0", (65, T), BF16), ("qa1", (65, T), BF16),
    ("ka0", (65, T), BF16), ("ka1", (65, T), BF16),
    ("vp", (128, 32 * 256), BF16),
    ("strip0", (128, 3072), BF16), ("strip1", (128, 3072), BF16),
    ("scol", (128, 1), F32), ("srow", (1, 128), F32),
    ("bmax", (1, 1), F32), ("kmax", (1, 1), F32),
    ("sq_sb", (128, T), BF16),
    ("rows_sb", (65, T), F32),
    ("e_sb", (128, 4 * CH), BF16), ("e_pre", (128, 4 * CH), F32),
    ("den", (1, CH), F32), ("onum", (128, CH), F32),
    ("ot_tmp", (128, 2 * CH), BF16),
]

L0P_SB = [
    ("bout", (128, 8), F32), ("onesrow", (1, 128), BF16),
    ("wo_full", (128, 8 * 1024), BF16),
    ("otb", (128, 8 * CH), BF16), ("xcb", (128, 8 * CH), F32), ("rescb", (128, 8 * CH), F32),
]



def st_l0_ffn(k, tens, sb, xn_src, x1p_src, dst):
    pb = k.pb
    CK = 1024
    NC2 = T // CK
    k.dma("gpsimd", tens["moe_w1bf"].rearrange("e a b -> (e a b)"),
          tens["moe_w1"].rearrange("e a b -> (e a b)"), "d_cw")
    k.dma("gpsimd", tens["moe_w2bf"].rearrange("e a b -> (e a b)"),
          tens["moe_w2"].rearrange("e a b -> (e a b)"), "d_cw")
    k.dma("sync", sb["wg_sb"][:], tens["w_gate"].rearrange("(a p) c -> p a c", p=128), "d_c2")
    k.dma("sync", sb["bgate"][0:1, :], tens["b_gate"], "d_c2")
    k.dma("sync", sb["b1c"][:], tens["moe_b1c"], "d_c2")
    k.dma("sync", sb["b2sb"][:], tens["moe_b2"], "d_c2")
    k.dma("sync", sb["id_f32"][:], tens["ident"], "d_c2")
    k.dma("gpsimd", sb["id_bf"][:], tens["ident"], "d_cg")
    k.dma("sync", sb["emask"][:], tens["emask8"], "d_c2")
    ld_c = k.v("d_c2")
    _ = k.v("d_c3")
    k.on("vector", lambda e: e.memset(sb["onesrow"][:], 1.0), inc=("ve", 1))
    k.barrier()
    cw = k.v("d_cw")

    for c in range(NC2):
        k.barrier()
        col0 = CK * c
        ldxn = k.dma("sync", sb["xnb"][:],
                     tens[xn_src][:, col0:col0 + CK].rearrange("(a p) c -> p a c", p=128), "d_xnb")
        ldxnf = k.dma("sync", sb["xnfb"][:],
                      tens["xn_f32"][:, col0:col0 + CK].rearrange("(a p) c -> p a c", p=128), "d_xnf")
        ldx1 = k.dma("sync", sb["x1b"][:],
                     tens[x1p_src][:, col0:col0 + CK].rearrange("(a p) c -> p a c", p=128), "d_x1b")
        tr_ac = 0
        for tl in range(8):
            bank = 6 + tl % 2
            w = [("d_c2", ld_c), ("d_xnf", ldxnf), ("ac", k.v("ac"))]
            for dm in range(DT):
                k.on("tensor", lambda e, tl=tl, dm=dm, bank=bank: e.matmul(
                    pb[bank][:, 0:8], sb["xnfb"][:, CK * dm + 128 * tl:CK * dm + 128 * (tl + 1)],
                    sb["wg_sb"][:, 8 * dm:8 * dm + 8],
                    start=(dm == 0), stop=False), wait=w if dm == 0 else [])
            pev = k.on("tensor", lambda e, bank=bank: e.matmul(
                pb[bank][:, 0:8], sb["ones1f"][0:1, :], sb["bgate"][0:1, :],
                start=False, stop=True), inc=("pe", 1))
            k.on("scalar", lambda e, tl=tl, bank=bank: e.copy(
                sb["scr"][:, 8 * tl:8 * tl + 8], pb[bank][:, 0:8]),
                wait=[("pe", pev)], inc=("ac", 1))
            ac_s = k.v("ac")
            s8 = sb["scr"][:, 8 * tl:8 * tl + 8]
            k.on("vector", lambda e, s8=s8: e.reduce_max(sb["gt_m1"][:], s8, axis=AX.X),
                 wait=[("ac", ac_s), ("ve", k.v("ve"))], inc=("ve", 1))
            k.on("vector", lambda e, s8=s8: e.tensor_scalar(
                sb["keep"][:], s8, sb["gt_m1"][:], None, ALU.is_equal),
                wait=[("ve", k.v("ve"))], inc=("ve", 1))
            k.on("vector", lambda e: e.reduce_sum(sb["gt_cnt"][:], sb["keep"][:], axis=AX.X),
                 wait=[("ve", k.v("ve"))], inc=("ve", 1))
            k.on("vector", lambda e, s8=s8: e.scalar_tensor_tensor(
                sb["tmp8"][:], sb["keep"][:], -3.0e38, s8, ALU.mult, ALU.add),
                wait=[("ve", k.v("ve"))], inc=("ve", 1))
            k.on("vector", lambda e: e.reduce_max(sb["gt_m2"][:], sb["tmp8"][:], axis=AX.X),
                 wait=[("ve", k.v("ve"))], inc=("ve", 1))
            k.on("vector", lambda e: e.tensor_scalar(
                sb["gt_c2"][:], sb["gt_cnt"][:], 2.0, None, ALU.is_ge),
                wait=[("ve", k.v("ve"))], inc=("ve", 1))
            k.on("vector", lambda e: e.tensor_tensor(
                sb["gt_d"][:], sb["gt_m1"][:], sb["gt_m2"][:], ALU.subtract),
                wait=[("ve", k.v("ve"))], inc=("ve", 1))
            k.on("vector", lambda e: e.tensor_tensor(
                sb["gt_d"][:], sb["gt_d"][:], sb["gt_c2"][:], ALU.mult),
                wait=[("ve", k.v("ve"))], inc=("ve", 1))
            k.on("vector", lambda e: e.tensor_tensor(
                sb["gt_thr"][:], sb["gt_m2"][:], sb["gt_d"][:], ALU.add),
                wait=[("ve", k.v("ve"))], inc=("ve", 1))
            k.on("vector", lambda e, s8=s8: e.tensor_scalar(
                sb["keep"][:], s8, sb["gt_thr"][:], None, ALU.is_ge),
                wait=[("ve", k.v("ve"))], inc=("ve", 1))
            vv = k.on("vector", lambda e, s8=s8: e.tensor_tensor(
                sb["wbf"][:], s8, sb["keep"][:], ALU.mult),
                wait=[("ve", k.v("ve"))], inc=("ve", 1))
            ptr = k.on("tensor", lambda e: e.transpose(
                pb[5][0:8, 0:128], sb["wbf"][:], sb["id_f32"][:]),
                wait=[("ve", vv), ("ac", tr_ac), ("d_c2", ld_c)], inc=("pe", 1))
            tr_ac = k.on("scalar", lambda e, tl=tl: e.copy(
                sb["wT"][0:8, 128 * tl:128 * (tl + 1)], pb[5][0:8, 0:128]),
                wait=[("pe", ptr)], inc=("ac", 1))
        for ex in range(8):
            wld = [("pe", k.v("pe"))] if ex > 0 else [("d_cw", cw)]
            ldw1 = k.dma("gpsimd", sb["w1sb"][:],
                         tens["moe_w1bf"][ex].rearrange("(a p) c -> p a c", p=128), "d_w1",
                         wait=wld)
            ldw2 = k.dma("gpsimd", sb["w2sb"][:],
                         tens["moe_w2bf"][ex].rearrange("(a p) c -> p a c", p=128), "d_w2",
                         wait=wld)
            for sub in range(2):
                pwb = k.on("tensor", lambda e, ex=ex, sub=sub: e.matmul(
                    pb[4][:, 0:512], sb["emask"][0:8, 128 * ex:128 * (ex + 1)],
                    sb["wT"][0:8, 512 * sub:512 * (sub + 1)],
                    start=True, stop=True),
                    wait=[("ac", k.v("ac")), ("ve", k.v("ve"))], inc=("pe", 1))
                k.on("scalar", lambda e, sub=sub: e.copy(
                    sb["wb_sb"][:, 512 * sub:512 * (sub + 1)], pb[4][:, 0:512]),
                    wait=[("pe", pwb)], inc=("ac", 1))
            ac_wb = k.v("ac")
            for ht in range(DT):
                for sub in range(2):
                    bank = (2 * ht + sub) % 2
                    w = [("d_w1", ldw1), ("d_xnb", ldxn), ("ac", k.v("ac")), ("ve", k.v("ve"))]
                    for dm in range(DT):
                        k.on("tensor", lambda e, ht=ht, sub=sub, dm=dm, bank=bank: e.matmul(
                            pb[bank][:], sb["w1sb"][:, CK * dm + 128 * ht:CK * dm + 128 * (ht + 1)],
                            sb["xnb"][:, CK * dm + 512 * sub:CK * dm + 512 * (sub + 1)],
                            start=(dm == 0), stop=(dm == DT - 1)),
                            wait=w if dm == 0 else [],
                            inc=("pe", 1) if dm == DT - 1 else None)
                    pev = k.v("pe")
                    k.on("scalar", lambda e, ht=ht, sub=sub, bank=bank, ex=ex: e.activation(
                        sb["hsc"][:, CK * ht + 512 * sub:CK * ht + 512 * (sub + 1)],
                        pb[bank][:], AF.Relu, bias=sb["b1c"][:, 8 * ex + ht:8 * ex + ht + 1]),
                        wait=[("pe", pev), ("ac", ac_wb)], inc=("ac", 1))
                    k.on("vector", lambda e, ht=ht, sub=sub: e.tensor_tensor(
                        sb["hsc"][:, CK * ht + 512 * sub:CK * ht + 512 * (sub + 1)],
                        sb["hsc"][:, CK * ht + 512 * sub:CK * ht + 512 * (sub + 1)],
                        sb["wb_sb"][:, 512 * sub:512 * (sub + 1)], ALU.mult),
                        wait=[("ac", k.v("ac")), ("ve", k.v("ve"))], inc=("ve", 1))
            ve_h = k.v("ve")
            for do in range(DT):
                for sub in range(2):
                    bank = 2 + (2 * do + sub) % 2
                    w = [("d_w2", ldw2), ("ve", k.v("ve"))]
                    first = True
                    if ex == 0:
                        k.on("tensor", lambda e, do=do, sub=sub, bank=bank: e.matmul(
                            pb[bank][:], sb["b2sb"][0:8, 128 * do:128 * (do + 1)],
                            sb["wT"][0:8, 512 * sub:512 * (sub + 1)],
                            start=True, stop=False), wait=w)
                        first = False
                    for ht in range(DT):
                        k.on("tensor", lambda e, do=do, sub=sub, ht=ht, bank=bank, first=first: e.matmul(
                            pb[bank][:], sb["w2sb"][:, CK * ht + 128 * do:CK * ht + 128 * (do + 1)],
                            sb["hsc"][:, CK * ht + 512 * sub:CK * ht + 512 * (sub + 1)],
                            start=(first and ht == 0), stop=(ht == DT - 1)),
                            wait=w if (first and ht == 0) else [],
                            inc=("pe", 1) if ht == DT - 1 else None)
                        first = first and False
                    pev = k.v("pe")
                    aslc = sb["acc"][:, CK * do + 512 * sub:CK * do + 512 * (sub + 1)]
                    if ex == 0:
                        k.on("vector", lambda e, do=do, sub=sub, bank=bank, aslc=aslc: e.tensor_tensor(
                            aslc, pb[bank][:],
                            sb["x1b"][:, CK * do + 512 * sub:CK * do + 512 * (sub + 1)], ALU.add),
                            wait=[("pe", pev), ("d_x1b", ldx1), ("ve", k.v("ve"))], inc=("ve", 1))
                    else:
                        k.on("vector", lambda e, bank=bank, aslc=aslc: e.tensor_tensor(
                            aslc, pb[bank][:], aslc, ALU.add),
                            wait=[("pe", pev), ("ve", k.v("ve"))], inc=("ve", 1))
        k.dma("gpsimd", tens[dst][:, col0:col0 + CK].rearrange("(a p) c -> p a c", p=128),
              sb["acc"][:], "d_st1", wait=[("ve", k.v("ve"))])
    k.barrier()


MOE_SB = [
    ("xnb", (128, 8 * 1024), BF16), ("hsc", (128, 8 * 1024), BF16),
    ("acc", (128, 8 * 1024), F32), ("x1b", (128, 8 * 1024), F32),
    ("w1sb", (128, 8 * 1024), BF16), ("w2sb", (128, 8 * 1024), BF16),
    ("wg_sb", (128, 64), F32), ("bgate", (1, 8), F32),
    ("b1c", (128, 64), F32), ("b2sb", (8, 1024), F32),
    ("xnfb", (128, 8 * 1024), F32), ("id_f32", (128, 128), F32),
    ("scr", (128, 64), F32), ("keep", (128, 8), F32), ("tmp8", (128, 8), F32),
    ("gt_m1", (128, 1), F32), ("gt_m2", (128, 1), F32), ("gt_cnt", (128, 1), F32),
    ("gt_c2", (128, 1), F32), ("gt_d", (128, 1), F32), ("gt_thr", (128, 1), F32),
    ("wbf", (128, 8), F32), ("wT", (8, 1024), F32), ("wb_sb", (128, 1024), F32),
    ("id_bf", (128, 128), BF16), ("onesrow", (1, 128), BF16),
    ("emask", (8, 1024), F32),
]


def build_l0ffn_launch(kinds):
    nc = bass.Bass("TRN2", target_bir_lowering=False, debug=False)
    decls = [
        ("xn_bf", (D, T), BF16, "ExternalInput"),
        ("xn_f32", (D, T), F32, "ExternalInput"),
        ("x1p", (D, T), F32, "ExternalInput"),
        ("w_gate", (D, 8), F32, "ExternalInput"),
        ("b_gate", (1, 8), F32, "ExternalInput"),
        ("moe_w1", (8, D, D), F32, "ExternalInput"),
        ("moe_w2", (8, D, D), F32, "ExternalInput"),
        ("moe_b1c", (128, 64), F32, "ExternalInput"),
        ("moe_b2", (8, D), F32, "ExternalInput"),
        ("ident", (128, 128), F32, "ExternalInput"),
        ("emask8", (8, 1024), F32, "ExternalInput"),
        ("x1", (D, T), F32, "ExternalOutput"),
        ("moe_w1bf", (8, D, D), BF16, "Internal"),
        ("moe_w2bf", (8, D, D), BF16, "Internal"),
    ]
    tens = {}
    for name, shape, dt, kind in decls:
        tens[name] = nc.dram_tensor(name, list(shape), dt, kind=kind).ap()
    ctx = ExitStack()
    with ctx:
        k, sb = _mk_k(nc, ctx, MOE_SB)
        st_l0_ffn(k, tens, sb, "xn_bf", "x1p", "x1")
        _finish(nc, k)
    return nc




def _out_proj(k, tens, sb, w_dram, bcol, x_src, dst, ot_name):
    pb = k.pb
    wsrc = tens[w_dram].rearrange("(a p) (b c) -> p a b c", p=128, c=128)
    ldwo = k.dma("gpsimd", sb["wo_full"][:], wsrc, "d_wf")
    mm_done = [0] * NCH
    st_done = [0] * NCH
    for ch in range(NCH):
        w = [("pe", mm_done[ch - 1])] if ch >= 1 else []
        ldo = k.dma("sync", sb["otb"][:],
                    tens[ot_name][:, CH * ch:CH * (ch + 1)].rearrange("(a p) c -> p a c", p=128),
                    "d_ob", wait=w)
        wx = [("ve", k.v("ve"))] if ch >= 1 else []
        ldx = k.dma("sync", sb["xcb"][:],
                    tens[x_src][:, CH * ch:CH * (ch + 1)].rearrange("(a p) c -> p a c", p=128),
                    "d_xc", wait=wx)
        for dot in range(DT):
            bank = 6 + dot % 2
            w = [("d_wf", ldwo), ("d_ob", ldo), ("ve", k.v("ve"))]
            for dm in range(DT):
                k.on("tensor", lambda e, ch=ch, dot=dot, dm=dm, bank=bank: e.matmul(
                    pb[bank][:], sb["wo_full"][:, 1024 * dm + 128 * dot:1024 * dm + 128 * (dot + 1)],
                    sb["otb"][:, CH * dm:CH * (dm + 1)],
                    start=(dm == 0), stop=(dm == DT - 1)), wait=w if dm == 0 else [],
                    inc=("pe", 1) if dm == DT - 1 else None)
            pev = k.v("pe")
            wst = [("pe", pev), ("d_xc", ldx)]
            if dot == 0 and ch >= 1:
                wst.append(("d_rs", st_done[ch - 1]))
            k.on("vector", lambda e, dot=dot, bank=bank, bcol=bcol: e.scalar_tensor_tensor(
                sb["rescb"][:, CH * dot:CH * (dot + 1)],
                pb[bank][:], bcol[:, dot:dot + 1],
                sb["xcb"][:, CH * dot:CH * (dot + 1)],
                ALU.add, ALU.add), wait=wst, inc=("ve", 1))
        mm_done[ch] = k.v("pe")
        st_done[ch] = k.dma("gpsimd",
                            tens[dst][:, CH * ch:CH * (ch + 1)].rearrange("(a p) c -> p a c", p=128),
                            sb["rescb"][:], "d_rs", wait=[("ve", k.v("ve"))])
    k.barrier()


def st_l1_attn(k, tens, sb, xn_src):
    pb = k.pb
    for nm, key in (("pbq", "p_bq_t"), ("pbk", "p_bk_t"), ("pbv", "p_bv_t")):
        k.dma("sync", sb[nm][:], tens[key], "d_c")
    k.dma("gpsimd", sb["rm_sb"][:], tens["p_rm"], "d_cg")
    ld_c, ld_cg = k.v("d_c"), k.v("d_cg")
    k.on("vector", lambda e: e.memset(sb["onesrow"][:], 1.0), inc=("ve", 1))
    k.on("vector", lambda e: e.memset(sb["vp"][:], 0.0), inc=("ve", 1))
    for tt in range(32):
        k.on("vector", lambda e, tt=tt: e.memset(sb["vp"][:, 132 * tt + 64:132 * tt + 65], 1.0))
        k.on("vector", lambda e, tt=tt: e.memset(sb["vp"][:, 132 * tt + 130:132 * tt + 131], 1.0),
             inc=("ve", 1) if tt == 31 else None)
    k.on("vector", lambda e: e.memset(sb["kv_sb"][:], 0.0), inc=("ve", 1))
    k.on("vector", lambda e: e.memset(sb["c_l16"][:], -2.772588722239781), inc=("ve", 1))
    for d in range(DT):
        k.dma("sync", sb["xn2"][:, d * T:(d + 1) * T],
              tens[xn_src][d * 128:(d + 1) * 128, :], "d_xn2")
    ldxn = k.v("d_xn2")
    k.barrier()
    for p in range(8):
        k.barrier()
        for nm, c0, wt in (("wq_sb", 128 * p, "p_Wq"), ("wk_sb", 128 * p, "p_Wk"), ("wv_sb", 128 * p, "p_Wv")):
            srcw = tens[wt][:, c0:c0 + 128].rearrange("(a p) c -> p a c", p=128)
            k.dma("gpsimd", sb[nm][:], srcw, "d_w")
        k.dma("gpsimd", sb["bv_row"][0:1, :], tens["p_bv_raw"][0:1, 128 * p:128 * (p + 1)], "d_w")
        ld_w = k.v("d_w")
        for which, wsb, dstt, bsel in (("q", "wq_sb", "qt", "pbq"), ("kk", "wk_sb", "kt", "pbk")):
            for ch in range(NCH):
                bank = 6 + ch % 2
                w = [("d_w", ld_w), ("d_xn2", ldxn), ("ac", k.v("ac"))]
                for dm in range(DT):
                    k.on("tensor", lambda e, ch=ch, dm=dm, bank=bank, wsb=wsb: e.matmul(
                        pb[bank][:], sb[wsb][:, 128 * dm:128 * dm + 128],
                        sb["xn2"][:, T * dm + CH * ch: T * dm + CH * (ch + 1)],
                        start=(dm == 0), stop=(dm == DT - 1)),
                        wait=w if dm == 0 else [],
                        inc=("pe", 1) if dm == DT - 1 else None)
                pev = k.v("pe")
                for hl in range(2):
                    k.on("scalar", lambda e, dstt=dstt, ch=ch, bank=bank, hl=hl, bsel=bsel: e.activation(
                        sb[dstt][64 * hl:64 * hl + 64, CH * ch:CH * (ch + 1)],
                        pb[bank][64 * hl:64 * hl + 64, :],
                        AF.Identity, bias=sb[bsel][64 * hl:64 * hl + 64, p:p + 1], scale=1.0),
                        wait=[("pe", pev), ("d_c", ld_c)] if hl == 0 else [], inc=("ac", 1))
        for tt in range(32):
            bank = 6 + tt % 2
            w = [("d_w", ld_w), ("d_xn2", ldxn), ("ac", k.v("ac"))]
            for dm in range(DT):
                k.on("tensor", lambda e, tt=tt, dm=dm, bank=bank: e.matmul(
                    pb[bank][:, 0:128], sb["xn2"][:, T * dm + 128 * tt: T * dm + 128 * (tt + 1)],
                    sb["wv_sb"][:, 128 * dm:128 * dm + 128],
                    start=(dm == 0), stop=False),
                    wait=w if dm == 0 else [])
            pev = k.on("tensor", lambda e, tt=tt, bank=bank, p=p: e.matmul(
                pb[bank][:, 0:128], sb["onesrow"][0:1, :], sb["bv_row"][0:1, :],
                start=False, stop=True), inc=("pe", 1))
            for hl in range(2):
                k.on("scalar", lambda e, tt=tt, bank=bank, hl=hl: e.copy(
                    sb["vp"][:, 132 * tt + 66 * hl:132 * tt + 66 * hl + 64],
                    pb[bank][:, 64 * hl:64 * hl + 64]),
                    wait=[("pe", pev)] if hl == 0 else [], inc=("ac", 1))
        ac_proj = k.v("ac")
        pe_prev_hl = 0
        for hl in range(2):
            h = 2 * p + hl
            mm_cnt, max_cnt, exp_cnt, cmb_cnt = {}, {}, {}, {}
            combine_pend = None

            def emit_elu(u, bank, width, dslc, hl=hl):
                t1 = sb["el_t1"][:, 512 * (u % 4):512 * (u % 4) + width]
                r1 = sb["el_r1"][:, 512 * (u % 4):512 * (u % 4) + width]
                wmn = [("pe", mm_cnt[u])]
                if (u - 4) in cmb_cnt:
                    wmn.append(("po", cmb_cnt[u - 4]))
                elif u < 4:
                    wmn.append(("po", k.v("po")))
                mn = k.on("vector", lambda e, t1=t1, bank=bank, width=width: e.tensor_scalar(
                    t1, pb[bank][:, 0:width], 0.0, None, ALU.min),
                    wait=wmn, inc=("ve", 1))
                max_cnt[u] = k.on("vector", lambda e, r1=r1, bank=bank, width=width: e.tensor_scalar(
                    r1, pb[bank][:, 0:width], 0.0, 0.0625, ALU.max, ALU.mult),
                    inc=("ve", 1))
                exp_cnt[u] = k.on("scalar", lambda e, t1=t1: e.activation(
                    t1, t1, AF.Exp, bias=sb["c_l16"][:]),
                    wait=[("ve", mn)], inc=("ac", 1))

            def emit_combine(u, dslc, hl=hl):
                t1 = sb["el_t1"][:, 512 * (u % 4):512 * (u % 4) + dslc.shape[1]]
                r1 = sb["el_r1"][:, 512 * (u % 4):512 * (u % 4) + dslc.shape[1]]
                w = [("ac", exp_cnt[u]), ("ve", max_cnt[u])]
                if pe_prev_hl:
                    w.append(("pe", pe_prev_hl))
                cmb_cnt[u] = k.on("gpsimd", lambda e, dslc=dslc, t1=t1, r1=r1: e.tensor_tensor(
                    dslc, t1, r1, ALU.add),
                    wait=w, inc=("po", 1))

            for u in range(16):
                mt, ch = u // 8, u % 8
                bank = u % 2
                w = [("ac", ac_proj), ("d_cg", ld_cg)]
                if (u - 2) in max_cnt:
                    w.append(("ve", max_cnt[u - 2]))
                elif hl == 1:
                    w.append(("ve", k.v("ve")))
                mm_cnt[u] = k.on("tensor", lambda e, mt=mt, ch=ch, bank=bank, hl=hl: e.matmul(
                    pb[bank][:], sb["rm_sb"][64 * hl:64 * hl + 64, 128 * mt:128 * (mt + 1)],
                    sb["qt"][64 * hl:64 * hl + 64, CH * ch:CH * (ch + 1)],
                    start=True, stop=True), wait=w, inc=("pe", 1))
                emit_elu(u, bank, CH, None)
                if combine_pend is not None:
                    emit_combine(*combine_pend)
                combine_pend = (u, sb["qp"][:, T * mt + CH * ch:T * mt + CH * (ch + 1)])
            kv_mm = 0
            kv_pend = []

            def emit_kv(ptt, hl=hl):
                nonlocal kv_mm
                for mt in range(2):
                    kv_mm = k.on("tensor", lambda e, mt=mt, ptt=ptt, hl=hl: e.matmul(
                        pb[2 + mt][:, 0:65],
                        sb["kp"][:, 256 * ptt + 128 * mt:256 * ptt + 128 * (mt + 1)],
                        sb["vp"][:, 132 * ptt + 66 * hl:132 * ptt + 66 * hl + 65],
                        start=(ptt == 0), stop=(ptt == 31)),
                        wait=[("po", cmb_cnt[16 + ptt])] if mt == 0 else [], inc=("pe", 1))

            for tt in range(32):
                u = 16 + tt
                bank = u % 2
                w = [("ac", ac_proj)]
                if (u - 2) in max_cnt:
                    w.append(("ve", max_cnt[u - 2]))
                mm_cnt[u] = k.on("tensor", lambda e, tt=tt, bank=bank, hl=hl: e.matmul(
                    pb[bank][:, 0:256], sb["kt"][64 * hl:64 * hl + 64, 128 * tt:128 * (tt + 1)],
                    sb["rm_sb"][64 * hl:64 * hl + 64, :],
                    start=True, stop=True), wait=w, inc=("pe", 1))
                emit_elu(u, bank, 256, None)
                if combine_pend is not None:
                    emit_combine(*combine_pend)
                    pu = combine_pend[0]
                    if pu >= 16:
                        kv_pend.append(pu - 16)
                        if len(kv_pend) > 4:
                            emit_kv(kv_pend.pop(0))
                combine_pend = (u, sb["kp"][:, 256 * tt:256 * (tt + 1)])
            emit_combine(*combine_pend)
            kv_pend.append(combine_pend[0] - 16)
            combine_pend = None
            while kv_pend:
                emit_kv(kv_pend.pop(0))
            for mt in range(2):
                k.on("scalar", lambda e, mt=mt: e.copy(
                    sb["kv_sb"][:, 128 * mt:128 * mt + 65], pb[2 + mt][:, 0:65]),
                    wait=[("pe", kv_mm)] if mt == 0 else [], inc=("ac", 1))
            ac_kv = k.v("ac")
            recip_done, tt_done, ot_st = {}, {}, {}
            pdv_todo = []

            def emit_pdv(hl=hl, h=h):
                ch = pdv_todo.pop(0)
                sl = CH * (ch % 2)
                pdv = k.on("tensor", lambda e, sl=sl: e.matmul(
                    pb[6][:], sb["ones1f"][0:1, :], sb["den"][0:1, sl:sl + CH],
                    start=True, stop=True), wait=[("ve", k.v("ve"))], inc=("pe", 1))
                wtt = [("pe", pdv)]
                if (ch - 2) in ot_st:
                    wtt.append(("d_ot%d" % (ch % 2), ot_st[ch - 2]))
                else:
                    wtt.append(("d_ot%d" % (ch % 2), k.v("d_ot%d" % (ch % 2))))
                tt_done[ch] = k.on("vector", lambda e, ch=ch, sl=sl: e.tensor_tensor(
                    sb["ot_tmp"][:, CH * (ch % 2):CH * (ch % 2 + 1)],
                    sb["onum"][:, sl:sl + CH], pb[6][:], ALU.mult),
                    wait=wtt, inc=("ve", 1))
                ot_st[ch] = k.dma("gpsimd", tens["otT"][64 * h:64 * h + 64, CH * ch:CH * (ch + 1)],
                                  sb["ot_tmp"][0:64, CH * (ch % 2):CH * (ch % 2 + 1)],
                                  "d_ot%d" % (ch % 2), wait=[("ve", tt_done[ch])])

            o_evac = {}
            po_cmb = k.v("po")
            for ch in range(NCH):
                ob = 4 + ch % 2
                if ch >= 2 and pdv_todo:
                    emit_pdv()
                w = [("ac", ac_kv), ("po", po_cmb)]
                if (ch - 2) in o_evac:
                    w.append(("ac", o_evac[ch - 2]))
                for mt in range(2):
                    k.on("tensor", lambda e, ch=ch, mt=mt, ob=ob: e.matmul(
                        pb[ob][:], sb["kv_sb"][:, 128 * mt:128 * (mt + 1)],
                        sb["qp"][:, T * mt + CH * ch:T * mt + CH * (ch + 1)],
                        start=(mt == 0), stop=(mt == 1)),
                        wait=w if mt == 0 else [],
                        inc=("pe", 1) if mt == 1 else None)
                pev = k.v("pe")
                sl = CH * (ch % 2)
                wdn = [("pe", pev)]
                if (ch - 2) in tt_done:
                    wdn.append(("ve", tt_done[ch - 2]))
                k.on("scalar", lambda e, ob=ob, sl=sl: e.copy(
                    sb["den"][0:1, sl:sl + CH], pb[ob][64:65, :]),
                    wait=wdn, inc=("ac", 1))
                k.on("scalar", lambda e, ob=ob, sl=sl: e.copy(
                    sb["onum"][:, sl:sl + CH], pb[ob][:]),
                    inc=("ac", 1))
                o_evac[ch] = k.v("ac")
                k.on("vector", lambda e, sl=sl: e.tensor_scalar(
                    sb["den"][0:1, sl:sl + CH], sb["den"][0:1, sl:sl + CH], 1e-6, None, ALU.max),
                    wait=[("ac", o_evac[ch])], inc=("ve", 1))
                k.on("vector", lambda e, sl=sl: e.reciprocal(
                    sb["den"][0:1, sl:sl + CH], sb["den"][0:1, sl:sl + CH]),
                    inc=("ve", 1))
                recip_done[ch] = k.v("ve")
                pdv_todo.append(ch)
            while pdv_todo:
                emit_pdv()
            pe_prev_hl = k.v("pe")
    k.barrier()


def st_l1_outproj(k, tens, sb, x_src, dst):
    k.dma("sync", sb["pbo"][:], tens["p_bo_t"], "d_c")
    k.barrier()
    _out_proj(k, tens, sb, "p_Wo", sb["pbo"], x_src, dst, "otT")


L1A_SB = [
    ("xn2", (128, 8 * T), BF16),
    ("pbq", (128, 8), F32), ("pbk", (128, 8), F32), ("pbv", (128, 8), F32),
    ("rm_sb", (128, 256), BF16),
    ("wq_sb", (128, 1024), BF16), ("wk_sb", (128, 1024), BF16), ("wv_sb", (128, 1024), BF16),
    ("bv_row", (1, 128), BF16), ("onesrow", (1, 128), BF16),
    ("qt", (128, T), BF16), ("kt", (128, T), BF16),
    ("vp", (128, 32 * 132), BF16),
    ("qp", (128, 2 * T), BF16), ("kp", (128, 32 * 256), BF16),
    ("kv_sb", (128, 256), BF16),
    ("el_t1", (128, 4 * 512), F32), ("el_r1", (128, 4 * 512), F32),
    ("den", (1, 2 * CH), F32), ("onum", (128, 2 * CH), F32),
    ("ot_tmp", (128, 2 * CH), BF16), ("c_l16", (128, 1), F32),
]

L1P_SB = [
    ("pbo", (128, 8), F32),
    ("wo_full", (128, 8 * 1024), BF16),
    ("otb", (128, 8 * CH), BF16), ("xcb", (128, 8 * CH), F32), ("rescb", (128, 8 * CH), F32),
]


def build_l1attn_launch(kinds):
    nc = bass.Bass("TRN2", target_bir_lowering=False, debug=False)
    decls = [
        ("xn1_bf", (D, T), BF16, "ExternalInput"),
        ("x1", (D, T), F32, "ExternalInput"),
        ("p_Wq", (D, D), F32, "ExternalInput"), ("p_Wk", (D, D), F32, "ExternalInput"),
        ("p_Wv", (D, D), F32, "ExternalInput"), ("p_Wo", (D, D), F32, "ExternalInput"),
        ("p_bq_t", (128, 8), F32, "ExternalInput"), ("p_bk_t", (128, 8), F32, "ExternalInput"),
        ("p_bv_t", (128, 8), F32, "ExternalInput"), ("p_bo_t", (128, 8), F32, "ExternalInput"),
        ("p_bv_raw", (1, D), F32, "ExternalInput"),
        ("p_rm", (128, 256), F32, "ExternalInput"),
        ("x2p", (D, T), F32, "ExternalOutput"),
        ("otT", (D, T), BF16, "Internal"),
    ]
    tens = {}
    for name, shape, dt, kind in decls:
        tens[name] = nc.dram_tensor(name, list(shape), dt, kind=kind).ap()
    outer = ExitStack()
    with outer:
        k, sbc = _mk_common(nc, outer)
        for i, (sbl, fn) in enumerate([
                (L1A_SB, lambda sb: st_l1_attn(k, tens, sb, "xn1_bf")),
                (L1P_SB, lambda sb: st_l1_outproj(k, tens, sb, "x1", "x2p"))]):
            ctx = ExitStack()
            sb = dict(sbc)
            seen = set()
            for nm, shape, dt in sbl:
                if nm in seen:
                    continue
                seen.add(nm)
                sb[nm] = ctx.enter_context(nc.sbuf_tensor("t%d_%s" % (i, nm), list(shape), dt))
            fn(sb)
            ctx.close()
        _finish(nc, k)
    return nc





def st_lf_attn2(k, tens, sb, l, yin):
    pb = k.pb
    for nm, key in (("csq", "rot_cq"), ("ssq", "rot_sq"), ("csk", "rot_ck"), ("ssk", "rot_sk")):
        k.dma("gpsimd", sb[nm][0:64, :], tens[key], "d_cg")
    k.dma("gpsimd", sb["mask_sb"][:], tens["la_mask"].rearrange("(a p) c -> p a c", p=128), "d_cg")
    k.dma("gpsimd", sb["maskT"][:], tens["la_maskT"], "d_cg")
    k.dma("sync", sb["id_f32"][:], tens["ident"], "d_c")
    k.dma("sync", sb["qs_col"][0:64, 0:1], tens["lf_qs"][l].unsqueeze(1), "d_c")
    k.dma("sync", sb["ks_col"][0:64, 0:1], tens["lf_ks"][l].unsqueeze(1), "d_c")
    ld_cg, ld_c = k.v("d_cg"), k.v("d_c")
    k.on("vector", lambda e: e.memset(sb["kra0"][64:65, :], 1.0), inc=("ve", 1))
    k.on("vector", lambda e: e.memset(sb["kra1"][64:65, :], 1.0), inc=("ve", 1))
    k.on("vector", lambda e: e.memset(sb["ones64b"][:], 1.0), inc=("ve", 1))
    k.on("vector", lambda e: e.memset(sb["vp"][:], 0.0), inc=("ve", 1))
    for tt in range(32):
        k.on("vector", lambda e, tt=tt: e.memset(sb["vp"][:, 256 * tt + 64:256 * tt + 65], 1.0))
        k.on("vector", lambda e, tt=tt: e.memset(sb["vp"][:, 256 * tt + 192:256 * tt + 193], 1.0),
             inc=("ve", 1) if tt == 31 else None)
    k.barrier()
    for p in range(4):
        k.barrier()
        for nm, c0 in (("wq_sb", 128 * p), ("wk_sb", 512 + 128 * p), ("wv_sb", 1024 + 128 * p)):
            srcw = tens["lf_wqkv"][l][:, c0:c0 + 128].rearrange("(a p) c -> p a c", p=128)
            k.dma("gpsimd", sb[nm][:], srcw, "d_w")
        ld_w = k.v("d_w")
        for ch in range(NCH):
            c0 = CH * ch
            wldh = [("pe", k.v("pe"))] if (ch > 0 or p > 0) else []
            for d in range(DT):
                dst = sb["hb"][:, CH * d:CH * (d + 1)]
                if d < 4:
                    k.dma("gpsimd", dst, tens[yin][128 * d:128 * (d + 1), c0:c0 + CH],
                          "d_hb", wait=wldh if d == 0 else [])
                elif ch == 0:
                    k.on("vector", lambda e, dst=dst: e.memset(dst[:, 0:1], 0.0),
                         wait=wldh + [("ve", k.v("ve"))], inc=("ve", 1))
                    k.dma("gpsimd", dst[:, 1:CH], tens[yin][128 * d:128 * (d + 1), 0:CH - 1],
                          "d_hb")
                else:
                    k.dma("gpsimd", dst, tens[yin][128 * d:128 * (d + 1), c0 - 1:c0 + CH - 1],
                          "d_hb", wait=wldh if d == 0 else [])
            ldh = k.v("d_hb")
            ve_hb = k.v("ve")
            for which, wsb, dstt in (("q", "wq_sb", "qt"), ("kk", "wk_sb", "kt")):
                bank = 6 + ch % 2
                w = [("d_w", ld_w), ("d_hb", ldh), ("ve", ve_hb), ("ac", k.v("ac"))]
                for dm in range(DT):
                    k.on("tensor", lambda e, ch=ch, dm=dm, bank=bank, wsb=wsb: e.matmul(
                        pb[bank][:], sb[wsb][:, 128 * dm:128 * dm + 128],
                        sb["hb"][:, CH * dm:CH * (dm + 1)],
                        start=(dm == 0), stop=(dm == DT - 1)),
                        wait=w if dm == 0 else [],
                        inc=("pe", 1) if dm == DT - 1 else None)
                pev = k.v("pe")
                k.on("scalar", lambda e, ch=ch, bank=bank, dstt=dstt: e.copy(
                    sb[dstt][:, CH * ch:CH * (ch + 1)], pb[bank][:]),
                    wait=[("pe", pev)], inc=("ac", 1))
            for ttl in range(4):
                bank = 6 + ttl % 2
                w = [("d_w", ld_w), ("d_hb", ldh), ("ve", ve_hb), ("ac", k.v("ac"))]
                for dm in range(DT):
                    k.on("tensor", lambda e, ch=ch, ttl=ttl, dm=dm, bank=bank: e.matmul(
                        pb[bank][:, 0:128], sb["hb"][:, CH * dm + 128 * ttl:CH * dm + 128 * (ttl + 1)],
                        sb["wv_sb"][:, 128 * dm:128 * dm + 128],
                        start=(dm == 0), stop=(dm == DT - 1)),
                        wait=w if dm == 0 else [],
                        inc=("pe", 1) if dm == DT - 1 else None)
                pev = k.v("pe")
                for hl in range(2):
                    k.on("scalar", lambda e, ch=ch, ttl=ttl, bank=bank, hl=hl: e.copy(
                        sb["vp"][:, 256 * (4 * ch + ttl) + 128 * hl:256 * (4 * ch + ttl) + 128 * hl + 64],
                        pb[bank][:, 64 * hl:64 * hl + 64]),
                        wait=[("pe", pev)] if hl == 0 else [], inc=("ac", 1))
        for hl in range(2):
            for which in ("q", "kk"):
                srct = sb["qt" if which == "q" else "kt"]
                cse = sb["csq" if which == "q" else "csk"]
                sse = sb["ssq" if which == "q" else "ssk"]
                dstr = sb[("qra%d" if which == "q" else "kra%d") % hl]
                mmv = {}
                for ch in range(NCH):
                    row, bank = (ch % 3) * 32, ch // 3
                    wsq = [("ve", k.v("ve")), ("ac", k.v("ac"))]
                    if ch >= 1:
                        wsq.append(("pe", mmv[ch - 1]))
                    k.on("scalar", lambda e, srct=srct, hl=hl, ch=ch: e.activation(
                        sb["sqc"][64 * hl:64 * hl + 64, :],
                        srct[64 * hl:64 * hl + 64, CH * ch:CH * (ch + 1)], AF.Square),
                        wait=wsq, inc=("ac", 1))
                    mmv[ch] = k.on("tensor", lambda e, ch=ch, row=row, bank=bank, hl=hl: e.matmul(
                        pb[bank][row:row + 1, :], sb["onesb"][64 * hl:64 * hl + 64, :],
                        sb["sqc"][64 * hl:64 * hl + 64, :],
                        start=True, stop=True),
                        wait=[("ac", k.v("ac"))], inc=("pe", 1))
                pev = k.v("pe")
                for ch in range(NCH):
                    row, bank = (ch % 3) * 32, ch // 3
                    rr2, cc2 = 32 * (ch // 4), 512 * (ch % 4)
                    k.on("scalar", lambda e, row=row, bank=bank, rr2=rr2, cc2=cc2: e.copy(
                        sb["rows2"][rr2:rr2 + 1, cc2:cc2 + 512], pb[bank][row:row + 1, :]),
                        wait=[("pe", pev)] if ch == 0 else [], inc=("ac", 1))
                for rr2 in (0, 32):
                    k.on("vector", lambda e, rr2=rr2: e.tensor_scalar(
                        sb["rows2"][rr2:rr2 + 1, :], sb["rows2"][rr2:rr2 + 1, :], 1e-24, None, ALU.max),
                        wait=[("ac", k.v("ac")), ("ve", k.v("ve"))], inc=("ve", 1))
                    k.on("scalar", lambda e, rr2=rr2: e.activation(
                        sb["rows2"][rr2:rr2 + 1, :], sb["rows2"][rr2:rr2 + 1, :], AF.Ln),
                        wait=[("ve", k.v("ve"))], inc=("ac", 1))
                    k.on("scalar", lambda e, rr2=rr2: e.activation(
                        sb["rows2b"][rr2:rr2 + 1, :], sb["rows2"][rr2:rr2 + 1, :], AF.Exp, scale=-0.5),
                        wait=[("ac", k.v("ac"))], inc=("ac", 1))
                for ch in range(NCH):
                    rr2, cc2 = 32 * (ch // 4), 512 * (ch % 4)
                    pbc = k.on("tensor", lambda e, rr2=rr2, cc2=cc2: e.matmul(
                        pb[3][0:64, :], sb["ones64b"][rr2:rr2 + 1, :],
                        sb["rows2b"][rr2:rr2 + 1, cc2:cc2 + 512], start=True, stop=True),
                        wait=[("ac", k.v("ac")), ("ve", k.v("ve"))], inc=("pe", 1))
                    k.on("scalar", lambda e, ch=ch: e.copy(
                        sb["rnb"][0:64, CH * ch:CH * (ch + 1)], pb[3][0:64, :]),
                        wait=[("pe", pbc)], inc=("ac", 1))
                ac_bc = k.v("ac")
                k.on("vector", lambda e, srct=srct, hl=hl: e.tensor_copy(
                    sb["q0"][0:64, :], srct[64 * hl:64 * hl + 64, :]),
                    wait=[("ac", ac_bc), ("ve", k.v("ve"))], inc=("ve", 1))
                k.on("vector", lambda e: e.tensor_tensor(
                    sb["q0"][0:64, :], sb["q0"][0:64, :], sb["rnb"][0:64, :], ALU.mult),
                    wait=[("ve", k.v("ve"))], inc=("ve", 1))
                scol = sb["qs_col" if which == "q" else "ks_col"]
                k.on("vector", lambda e, scol=scol: e.tensor_scalar(
                    sb["q0"][0:64, :], sb["q0"][0:64, :], scol[0:64, 0:1], None, ALU.mult),
                    wait=[("ve", k.v("ve")), ("d_c", ld_c)], inc=("ve", 1))
                k.on("vector", lambda e: e.tensor_copy(sb["qsw"][0:32, :], sb["q0"][32:64, :]),
                     wait=[("ve", k.v("ve"))], inc=("ve", 1))
                k.on("vector", lambda e: e.tensor_copy(sb["qsw"][32:64, :], sb["q0"][0:32, :]),
                     wait=[("ve", k.v("ve"))], inc=("ve", 1))
                k.on("vector", lambda e, sse=sse: e.tensor_tensor(
                    sb["qsw"][0:64, :], sb["qsw"][0:64, :], sse[0:64, :], ALU.mult),
                    wait=[("ve", k.v("ve"))], inc=("ve", 1))
                k.on("vector", lambda e, cse=cse: e.tensor_tensor(
                    sb["q0"][0:64, :], sb["q0"][0:64, :], cse[0:64, :], ALU.mult),
                    wait=[("ve", k.v("ve"))], inc=("ve", 1))
                k.on("vector", lambda e, dstr=dstr: e.tensor_tensor(
                    dstr[0:64, :], sb["q0"][0:64, :], sb["qsw"][0:64, :], ALU.add),
                    wait=[("ve", k.v("ve"))], inc=("ve", 1))
        chain_done = k.v("ve")
        prev_p1 = None
        prev_p2bank = None
        prev_p2exp = None
        prev_p2evac = None
        for hl in range(2):
            h = 2 * p + hl
            qra, kra = sb["qra%d" % hl], sb["kra%d" % hl]
            p1add = {}
            for wdx in range(8):
                for qtl in range(4):
                    qi = 4 * wdx + qtl
                    off = 512 if wdx == 0 else 0
                    for kc in range(2):
                        if wdx == 0 and kc == 0:
                            continue
                        bk = 4 + kc
                        w = []
                        if (qi - 1, kc) in p1add:
                            w.append(("ve", p1add[(qi - 1, kc)]))
                        elif (qi - 2, kc) in p1add:
                            w.append(("ve", p1add[(qi - 2, kc)]))
                        elif qi <= 4 and prev_p1 is not None:
                            w.append(("ve", prev_p1[kc]))
                        if qi == 0:
                            w.append(("ve", chain_done))
                        k.on("tensor", lambda e, hl=hl, qi=qi, kc=kc, bk=bk, wdx=wdx, qra=qra, kra=kra: e.matmul(
                            pb[bk][:], qra[0:64, 128 * qi:128 * (qi + 1)],
                            kra[0:64, 512 * (wdx - 1) + 512 * kc:512 * wdx + 512 * kc],
                            start=True, stop=True), wait=w, inc=("pe", 1))
                        pev = k.v("pe")
                        p1add[(qi, kc)] = k.on("vector", lambda e, qi=qi, qtl=qtl, kc=kc, bk=bk: e.tensor_tensor(
                            sb["sm"][:, 512 * kc:512 * (kc + 1)], pb[bk][:],
                            sb["mask_sb"][:, 1024 * qtl + 512 * kc:1024 * qtl + 512 * (kc + 1)],
                            ALU.add),
                            wait=[("pe", pev), ("d_cg", ld_cg)], inc=("ve", 1))
                    k.on("vector", lambda e, qi=qi, off=off: e.reduce_max(
                        sb["nm"][:, qi:qi + 1], sb["sm"][:, off:1024], axis=AX.X, negate=True),
                        inc=("ve", 1))
            ve_nm = k.v("ve")
            ptr = k.on("tensor", lambda e: e.transpose(
                pb[7][0:32, 0:128], sb["nm"][:], sb["id_f32"][:]),
                wait=[("ve", ve_nm), ("ac", k.v("ac")), ("d_c", ld_c)], inc=("pe", 1))
            k.on("scalar", lambda e: e.copy(sb["tr"][0:32, :], pb[7][0:32, 0:128]),
                 wait=[("pe", ptr), ("d_tr", k.v("d_tr"))], inc=("ac", 1))
            ld_tr = k.dma("gpsimd", qra[64:65, :], sb["tr"][0:32, :], "d_tr",
                          wait=[("ac", k.v("ac"))])
            exp_cnt, dve_cnt, av_cnt, bankfree = {}, {}, {}, {}
            evac_ve, recip_done, tt_done, ot_st = {}, {}, {}, {}
            pend = []
            pdv_todo = []
            tiles = []
            for wdx in range(8):
                for kt in range(4 if wdx == 0 else 0, 8):
                    tiles.append((wdx, kt))
            nti = len(tiles)
            last_of_wdx = {wdx: max(i for i, t in enumerate(tiles) if t[0] == wdx)
                           for wdx in range(8)}
            first_of_wdx = {wdx: min(i for i, t in enumerate(tiles) if t[0] == wdx)
                            for wdx in range(8)}

            def emit_av(ti, hl=hl):
                wdx, kt = tiles[ti]
                kg = 4 * (wdx - 1) + kt
                wav = [("ac", exp_cnt[ti])]
                if ti == first_of_wdx[wdx]:
                    if (wdx - 2) in evac_ve:
                        wav.append(evac_ve[wdx - 2])
                    elif prev_p2evac is not None and wdx < 2:
                        wav.append(prev_p2evac[6 + wdx])
                av_cnt[ti] = k.on("tensor", lambda e, ti=ti, wdx=wdx, kg=kg, hl=hl: e.matmul(
                    pb[2 + wdx % 2][:], sb["vp"][:, 256 * kg + 128 * hl:256 * kg + 128 * (hl + 1)],
                    sb["e_sb"][:, CH * (ti % 4):CH * (ti % 4 + 1)],
                    start=(ti == first_of_wdx[wdx]), stop=(ti == last_of_wdx[wdx])),
                    wait=wav, inc=("pe", 1))
                if ti == last_of_wdx[wdx]:
                    sl = CH * (wdx % 2)
                    k.on("scalar", lambda e, wdx=wdx, sl=sl: e.copy(
                        sb["den"][0:1, sl:sl + CH], pb[2 + wdx % 2][64:65, :]),
                        wait=[("pe", av_cnt[ti])], inc=("ac", 1))
                    k.on("scalar", lambda e, wdx=wdx, sl=sl: e.copy(
                        sb["onum"][:, sl:sl + CH], pb[2 + wdx % 2][:]),
                        inc=("ac", 1))
                    evac_ve[wdx] = ("ac", k.v("ac"))
                    k.on("vector", lambda e, sl=sl: e.reciprocal(
                        sb["den"][0:1, sl:sl + CH], sb["den"][0:1, sl:sl + CH]),
                        wait=[("ac", k.v("ac"))], inc=("ve", 1))
                    recip_done[wdx] = k.v("ve")
                    pdv_todo.append(wdx)

            def emit_pdv(hl=hl):
                wdx = pdv_todo.pop(0)
                sl = CH * (wdx % 2)
                pdv = k.on("tensor", lambda e, sl=sl: e.matmul(
                    pb[6][:], sb["ones1f"][0:1, :], sb["den"][0:1, sl:sl + CH],
                    start=True, stop=True), wait=[("ve", k.v("ve"))], inc=("pe", 1))
                wtt = [("pe", pdv)]
                if (wdx - 2) in ot_st:
                    wtt.append(("d_ot%d" % (wdx % 2), ot_st[wdx - 2]))
                tt_done[wdx] = k.on("vector", lambda e, wdx=wdx, sl=sl: e.tensor_tensor(
                    sb["ot_tmp"][:, CH * (wdx % 2):CH * (wdx % 2 + 1)],
                    sb["onum"][:, sl:sl + CH], pb[6][:], ALU.mult),
                    wait=wtt, inc=("ve", 1))
                ot_st[wdx] = k.dma("gpsimd",
                                   tens["otT"][128 * p + 64 * hl:128 * p + 64 * hl + 64,
                                               CH * wdx:CH * (wdx + 1)],
                                   sb["ot_tmp"][0:64, CH * (wdx % 2):CH * (wdx % 2 + 1)],
                                   "d_ot%d" % (wdx % 2), wait=[("ve", tt_done[wdx])])

            for ti, (wdx, kt) in enumerate(tiles):
                kg = 4 * (wdx - 1) + kt
                sbk = ti % 2
                if ti == first_of_wdx[wdx] and wdx >= 1 and pdv_todo:
                    emit_pdv()
                w = []
                if (ti - 2) >= 0:
                    w.append(bankfree[ti - 2])
                elif prev_p2bank is not None:
                    w.append(prev_p2bank[ti])
                if ti < 2:
                    w.append(("d_tr", ld_tr))
                k.on("tensor", lambda e, kg=kg, wdx=wdx, sbk=sbk, qra=qra, kra=kra: e.matmul(
                    pb[sbk][:], kra[:, 128 * kg:128 * (kg + 1)],
                    qra[:, CH * wdx:CH * (wdx + 1)],
                    start=True, stop=True), wait=w, inc=("pe", 1))
                pev = k.v("pe")
                ep = sb["e_pre"][:, CH * (ti % 4):CH * (ti % 4 + 1)]
                eo = sb["e_sb"][:, CH * (ti % 4):CH * (ti % 4 + 1)]
                wv = [("pe", pev)]
                if (ti - 4) >= 0:
                    wv.append(("ac", exp_cnt[ti - 4]))
                elif prev_p2exp is not None:
                    wv.append(("ac", prev_p2exp[ti]))
                moff = 896 - 128 * kt
                dve_cnt[ti] = k.on("vector", lambda e, ep=ep, sbk=sbk, moff=moff: e.tensor_tensor(
                    ep, pb[sbk][:], sb["maskT"][:, moff:moff + 512], ALU.add),
                    wait=wv, inc=("ve", 1))
                bankfree[ti] = ("ve", dve_cnt[ti])
                exp_cnt[ti] = k.on("scalar", lambda e, eo=eo, ep=ep: e.activation(
                    eo, ep, AF.Exp),
                    wait=[("ve", dve_cnt[ti])], inc=("ac", 1))
                if len(pend) == 3:
                    emit_av(pend.pop(0))
                pend.append(ti)
            while pend:
                emit_av(pend.pop(0))
            while pdv_todo:
                emit_pdv()
            prev_p1 = {kc: p1add.get((31, kc), p1add.get((30, kc), 0)) for kc in range(2)}
            prev_p2bank = {i: bankfree[nti - 2 + i] for i in range(2)}
            prev_p2exp = {i: exp_cnt[nti - 4 + i] for i in range(4)}
            prev_p2evac = evac_ve
    k.barrier()


def st_lf_wo(k, tens, sb, l, yin, ya):
    pb = k.pb
    k.dma("gpsimd", sb["wo_sb"][:],
          tens["lf_wo"][l].rearrange("(a p) (b c) -> p a b c", p=128, c=128), "d_wf")
    ldwo = k.v("d_wf")
    k.barrier()
    mm_done = [0] * NCH
    st_done = [0] * NCH
    for ch in range(NCH):
        w = [("pe", mm_done[ch - 1])] if ch >= 1 else []
        ldo = k.dma("sync", sb["otb"][:],
                    tens["otT"][0:512, CH * ch:CH * (ch + 1)].rearrange("(a p) c -> p a c", p=128),
                    "d_ob", wait=w)
        wx = [("ve", k.v("ve"))] if ch >= 1 else []
        ldy = k.dma("sync", sb["yinb"][:],
                    tens[yin][:, CH * ch:CH * (ch + 1)].rearrange("(a p) c -> p a c", p=128),
                    "d_xc", wait=wx)
        for do in range(DT):
            bank = 6 + do % 2
            w = [("d_wf", ldwo), ("d_ob", ldo), ("ve", k.v("ve"))]
            for a in range(4):
                k.on("tensor", lambda e, a=a, do=do, bank=bank: e.matmul(
                    pb[bank][:], sb["wo_sb"][:, 1024 * a + 128 * do:1024 * a + 128 * (do + 1)],
                    sb["otb"][:, CH * a:CH * (a + 1)],
                    start=(a == 0), stop=(a == 3)), wait=w if a == 0 else [],
                    inc=("pe", 1) if a == 3 else None)
            pev = k.v("pe")
            wst = [("pe", pev), ("d_xc", ldy)]
            if do == 0 and ch >= 1:
                wst.append(("d_rs", st_done[ch - 1]))
            k.on("vector", lambda e, do=do, bank=bank: e.tensor_tensor(
                sb["rescb"][:, CH * do:CH * (do + 1)], pb[bank][:],
                sb["yinb"][:, CH * do:CH * (do + 1)], ALU.add),
                wait=wst, inc=("ve", 1))
        mm_done[ch] = k.v("pe")
        st_done[ch] = k.dma("gpsimd",
                            tens[ya][:, CH * ch:CH * (ch + 1)].rearrange("(a p) c -> p a c", p=128),
                            sb["rescb"][:], "d_rs", wait=[("ve", k.v("ve"))])
    k.barrier()


LFA2_SB = [
    ("csq", (64, T), BF16), ("ssq", (64, T), BF16), ("csk", (64, T), BF16), ("ssk", (64, T), BF16),
    ("mask_sb", (128, 4 * 1024), BF16), ("maskT", (128, 1408), BF16),
    ("id_f32", (128, 128), F32),
    ("qs_col", (64, 1), F32), ("ks_col", (64, 1), F32),
    ("wq_sb", (128, 1024), BF16), ("wk_sb", (128, 1024), BF16), ("wv_sb", (128, 1024), BF16),
    ("hb", (128, 8 * CH), BF16),
    ("qt", (128, T), BF16), ("kt", (128, T), BF16),
    ("vp", (128, 32 * 256), BF16),
    ("sqc", (128, CH), BF16), ("rows2", (65, 2048), F32),
    ("rows2b", (33, 2048), BF16), ("ones64b", (33, 64), BF16), ("rnb", (64, T), BF16),
    ("q0", (64, T), BF16), ("qsw", (64, T), BF16),
    ("qra0", (65, T), BF16), ("qra1", (65, T), BF16),
    ("kra0", (65, T), BF16), ("kra1", (65, T), BF16),
    ("sm", (128, 1024), F32), ("nm", (128, 32), F32), ("tr", (32, 128), F32),
    ("e_sb", (128, 4 * CH), BF16), ("e_pre", (128, 4 * CH), F32),
    ("den", (1, 2 * CH), F32), ("onum", (128, 2 * CH), F32),
    ("ot_tmp", (128, 2 * CH), BF16),
]

LFW_SB = [
    ("wo_sb", (128, 4 * 1024), BF16),
    ("otb", (128, 4 * CH), BF16), ("yinb", (128, 8 * CH), F32), ("rescb", (128, 8 * CH), F32),
]


def build_lfattn_launch(l, yin_name, ya_name):
    nc = bass.Bass("TRN2", target_bir_lowering=False, debug=False)
    decls = [
        (yin_name, (D, T), F32, "ExternalInput"),
        ("lf_wqkv", (2, D, 1536), F32, "ExternalInput"),
        ("lf_qs", (2, 64), F32, "ExternalInput"),
        ("lf_ks", (2, 64), F32, "ExternalInput"),
        ("lf_wo", (2, 512, D), F32, "ExternalInput"),
        ("rot_cq", (64, T), F32, "ExternalInput"), ("rot_sq", (64, T), F32, "ExternalInput"),
        ("rot_ck", (64, T), F32, "ExternalInput"), ("rot_sk", (64, T), F32, "ExternalInput"),
        ("la_mask", (512, 1024), F32, "ExternalInput"),
        ("la_maskT", (128, 1408), F32, "ExternalInput"),
        ("ident", (128, 128), F32, "ExternalInput"),
        (ya_name, (D, T), F32, "ExternalOutput"),
        ("otT", (D, T), BF16, "Internal"),
    ]
    tens = {}
    for name, shape, dt, kind in decls:
        tens[name] = nc.dram_tensor(name, list(shape), dt, kind=kind).ap()
    outer = ExitStack()
    with outer:
        k, sbc = _mk_common(nc, outer)
        for i, (sbl, fn) in enumerate([
                (LFA2_SB, lambda sb: st_lf_attn2(k, tens, sb, l, yin_name)),
                (LFW_SB, lambda sb: st_lf_wo(k, tens, sb, l, yin_name, ya_name))]):
            ctx = ExitStack()
            sb = dict(sbc)
            seen = set()
            for nm, shape, dt in sbl:
                if nm in seen:
                    continue
                seen.add(nm)
                sb[nm] = ctx.enter_context(nc.sbuf_tensor("s%d_%s" % (i, nm), list(shape), dt))
            fn(sb)
            ctx.close()
        _finish(nc, k)
    return nc





def st_lf_ffn(k, tens, sb, l, ya, yout):
    pb = k.pb
    HT, HP = 44, 22
    k.dma("gpsimd", tens["w1bf"].rearrange("a b -> (a b)"),
          tens["lf_w1p"][l].rearrange("a b -> (a b)"), "d_cw")
    k.dma("gpsimd", tens["w2bf"].rearrange("a b -> (a b)"),
          tens["lf_w2p"][l].rearrange("a b -> (a b)"), "d_cw")
    k.dma("sync", sb["b1c"][:], tens["lf_b1p_t"][l], "d_c")
    k.dma("sync", sb["lgc"][:], tens["lf_lng_t"][l], "d_c")
    k.dma("sync", sb["lbc"][:], tens["lf_lnb_t"][l], "d_c")
    k.dma("sync", sb["b2c"][:], tens["lf_b2_t"][l], "d_c")
    ld_c = k.v("d_c")
    cw = k.v("d_cw")
    _emit_shift(k, tens, sb, ya)
    k.barrier()
    mu = sb["rows"][0:1, 0:T]
    ms = sb["rows"][32:33, 0:T]
    rr = sb["rows"][32:33, T:2 * T]
    sq_done, mmB, ld_done = [0] * DT, [0] * DT, [0] * DT
    for d in range(DT):
        hsb = sb["hsb"][:]
        w = []
        if d >= 1:
            w = [("ac", sq_done[d - 1]), ("pe", mmB[d - 1])]
        ld_done[d] = k.dma("sync", hsb, tens["hshiftT"][d * 128:(d + 1) * 128, :],
                           "d_a0", wait=w)
        sq_done[d] = k.on("scalar", lambda e, hsb=hsb: e.activation(sb["hsq"][:], hsb, AF.Square),
                          wait=[("d_a0", ld_done[d]), ("ac", k.v("ac")),
                                ("pe", mmB[d - 1] if d >= 1 else 0)], inc=("ac", 1))
        for ch in range(NCH):
            row, bank = (ch % 3) * 32, ch // 3
            k.on("tensor", lambda e, hsb=hsb, ch=ch, row=row, bank=bank, d=d: e.matmul(
                pb[bank][row:row + 1, :], sb["onesb"][:], hsb[:, ch * CH:(ch + 1) * CH],
                start=(d == 0), stop=(d == DT - 1)),
                wait=[("d_a0", ld_done[d])] if ch == 0 else [])
            k.on("tensor", lambda e, ch=ch, row=row, bank=bank, d=d: e.matmul(
                pb[3 + bank][row:row + 1, :], sb["onesb"][:], sb["hsq"][:, ch * CH:(ch + 1) * CH],
                start=(d == 0), stop=(d == DT - 1)),
                wait=[("ac", sq_done[d])] if ch == 0 else [],
                inc=("pe", 1) if ch == NCH - 1 else None)
        mmB[d] = k.v("pe")
    pe_stats = mmB[DT - 1]
    for ch in range(NCH):
        row, bank = (ch % 3) * 32, ch // 3
        k.on("scalar", lambda e, ch=ch, row=row, bank=bank: e.activation(
            sb["rows"][0:1, ch * CH:(ch + 1) * CH], pb[bank][row:row + 1, :], AF.Copy, scale=1.0 / D),
            wait=[("pe", pe_stats)] if ch == 0 else [], inc=("ac", 1))
        k.on("scalar", lambda e, ch=ch, row=row, bank=bank: e.activation(
            sb["rows"][32:33, ch * CH:(ch + 1) * CH], pb[3 + bank][row:row + 1, :], AF.Copy, scale=1.0 / D),
            inc=("ac", 1))
    ac_mu = k.v("ac")
    k.on("vector", lambda e: e.tensor_tensor(rr, mu, mu, ALU.mult),
         wait=[("ac", ac_mu)], inc=("ve", 1))
    k.on("vector", lambda e: e.tensor_tensor(rr, ms, rr, ALU.subtract),
         wait=[("ve", k.v("ve"))], inc=("ve", 1))
    k.on("scalar", lambda e: e.activation(rr, rr, AF.Sqrt, bias=sb["c_eps"][32:33, :]),
         wait=[("ve", k.v("ve"))], inc=("ac", 1))
    k.on("vector", lambda e: e.reciprocal(rr, rr),
         wait=[("ac", k.v("ac"))], inc=("ve", 1))
    ve_r = k.v("ve")
    bc_done = 0
    for ch in range(NCH):
        ba = 4 + 2 * (ch % 2)
        wmm = [("ac", ac_mu), ("ve", ve_r)]
        if ch >= 2:
            wmm.append(("ac", bc_done))
        k.on("tensor", lambda e, ch=ch, ba=ba: e.matmul(
            pb[ba][:], sb["ones1f"][0:1, :], sb["rows"][0:1, ch * CH:(ch + 1) * CH],
            start=True, stop=True), wait=wmm, inc=("pe", 1))
        pev = k.on("tensor", lambda e, ch=ch, ba=ba: e.matmul(
            pb[ba + 1][:], sb["ones1f"][32:33, :], sb["rows"][32:33, T + ch * CH:T + (ch + 1) * CH],
            start=True, stop=True), inc=("pe", 1))
        k.on("scalar", lambda e, ch=ch, ba=ba: e.copy(
            sb["mub"][:, ch * CH:(ch + 1) * CH], pb[ba][:]),
            wait=[("pe", pev)], inc=("ac", 1))
        bc_done = k.on("scalar", lambda e, ch=ch, ba=ba: e.copy(
            sb["rb"][:, ch * CH:(ch + 1) * CH], pb[ba + 1][:]), inc=("ac", 1))
    ac_bc = k.v("ac")
    xst = [0] * DT
    for d in range(DT):
        hsb = sb["hsb"][:]
        ld = k.dma("sync", hsb, tens["hshiftT"][d * 128:(d + 1) * 128, :], "d_a0",
                   wait=[("ve", k.v("ve")), ("ac", k.v("ac")), ("pe", k.v("pe")),
                         ("d_x0", xst[d - 1] if d >= 1 else 0)])
        k.on("vector", lambda e, hsb=hsb: e.tensor_tensor(sb["hsq"][:], hsb, sb["mub"][:], ALU.subtract),
             wait=[("d_a0", ld), ("ac", ac_bc), ("ve", k.v("ve")), ("ac", k.v("ac"))], inc=("ve", 1))
        k.on("vector", lambda e: e.tensor_tensor(sb["hsq"][:], sb["hsq"][:], sb["rb"][:], ALU.mult),
             wait=[("ve", k.v("ve"))], inc=("ve", 1))
        vv = k.on("scalar", lambda e, d=d, hsb=hsb: e.activation(
            hsb, sb["hsq"][:], AF.Identity,
            bias=sb["lbc"][:, d:d + 1], scale=sb["lgc"][:, d:d + 1]),
            wait=[("ve", k.v("ve")), ("d_c", ld_c)], inc=("ac", 1))
        xst[d] = k.dma("gpsimd", tens["xn2T"][d * 128:(d + 1) * 128, :], hsb, "d_x0",
                       wait=[("ac", vv)])
    k.barrier()
    w1_rd = {}
    w2_rd = {}
    st_done = [0] * NCH
    for ch in range(NCH):
        ldy = k.dma("sync", sb["yab"][:],
                    tens[ya][:, CH * ch:CH * (ch + 1)].rearrange("(a p) c -> p a c", p=128),
                    "d_xc", wait=[("ve", k.v("ve"))] if ch >= 1 else [])
        ldxn = k.dma("sync", sb["xnb"][:],
                     tens["xn2T"][:, CH * ch:CH * (ch + 1)].rearrange("(a p) c -> p a c", p=128),
                     "d_xnb", wait=[("pe", k.v("pe"))] if ch >= 1 else [])
        for ht in range(HT):
            b2 = ht % 2
            wld = [("d_cw", cw)]
            if (ch, ht - 2) in w1_rd:
                wld.append(("pe", w1_rd[(ch, ht - 2)]))
            elif ch >= 1 and (ch - 1, HT - 2 + ht) in w1_rd:
                wld.append(("pe", w1_rd[(ch - 1, HT - 2 + ht)]))
            ldw = k.dma("gpsimd", sb["w1sl"][:, 1024 * b2:1024 * (b2 + 1)],
                        tens["w1bf"][:, 128 * ht:128 * (ht + 1)].rearrange("(a p) c -> p a c", p=128),
                        "d_w%d" % b2, wait=wld)
            bank = ht % 2
            w = [("d_w%d" % b2, ldw), ("ac", k.v("ac"))]
            w.append(("d_xnb", ldxn))
            for dm in range(DT):
                k.on("tensor", lambda e, ht=ht, dm=dm, bank=bank, b2=b2: e.matmul(
                    pb[bank][:], sb["w1sl"][:, 1024 * b2 + 128 * dm:1024 * b2 + 128 * (dm + 1)],
                    sb["xnb"][:, CH * dm:CH * (dm + 1)],
                    start=(dm == 0), stop=(dm == DT - 1)),
                    wait=w if dm == 0 else [],
                    inc=("pe", 1) if dm == DT - 1 else None)
            w1_rd[(ch, ht)] = k.v("pe")
            k.on("scalar", lambda e, ht=ht, bank=bank: e.activation(
                sb["h1c"][:, 512 * ht:512 * (ht + 1)], pb[bank][:], AF.Identity,
                bias=sb["b1c"][:, ht:ht + 1], scale=1.0),
                wait=[("pe", w1_rd[(ch, ht)])], inc=("ac", 1))
        ac_h1 = k.v("ac")
        for hp in range(HP):
            k.on("scalar", lambda e, hp=hp: e.activation(
                sb["gg"][:], sb["h1c"][:, 512 * (HP + hp):512 * (HP + hp + 1)], AF.Gelu),
                wait=[("ac", k.v("ac")), ("ve", k.v("ve"))], inc=("ac", 1))
            k.on("vector", lambda e, hp=hp: e.tensor_tensor(
                sb["h1c"][:, 512 * hp:512 * (hp + 1)], sb["h1c"][:, 512 * hp:512 * (hp + 1)],
                sb["gg"][:], ALU.mult),
                wait=[("ac", k.v("ac"))], inc=("ve", 1))
        ve_gg = k.v("ve")
        for do in range(DT):
            b2 = do % 2
            wld = [("d_cw", cw)]
            if (ch, do - 2) in w2_rd:
                wld.append(("pe", w2_rd[(ch, do - 2)]))
            elif ch >= 1 and (ch - 1, DT - 2 + do) in w2_rd:
                wld.append(("pe", w2_rd[(ch - 1, DT - 2 + do)]))
            ldw = k.dma("gpsimd", sb["w2sl"][:, 2816 * b2:2816 * (b2 + 1)],
                        tens["w2bf"][:, 128 * do:128 * (do + 1)].rearrange("(a p) c -> p a c", p=128),
                        "d_v%d" % b2, wait=wld)
            bank = 2 + do % 2
            w = [("d_v%d" % b2, ldw), ("ve", ve_gg), ("ve", k.v("ve"))]
            for ht in range(HP):
                k.on("tensor", lambda e, do=do, ht=ht, bank=bank, b2=b2: e.matmul(
                    pb[bank][:], sb["w2sl"][:, 2816 * b2 + 128 * ht:2816 * b2 + 128 * (ht + 1)],
                    sb["h1c"][:, 512 * ht:512 * (ht + 1)],
                    start=(ht == 0), stop=(ht == HP - 1)),
                    wait=w if ht == 0 else [],
                    inc=("pe", 1) if ht == HP - 1 else None)
            w2_rd[(ch, do)] = k.v("pe")
            wst = [("pe", w2_rd[(ch, do)]), ("d_xc", ldy)]
            if do == 0 and ch >= 1:
                wst.append(("d_rs", st_done[ch - 1]))
            k.on("vector", lambda e, do=do, bank=bank: e.scalar_tensor_tensor(
                sb["youtc"][:, CH * do:CH * (do + 1)], pb[bank][:], sb["b2c"][:, do:do + 1],
                sb["yab"][:, CH * do:CH * (do + 1)], ALU.add, ALU.add),
                wait=wst, inc=("ve", 1))
        st_done[ch] = k.dma("gpsimd",
                            tens[yout][:, CH * ch:CH * (ch + 1)].rearrange("(a p) c -> p a c", p=128),
                            sb["youtc"][:], "d_rs", wait=[("ve", k.v("ve"))])
    k.barrier()


def _emit_shift(k, tens, sb, yin):
    for d in range(DT):
        for cc in range(4):
            yb = sb["ybuf"][:, 0:1025]
            hf = sb["hbf"][:, 0:1024]
            wld = [("d_h0", k.v("d_h0")), ("ve", k.v("ve"))]
            if d < 4:
                ld = k.dma("sync", yb[:, 0:1024],
                           tens[yin][128 * d:128 * (d + 1), 1024 * cc:1024 * (cc + 1)],
                           "d_y0", wait=wld)
                vv = k.on("vector", lambda e, yb=yb, hf=hf: e.tensor_copy(hf, yb[:, 0:1024]),
                          wait=[("d_y0", ld), ("ve", k.v("ve"))], inc=("ve", 1))
            elif cc == 0:
                ld = k.dma("sync", yb[:, 0:1023],
                           tens[yin][128 * d:128 * (d + 1), 0:1023], "d_y0", wait=wld)
                k.on("vector", lambda e, hf=hf: e.memset(hf[:, 0:1], 0.0),
                     wait=[("ve", k.v("ve"))] + wld, inc=("ve", 1))
                vv = k.on("vector", lambda e, yb=yb, hf=hf: e.tensor_copy(hf[:, 1:1024], yb[:, 0:1023]),
                          wait=[("d_y0", ld), ("ve", k.v("ve"))], inc=("ve", 1))
            else:
                ld = k.dma("sync", yb[:, 0:1024],
                           tens[yin][128 * d:128 * (d + 1), 1024 * cc - 1:1024 * (cc + 1) - 1],
                           "d_y0", wait=wld)
                vv = k.on("vector", lambda e, yb=yb, hf=hf: e.tensor_copy(hf, yb[:, 0:1024]),
                          wait=[("d_y0", ld), ("ve", k.v("ve"))], inc=("ve", 1))
            k.dma("gpsimd", tens["hshiftT"][128 * d:128 * (d + 1), 1024 * cc:1024 * (cc + 1)],
                  hf, "d_h0", wait=[("ve", vv)])


LFF_SB = [
    ("b1c", (128, 44), F32), ("lgc", (128, 8), F32), ("lbc", (128, 8), F32), ("b2c", (128, 8), F32),
    ("ybuf", (128, 1025), F32), ("hbf", (128, 1024), BF16),
    ("hsb", (128, T), BF16), ("hsq", (128, T), BF16),
    ("rows", (65, 2 * T), F32),
    ("mub", (128, T), BF16), ("rb", (128, T), BF16),
    ("xnb", (128, 8 * CH), BF16),
    ("w1sl", (128, 2 * 1024), BF16), ("w2sl", (128, 2 * 2816), BF16),
    ("h1c", (128, 44 * CH), BF16), ("gg", (128, CH), BF16),
    ("yab", (128, 8 * CH), F32), ("youtc", (128, 8 * CH), F32),
]


def build_lfffn_launch(l, ya_name, yout_name):
    nc = bass.Bass("TRN2", target_bir_lowering=False, debug=False)
    decls = [
        (ya_name, (D, T), F32, "ExternalInput"),
        ("lf_w1p", (2, D, 5632), F32, "ExternalInput"),
        ("lf_b1p_t", (2, 128, 44), F32, "ExternalInput"),
        ("lf_w2p", (2, 2816, D), F32, "ExternalInput"),
        ("lf_b2_t", (2, 128, 8), F32, "ExternalInput"),
        ("lf_lng_t", (2, 128, 8), F32, "ExternalInput"),
        ("lf_lnb_t", (2, 128, 8), F32, "ExternalInput"),
        (yout_name, (D, T), F32, "ExternalOutput"),
        ("hshiftT", (D, T), BF16, "Internal"),
        ("xn2T", (D, T), BF16, "Internal"),
        ("w1bf", (D, 5632), BF16, "Internal"),
        ("w2bf", (2816, D), BF16, "Internal"),
    ]
    tens = {}
    for name, shape, dt, kind in decls:
        tens[name] = nc.dram_tensor(name, list(shape), dt, kind=kind).ap()
    ctx = ExitStack()
    with ctx:
        k, sb = _mk_k(nc, ctx, LFF_SB)
        st_lf_ffn(k, tens, sb, l, ya_name, yout_name)
        _finish(nc, k)
    return nc


def st_final(k, tens, sb, a_src, b_src, dst):
    st_done = [0] * DT
    for d in range(DT):
        b2 = d % 2
        aslc = sb["fa"][:, T * b2:T * (b2 + 1)]
        bslc = sb["fb"][:, T * b2:T * (b2 + 1)]
        w = [("ve", k.v("ve"))]
        if d >= 2:
            w.append(("d_rs", st_done[d - 2]))
        lda = k.dma("sync", aslc, tens[a_src][128 * d:128 * (d + 1), :], "d_a%d" % b2, wait=w)
        ldb = k.dma("sync", bslc, tens[b_src][128 * d:128 * (d + 1), :], "d_b%d" % b2, wait=[])
        vv = k.on("vector", lambda e, aslc=aslc, bslc=bslc: e.tensor_tensor(
            aslc, aslc, bslc, ALU.add),
            wait=[("d_a%d" % b2, lda), ("d_b%d" % b2, ldb), ("ve", k.v("ve"))], inc=("ve", 1))
        st_done[d] = k.dma("gpsimd", tens[dst][128 * d:128 * (d + 1), :], aslc, "d_rs",
                           wait=[("ve", vv)])
    k.barrier()


FIN_SB = [("fa", (128, 2 * T), F32), ("fb", (128, 2 * T), F32)]


def build_final_launch(a_name, b_name, out_name):
    nc = bass.Bass("TRN2", target_bir_lowering=False, debug=False)
    decls = [
        (a_name, (D, T), F32, "ExternalInput"),
        (b_name, (D, T), F32, "ExternalInput"),
        (out_name, (D, T), F32, "ExternalOutput"),
    ]
    tens = {}
    for name, shape, dt, kind in decls:
        tens[name] = nc.dram_tensor(name, list(shape), dt, kind=kind).ap()
    ctx = ExitStack()
    with ctx:
        k, sb = _mk_k(nc, ctx, FIN_SB)
        st_final(k, tens, sb, a_name, b_name, out_name)
        _finish(nc, k)
    return nc


def _mk_k(nc, ctx, sb_decls):
    k = K(nc, ctx)
    sb = {}
    seen = set()
    for nm, shape, dt in sb_decls:
        if nm in seen:
            continue
        seen.add(nm)
        sb[nm] = ctx.enter_context(nc.sbuf_tensor(nm, list(shape), dt))
    for nm, shape, dt in [("onesb", (128, 1), BF16), ("ones1f", (33, 128), F32),
                          ("c_eps", (128, 1), F32)]:
        sb[nm] = ctx.enter_context(nc.sbuf_tensor(nm, list(shape), dt))
    k.pb = [ctx.enter_context(nc.psum_tensor("pb%d" % i, [128, 512], F32))
            for i in range(8)]
    k.on("vector", lambda e: e.memset(sb["onesb"][:], 1.0), inc=("ve", 1))
    k.on("vector", lambda e: e.memset(sb["ones1f"][:], 1.0), inc=("ve", 1))
    k.on("vector", lambda e: e.memset(sb["c_eps"][:], 1e-5), inc=("ve", 1))
    return k, sb


def _finish(nc, k):
    waits = [(s, k.val[s]) for s in list(k.val) if s.startswith("d_") and k.val[s] > 0]
    k.q["sync"].append((None, waits, []))
    with nc.Block() as block:
        k.finalize(block)
    return nc


def build_l0attn_launch(kinds):
    nc = bass.Bass("TRN2", target_bir_lowering=False, debug=False)
    decls = [
        ("xn_bf", (D, T), BF16, "ExternalInput"),
        ("xT", (D, T), F32, "ExternalInput"),
        ("w_qkv", (D, 3072), F32, "ExternalInput"),
        ("b_qkv_t", (128, 24), F32, "ExternalInput"),
        ("b_qkv_raw", (1, 3072), F32, "ExternalInput"),
        ("w_out", (D, D), F32, "ExternalInput"),
        ("b_out_t", (128, 8), F32, "ExternalInput"),
        ("strips", (16, 128, 3072), F32, "ExternalInput"),
        ("rb_lo", (128, 16), F32, "ExternalInput"),
        ("rb_hi", (128, 16), F32, "ExternalInput"),
        ("ident", (128, 128), F32, "ExternalInput"),
        ("b_gate", (1, 8), F32, "ExternalInput"),
        ("x1p", (D, T), F32, "ExternalOutput"),
        ("otT", (D, T), BF16, "Internal"),
    ]
    tens = {}
    for name, shape, dt, kind in decls:
        tens[name] = nc.dram_tensor(name, list(shape), dt, kind=kind).ap()
    outer = ExitStack()
    with outer:
        k, sbc = _mk_common(nc, outer)
        for i, (sbl, fn) in enumerate([
                (L0A_SB, lambda sb: st_l0_attn(k, tens, sb, "xn_bf")),
                (L0P_SB, lambda sb: st_l0_outproj(k, tens, sb, "xT", "x1p"))]):
            ctx = ExitStack()
            sb = dict(sbc)
            seen = set()
            for nm, shape, dt in sbl:
                if nm in seen:
                    continue
                seen.add(nm)
                sb[nm] = ctx.enter_context(nc.sbuf_tensor("s%d_%s" % (i, nm), list(shape), dt))
            fn(sb)
            ctx.close()
        _finish(nc, k)
    return nc


def _mk_common(nc, outer):
    k = K(nc, outer)
    sbc = {}
    for nm, shape, dt in [("onesb", (128, 1), BF16), ("ones1f", (33, 128), F32),
                          ("c_eps", (128, 1), F32)]:
        sbc[nm] = outer.enter_context(nc.sbuf_tensor(nm, list(shape), dt))
    k.pb = [outer.enter_context(nc.psum_tensor("pb%d" % i, [128, 512], F32))
            for i in range(8)]
    k.on("vector", lambda e: e.memset(sbc["onesb"][:], 1.0), inc=("ve", 1))
    k.on("vector", lambda e: e.memset(sbc["ones1f"][:], 1.0), inc=("ve", 1))
    k.on("vector", lambda e: e.memset(sbc["c_eps"][:], 1e-5), inc=("ve", 1))
    return k, sbc


def build_ln_launch(src, dst_bf, dst_f32, kinds):
    nc = bass.Bass("TRN2", target_bir_lowering=False, debug=False)
    decls = [(src, (D, T), F32, kinds.get(src, "ExternalInput")),
             (dst_bf, (D, T), BF16, kinds.get(dst_bf, "ExternalOutput"))]
    if dst_f32:
        decls.append((dst_f32, (D, T), F32, kinds.get(dst_f32, "ExternalOutput")))
    tens = {}
    for name, shape, dt, kind in decls:
        tens[name] = nc.dram_tensor(name, list(shape), dt, kind=kind).ap()
    ctx = ExitStack()
    with ctx:
        k, sb = _mk_k(nc, ctx, LN_SB)
        st_layernorm(k, tens, sb, src, dst_bf, dst_f32)
        _finish(nc, k)
    return nc


FUSED_DECLS = [
    ("xT", (D, T), F32, "ExternalInput"),
    ("w_qkv", (D, 3072), F32, "ExternalInput"),
    ("b_qkv_t", (128, 24), F32, "ExternalInput"),
    ("b_qkv_raw", (1, 3072), F32, "ExternalInput"),
    ("w_out", (D, D), F32, "ExternalInput"),
    ("b_out_t", (128, 8), F32, "ExternalInput"),
    ("strips", (16, 128, 3072), F32, "ExternalInput"),
    ("rb_lo", (128, 16), F32, "ExternalInput"),
    ("rb_hi", (128, 16), F32, "ExternalInput"),
    ("ident", (128, 128), F32, "ExternalInput"),
    ("w_gate", (D, 8), F32, "ExternalInput"),
    ("b_gate", (1, 8), F32, "ExternalInput"),
    ("moe_w1", (8, D, D), F32, "ExternalInput"),
    ("moe_w2", (8, D, D), F32, "ExternalInput"),
    ("moe_b1c", (128, 64), F32, "ExternalInput"),
    ("moe_b2", (8, D), F32, "ExternalInput"),
    ("emask8", (8, 1024), F32, "ExternalInput"),
    ("p_Wq", (D, D), F32, "ExternalInput"), ("p_Wk", (D, D), F32, "ExternalInput"),
    ("p_Wv", (D, D), F32, "ExternalInput"), ("p_Wo", (D, D), F32, "ExternalInput"),
    ("p_bq_t", (128, 8), F32, "ExternalInput"), ("p_bk_t", (128, 8), F32, "ExternalInput"),
    ("p_bv_t", (128, 8), F32, "ExternalInput"), ("p_bo_t", (128, 8), F32, "ExternalInput"),
    ("p_bv_raw", (1, D), F32, "ExternalInput"),
    ("p_rm", (128, 256), F32, "ExternalInput"),
    ("lf_wqkv", (2, D, 1536), F32, "ExternalInput"),
    ("lf_qs", (2, 64), F32, "ExternalInput"),
    ("lf_ks", (2, 64), F32, "ExternalInput"),
    ("lf_wo", (2, 512, D), F32, "ExternalInput"),
    ("rot_cq", (64, T), F32, "ExternalInput"), ("rot_sq", (64, T), F32, "ExternalInput"),
    ("rot_ck", (64, T), F32, "ExternalInput"), ("rot_sk", (64, T), F32, "ExternalInput"),
    ("la_mask", (512, 1024), F32, "ExternalInput"),
    ("la_maskT", (128, 1408), F32, "ExternalInput"),
    ("lf_w1p", (2, D, 5632), F32, "ExternalInput"),
    ("lf_b1p_t", (2, 128, 44), F32, "ExternalInput"),
    ("lf_w2p", (2, 2816, D), F32, "ExternalInput"),
    ("lf_b2_t", (2, 128, 8), F32, "ExternalInput"),
    ("lf_lng_t", (2, 128, 8), F32, "ExternalInput"),
    ("lf_lnb_t", (2, 128, 8), F32, "ExternalInput"),
    ("outT", (D, T), F32, "ExternalOutput"),
    ("xn_bf", (D, T), BF16, "Internal"), ("xn_f32", (D, T), F32, "Internal"),
    ("x1p", (D, T), F32, "Internal"), ("x1", (D, T), F32, "Internal"),
    ("xn1_bf", (D, T), BF16, "Internal"), ("xn1_f32", (D, T), F32, "Internal"),
    ("x2p", (D, T), F32, "Internal"),
    ("y_a0", (D, T), F32, "Internal"), ("y_m0", (D, T), F32, "Internal"),
    ("y_a1", (D, T), F32, "Internal"), ("y_m1", (D, T), F32, "Internal"),
    ("otT", (D, T), BF16, "Internal"),
    ("hshiftT", (D, T), BF16, "Internal"), ("xn2T", (D, T), BF16, "Internal"),
    ("moe_w1bf", (8, D, D), BF16, "Internal"), ("moe_w2bf", (8, D, D), BF16, "Internal"),
    ("w1bf", (D, 5632), BF16, "Internal"), ("w2bf", (2816, D), BF16, "Internal"),
]


def build_fused():
    nc = bass.Bass("TRN2", target_bir_lowering=False, debug=False)
    tens = {}
    for name, shape, dt, kind in FUSED_DECLS:
        tens[name] = nc.dram_tensor(name, list(shape), dt, kind=kind).ap()
    outer = ExitStack()
    with outer:
        k = K(nc, outer)
        sbc = {}
        for nm, shape, dt in [("onesb", (128, 1), BF16), ("ones1f", (33, 128), F32),
                              ("c_eps", (128, 1), F32)]:
            sbc[nm] = outer.enter_context(nc.sbuf_tensor(nm, list(shape), dt))
        k.pb = [outer.enter_context(nc.psum_tensor("pb%d" % i, [128, 512], F32))
                for i in range(8)]
        k.on("vector", lambda e: e.memset(sbc["onesb"][:], 1.0), inc=("ve", 1))
        k.on("vector", lambda e: e.memset(sbc["ones1f"][:], 1.0), inc=("ve", 1))
        k.on("vector", lambda e: e.memset(sbc["c_eps"][:], 1e-5), inc=("ve", 1))
        stage_idx = [0]

        def stage(sb_decls, fn):
            si = stage_idx[0]
            stage_idx[0] += 1
            ctx = ExitStack()
            sb = dict(sbc)
            seen = set()
            for nm, shape, dt in sb_decls:
                if nm in seen:
                    continue
                seen.add(nm)
                sb[nm] = ctx.enter_context(
                    nc.sbuf_tensor("s%d_%s" % (si, nm), list(shape), dt))
            fn(sb)
            ctx.close()

        stage(LN_SB, lambda sb: st_layernorm(k, tens, sb, "xT", "xn_bf", "xn_f32"))
        stage(L0A_SB, lambda sb: st_l0_attn(k, tens, sb, "xn_bf"))
        stage(L0P_SB, lambda sb: st_l0_outproj(k, tens, sb, "xT", "x1p"))
        stage(MOE_SB, lambda sb: st_l0_ffn(k, tens, sb, "xn_bf", "x1p", "x1"))
        stage(LN_SB, lambda sb: st_layernorm(k, tens, sb, "x1", "xn1_bf", "xn1_f32"))
        stage(L1A_SB, lambda sb: st_l1_attn(k, tens, sb, "xn1_bf"))
        stage(L1P_SB, lambda sb: st_l1_outproj(k, tens, sb, "x1", "x2p"))
        lf_names = {0: ("xn1_f32", "y_a0", "y_m0"), 1: ("y_m0", "y_a1", "y_m1")}
        for l in range(2):
            yin, ya, ym = lf_names[l]
            stage(LFA2_SB, lambda sb, l=l, yin=yin: st_lf_attn2(k, tens, sb, l, yin))
            stage(LFW_SB, lambda sb, l=l, yin=yin, ya=ya: st_lf_wo(k, tens, sb, l, yin, ya))
            stage(LFF_SB, lambda sb, l=l, ya=ya, ym=ym: st_lf_ffn(k, tens, sb, l, ya, ym))
        stage(FIN_SB, lambda sb: st_final(k, tens, sb, "x2p", "y_m1", "outT"))
        _finish(nc, k)
    return nc


def kernel(**inputs):
    x = np.asarray(inputs["L0_attn_x"], np.float32)
    sh = _prep_shared(inputs)
    xT = [np.ascontiguousarray(x[b].T) for b in range(NCORE)]
    cores = list(range(NCORE))
    b_qkv_raw = np.ascontiguousarray(np.asarray(inputs["L0_attn_b_qkv"], np.float32).reshape(1, 3072))
    p_bv_raw = np.ascontiguousarray(np.asarray(inputs["L1_attn_bv"], np.float32).reshape(1, 1024))

    nc = build_fused()
    shared_keys = [name for name, _s, _d, kind in FUSED_DECLS
                   if kind == "ExternalInput" and name not in ("xT", "b_qkv_raw", "p_bv_raw")]
    maps = []
    for b in range(NCORE):
        m = {kk: sh[kk] for kk in shared_keys}
        m["xT"] = xT[b]
        m["b_qkv_raw"] = b_qkv_raw
        m["p_bv_raw"] = p_bv_raw
        maps.append(m)
    r = run_bass_kernel_spmd(nc, maps, core_ids=cores).results
    out = np.stack([np.asarray(r[b]["outT"], np.float32).T for b in range(NCORE)])
    return np.ascontiguousarray(out)


def kernel_multilaunch(**inputs):
    x = np.asarray(inputs["L0_attn_x"], np.float32)
    sh = _prep_shared(inputs)
    xT = [np.ascontiguousarray(x[b].T) for b in range(NCORE)]
    cores = list(range(NCORE))
    b_qkv_raw = np.ascontiguousarray(np.asarray(inputs["L0_attn_b_qkv"], np.float32).reshape(1, 3072))
    p_bv_raw = np.ascontiguousarray(np.asarray(inputs["L1_attn_bv"], np.float32).reshape(1, 1024))

    def run(nc, per_core, shared_keys):
        maps = []
        for b in range(NCORE):
            m = {kk: sh[kk] for kk in shared_keys}
            m.update(per_core[b])
            maps.append(m)
        return run_bass_kernel_spmd(nc, maps, core_ids=cores).results

    nc = build_ln_launch("xT", "xn_bf", "xn_f32", {})
    r = run(nc, [{"xT": xT[b]} for b in range(NCORE)], [])
    xn_bf = [r[b]["xn_bf"] for b in range(NCORE)]
    xn_f32 = [r[b]["xn_f32"] for b in range(NCORE)]
    nc = build_l0attn_launch({})
    r = run(nc, [{"xn_bf": xn_bf[b], "xT": xT[b], "b_qkv_raw": b_qkv_raw}
                 for b in range(NCORE)],
            ["w_qkv", "b_qkv_t", "w_out", "b_out_t", "strips", "rb_lo", "rb_hi",
             "ident", "b_gate"])
    x1p = [r[b]["x1p"] for b in range(NCORE)]
    nc = build_l0ffn_launch({})
    r = run(nc, [{"xn_bf": xn_bf[b], "xn_f32": xn_f32[b], "x1p": x1p[b]}
                 for b in range(NCORE)],
            ["w_gate", "b_gate", "moe_w1", "moe_w2", "moe_b1c", "moe_b2", "ident", "emask8"])
    x1 = [r[b]["x1"] for b in range(NCORE)]
    nc = build_ln_launch("x1", "xn1_bf", "xn1_f32", {})
    r = run(nc, [{"x1": x1[b]} for b in range(NCORE)], [])
    xn1_bf = [r[b]["xn1_bf"] for b in range(NCORE)]
    xn1_f32 = [r[b]["xn1_f32"] for b in range(NCORE)]
    nc = build_l1attn_launch({})
    r = run(nc, [{"xn1_bf": xn1_bf[b], "x1": x1[b], "p_bv_raw": p_bv_raw}
                 for b in range(NCORE)],
            ["p_Wq", "p_Wk", "p_Wv", "p_Wo", "p_bq_t", "p_bk_t", "p_bv_t", "p_bo_t", "p_rm"])
    x2p = [r[b]["x2p"] for b in range(NCORE)]
    lf_shared_a = ["lf_wqkv", "lf_qs", "lf_ks", "lf_wo", "rot_cq", "rot_sq",
                   "rot_ck", "rot_sk", "la_mask", "ident"]
    lf_shared_f = ["lf_w1p", "lf_b1p_t", "lf_w2p", "lf_b2_t", "lf_lng_t", "lf_lnb_t"]
    yin = xn1_f32
    for l in range(2):
        nc = build_lfattn_launch(l, "y_in", "y_a")
        r = run(nc, [{"y_in": yin[b]} for b in range(NCORE)], lf_shared_a)
        ya = [r[b]["y_a"] for b in range(NCORE)]
        nc = build_lfffn_launch(l, "y_a", "y_m")
        r = run(nc, [{"y_a": ya[b]} for b in range(NCORE)], lf_shared_f)
        yin = [r[b]["y_m"] for b in range(NCORE)]
    nc = build_final_launch("x2p", "y2", "outT")
    r = run(nc, [{"x2p": x2p[b], "y2": yin[b]} for b in range(NCORE)], [])
    out = np.stack([np.asarray(r[b]["outT"], np.float32).T for b in range(NCORE)])
    return np.ascontiguousarray(out)
